# version 16
# baseline (speedup 1.0000x reference)
import numpy as np
import ml_dtypes
from contextlib import ExitStack
import concourse.bass as bass
import concourse.mybir as mybir
from concourse.bass_utils import run_bass_kernel_spmd

F32 = mybir.dt.float32
BF16 = mybir.dt.bfloat16
I32 = mybir.dt.int32
AF = mybir.ActivationFunctionType
ALU = mybir.AluOpType
AX = mybir.AxisListType
BF = ml_dtypes.bfloat16


class CFG:
    T = 2048
    D = 2048
    H = 16
    HD = 128
    QL = 768; KVL = 512; NOPE = 128; ROPE = 64; VD = 128; QKD = 192
    F0 = 5632; F1 = 5504; LUT = 16
    EPS_LN = 1e-5; EPS_RMS = 1e-6; EPS_BIT = 1e-8; EPS_SM = 1e-8
    THETA = 10000.0


ENG_ATTR = {"pe": "tensor", "act": "scalar", "dve": "vector", "po": "gpsimd", "sp": "sync"}
ENGINES = ("pe", "act", "dve", "po", "sp")


class Prog:

    def __init__(self):
        self.ops = {e: [] for e in ENGINES}
        self.counts = {}
        self.sem_names = []
        self.pending_barrier = {}
        self.region_inject = {}

    def barrier(self, exclude=()):
        toks = [(s, v) for (s, v) in self.all_toks() if s not in exclude]
        self.pending_barrier = {e: list(toks) for e in ENGINES}

    def region(self, toks):
        self.region_inject = {e: list(toks) for e in ENGINES}

    def _reg(self, sem):
        if sem not in self.counts:
            self.counts[sem] = 0
            self.sem_names.append(sem)

    def tok(self, sem):
        return (sem, self.counts.get(sem, 0))

    def op(self, eng, fn, waits=(), inc=None, n=1):
        if eng in self.pending_barrier:
            waits = list(waits) + self.pending_barrier.pop(eng)
        if eng in self.region_inject:
            waits = list(waits) + self.region_inject.pop(eng)
        waits = tuple((s, v) for (s, v) in waits if s is not None and v > 0)
        for s, _ in waits:
            self._reg(s)
        tok = None
        if inc:
            self._reg(inc)
            self.counts[inc] += n
            tok = (inc, self.counts[inc])
        self.ops[eng].append((fn, waits, inc, n))
        return tok

    def dma(self, q, out, in_, waits=(), sem=None, transpose=False, slow_ok=False):
        assert q in ("sp", "act", "po")
        sem = sem or ("q_" + q)
        return self.op(
            q,
            (lambda e, o=out, i=in_, t=transpose, s=slow_ok:
             e.dma_start(out=o, in_=i, transpose=t, allow_slow_non_contiguous=s)),
            waits, sem, 16,
        )

    def all_toks(self):
        return [(s, self.counts[s]) for s in self.sem_names if self.counts[s] > 0]

    def emit(self, nc, es):
        sems = {name: es.enter_context(nc.semaphore(name)) for name in self.sem_names}
        block = es.enter_context(nc.Block())
        for eng in ENGINES:
            ops = self.ops[eng]
            if not ops:
                continue

            def make(ops=ops):
                def _(e):
                    waited = {}
                    for fn, waits, inc, n in ops:
                        for s, v in waits:
                            if waited.get(s, 0) < v:
                                e.wait_ge(sems[s], v)
                                waited[s] = v
                        inst = fn(e)
                        if inc:
                            inst.then_inc(sems[inc], n)
                return _

            getattr(block, ENG_ATTR[eng])(make())
        return sems


def host_constants(c: CFG):
    T = c.T
    consts = {}
    consts["c_id_bf"] = np.eye(128, dtype=BF)
    consts["c_ones_col_bf"] = np.ones((128, 1), dtype=BF)
    consts["c_ones_row_bf"] = np.ones((1, 128), dtype=BF)
    consts["c_ones_row_f32"] = np.ones((1, 128), dtype=np.float32)
    kk = np.arange(128)[:, None]
    jj = np.arange(896)[None, :]
    consts["c_trilw"] = (kk <= jj - 384).astype(BF)
    d = c.ROPE
    inv_freq = 1.0 / c.THETA ** (np.arange(0, d, 2, dtype=np.float32) / d)
    t = np.arange(T, dtype=np.float32)
    fr = np.outer(inv_freq, t).astype(np.float32)
    cos, sin = np.cos(fr).astype(np.float32), np.sin(fr).astype(np.float32)
    scale = np.float32(c.QKD ** (-0.5))
    A = np.concatenate([cos, cos], 0)
    Bm = np.concatenate([-sin, sin], 0)
    consts["c_rope_a_k"] = A.astype(BF)
    consts["c_rope_b_k"] = Bm.astype(BF)
    return consts


class Ctx:
    def __init__(self, nc, es, cfg, const_aps):
        self.nc = nc
        self.es = es
        self.c = cfg
        self.p = Prog()
        self.cap = const_aps
        self.psum = [es.enter_context(nc.psum_tensor(f"pb{i}", [128, 512], F32)) for i in range(8)]
        self.bank_free = [[] for _ in range(8)]
        self._uid = 0
        self.const_sb = {}
        self.scopes = []
        self.pool = {}
        self._swb = None
        self.sw_chain = []

    def sandwich_bufs(self):
        if self._swb is None:
            assert not self.scopes, "allocate sandwich bufs before any scope"
            D = self.c.D
            self._swb = dict(
                xt=[self.sb(f"swb_x{b}", [128, D], F32) for b in range(2)],
                ln=[self.sb(f"swb_l", [128, D], F32)] * 2,
                xl=[self.sb(f"swb_xl", [128, D], BF16)] * 2,
                yt=[self.sb(f"swb_y", [128, D], BF16)] * 2,

                st=self.sb("swb_st", [128, 12], F32),
            )
        return self._swb

    def uid(self):
        self._uid += 1
        return self._uid

    def sb(self, name, shape, dt):
        es = self.scopes[-1] if self.scopes else self.es
        side = "left" if self.scopes else "right"
        return es.enter_context(self.nc.sbuf_tensor(name, list(shape), dt, side=side))

    def sb_outer(self, name, shape, dt):
        return self.es.enter_context(self.nc.sbuf_tensor(name, list(shape), dt, side="right"))

    def push_scope(self):
        self.scopes.append(ExitStack())

    def pop_scope(self):
        self.scopes.pop().close()

    def strip_set(self, key, n=None):
        if n is None:
            n = self.c.D // 128
        u = self.uid()
        return [self.sb(f"pool_{key}{u}_{i}", [128, self.c.T], BF16) for i in range(n)]

    def bcast_f32(self, row_ap, bank, waits):
        p = self.p
        if not hasattr(self, "_bc") or self._bc is None:
            self._bc = (self.sb_outer("bc_hi", [1, 512], BF16),
                        self.sb_outer("bc_lo", [1, 512], BF16))
            self._bc_chain = None
        hi, lo = self._bc
        ones_row, or_tok = self.load_const("c_ones_row_bf")
        w = list(waits) + ([self._bc_chain] if self._bc_chain else [])
        t1 = p.op("dve", lambda e: e.tensor_copy(hi[0:1, :], row_ap), waits=w, inc="dve")
        t2 = p.op("dve", lambda e: e.tensor_tensor(lo[0:1, :], row_ap, hi[0:1, :], ALU.subtract),
                  waits=[t1], inc="dve")
        m1 = p.op("pe", lambda e: e.matmul(self.psum[bank][:], ones_row[:], hi[0:1, :],
                                           start=True, stop=False),
                  waits=[t1, or_tok] + (self.bank_free[bank] or []), inc="pe")
        m2 = p.op("pe", lambda e: e.matmul(self.psum[bank][:], ones_row[:], lo[0:1, :],
                                           start=False, stop=True),
                  waits=[t2], inc="pe")
        self._bc_chain = m2
        return m2

    def const_col(self, val):
        key = float(val)
        if key not in self.const_sb:
            t = self.sb_outer(f"ccol_{self._uid}_{len(self.const_sb)}", [128, 1], F32)
            self._uid += 1
            tok = self.p.op("dve", lambda e, t=t, v=key: e.memset(t[:], v), inc="dve")
            self.const_sb[key] = (t, tok)
        return self.const_sb[key]

    def load_const(self, name, scoped=False):
        if not scoped and name in self.const_sb:
            return self.const_sb[name]
        ap = self.cap[name]
        shape = list(ap.tensor.shape)
        t = (self.sb if scoped else self.sb_outer)(f"cs{self.uid()}_" + name, shape, ap.dtype)
        tok = self.p.dma("act", t[:], ap, sem=f"qc{self._uid}_" + name)
        if not scoped:
            self.const_sb[name] = (t, tok)
        return (t, tok)


def emit_sandwich(ctx, x_in, y_fm, x_out, xln_out, first, state, y_ready=(), write_xln=True):
    c, p = ctx.c, ctx.p
    T, D = c.T, c.D
    NT = T // 128
    u = ctx.uid()
    swb = ctx.sandwich_bufs()
    xt, ln, xl, yt, st = swb["xt"], swb["ln"], swb["xl"], swb["yt"], swb["st"]
    prev_chain = list(ctx.sw_chain)
    _seen = set()
    _real_op = p.op

    def _op(eng, fn, waits=(), inc=None, n=1):
        if eng not in _seen:
            _seen.add(eng)
            waits = list(waits) + prev_chain
        return _real_op(eng, fn, waits, inc, n)
    p = type("POpWrap", (), {"op": staticmethod(_op), "dma": ctx.p.dma, "tok": ctx.p.tok,
                             "all_toks": ctx.p.all_toks})()
    if first:
        state["negmu"] = [ctx.sb_outer(f"sw{u}_nm{i}", [128, 1], F32) for i in range(NT)]

    xs = [f"sw_xs{b}" for b in range(2)]
    ys = [f"sw_ys{b}" for b in range(2)]
    oss = [f"sw_os{b}" for b in range(2)]
    lss = [f"sw_ls{b}" for b in range(2)]
    last_use_x = [None, None]
    last_use_y = [None, None]
    last_store_o = [None, None]
    last_store_l = [None, None]
    sq_tok = None
    st_tok = None
    last_lt = None
    for i in range(NT):
        b = i % 2
        row = slice(i * 128, (i + 1) * 128)
        xtok = p.dma("sp", xt[b][:], x_in[row, :],
                     waits=(prev_chain if i < 2 else []) + ([last_use_x[b]] if last_use_x[b] else []),
                     sem=xs[b])
        if y_fm is not None:
            ytok = p.dma("sp", yt[b][:], y_fm[:, row],
                         waits=list(y_ready) + [t for t in last_use_y if t],
                         sem=ys[b], transpose=True)
            w_st = [st_tok] if st_tok else []
            t1 = p.op("dve", lambda e, b=b: e.tensor_reduce(st[:, 0:1], yt[b][:], AX.X, ALU.add, negate=True),
                      waits=[ytok] + w_st, inc="dve")
            t2 = p.op("act", lambda e, b=b: e.activation(xl[b][:], yt[b][:], AF.Square,
                                                         accum_out=st[:, 2:3]),
                      waits=[ytok, t1] + ([sq_tok] if sq_tok else []) +
                            [t for t in last_store_l if t], inc="act")
            t4 = p.op("dve", lambda e: e.tensor_scalar(st[:, 0:1], st[:, 0:1], 1.0 / D, None, ALU.mult),
                      waits=[t1], inc="dve")
            t5 = p.op("dve", lambda e: e.tensor_tensor(st[:, 4:5], st[:, 0:1], st[:, 0:1], ALU.mult),
                      waits=[t4], inc="dve")
            t6 = p.op("dve", lambda e: e.tensor_scalar(st[:, 3:4], st[:, 2:3], 1.0 / D, None, ALU.mult),
                      waits=[t2], inc="dve")
            t7 = p.op("dve", lambda e: e.tensor_tensor(st[:, 3:4], st[:, 3:4], st[:, 4:5], ALU.subtract),
                      waits=[t5, t6], inc="dve")
            epsc, epsc_tok = ctx.const_col(c.EPS_LN)
            t8 = p.op("act", lambda e, epsc=epsc: e.activation(st[:, 1:2], st[:, 3:4], AF.Ln, bias=epsc[:]),
                      waits=[t7, epsc_tok], inc="act")
            t9 = p.op("act", lambda e: e.activation(st[:, 1:2], st[:, 1:2], AF.Exp, scale=-0.5),
                      waits=[t8], inc="act")
            prev_store = last_store_o[0] if (last_store_o[0] and (not last_store_o[1] or last_store_o[0][1] > last_store_o[1][1])) else last_store_o[1]
            t10 = p.op("dve", lambda e, b=b: e.tensor_scalar(ln[b][:], yt[b][:], st[:, 0:1], st[:, 1:2],
                                                             ALU.add, ALU.mult),
                       waits=[t9] + ([prev_store] if prev_store else []), inc="dve")
            last_use_y[b] = t10
            t11 = p.op("dve", lambda e, b=b: e.tensor_tensor(ln[b][:], ln[b][:], xt[b][:], ALU.add),
                       waits=[t10, xtok], inc="dve")
            last_use_x[b] = t11
            sot = p.dma("sp", x_out[row, :], ln[b][:], waits=[t11], sem=oss[b])
            last_store_o[b] = sot
            xn, xn_tok = ln[b], t11
        else:
            xn, xn_tok = xt[b], xtok

        if not write_xln:
            continue
        if first:
            nm = state["negmu"][i]
            tm = p.op("dve", lambda e, xn=xn, nm=nm: e.tensor_reduce(nm[:], xn[:], AX.X, ALU.add, negate=True),
                      waits=[xn_tok], inc="dve")
            tm = p.op("dve", lambda e, nm=nm: e.tensor_scalar(nm[:], nm[:], 1.0 / D, None, ALU.mult),
                      waits=[tm], inc="dve")
        else:
            nm = state["negmu"][i]
            tm = state["negmu_tok"]
        s2 = p.op("act", lambda e, xn=xn, b=b: e.activation(xl[b][:], xn[:], AF.Square, accum_out=st[:, 7:8]),
                  waits=([xn_tok, t2] if y_fm is not None else [xn_tok]) +
                        [t for t in last_store_l if t] + ([st_tok] if st_tok else []),
                  inc="act")
        sq_tok = s2
        s3 = p.op("dve", lambda e, nm=nm: e.tensor_tensor(st[:, 8:9], nm[:], nm[:], ALU.mult),
                  waits=[tm], inc="dve")
        s4 = p.op("dve", lambda e: e.tensor_scalar(st[:, 9:10], st[:, 7:8], 1.0 / D, None, ALU.mult),
                  waits=[s2], inc="dve")
        s5 = p.op("dve", lambda e: e.tensor_tensor(st[:, 9:10], st[:, 9:10], st[:, 8:9], ALU.subtract),
                  waits=[s3, s4], inc="dve")
        epsc2, epsc2_tok = ctx.const_col(c.EPS_LN)
        s6 = p.op("act", lambda e, epsc2=epsc2: e.activation(st[:, 10:11], st[:, 9:10], AF.Ln, bias=epsc2[:]),
                  waits=[s5, epsc2_tok], inc="act")
        s7 = p.op("act", lambda e: e.activation(st[:, 10:11], st[:, 10:11], AF.Exp, scale=-0.5),
                  waits=[s6], inc="act")
        s8 = p.op("dve", lambda e, xn=xn, nm=nm, b=b: e.tensor_scalar(xl[b][:], xn[:], nm[:], st[:, 10:11],
                                                                      ALU.add, ALU.mult),
                  waits=[s7] + [t for t in last_store_l if t], inc="dve")
        st_tok = s8
        if y_fm is None:
            last_use_x[b] = s8
        lt = p.dma("sp", xln_out[row, :], xl[b][:], waits=[s8], sem=lss[b])
        last_store_l[b] = lt
        last_lt = lt
    if first:
        state["negmu_tok"] = p.tok("dve")
    ctx.sw_chain = [p.tok("dve"), p.tok("act"), p.tok("po")] +         [p.tok(s) for s in oss + lss + ys]
    if not write_xln:
        return [p.tok(oss[0]), p.tok(oss[1])]
    return [p.tok(lss[0]), p.tok(lss[1])]


def emit_load_xln_fm(ctx, xln_dram, ready_toks, dst_strips, reuse_waits=()):
    c, p = ctx.c, ctx.p
    T, D = c.T, c.D
    NS = D // 128
    u = ctx.uid()
    sem = f"xlnfm{u}"
    tok = None
    for i in range(NS):
        tok = p.dma("sp", dst_strips[i][:], xln_dram[:, i * 128:(i + 1) * 128],
                    waits=list(ready_toks) + list(reuse_waits), sem=sem, transpose=True)
    return (sem, tok[1])


def emit_l1_attn(ctx, xln_dram, xln_ready, w_qkv, w_out, y_fm, scr, y_guard=()):
    c, p = ctx.c, ctx.p
    T, D, H = c.T, c.D, c.H
    NS = D // 128
    NQ = T // 512
    NKT = T // 128
    u = ctx.uid()
    scale = float(c.HD ** (-0.5))
    SAMP_STEP = 8
    NSAMP = T // SAMP_STEP
    bf = ctx.bank_free

    idn, idn_tok = ctx.load_const("c_id_bf")
    ones_col, ocol_tok = ctx.load_const("c_ones_col_bf")
    ones_row, orow_tok = ctx.load_const("c_ones_row_bf")

    wq_bf, wo_bf = scr["l1_wqkv_bf"], scr["l1_wout_bf"]
    q_scr, v_scr = scr["l1_q"], scr["l1_v"]

    csem = f"l1c{u}"
    for j in range(12):
        p.dma("po", wq_bf[j * 512:(j + 1) * 512, :], w_qkv[j * 512:(j + 1) * 512, :], sem=csem)
    for j in range(4):
        ctok = p.dma("po", wo_bf[j * 512:(j + 1) * 512, :], w_out[j * 512:(j + 1) * 512, :], sem=csem)
    cast_all = (csem, ctok[1])

    xfm = ctx.strip_set("A")
    xfm_tok = emit_load_xln_fm(ctx, xln_dram, xln_ready, xfm,
                               reuse_waits=scr.pop("poolA_free", []))
    k_fm = ctx.strip_set("B")
    kfm_free = scr.pop("poolB_free", [])

    OC = 256
    wall = [ctx.sb(f"l1wa{u}_{b}", [128, NS * OC], BF16) for b in range(2)]
    wsem = [f"l1ws{u}_{b}" for b in range(2)]
    wall_use = [None, None]
    evb = [ctx.sb(f"l1ev{u}_{b}", [128, 512], BF16) for b in range(4)]
    ev_free = [None] * 4
    qs_sem = [f"l1qs{u}_{b}" for b in range(4)]
    vs_sem = [f"l1vs{u}_{b}" for b in range(4)]

    n_oc = 3 * D // OC
    for oc in range(n_oc):
        b = oc % 2
        o0 = oc * OC
        wtok = None
        for i in range(NS):
            wtok = p.dma("sp", wall[b][:, i * OC:(i + 1) * OC],
                         wq_bf[o0:o0 + OC, i * 128:(i + 1) * 128],
                         waits=[cast_all] + ([wall_use[b]] if wall_use[b] else []),
                         sem=wsem[b], transpose=True)
        wready = (wsem[b], wtok[1])
        if oc < 16:
            part_q = oc < 8
            for ho in range(2):
                h = (oc % 8) * 2 + ho
                for tc in range(NQ):
                    bi = (ho * NQ + tc) % 4
                    bank = ctx.psum[bi]
                    mm = None
                    for i in range(NS):
                        mm = p.op("pe", lambda e, bank=bank, b=b, i=i, ho=ho, tc=tc:
                                  e.matmul(bank[:], wall[b][:, i * OC + ho * 128:i * OC + (ho + 1) * 128],
                                           xfm[i][:, tc * 512:(tc + 1) * 512],
                                           start=(i == 0), stop=(i == NS - 1)),
                                  waits=[wready, xfm_tok] + (bf[bi] if i == 0 else []),
                                  inc="pe")
                    wall_use[b] = mm
                    if part_q:
                        es_ = bi
                        et = p.op("act", lambda e, bank=bank, es_=es_:
                                  e.activation(evb[es_][:], bank[:], AF.Copy, scale=scale),
                                  waits=[mm] + ([ev_free[es_]] if ev_free[es_] else []), inc="act")
                        bf[bi] = [et]
                        sx = p.dma("sp", q_scr[h * 128:(h + 1) * 128, tc * 512:(tc + 1) * 512],
                                   evb[es_][:], waits=[et], sem=qs_sem[es_])
                        ev_free[es_] = sx
                    else:
                        et = p.op("act", lambda e, bank=bank, h=h, tc=tc:
                                  e.activation(k_fm[h][:, tc * 512:(tc + 1) * 512], bank[:], AF.Copy),
                                  waits=[mm] + list(kfm_free), inc="act")
                        kfm_free = []
                        bf[bi] = [et]
        else:
            for tk in range(NKT):
                bi = tk % 4
                bank = ctx.psum[bi]
                mm = None
                for i in range(NS):
                    mm = p.op("pe", lambda e, bank=bank, b=b, i=i, tk=tk:
                              e.matmul(bank[:, 0:OC], xfm[i][:, tk * 128:(tk + 1) * 128],
                                       wall[b][:, i * OC:(i + 1) * OC],
                                       start=(i == 0), stop=(i == NS - 1)),
                              waits=[wready, xfm_tok] + (bf[bi] if i == 0 else []),
                              inc="pe")
                wall_use[b] = mm
                es_ = tk % 4
                et = p.op("act", lambda e, bank=bank, es_=es_:
                          e.activation(evb[es_][:, 0:OC], bank[:, 0:OC], AF.Copy),
                          waits=[mm] + ([ev_free[es_]] if ev_free[es_] else []), inc="act")
                bf[bi] = [et]
                sx = None
                h2 = (oc - 16) * 2
                for ho in range(2):
                    sx = p.dma("sp", v_scr[tk, h2 + ho], evb[es_][:, ho * 128:(ho + 1) * 128],
                               waits=[et], sem=vs_sem[es_])
                ev_free[es_] = sx
    qkv_done_pe = p.tok("pe")
    scat_done = [p.tok(s) for s in qs_sem] + [p.tok(s) for s in vs_sem]

    o_all = xfm
    pt = [ctx.sb(f"l1pt{u}_{i}", [128, 512], BF16) for i in range(3)]
    pt_use = [None] * 3
    qb_buf = [ctx.sb(f"l1qb{u}_{b}", [128, 512], BF16) for b in range(2)]
    qb_sem = [f"l1qbs{u}_{b}" for b in range(2)]
    qb_use = [None, None]
    vb_buf = [ctx.sb(f"l1vb{u}_{b}", [128, 128], BF16) for b in range(3)]
    vb_sem = [f"l1vbs{u}_{b}" for b in range(3)]
    vb_use = [None, None, None]
    nmcol = ctx.sb(f"l1nm{u}", [128, 4], F32)
    nmcol_bf = ctx.sb(f"l1nmb{u}", [128, 4], BF16)
    aug = ctx.sb(f"l1aug{u}", [1, 512], BF16)
    den = ctx.sb(f"l1den{u}", [1, 512], F32)
    rbc = ctx.sb(f"l1rbc{u}", [128, 512], F32)
    stb = [0, 1, 6]
    nm_use = None
    aug_use = None
    for h in range(H):
        for qb in range(NQ):
            bslot = (h * NQ + qb) % 2
            qtok = p.dma("sp", qb_buf[bslot][:], q_scr[h * 128:(h + 1) * 128, qb * 512:(qb + 1) * 512],
                         waits=scat_done + ([qb_use[bslot]] if qb_use[bslot] else []),
                         sem=qb_sem[bslot])
            red_toks = []
            for j in range(4):
                bi = stb[j % 2]
                bank = ctx.psum[bi]
                mm = p.op("pe", lambda e, bank=bank, bslot=bslot, j=j, h=h:
                          e.matmul(bank[:, 0:NSAMP], qb_buf[bslot][:, j * 128:(j + 1) * 128],
                                   k_fm[h][:, 0:T:SAMP_STEP], start=True, stop=True),
                          waits=[qtok] + bf[bi], inc="pe")
                rd = p.op("dve", lambda e, bank=bank, j=j:
                          e.tensor_reduce(nmcol[:, j:j + 1], bank[:, 0:NSAMP], AX.X, ALU.max, negate=True),
                          waits=[mm] + ([nm_use] if (j == 0 and nm_use) else []), inc="dve")
                bf[bi] = [rd]
                red_toks.append(rd)
            cv = p.op("dve", lambda e: e.tensor_copy(nmcol_bf[:], nmcol[:]),
                      waits=red_toks, inc="dve")
            as_tok = None
            for j in range(4):
                as_tok = p.op("pe", lambda e, j=j: e.matmul(ctx.psum[2][0:1, j * 128:(j + 1) * 128],
                                                            nmcol_bf[:, j:j + 1], idn[:],
                                                            start=True, stop=True),
                              waits=[cv, idn_tok] + (bf[2] if j == 0 else []),
                              inc="pe")
            nm_use = as_tok
            ag = p.op("act", lambda e: e.activation(aug[:], ctx.psum[2][0:1, :], AF.Copy),
                      waits=[as_tok] + ([aug_use] if aug_use else []), inc="act")
            bf[2] = [ag]
            den_mm = None
            o_mm = None
            for kt in range(NKT):
                bi = stb[kt % 3]
                bank = ctx.psum[bi]
                mm1 = p.op("pe", lambda e, bank=bank, h=h, kt=kt, bslot=bslot:
                           e.matmul(bank[:], k_fm[h][:, kt * 128:(kt + 1) * 128], qb_buf[bslot][:],
                                    start=True, stop=False),
                           waits=[qtok] + bf[bi], inc="pe")
                mm2 = p.op("pe", lambda e, bank=bank:
                           e.matmul(bank[:], ones_row[:], aug[:], start=False, stop=True),
                           waits=[ag, orow_tok], inc="pe")
                ps_ = kt % 3
                ex = p.op("act", lambda e, bank=bank, ps_=ps_:
                          e.activation(pt[ps_][:], bank[:], AF.Exp),
                          waits=[mm2] + ([pt_use[ps_]] if pt_use[ps_] else []), inc="act")
                bf[bi] = [ex]
                den_mm = p.op("pe", lambda e, kt=kt, ps_=ps_:
                              e.matmul(ctx.psum[3][0:1, :], ones_col[:], pt[ps_][:],
                                       start=(kt == 0), stop=(kt == NKT - 1)),
                              waits=[ex, ocol_tok] + (bf[3] if kt == 0 else []),
                              inc="pe")
                vslot = kt % 3
                vtok = p.dma("sp", vb_buf[vslot][:], v_scr[kt, h],
                             waits=scat_done + ([vb_use[vslot]] if vb_use[vslot] else []),
                             sem=vb_sem[vslot])
                o_mm = p.op("pe", lambda e, kt=kt, vslot=vslot, ps_=ps_:
                            e.matmul(ctx.psum[4][:], vb_buf[vslot][:], pt[ps_][:],
                                     start=(kt == 0), stop=(kt == NKT - 1)),
                            waits=[ex, vtok] + (bf[4] if kt == 0 else []),
                            inc="pe")
                vb_use[vslot] = o_mm
                pt_use[ps_] = o_mm
            d1 = p.op("act", lambda e: e.activation(den[:], ctx.psum[3][0:1, :], AF.Copy),
                      waits=[den_mm], inc="act")
            bf[3] = [d1]
            d2 = p.op("dve", lambda e: e.tensor_scalar(den[:], den[:], float(c.EPS_SM), None, ALU.add),
                      waits=[d1], inc="dve")
            d3 = p.op("dve", lambda e: e.reciprocal(den[:], den[:]), waits=[d2], inc="dve")
            bcm = ctx.bcast_f32(den[0:1, :], 5, [d3])
            rb = p.op("act", lambda e: e.activation(rbc[:], ctx.psum[5][:], AF.Copy),
                      waits=[bcm], inc="act")
            bf[5] = [rb]
            oe = p.op("dve", lambda e, h=h, qb=qb:
                      e.tensor_tensor(o_all[h][:, qb * 512:(qb + 1) * 512], ctx.psum[4][:], rbc[:], ALU.mult),
                      waits=[o_mm, rb, qkv_done_pe], inc="dve")
            bf[4] = [oe]
            aug_use = den_mm
    attn_done_dve = p.tok("dve")

    yev = [ctx.sb(f"l1ye{u}_{b}", [128, 512], BF16) for b in range(2)]
    yev_free = [None, None]
    ys_sem = [f"l1ys{u}_{b}" for b in range(2)]
    for dc in range(8):
        b = dc % 2
        wtok = None
        for hh in range(H):
            wtok = p.dma("sp", wall[b][:, hh * 256:(hh + 1) * 256],
                         wo_bf[dc * 256:(dc + 1) * 256, hh * 128:(hh + 1) * 128],
                         waits=[cast_all] + ([wall_use[b]] if wall_use[b] else []),
                         sem=wsem[b], transpose=True)
        wready = (wsem[b], wtok[1])
        for ds in range(2):
            for qb in range(NQ):
                bi = (ds * NQ + qb) % 4
                bank = ctx.psum[bi]
                mm = None
                for hh in range(H):
                    mm = p.op("pe", lambda e, bank=bank, b=b, hh=hh, ds=ds, qb=qb:
                              e.matmul(bank[:], wall[b][:, hh * 256 + ds * 128:hh * 256 + (ds + 1) * 128],
                                       o_all[hh][:, qb * 512:(qb + 1) * 512],
                                       start=(hh == 0), stop=(hh == H - 1)),
                              waits=[wready, attn_done_dve] + (bf[bi] if hh == 0 else []),
                              inc="pe")
                wall_use[b] = mm
                es_ = bi % 2
                et = p.op("act", lambda e, bank=bank, es_=es_:
                          e.activation(yev[es_][:], bank[:], AF.Copy),
                          waits=[mm] + ([yev_free[es_]] if yev_free[es_] else []), inc="act")
                bf[bi] = [et]
                sx = p.dma("sp", y_fm[(dc * 2 + ds) * 128:(dc * 2 + ds + 1) * 128, qb * 512:(qb + 1) * 512],
                           yev[es_][:], waits=[et] + list(y_guard), sem=ys_sem[es_])
                yev_free[es_] = sx
    scr["poolA_free"] = [p.tok("pe")]
    scr["poolB_free"] = [p.tok("pe")]
    return [p.tok(s) for s in ys_sem]


def build_l1_attn_launch(cfg, consts):
    nc = bass.Bass("TRN2", target_bir_lowering=False, debug=False)
    c = cfg
    T, D = c.T, c.D
    x_in = nc.dram_tensor("x_in", [T, D], F32, kind="ExternalInput").ap()
    w_qkv = nc.dram_tensor("w_qkv", [3 * D, D], F32, kind="ExternalInput").ap()
    w_out = nc.dram_tensor("w_out", [D, D], F32, kind="ExternalInput").ap()
    x_out = nc.dram_tensor("x_out", [T, D], F32, kind="ExternalOutput").ap()
    const_aps = {}
    for name, arr in consts.items():
        dt = {np.dtype(np.float32): F32, np.dtype(BF): BF16, np.dtype(np.int32): I32}[arr.dtype]
        const_aps[name] = nc.dram_tensor(name, list(arr.shape), dt, kind="ExternalInput").ap()

    xln = nc.dram_tensor("s_xln", [T, D], BF16).ap()
    y_fm = nc.dram_tensor("s_yfm", [D, T], BF16).ap()
    scr = {
        "l1_wqkv_bf": nc.dram_tensor("s_wqkv_bf", [3 * D, D], BF16).ap(),
        "l1_wout_bf": nc.dram_tensor("s_wout_bf", [D, D], BF16).ap(),
        "l1_q": nc.dram_tensor("s_q", [D, T], BF16).ap(),
        "l1_v": nc.dram_tensor("s_v", [T // 128, c.H, 128, 128], BF16).ap(),
    }
    es = ExitStack()
    with es:
        ctx = Ctx(nc, es, c, const_aps)
        state = {}
        ln_toks = emit_sandwich(ctx, x_in, None, None, xln, True, state)
        y_toks = emit_l1_attn(ctx, xln, ln_toks, w_qkv, w_out, y_fm, scr)
        emit_sandwich(ctx, x_in, y_fm, x_out, xln, False, state, y_ready=y_toks, write_xln=False)
        final = ctx.p.all_toks()
        ctx.p.op("sp", lambda e: e.nop(), waits=final)
        ctx.p.emit(nc, es)
    return nc



def emit_l0_attn(ctx, xln_dram, xln_ready, w, y_fm, scr, wbf=None, cast_ready=(), phase_limit=99, y_guard=()):
    c, p = ctx.c, ctx.p
    T, D, H = c.T, c.D, c.H
    NS = D // 128
    NQ = T // 512
    NKT = T // 128
    NTC = T // 512
    QL, KVL, NOPE, ROPE, VD = c.QL, c.KVL, c.NOPE, c.ROPE, c.VD
    u = ctx.uid()
    scale = float(c.QKD ** (-0.5))
    bf = ctx.bank_free

    ones_col, ocol_tok = ctx.load_const("c_ones_col_bf")
    ra, ra_tok = ctx.load_const("c_rope_a_k", scoped=True)
    rb, rb_tok = ctx.load_const("c_rope_b_k", scoped=True)
    trilw, trilw_tok = ctx.load_const("c_trilw", scoped=True)

    q_scr, k_scr, v_scr = scr["l0_q"], scr["l0_k"], scr["l0_v"]

    xfm = ctx.strip_set("A")
    xfm_tok = emit_load_xln_fm(ctx, xln_dram, xln_ready, xfm,
                               reuse_waits=scr.pop("poolA_free", []))

    wsrc_d = wbf if wbf is not None else w

    OC = 128
    wall = [ctx.sb(f"l0wa{u}_{b}", [128, NS * OC], BF16) for b in range(2)]
    wsem = [f"l0ws{u}_{b}" for b in range(2)]
    wall_use = [None, None]

    def wall_load(b, wsrc, o0, oc_w, nstrips, extra=()):
        tok = None
        for i in range(nstrips):
            tok = p.dma("sp", wall[b][:, i * OC:i * OC + oc_w],
                        wsrc[i * 128:(i + 1) * 128, o0:o0 + oc_w],
                        waits=list(extra) + list(cast_ready) + ([wall_use[b]] if wall_use[b] else []),
                        sem=wsem[b])
        return (wsem[b], tok[1])

    evb = [ctx.sb(f"l0ev{u}_{b}", [128, 512], BF16) for b in range(2)]
    ev_free = [None, None]
    sqq = ctx.sb(f"l0sqq{u}", [128, 512], BF16)
    sqq_use = None
    rbc = ctx.sb(f"l0rbc{u}", [128, 512], F32)
    rbckv = ctx.sb(f"l0rbckv{u}", [128, T], F32)
    ctx.push_scope()
    gq = ctx.sb(f"l0gq{u}", [128, QL // 128], F32)
    gkv = ctx.sb(f"l0gkv{u}", [128, KVL // 128], F32)
    gq_tok = p.dma("sp", gq[:], w["g_q"].rearrange("(s p) -> p s", p=128), sem=f"l0gq{u}", slow_ok=True)
    gkv_tok = p.dma("sp", gkv[:], w["g_kv"].rearrange("(s p) -> p s", p=128), sem=f"l0gkv{u}", slow_ok=True)
    ssq_q = ctx.sb(f"l0ssqq{u}", [1, T], F32)
    ssq_kv = ctx.sb(f"l0ssqkv{u}", [1, T], F32)
    rstdq = ssq_q
    rstdkv = ssq_kv
    rstdkv_col = ctx.sb(f"l0rskvc{u}", [128, NKT], F32)

    kv_g = [ctx.sb(f"l0kv{u}_{i}", [128, T], BF16) for i in range(KVL // 128)]
    qs_sem = [f"l0qs{u}_{b}" for b in range(2)]

    def lat_proj(wsrc, LO, gcol, gtok, ssq_row, dst_kind):
        nonlocal sqq_use
        no_stats = (phase_limit == 101)
        no_evac = (phase_limit == 102)
        NOC = LO // OC
        for oc in range(NOC):
            b = oc % 2
            wtok = wall_load(b, wsrc, oc * OC, OC, NS)
            for tc in range(NTC):
                bi = (oc * NTC + tc) % 2
                bank = ctx.psum[bi]
                sbank = 4 + tc
                mm = None
                for i in range(NS):
                    mm = p.op("pe", lambda e, bank=bank, b=b, i=i, tc=tc:
                              e.matmul(bank[:], wall[b][:, i * OC:(i + 1) * OC],
                                       xfm[i][:, tc * 512:(tc + 1) * 512],
                                       start=(i == 0), stop=(i == NS - 1)),
                              waits=[wtok, xfm_tok] + (bf[bi] if i == 0 else []),
                              inc="pe")
                wall_use[b] = mm
                if no_stats:
                    sget = smm = mm
                else:
                    sget = p.op("act", lambda e, bank=bank: e.activation(sqq[:], bank[:], AF.Square),
                                waits=[mm] + ([sqq_use] if sqq_use else []), inc="act")
                    smm = p.op("pe", lambda e, sbank=sbank, oc=oc, NOC=NOC:
                               e.matmul(ctx.psum[sbank][0:1, 0:512], ones_col[:], sqq[:],
                                        start=(oc == 0), stop=(oc == NOC - 1)),
                               waits=[sget, ocol_tok] + (bf[sbank] if oc == 0 else []),
                               inc="pe")
                    sqq_use = smm
                if no_evac:
                    ge = p.op("act", lambda e, bank=bank: e.activation(sqq[:], bank[:], AF.Copy),
                              waits=[mm, gtok] + ([sqq_use] if (no_stats and sqq_use) else []), inc="act")
                    if no_stats:
                        sqq_use = ge
                elif dst_kind == "q":
                    es_ = bi
                    ge = p.op("dve", lambda e, bank=bank, oc=oc, es_=es_:
                              e.tensor_scalar(evb[es_][:], bank[:], gcol[:, oc:oc + 1], None, ALU.mult),
                              waits=[mm, gtok, sget] + ([ev_free[es_]] if ev_free[es_] else []), inc="dve")
                    sx = p.dma("sp", q_scr[oc * 128:(oc + 1) * 128, tc * 512:(tc + 1) * 512],
                               evb[es_][:], waits=[ge], sem=qs_sem[es_])
                    ev_free[es_] = sx
                else:
                    ge = p.op("dve", lambda e, bank=bank, oc=oc, tc=tc:
                              e.tensor_scalar(kv_g[oc][:, tc * 512:(tc + 1) * 512], bank[:],
                                              gcol[:, oc:oc + 1], None, ALU.mult),
                              waits=[mm, gtok, sget], inc="dve")
                bf[bi] = [sget, ge]
                if oc == NOC - 1 and not no_stats:
                    se = p.op("act", lambda e, tc=tc, sbank=sbank, ssq_row=ssq_row:
                              e.activation(ssq_row[:, tc * 512:(tc + 1) * 512],
                                           ctx.psum[sbank][0:1, 0:512], AF.Copy),
                              waits=[smm], inc="act")
                    bf[sbank] = [se]

    if phase_limit <= 0:
        ctx.pop_scope()
        return [p.tok("pe")]
    lat_proj(wsrc_d["w_dq"], QL, gq, gq_tok, ssq_q, "q")
    lat_proj(wsrc_d["w_dkv"], KVL, gkv, gkv_tok, ssq_kv, "kv")
    qlat_stored = [p.tok(s) for s in qs_sem]
    kv_ready = p.tok("dve")
    if phase_limit <= 1 or phase_limit in (101, 102):
        ctx.pop_scope()
        return [p.tok("pe")]

    epsc, epsc_tok = ctx.const_col(c.EPS_RMS)
    act_now = p.tok("act")
    r1 = p.op("dve", lambda e: e.tensor_scalar(rstdq[:], ssq_q[:], 1.0 / QL, None, ALU.mult),
              waits=[act_now], inc="dve")
    r2 = p.op("act", lambda e: e.activation(rstdq[:], rstdq[:], AF.Ln, bias=epsc[0:1, :]),
              waits=[r1, epsc_tok], inc="act")
    r3 = p.op("act", lambda e: e.activation(rstdq[:], rstdq[:], AF.Exp, scale=-0.5),
              waits=[r2], inc="act")
    rq_tok = p.op("dve", lambda e: e.tensor_scalar(rstdq[:], rstdq[:], scale, None, ALU.mult),
                  waits=[r3], inc="dve")
    r5 = p.op("dve", lambda e: e.tensor_scalar(rstdkv[:], ssq_kv[:], 1.0 / KVL, None, ALU.mult),
              waits=[act_now], inc="dve")
    r6 = p.op("act", lambda e: e.activation(rstdkv[:], rstdkv[:], AF.Ln, bias=epsc[0:1, :]),
              waits=[r5, epsc_tok], inc="act")
    rkv_tok = p.op("act", lambda e: e.activation(rstdkv[:], rstdkv[:], AF.Exp, scale=-0.5),
                   waits=[r6], inc="act")
    onescal = ctx.sb(f"l0one1{u}", [1, 1], F32)
    o1t = p.op("dve", lambda e: e.memset(onescal[:], 1.0), inc="dve")
    rcol_prev = None
    for tk in range(NKT):
        mmr = p.op("pe", lambda e, tk=tk: e.matmul(ctx.psum[7][:, tk:tk + 1],
                                                   rstdkv[:, tk * 128:(tk + 1) * 128], onescal[:],
                                                   start=True, stop=True),
                   waits=[rkv_tok, o1t] + (bf[7] if tk == 0 else []) +
                         ([rcol_prev] if rcol_prev else []),
                   inc="pe")
        rcol_prev = p.op("act", lambda e, tk=tk: e.activation(rstdkv_col[:, tk:tk + 1],
                                                              ctx.psum[7][:, tk:tk + 1], AF.Copy),
                         waits=[mmr], inc="act")
    bf[7] = [rcol_prev]
    rcol_all = rcol_prev
    if phase_limit <= 2:
        ctx.pop_scope()
        return [p.tok("pe")]

    kpe_r = ctx.sb(f"l0kpe{u}", [65, T], BF16)
    kpe_tmp = ctx.sb(f"l0kpt{u}", [64, 512], F32)
    kpe_sw = ctx.sb(f"l0kps{u}", [64, 512], F32)
    on_tok = p.op("dve", lambda e: e.memset(kpe_r[64:65, :], 1.0), inc="dve")
    wtok = wall_load(0, wsrc_d["w_kr"], 0, 64, NS)
    rope_use = None
    for tc in range(NTC):
        bank = ctx.psum[2]
        mm = None
        for i in range(NS):
            mm = p.op("pe", lambda e, bank=bank, i=i, tc=tc:
                      e.matmul(bank[0:64, :], wall[0][:, i * OC:i * OC + 64],
                               xfm[i][:, tc * 512:(tc + 1) * 512],
                               start=(i == 0), stop=(i == NS - 1)),
                      waits=[wtok, xfm_tok] + (bf[2] if i == 0 else []),
                      inc="pe")
        wall_use[0] = mm
        e1 = p.op("act", lambda e, bank=bank: e.activation(kpe_tmp[:], bank[0:64, :], AF.Copy),
                  waits=[mm] + ([rope_use] if rope_use else []), inc="act")
        e2 = p.op("act", lambda e, bank=bank: e.activation(kpe_sw[0:32, :], bank[32:64, :], AF.Copy),
                  waits=[mm], inc="act")
        e3 = p.op("act", lambda e, bank=bank: e.activation(kpe_sw[32:64, :], bank[0:32, :], AF.Copy),
                  waits=[mm], inc="act")
        bf[2] = [e1, e2, e3]
        t1 = p.op("dve", lambda e, tc=tc: e.tensor_tensor(kpe_tmp[:], kpe_tmp[:],
                                                          ra[:, tc * 512:(tc + 1) * 512], ALU.mult),
                  waits=[e1, ra_tok], inc="dve")
        t2 = p.op("dve", lambda e, tc=tc: e.tensor_tensor(kpe_sw[:], kpe_sw[:],
                                                          rb[:, tc * 512:(tc + 1) * 512], ALU.mult),
                  waits=[e3, rb_tok], inc="dve")
        t3 = p.op("dve", lambda e, tc=tc: e.tensor_tensor(kpe_r[0:64, tc * 512:(tc + 1) * 512],
                                                          kpe_tmp[:], kpe_sw[:], ALU.add),
                  waits=[t1, t2], inc="dve")
        rope_use = t3
    kpe_all = p.tok("dve")
    if phase_limit <= 3:
        ctx.pop_scope()
        return [p.tok("pe")]

    NKV = KVL // 128
    kab_pos = ctx.sb(f"l0kab{u}", [128, H], F32)
    kab_nope = ctx.sb(f"l0kabn{u}", [128, H], BF16)
    kab_pe = ctx.sb(f"l0kabp{u}", [64, 1], BF16)
    kab_tmp = ctx.sb(f"l0kabt{u}", [128, 1], F32)
    ktile = ctx.sb(f"l0ktile{u}", [128, 512], BF16)
    ktile_use = None
    rbc_use = None
    ks_sem = f"l0ks{u}"

    kab_prev = {}
    rbckv_toks = []
    for tc in range(NTC):
        bm = ctx.bcast_f32(rstdkv[0:1, tc * 512:(tc + 1) * 512], 3, [rkv_tok])
        be = p.op("act", lambda e, tc=tc: e.activation(rbckv[:, tc * 512:(tc + 1) * 512],
                                                       ctx.psum[3][:], AF.Copy),
                  waits=[bm], inc="act")
        bf[3] = [be]
        rbckv_toks.append(be)
    for h in range(H):
        b = h % 2
        wtok = wall_load(b, wsrc_d["w_uk"], h * 128, OC, NKV)
        for tc in range(NTC):
            bi = (h * NTC + tc) % 2
            bank = ctx.psum[bi]
            mm = None
            for i in range(NKV):
                mm = p.op("pe", lambda e, bank=bank, b=b, i=i, tc=tc:
                          e.matmul(bank[:], wall[b][:, i * OC:(i + 1) * OC],
                                   kv_g[i][:, tc * 512:(tc + 1) * 512],
                                   start=(i == 0), stop=(i == NKV - 1)),
                          waits=[wtok, kv_ready] + (bf[bi] if i == 0 else []),
                          inc="pe")
            wall_use[b] = mm
            ke = p.op("dve", lambda e, bank=bank, tc=tc:
                      e.tensor_tensor(ktile[:], bank[:], rbckv[:, tc * 512:(tc + 1) * 512], ALU.mult),
                      waits=[mm, rbckv_toks[tc]] + (ktile_use or []), inc="dve")
            rbc_use = [ke]
            bf[bi] = [ke]
            sx = None
            for j in range(4):
                sx = p.dma("sp", k_scr[h, tc * 4 + j], ktile[:, j * 128:(j + 1) * 128],
                           waits=[ke], sem=ks_sem)
            ab = p.op("dve", lambda e: e.tensor_reduce(kab_tmp[:], ktile[:], AX.X, ALU.max,
                                                       apply_absolute_value=True),
                      waits=[ke], inc="dve")
            am = p.op("dve", lambda e, h=h, tc=tc:
                      (e.tensor_copy(kab_pos[:, h:h + 1], kab_tmp[:]) if tc == 0 else
                       e.tensor_tensor(kab_pos[:, h:h + 1], kab_pos[:, h:h + 1], kab_tmp[:], ALU.max)),
                      waits=[ab] + ([kab_prev.get(h)] if kab_prev.get(h) else []), inc="dve")
            kab_prev[h] = am
            ktile_use = [ab, sx]
        ng = p.op("dve", lambda e, h=h: e.tensor_scalar(kab_nope[:, h:h + 1], kab_pos[:, h:h + 1],
                                                        -1.0, None, ALU.mult),
                  waits=[kab_prev[h]], inc="dve")
        kab_prev[h] = ng
    k_spilled = p.tok(ks_sem)
    pb = p.op("dve", lambda e: e.tensor_reduce(kab_tmp[0:64, :], kpe_r[0:64, :], AX.X, ALU.max,
                                               apply_absolute_value=True, negate=True),
              waits=[kpe_all], inc="dve")
    pe_ab = p.op("dve", lambda e: e.tensor_copy(kab_pe[:], kab_tmp[0:64, :]),
                 waits=[pb], inc="dve")

    vs_sem = f"l0vs{u}"
    vtile_use = None
    for oc in range(H * VD // OC):
        b = oc % 2
        wtok = wall_load(b, wsrc_d["w_uv"], oc * OC, OC, NKV)
        for tk in range(NKT):
            bi = (oc * NKT + tk) % 2
            bank = ctx.psum[bi]
            mm = None
            for i in range(NKV):
                mm = p.op("pe", lambda e, bank=bank, b=b, i=i, tk=tk:
                          e.matmul(bank[:, 0:OC], kv_g[i][:, tk * 128:(tk + 1) * 128],
                                   wall[b][:, i * OC:(i + 1) * OC],
                                   start=(i == 0), stop=(i == NKV - 1)),
                          waits=[wtok, kv_ready] + (bf[bi] if i == 0 else []),
                          inc="pe")
            wall_use[b] = mm
            ve = p.op("dve", lambda e, bank=bank, tk=tk:
                      e.tensor_scalar(ktile[:, 0:OC], bank[:, 0:OC], rstdkv_col[:, tk:tk + 1],
                                      None, ALU.mult),
                      waits=[mm, rcol_all] + (vtile_use if vtile_use else (ktile_use or [])),
                      inc="dve")
            bf[bi] = [ve]
            sx = p.dma("sp", v_scr[tk, oc], ktile[:, 0:OC], waits=[ve], sem=vs_sem)
            vtile_use = [sx]
    v_spilled = p.tok(vs_sem)
    rbcq = rbckv
    kv_rbc_done = p.tok("dve")
    rbcq_toks = []
    for qb in range(NQ):
        bm = ctx.bcast_f32(rstdq[0:1, qb * 512:(qb + 1) * 512], 3, [rq_tok])
        be = p.op("act", lambda e, qb=qb: e.activation(rbcq[:, qb * 512:(qb + 1) * 512],
                                                       ctx.psum[3][:], AF.Copy),
                  waits=[bm, kv_rbc_done] + bf[3], inc="act")
        bf[3] = [be]
        rbcq_toks.append(be)
    ctx.pop_scope()
    p.region([(s2_, v2_) for (s2_, v2_) in p.all_toks() if not s2_.startswith("cc")])
    if phase_limit <= 4:
        ctx.pop_scope()
        return [p.tok("pe")]

    o_all = xfm
    qkv_done_pe = p.tok("pe")
    pt = [ctx.sb(f"l0pt{u}_{i}", [128, 512], BF16) for i in range(2)]
    pt_use = [None] * 2
    qn = ctx.sb(f"l0qn{u}", [128, 512], BF16)
    qaug = ctx.sb(f"l0qaug{u}", [65, 512], BF16)
    qpe_tmp, qpe_sw = kpe_tmp, kpe_sw
    absq = ctx.sb(f"l0absq{u}", [128, 512], BF16)
    absp = ctx.sb(f"l0absp{u}", [64, 512], BF16)
    den = ctx.sb(f"l0den{u}", [1, 512], F32)
    NQL = QL // 128
    qlb = [ctx.sb(f"l0qlb{u}_{b}", [128, NQL * 512], BF16) for b in range(2)]
    qlb_sem = [f"l0qlbs{u}_{b}" for b in range(2)]
    qlb_use = [None, None]
    _whb1 = ctx.sb(f"l0whb{u}", [128, NQL * 192], BF16)
    whb = [_whb1, _whb1]
    whb_sem = [f"l0whbs{u}", f"l0whbs{u}"]
    whb_use = [None, None]
    kb = [ctx.sb(f"l0kb{u}_{b}", [128, 128], BF16) for b in range(3)]
    kb_sem = [f"l0kbs{u}_{b}" for b in range(3)]
    kb_use = [None, None, None]
    vb = [ctx.sb(f"l0vb{u}_{b}", [128, 128], BF16) for b in range(3)]
    vb_sem = [f"l0vbs{u}_{b}" for b in range(3)]
    vb_use = [None, None, None]
    stb = [0, 1, 6]
    aug_use = None
    qn_use = None
    rope_use2 = rope_use
    absq_use = None
    orbc_use = rbc_use
    for h in range(H):
        bw = 0
        wtok = None
        for i in range(NQL):
            wtok = p.dma("sp", whb[bw][:, i * 192:(i + 1) * 192],
                         wsrc_d["w_uq"][i * 128:(i + 1) * 128, h * 192:(h + 1) * 192],
                         waits=list(cast_ready) + ([whb_use[bw]] if whb_use[bw] else []),
                         sem=whb_sem[bw])
        wready = (whb_sem[bw], wtok[1])
        for qb in range(NQ):
            bq = (h * NQ + qb) % 2
            qltok = None
            for i in range(NQL):
                qltok = p.dma("sp", qlb[bq][:, i * 512:(i + 1) * 512],
                              q_scr[i * 128:(i + 1) * 128, qb * 512:(qb + 1) * 512],
                              waits=qlat_stored + ([qlb_use[bq]] if qlb_use[bq] else []),
                              sem=qlb_sem[bq])
            qlready = (qlb_sem[bq], qltok[1])
            mmn = None
            for i in range(NQL):
                mmn = p.op("pe", lambda e, bq=bq, bw=bw, i=i:
                           e.matmul(ctx.psum[5][:], whb[bw][:, i * 192:i * 192 + 128],
                                    qlb[bq][:, i * 512:(i + 1) * 512],
                                    start=(i == 0), stop=(i == NQL - 1)),
                           waits=[wready, qlready] + (bf[5] if i == 0 else []), inc="pe")
            mmp = None
            for i in range(NQL):
                mmp = p.op("pe", lambda e, bq=bq, bw=bw, i=i:
                           e.matmul(ctx.psum[2][0:64, :], whb[bw][:, i * 192 + 128:(i + 1) * 192],
                                    qlb[bq][:, i * 512:(i + 1) * 512],
                                    start=(i == 0), stop=(i == NQL - 1)),
                           waits=[wready, qlready] + (bf[2] if i == 0 else []), inc="pe")
            whb_use[bw] = mmp
            qlb_use[bq] = mmp
            qe = p.op("dve", lambda e, qb=qb: e.tensor_tensor(qn[:], ctx.psum[5][:],
                                                              rbcq[:, qb * 512:(qb + 1) * 512], ALU.mult),
                      waits=[mmn, rbcq_toks[qb]] + ([qn_use] if qn_use else []), inc="dve")
            bf[5] = [qe]
            pe1 = p.op("act", lambda e: e.activation(qpe_tmp[:], ctx.psum[2][0:64, :], AF.Copy),
                       waits=[mmp] + ([rope_use2] if rope_use2 else []), inc="act")
            pe2 = p.op("act", lambda e: e.activation(qpe_sw[0:32, :], ctx.psum[2][32:64, :], AF.Copy),
                       waits=[mmp], inc="act")
            pe3 = p.op("act", lambda e: e.activation(qpe_sw[32:64, :], ctx.psum[2][0:32, :], AF.Copy),
                       waits=[mmp], inc="act")
            bf[2] = [pe1, pe2, pe3]
            u1 = p.op("dve", lambda e, qb=qb: e.tensor_tensor(qpe_tmp[:], qpe_tmp[:],
                                                              ra[:, qb * 512:(qb + 1) * 512], ALU.mult),
                      waits=[pe1, ra_tok], inc="dve")
            u2 = p.op("dve", lambda e, qb=qb: e.tensor_tensor(qpe_sw[:], qpe_sw[:],
                                                              rb[:, qb * 512:(qb + 1) * 512], ALU.mult),
                      waits=[pe3, rb_tok], inc="dve")
            u3 = p.op("dve", lambda e: e.tensor_tensor(qpe_tmp[:], qpe_tmp[:], qpe_sw[:], ALU.add),
                      waits=[u1, u2], inc="dve")
            u4 = p.op("dve", lambda e, qb=qb: e.tensor_tensor(qaug[0:64, :], qpe_tmp[:],
                                                              rbcq[0:64, qb * 512:(qb + 1) * 512],
                                                              ALU.mult),
                      waits=[u3, rbcq_toks[qb]] + ([aug_use] if aug_use else []), inc="dve")
            rope_use2 = u4
            a1 = p.op("act", lambda e: e.activation(absq[:], qn[:], AF.Abs),
                      waits=[qe] + ([absq_use] if absq_use else []), inc="act")
            a2 = p.op("act", lambda e: e.activation(absp[:], qaug[0:64, :], AF.Abs),
                      waits=[u4], inc="act")
            bm1 = p.op("pe", lambda e, h=h: e.matmul(ctx.psum[2][0:1, :], kab_nope[:, h:h + 1],
                                                     absq[:], start=True, stop=False),
                       waits=[a1, kab_prev[h], pe1, pe2, pe3], inc="pe")
            bm2 = p.op("pe", lambda e: e.matmul(ctx.psum[2][0:1, :], kab_pe[:], absp[:],
                                                start=False, stop=True),
                       waits=[a2, pe_ab], inc="pe")
            me = p.op("act", lambda e: e.activation(qaug[64:65, :], ctx.psum[2][0:1, :], AF.Copy),
                      waits=[bm2], inc="act")
            bf[2] = [me]
            aug_use = me
            absq_use = bm1
            nkt_q = 4 * qb + 4
            den_mm = None
            o_mm = None
            for kt in range(nkt_q):
                bi = stb[kt % 3]
                bank = ctx.psum[bi]
                kslot = kt % 3
                ktok = p.dma("sp", kb[kslot][:], k_scr[h, kt],
                             waits=[k_spilled] + ([kb_use[kslot]] if kb_use[kslot] else []),
                             sem=kb_sem[kslot])
                mm1 = p.op("pe", lambda e, bank=bank, kslot=kslot:
                           e.matmul(bank[:], kb[kslot][:], qn[:], start=True, stop=False),
                           waits=[ktok, qe] + bf[bi], inc="pe")
                mm2 = p.op("pe", lambda e, bank=bank, kt=kt:
                           e.matmul(bank[:], kpe_r[:, kt * 128:(kt + 1) * 128], qaug[:],
                                    start=False, stop=True),
                           waits=[me, u4, kpe_all, on_tok], inc="pe")
                kb_use[kslot] = mm1
                ps_ = kt % 2
                diag_off = kt * 128 - qb * 512
                if 0 <= diag_off < 512:
                    ex = p.op("act", lambda e, bank=bank: e.activation(sqq[:], bank[:], AF.Exp),
                              waits=[mm2, sqq_use], inc="act")
                    exm = p.op("dve", lambda e, ps_=ps_, off=diag_off:
                               e.tensor_tensor(pt[ps_][:], sqq[:],
                                               trilw[:, 384 - off:896 - off], ALU.mult),
                               waits=[ex, trilw_tok] + ([pt_use[ps_]] if pt_use[ps_] else []),
                               inc="dve")
                    sqq_use = exm
                    exf = exm
                else:
                    exf = p.op("act", lambda e, bank=bank, ps_=ps_:
                               e.activation(pt[ps_][:], bank[:], AF.Exp),
                               waits=[mm2] + ([pt_use[ps_]] if pt_use[ps_] else []), inc="act")
                bf[bi] = [exf]
                den_mm = p.op("pe", lambda e, kt=kt, ps_=ps_, last=(kt == nkt_q - 1):
                              e.matmul(ctx.psum[3][0:1, :], ones_col[:], pt[ps_][:],
                                       start=(kt == 0), stop=last),
                              waits=[exf, ocol_tok] + (bf[3] if kt == 0 else []), inc="pe")
                vslot = kt % 3
                vtok = p.dma("sp", vb[vslot][:], v_scr[kt, h],
                             waits=[v_spilled] + ([vb_use[vslot]] if vb_use[vslot] else []),
                             sem=vb_sem[vslot])
                o_mm = p.op("pe", lambda e, kt=kt, vslot=vslot, ps_=ps_, last=(kt == nkt_q - 1):
                            e.matmul(ctx.psum[4][:], vb[vslot][:], pt[ps_][:],
                                     start=(kt == 0), stop=last),
                            waits=[exf, vtok] + (bf[4] if kt == 0 else []), inc="pe")
                vb_use[vslot] = o_mm
                pt_use[ps_] = o_mm
            d1 = p.op("act", lambda e: e.activation(den[:], ctx.psum[3][0:1, :], AF.Copy),
                      waits=[den_mm], inc="act")
            bf[3] = [d1]
            d3 = p.op("dve", lambda e: e.reciprocal(den[:], den[:]), waits=[d1], inc="dve")
            bcm = ctx.bcast_f32(den[0:1, :], 3, [d3])
            rb2 = p.op("act", lambda e: e.activation(rbc[:], ctx.psum[3][:], AF.Copy),
                       waits=[bcm] + (orbc_use or []), inc="act")
            bf[3] = [rb2]
            oe = p.op("dve", lambda e, h=h, qb=qb:
                      e.tensor_tensor(o_all[h][:, qb * 512:(qb + 1) * 512], ctx.psum[4][:], rbc[:],
                                      ALU.mult),
                      waits=[o_mm, rb2, qkv_done_pe], inc="dve")
            bf[4] = [oe]
            orbc_use = [oe]
            qn_use = mm1
    attn_done_dve = p.tok("dve")
    if phase_limit <= 5:
        return [p.tok("pe")]

    WOC = 128
    yev = evb
    yev_free = list(ev_free)
    ys_sem = [f"l0ys{u}_{b}" for b in range(2)]
    for dc in range(D // WOC):
        b = dc % 2
        wtok = None
        for hh in range(H):
            wtok = p.dma("sp", wall[b][:, hh * WOC:(hh + 1) * WOC],
                         wsrc_d["w_o"][hh * 128:(hh + 1) * 128, dc * WOC:(dc + 1) * WOC],
                         waits=list(cast_ready) + ([wall_use[b]] if wall_use[b] else []), sem=wsem[b])
        wready = (wsem[b], wtok[1])
        for ds in range(WOC // 128):
            for qb in range(NQ):
                bi = (ds * NQ + qb) % 4
                bank = ctx.psum[bi]
                mm = None
                for hh in range(H):
                    mm = p.op("pe", lambda e, bank=bank, b=b, hh=hh, ds=ds, qb=qb:
                              e.matmul(bank[:], wall[b][:, hh * WOC + ds * 128:hh * WOC + (ds + 1) * 128],
                                       o_all[hh][:, qb * 512:(qb + 1) * 512],
                                       start=(hh == 0), stop=(hh == H - 1)),
                              waits=[wready, attn_done_dve] + (bf[bi] if hh == 0 else []),
                              inc="pe")
                wall_use[b] = mm
                es_ = bi % 2
                et = p.op("act", lambda e, bank=bank, es_=es_:
                          e.activation(yev[es_][:], bank[:], AF.Copy),
                          waits=[mm] + ([yev_free[es_]] if yev_free[es_] else []), inc="act")
                bf[bi] = [et]
                sx = p.dma("sp", y_fm[(dc * (WOC // 128) + ds) * 128:(dc * (WOC // 128) + ds + 1) * 128,
                                      qb * 512:(qb + 1) * 512],
                           yev[es_][:], waits=[et] + list(y_guard), sem=ys_sem[es_])
                yev_free[es_] = sx
    scr["poolA_free"] = [p.tok("pe")]
    return [p.tok(s) for s in ys_sem]


def build_l0_attn_launch(cfg, consts, phase_limit=99):
    nc = bass.Bass("TRN2", target_bir_lowering=False, debug=False)
    c = cfg
    T, D = c.T, c.D
    x_in = nc.dram_tensor("x_in", [T, D], F32, kind="ExternalInput").ap()
    wnames = {
        "w_dq": [D, c.QL], "g_q": [c.QL], "w_uq": [c.QL, c.H * c.QKD],
        "w_dkv": [D, c.KVL], "w_kr": [D, c.ROPE], "g_kv": [c.KVL],
        "w_uk": [c.KVL, c.H * c.NOPE], "w_uv": [c.KVL, c.H * c.VD], "w_o": [c.H * c.VD, D],
    }
    w = {n: nc.dram_tensor(n, s, F32, kind="ExternalInput").ap() for n, s in wnames.items()}
    x_out = nc.dram_tensor("x_out", [T, D], F32, kind="ExternalOutput").ap()
    const_aps = {}
    for name, arr in consts.items():
        dt = {np.dtype(np.float32): F32, np.dtype(BF): BF16, np.dtype(np.int32): I32}[arr.dtype]
        const_aps[name] = nc.dram_tensor(name, list(arr.shape), dt, kind="ExternalInput").ap()
    xln = nc.dram_tensor("s_xln", [T, D], BF16).ap()
    y_fm = nc.dram_tensor("s_yfm", [D, T], BF16).ap()
    scr = {
        "l0_q": nc.dram_tensor("s_q", [c.QL, T], BF16).ap(),
        "l0_k": nc.dram_tensor("s_k", [c.H, T // 128, 128, 128], BF16).ap(),
        "l0_v": nc.dram_tensor("s_v", [T // 128, c.H, 128, 128], BF16).ap(),
    }
    wl0_bf = {n: nc.dram_tensor("s_bf_" + n, s, BF16).ap() for n, s in wnames.items()
              if n not in ("g_q", "g_kv")}
    es = ExitStack()
    with es:
        ctx = Ctx(nc, es, c, const_aps)
        for nm2, ap2 in wl0_bf.items():
            rows = wnames[nm2][0]
            r0 = 0
            while r0 < rows:
                rr = min(512, rows - r0)
                ctx.p.dma("po", ap2[r0:r0 + rr, :], w[nm2][r0:r0 + rr, :], sem="l0cast")
                r0 += rr
        l0c = ctx.p.tok("l0cast")
        state = {}
        ln_toks = emit_sandwich(ctx, x_in, None, None, xln, True, state)
        y_toks = emit_l0_attn(ctx, xln, ln_toks, w, y_fm, scr, wbf=wl0_bf, cast_ready=[l0c],
                              phase_limit=phase_limit)
        if phase_limit > 6:
            emit_sandwich(ctx, x_in, y_fm, x_out, xln, False, state, y_ready=y_toks, write_xln=False)
        final = ctx.p.all_toks()
        ctx.p.op("sp", lambda e: e.nop(), waits=final)
        ctx.p.emit(nc, es)
    return nc

def emit_gated_ffn(ctx, xln_dram, xln_ready, w1f, w2f, w3f, act_fn, F, y_fm, y3p, scr,
                   w_ready=(), y_guard=()):
    c, p = ctx.c, ctx.p
    T, D = c.T, c.D
    NS = D // 128
    NTC = T // 512
    NF = F // 128
    u = ctx.uid()
    bf = ctx.bank_free

    xfm = ctx.strip_set("A")
    xfm_tok = emit_load_xln_fm(ctx, xln_dram, xln_ready, xfm)
    hid = ctx.strip_set("H", 11)

    OCW = 256
    wall = [[ctx.sb(f"gf{u}_wa{wi}{b}", [128, NS * OCW], BF16) for b in range(2)] for wi in range(2)]
    wsem = [["gf_ws%d%d" % (wi, b) for b in range(2)] for wi in range(2)]
    wall_use = [[None, None], [None, None]]
    wall3 = [ctx.sb(f"gf{u}_w3{b}", [128, 11 * 512], BF16) for b in range(2)]
    w3sem = ["gf_w3s%d" % b for b in range(2)]
    wall3_use = [None, None]
    evs = ctx.sb(f"gf{u}_ev", [128, 512], BF16)
    evs_use = None
    ybuf = [ctx.sb(f"gf{u}_yb{b}", [128, 512], F32) for b in range(2)]
    ybuf_use = [None, None]
    ypin = [ctx.sb(f"gf{u}_yp{b}", [128, 512], F32) for b in range(2)]
    ypin_sem = ["gf_yps" + str(b) for b in range(2)]
    ypin_use = [None, None]
    ybf = ctx.sb(f"gf{u}_ybf", [128, 512], BF16)
    ybf_use = None
    ys_sem = "gf_ys"
    yp_sem = "gf_yo"

    quarters = []
    base = 0
    for qi in range(4):
        nq = (NF + 3 - qi) // 4
        quarters.append((base, nq))
        base += nq
    assert base == NF

    hb_toks = [None] * 11
    for qi, (qbase, qn) in enumerate(quarters):
        oc0 = qbase * 128
        oc1 = (qbase + qn) * 128
        chunks = []
        o0 = oc0
        while o0 < oc1:
            ow = min(OCW, oc1 - o0)
            chunks.append((o0, ow))
            o0 += ow
        wtoks = {}

        def load_chunk(ci):
            o0, ow = chunks[ci]
            b = ci % 2
            for wi in range(2):
                wf = (w1f, w2f)[wi]
                tok = None
                for i in range(NS):
                    tok = p.dma("sp", wall[wi][b][:, i * OCW:i * OCW + ow],
                                wf(o0, i * 128, ow, 128),
                                waits=list(w_ready) +
                                      ([wall_use[wi][b]] if wall_use[wi][b] else []),
                                sem=wsem[wi][b], transpose=True)
                wtoks[(ci, wi)] = (wsem[wi][b], tok[1])

        load_chunk(0)
        for ci, (o0, ow) in enumerate(chunks):
            if ci + 1 < len(chunks):
                load_chunk(ci + 1)
            b = ci % 2
            now = ow // 128
            w1r, w2r = wtoks[(ci, 0)], wtoks[(ci, 1)]
            mm_last = [None, None]
            for so in range(now):
                j = (o0 - qbase * 128) // 128 + so
                for tc in range(NTC):
                    b1, b2 = (tc % 2) * 2, (tc % 2) * 2 + 1
                    mm1 = mm2 = None
                    for i in range(NS):
                        mm1 = p.op("pe", lambda e, b1=b1, i=i, tc=tc, so=so, b=b:
                                   e.matmul(ctx.psum[b1][:],
                                            wall[0][b][:, i * OCW + so * 128:i * OCW + (so + 1) * 128],
                                            xfm[i][:, tc * 512:(tc + 1) * 512],
                                            start=(i == 0), stop=(i == NS - 1)),
                                   waits=[w1r, xfm_tok] + (bf[b1] if i == 0 else []), inc="pe")
                    for i in range(NS):
                        mm2 = p.op("pe", lambda e, b2=b2, i=i, tc=tc, so=so, b=b:
                                   e.matmul(ctx.psum[b2][:],
                                            wall[1][b][:, i * OCW + so * 128:i * OCW + (so + 1) * 128],
                                            xfm[i][:, tc * 512:(tc + 1) * 512],
                                            start=(i == 0), stop=(i == NS - 1)),
                                   waits=[w2r, xfm_tok] + (bf[b2] if i == 0 else []), inc="pe")
                    mm_last = [mm1, mm2]
                    a1 = p.op("act", lambda e, b1=b1: e.activation(evs[:], ctx.psum[b1][:], act_fn),
                              waits=[mm1] + ([evs_use] if evs_use else []), inc="act")
                    bf[b1] = [a1]
                    hv = p.op("dve", lambda e, b2=b2, j=j, tc=tc:
                              e.tensor_tensor(hid[j][:, tc * 512:(tc + 1) * 512], evs[:],
                                              ctx.psum[b2][:], ALU.mult),
                              waits=[a1, mm2] + ([hb_toks[j]] if (tc == 0 and hb_toks[j]) else []),
                              inc="dve")
                    bf[b2] = [hv]
                    evs_use = hv
                    hb_toks[j] = hv
            wall_use[0][b], wall_use[1][b] = mm_last
        w3toks = {}

        def load_w3(dcg):
            b = dcg % 2
            tok = None
            for j in range(qn):
                tok = p.dma("sp", wall3[b][:, j * 512:(j + 1) * 512],
                            w3f(dcg * 512, (qbase + j) * 128, 512, 128),
                            waits=list(w_ready) + ([wall3_use[b]] if wall3_use[b] else []),
                            sem=w3sem[b], transpose=True)
            w3toks[dcg] = (w3sem[b], tok[1])

        load_w3(0)
        for dcg in range(4):
            if dcg + 1 < 4:
                load_w3(dcg + 1)
            b = dcg % 2
            wr = w3toks[dcg]
            mml = None
            for ds in range(4):
                dc = dcg * 4 + ds
                for tc in range(NTC):
                    bi = 4 + (ds * NTC + tc) % 2
                    mm = None
                    for j in range(qn):
                        mm = p.op("pe", lambda e, bi=bi, j=j, tc=tc, ds=ds, qn=qn, b=b:
                                  e.matmul(ctx.psum[bi][:],
                                           wall3[b][:, j * 512 + ds * 128:j * 512 + (ds + 1) * 128],
                                           hid[j][:, tc * 512:(tc + 1) * 512],
                                           start=(j == 0), stop=(j == qn - 1)),
                                  waits=[wr, hb_toks[j]] + (bf[bi] if j == 0 else []), inc="pe")
                    mml = mm
                    row = slice(dc * 128, (dc + 1) * 128)
                    colc = slice(tc * 512, (tc + 1) * 512)
                    yslot = tc % 2
                    if qi == 0:
                        ye = p.op("act", lambda e, bi=bi, yslot=yslot:
                                  e.activation(ybuf[yslot][:], ctx.psum[bi][:], AF.Copy),
                                  waits=[mm] + ([ybuf_use[yslot]] if ybuf_use[yslot] else []),
                                  inc="act")
                        bf[bi] = [ye]
                        sx = p.dma("sp", y3p[row, colc], ybuf[yslot][:], waits=[ye], sem=yp_sem)
                        ybuf_use[yslot] = sx
                    else:
                        ptok = p.dma("sp", ypin[yslot][:], y3p[row, colc],
                                     waits=[p.tok(yp_sem)] +
                                           ([ypin_use[yslot]] if ypin_use[yslot] else []),
                                     sem=ypin_sem[yslot])
                        if qi < 3:
                            ye = p.op("dve", lambda e, bi=bi, yslot=yslot:
                                      e.tensor_tensor(ybuf[yslot][:], ctx.psum[bi][:],
                                                      ypin[yslot][:], ALU.add),
                                      waits=[mm, ptok] + ([ybuf_use[yslot]] if ybuf_use[yslot] else []),
                                      inc="dve")
                            bf[bi] = [ye]
                            ypin_use[yslot] = ye
                            sx = p.dma("sp", y3p[row, colc], ybuf[yslot][:], waits=[ye], sem=yp_sem)
                            ybuf_use[yslot] = sx
                        else:
                            ye = p.op("dve", lambda e, bi=bi, yslot=yslot:
                                      e.tensor_tensor(ybf[:], ctx.psum[bi][:], ypin[yslot][:],
                                                      ALU.add),
                                      waits=[mm, ptok] + ([ybf_use] if ybf_use else []), inc="dve")
                            bf[bi] = [ye]
                            ypin_use[yslot] = ye
                            sx = p.dma("sp", y_fm[row, colc], ybf[:], waits=[ye] + list(y_guard),
                                       sem=ys_sem)
                            ybf_use = sx
            wall3_use[b] = mml
    return [p.tok(ys_sem)]


def build_ffn_launch(cfg, consts, act_fn, F):
    nc = bass.Bass("TRN2", target_bir_lowering=False, debug=False)
    c = cfg
    T, D = c.T, c.D
    x_in = nc.dram_tensor("x_in", [T, D], F32, kind="ExternalInput").ap()
    w1T = nc.dram_tensor("w1_bf", [F, D], BF16, kind="ExternalInput").ap()
    w2T = nc.dram_tensor("w2_bf", [F, D], BF16, kind="ExternalInput").ap()
    w3T = nc.dram_tensor("w3_bf", [D, F], BF16, kind="ExternalInput").ap()
    x_out = nc.dram_tensor("x_out", [T, D], F32, kind="ExternalOutput").ap()
    const_aps = {}
    for name, arr in consts.items():
        dt = {np.dtype(np.float32): F32, np.dtype(BF): BF16, np.dtype(np.int32): I32}[arr.dtype]
        const_aps[name] = nc.dram_tensor(name, list(arr.shape), dt, kind="ExternalInput").ap()
    xln = nc.dram_tensor("s_xln", [T, D], BF16).ap()
    y_fm = nc.dram_tensor("s_yfm", [D, T], BF16).ap()
    y3p = nc.dram_tensor("s_y3p", [D, T], F32).ap()
    es = ExitStack()
    with es:
        ctx = Ctx(nc, es, c, const_aps)
        state = {}
        ln_toks = emit_sandwich(ctx, x_in, None, None, xln, True, state)
        w1f = lambda o0, d0, no, nd: w1T[o0:o0 + no, d0:d0 + nd]
        w2f = lambda o0, d0, no, nd: w2T[o0:o0 + no, d0:d0 + nd]
        w3f = lambda d0, h0, nd, nh: w3T[d0:d0 + nd, h0:h0 + nh]
        y_toks = emit_gated_ffn(ctx, xln, ln_toks, w1f, w2f, w3f, act_fn, F, y_fm, y3p, {})
        emit_sandwich(ctx, x_in, y_fm, x_out, xln, False, state, y_ready=y_toks, write_xln=False)
        final = ctx.p.all_toks()
        ctx.p.op("sp", lambda e: e.nop(), waits=final)
        ctx.p.emit(nc, es)
    return nc

COLS_W = 2048 // 8
ROWS_W3 = 2048 // 8
N_ELEM_W = 5632 * 2048


def _prep_tiles(cfg):
    return [
        ("w1s", "wq1", 5632 // 128, [(0, COLS_W)], 0),
        ("w2s", "wq2", 5632 // 128, [(0, COLS_W)], 1),
        ("w3s", "wq3", ROWS_W3 // 128, [(0, 2048), (2048, 2048), (4096, 1536)], 2),
    ]


def build_prep_a(cfg):
    nc = bass.Bass("TRN2", target_bir_lowering=False, debug=False)
    w1s = nc.dram_tensor("w1s", [5632, COLS_W], F32, kind="ExternalInput").ap()
    w2s = nc.dram_tensor("w2s", [5632, COLS_W], F32, kind="ExternalInput").ap()
    w3s = nc.dram_tensor("w3s", [ROWS_W3, 5632], F32, kind="ExternalInput").ap()
    gpart = nc.dram_tensor("gpart", [1, 3], F32, kind="ExternalOutput").ap()
    es = ExitStack()
    with es:
        ctx = Ctx(nc, es, cfg, {})
        emit_prep_a(ctx, {"w1s": w1s, "w2s": w2s, "w3s": w3s}, gpart)
        ctx.p.op("sp", lambda e: e.nop(), waits=ctx.p.all_toks())
        ctx.p.emit(nc, es)
    return nc


def emit_prep_a(ctx, w, gpart):
    p = ctx.p
    u = ctx.uid()
    buf = [ctx.sb(f"pa{u}_b{b}", [128, 512], F32) for b in range(2)]
    bsem = [f"pa{u}_s{b}" for b in range(2)]
    buse = [None, None]
    acc = ctx.sb(f"pa{u}_acc", [128, 3], F32)
    red = ctx.sb(f"pa{u}_red", [128, 1], F32)
    onesf = ctx.sb(f"pa{u}_ones", [128, 1], F32)
    gp = ctx.sb(f"pa{u}_gp", [1, 3], F32)
    ot = p.op("po", lambda e: e.memset(onesf[:], 1.0), inc="po")
    zt = p.op("po", lambda e: e.memset(acc[:], 0.0), inc="po")
    srcs = [(w["w1s"], 5632 // 128, [(0, COLS_W)], 0),
            (w["w2s"], 5632 // 128, [(0, COLS_W)], 1),
            (w["w3s"], ROWS_W3 // 128, [(i * 512, 512) for i in range(11)], 2)]
    red_use = None
    acc_use = zt
    for src, ntile, colspans, j in srcs:
        for it in range(ntile):
            for c0, cw in colspans:
                b = it % 2
                tok = p.dma("act", buf[b][:, 0:cw], src[it * 128:(it + 1) * 128, c0:c0 + cw],
                            waits=([buse[b]] if buse[b] else []), sem=bsem[b])
                r = p.op("dve", lambda e, b=b, cw=cw:
                         e.tensor_reduce(red[:], buf[b][:, 0:cw], AX.X, ALU.add,
                                         apply_absolute_value=True),
                         waits=[tok] + ([red_use] if red_use else []), inc="dve")
                buse[b] = r
                a = p.op("dve", lambda e, j=j: e.tensor_tensor(acc[:, j:j + 1], acc[:, j:j + 1],
                                                               red[:], ALU.add),
                         waits=[r, acc_use], inc="dve")
                red_use = a
                acc_use = a
    prev = ctx.bank_free[0] or []
    mm = None
    for j in range(3):
        mm = p.op("pe", lambda e, j=j: e.matmul(ctx.psum[0][0:1, j:j + 1], acc[:, j:j + 1],
                                                onesf[:], start=True, stop=True),
                  waits=[acc_use, ot] + prev + ([mm] if mm else []), inc="pe")
    ev = p.op("act", lambda e: e.activation(gp[:], ctx.psum[0][0:1, 0:3], AF.Copy),
              waits=[mm], inc="act")
    ctx.bank_free[0] = [ev]
    st = p.dma("sp", gpart, gp[:], waits=[ev], sem=f"pa{u}_st")
    return st


def build_prep_b(cfg):
    nc = bass.Bass("TRN2", target_bir_lowering=False, debug=False)
    w1s = nc.dram_tensor("w1s", [5632, COLS_W], F32, kind="ExternalInput").ap()
    w2s = nc.dram_tensor("w2s", [5632, COLS_W], F32, kind="ExternalInput").ap()
    w3s = nc.dram_tensor("w3s", [ROWS_W3, 5632], F32, kind="ExternalInput").ap()
    gparts = nc.dram_tensor("gparts", [8, 3], F32, kind="ExternalInput").ap()
    lut = nc.dram_tensor("lut", [1, 16], F32, kind="ExternalInput").ap()
    gis = nc.dram_tensor("gis", [5504, COLS_W], I32, kind="ExternalInput").ap()
    uis = nc.dram_tensor("uis", [5504, COLS_W], I32, kind="ExternalInput").ap()
    dis = nc.dram_tensor("dis", [ROWS_W3, 5504], I32, kind="ExternalInput").ap()
    wq1 = nc.dram_tensor("wq1", [5632, COLS_W], BF16, kind="ExternalOutput").ap()
    wq2 = nc.dram_tensor("wq2", [5632, COLS_W], BF16, kind="ExternalOutput").ap()
    wq3 = nc.dram_tensor("wq3", [ROWS_W3, 5632], BF16, kind="ExternalOutput").ap()
    gw = nc.dram_tensor("gw", [5504, COLS_W], BF16, kind="ExternalOutput").ap()
    uw = nc.dram_tensor("uw", [5504, COLS_W], BF16, kind="ExternalOutput").ap()
    dw = nc.dram_tensor("dw", [ROWS_W3, 5504], BF16, kind="ExternalOutput").ap()
    es = ExitStack()
    with es:
        ctx = Ctx(nc, es, cfg, {})
        emit_prep_b(ctx, {"w1s": w1s, "w2s": w2s, "w3s": w3s, "gis": gis, "uis": uis,
                          "dis": dis}, gparts, lut,
                    {"wq1": wq1, "wq2": wq2, "wq3": wq3, "gw": gw, "uw": uw, "dw": dw})
        ctx.p.op("sp", lambda e: e.nop(), waits=ctx.p.all_toks())
        ctx.p.emit(nc, es)
    return nc


def emit_prep_b(ctx, w, gparts, lut, outs, gamma_pre=None, gamma_wait=()):
    c, p = ctx.c, ctx.p
    u = ctx.uid()
    bf = ctx.bank_free
    onesr = ctx.sb(f"pb{u}_or", [1, 128], F32)
    ones8 = ctx.sb(f"pb{u}_o8", [8, 1], F32)
    ot = p.op("po", lambda e: e.memset(onesr[:], 1.0), inc="po")
    o8 = p.op("po", lambda e: e.memset(ones8[:], 1.0), inc="po")
    if gamma_pre is None:
        gpt = ctx.sb(f"pb{u}_gp", [8, 3], F32)
        gptok = p.dma("act", gpt[:], gparts, sem=f"pb{u}_gp")
    lutr = ctx.sb(f"pb{u}_lut", [1, 16], F32)
    luttok = p.dma("act", lutr[:], lut, sem=f"pb{u}_lut")
    lutc = ctx.sb(f"pb{u}_lutc", [128, 16], F32)
    mm = p.op("pe", lambda e: e.matmul(ctx.psum[0][:, 0:16], onesr[:], lutr[:],
                                       start=True, stop=True),
              waits=[ot, luttok] + (bf[0] or []), inc="pe")
    lc = p.op("act", lambda e: e.activation(lutc[:], ctx.psum[0][:, 0:16], AF.Copy),
              waits=[mm], inc="act")
    bf[0] = [lc]
    grow = ctx.sb(f"pb{u}_grow", [1, 6], F32)
    if gamma_pre is None:
        gprev = None
        for j in range(3):
            gm = p.op("pe", lambda e, j=j: e.matmul(ctx.psum[1][0:1, j:j + 1], gpt[:, j:j + 1],
                                                    ones8[:], start=True, stop=True),
                      waits=[gptok, o8] + (bf[1] if j == 0 else []) + ([gprev] if gprev else []),
                      inc="pe")
            gprev = gm
        ge = p.op("act", lambda e: e.activation(grow[0:1, 0:3], ctx.psum[1][0:1, 0:3], AF.Copy),
                  waits=[gprev], inc="act")
        bf[1] = [ge]
    else:
        ge = p.dma("act", grow[0:1, 0:3], gamma_pre, waits=list(gamma_wait), sem=f"pb{u}_gpre")
    g2 = p.op("po", lambda e: e.tensor_scalar(grow[0:1, 0:3], grow[0:1, 0:3],
                                              0.5 / N_ELEM_W, 0.5 * c.EPS_BIT,
                                              ALU.mult, ALU.add),
              waits=[ge], inc="po")
    g3 = p.op("po", lambda e: e.tensor_scalar(grow[0:1, 3:6], grow[0:1, 0:3], -1.0, None, ALU.mult),
              waits=[g2], inc="po")
    thr = ctx.sb(f"pb{u}_thr", [128, 6], F32)
    bm = p.op("pe", lambda e: e.matmul(ctx.psum[2][:, 0:6], onesr[:], grow[:],
                                       start=True, stop=True),
              waits=[g3, ot] + (bf[2] or []), inc="pe")
    bt = p.op("act", lambda e: e.activation(thr[:], ctx.psum[2][:, 0:6], AF.Copy),
              waits=[bm], inc="act")
    bf[2] = [bt]
    buf = [ctx.sb(f"pb{u}_b{b}", [128, 512], F32) for b in range(2)]
    bsem = [f"pb{u}_s{b}" for b in range(2)]
    buse = [None, None]
    qa = ctx.sb(f"pb{u}_qa", [128, 512], BF16)
    qb2 = ctx.sb(f"pb{u}_qb", [128, 512], BF16)
    qo = ctx.sb(f"pb{u}_qo", [128, 512], BF16)
    osem = f"pb{u}_os"
    qo_use = None
    qa_use = None
    qsrcs = [(w["w1s"], outs["wq1"], 5632 // 128, [(0, COLS_W)], 0),
             (w["w2s"], outs["wq2"], 5632 // 128, [(0, COLS_W)], 1),
             (w["w3s"], outs["wq3"], ROWS_W3 // 128, [(i * 512, 512) for i in range(11)], 2)]
    for src, dst, ntile, colspans, j in qsrcs:
        for it in range(ntile):
            for c0, cw in colspans:
                b = it % 2
                tok = p.dma("act", buf[b][:, 0:cw], src[it * 128:(it + 1) * 128, c0:c0 + cw],
                            waits=([buse[b]] if buse[b] else []), sem=bsem[b])
                t1 = p.op("po", lambda e, b=b, cw=cw, j=j:
                          e.tensor_scalar(qa[:, 0:cw], buf[b][:, 0:cw], thr[:, j:j + 1],
                                          None, ALU.is_ge),
                          waits=[tok, bt] + ([qa_use] if qa_use else []), inc="po")
                t2 = p.op("po", lambda e, b=b, cw=cw, j=j:
                          e.tensor_scalar(qb2[:, 0:cw], buf[b][:, 0:cw], thr[:, 3 + j:4 + j],
                                          None, ALU.is_le),
                          waits=[tok, bt, t1], inc="po")
                buse[b] = t2
                t3 = p.op("po", lambda e, cw=cw: e.tensor_tensor(qo[:, 0:cw], qa[:, 0:cw],
                                                                 qb2[:, 0:cw], ALU.subtract),
                          waits=[t2] + ([qo_use] if qo_use else []), inc="po")
                qa_use = t3
                st = p.dma("sp", dst[it * 128:(it + 1) * 128, c0:c0 + cw], qo[:, 0:cw],
                           waits=[t3], sem=osem)
                qo_use = st
    ib = [ctx.sb(f"pb{u}_i{b}", [128, 704], I32) for b in range(2)]
    isem = [f"pb{u}_is{b}" for b in range(2)]
    iuse = [None, None]
    ibf = ctx.sb(f"pb{u}_ibf", [128, 704], BF16)
    macc = [ctx.sb(f"pb{u}_m{b}", [128, 704], BF16) for b in range(2)]
    mtile = ctx.sb(f"pb{u}_mt", [128, 704], BF16)
    ibf_use = None
    m_use = None
    gsrcs = [(w["gis"], outs["gw"], 5504 // 128, [(0, COLS_W)]),
             (w["uis"], outs["uw"], 5504 // 128, [(0, COLS_W)]),
             (w["dis"], outs["dw"], ROWS_W3 // 128, [(i * 688, 688) for i in range(8)])]
    for src, dst, ntile, colspans in gsrcs:
        for it in range(ntile):
            for c0, cw in colspans:
                b = it % 2
                tok = p.dma("act", ib[b][:, 0:cw], src[it * 128:(it + 1) * 128, c0:c0 + cw],
                            waits=([iuse[b]] if iuse[b] else []), sem=isem[b])
                cv = p.op("po", lambda e, b=b, cw=cw: e.tensor_copy(ibf[:, 0:cw], ib[b][:, 0:cw]),
                          waits=[tok] + ([ibf_use] if ibf_use else []), inc="po")
                iuse[b] = cv
                prev = None
                for j in range(16):
                    am = j % 2
                    if j == 0:
                        prev = p.op("po", lambda e, cw=cw, j=j, am=am:
                                    e.tensor_scalar(macc[am][:, 0:cw], ibf[:, 0:cw],
                                                    float(j), lutc[:, j:j + 1],
                                                    ALU.is_equal, ALU.mult),
                                    waits=[cv, lc] + ([m_use] if m_use else []), inc="po")
                    else:
                        mt = p.op("po", lambda e, cw=cw, j=j:
                                  e.tensor_scalar(mtile[:, 0:cw], ibf[:, 0:cw],
                                                  float(j), lutc[:, j:j + 1],
                                                  ALU.is_equal, ALU.mult),
                                  waits=[cv, lc, prev], inc="po")
                        prev = p.op("po", lambda e, cw=cw, am=am:
                                    e.tensor_tensor(macc[am][:, 0:cw], macc[1 - am][:, 0:cw],
                                                    mtile[:, 0:cw], ALU.add),
                                    waits=[mt], inc="po")
                ibf_use = prev
                st = p.dma("sp", dst[it * 128:(it + 1) * 128, c0:c0 + cw],
                           macc[1][:, 0:cw], waits=[prev], sem=osem)
                m_use = st
    return osem

def _pad_rows(a, rows):
    if a.shape[0] == rows:
        return np.ascontiguousarray(a)
    out = np.zeros((rows,) + a.shape[1:], a.dtype)
    out[:a.shape[0]] = a
    return out


def kernel(L0_attn_x, L0_attn_w_dq, L0_attn_g_q, L0_attn_w_uq, L0_attn_w_dkv,
           L0_attn_w_kr, L0_attn_g_kv, L0_attn_w_uk, L0_attn_w_uv, L0_attn_w_o,
           L0_ffn_w1, L0_ffn_w2, L0_ffn_w3, L1_attn_w_qkv, L1_attn_w_out,
           L1_ffn_lut, L1_ffn_gate_idx, L1_ffn_up_idx, L1_ffn_down_idx):
    c = CFG
    n = 8
    cores = list(range(n))
    consts = host_constants(c)
    x = np.ascontiguousarray(np.asarray(L0_attn_x, dtype=np.float32))

    c = CFG
    n = 8
    cores = list(range(n))
    consts = host_constants(c)
    x = np.ascontiguousarray(np.asarray(L0_attn_x, dtype=np.float32))
    cw, r3 = 2048 // n, 2048 // n
    wl0 = {"w_dq": np.asarray(L0_attn_w_dq), "g_q": np.asarray(L0_attn_g_q),
           "w_uq": np.asarray(L0_attn_w_uq), "w_dkv": np.asarray(L0_attn_w_dkv),
           "w_kr": np.asarray(L0_attn_w_kr), "g_kv": np.asarray(L0_attn_g_kv),
           "w_uk": np.asarray(L0_attn_w_uk), "w_uv": np.asarray(L0_attn_w_uv),
           "w_o": np.asarray(L0_attn_w_o)}
    lutr = np.ascontiguousarray(np.asarray(L1_ffn_lut, dtype=np.float32).reshape(1, 16))
    nc = build_fused(c, consts, use_collectives=True)
    in_maps = []
    for i in range(n):
        im = {"x_in": x[i], **wl0, "w_qkv": np.asarray(L1_attn_w_qkv),
              "w_out": np.asarray(L1_attn_w_out), **consts,
              "w1s": np.ascontiguousarray(np.asarray(L0_ffn_w1)[:, i * cw:(i + 1) * cw]),
              "w2s": np.ascontiguousarray(np.asarray(L0_ffn_w2)[:, i * cw:(i + 1) * cw]),
              "w3s": np.ascontiguousarray(np.asarray(L0_ffn_w3)[i * r3:(i + 1) * r3]),
              "lut": lutr,
              "gis": np.ascontiguousarray(np.asarray(L1_ffn_gate_idx)[:, i * cw:(i + 1) * cw]),
              "uis": np.ascontiguousarray(np.asarray(L1_ffn_up_idx)[:, i * cw:(i + 1) * cw]),
              "dis": np.ascontiguousarray(np.asarray(L1_ffn_down_idx)[i * r3:(i + 1) * r3])}
        in_maps.append(im)
    res = run_bass_kernel_spmd(nc, in_maps, core_ids=cores)
    return np.stack([np.asarray(res.results[i]["x_out"]) for i in range(n)], 0)


SLAB_ROWS = 5632 + 5632 + 5504 + 5504
SLAB_OFF = {"wq1": 0, "wq2": 5632, "gw": 11264, "uw": 16768}


def build_fused(cfg, consts, use_collectives=True):
    nc = bass.Bass("TRN2", target_bir_lowering=False, debug=False)
    c = cfg
    T, D = c.T, c.D
    cores = list(range(8))
    x_in = nc.dram_tensor("x_in", [T, D], F32, kind="ExternalInput").ap()
    x_out = nc.dram_tensor("x_out", [T, D], F32, kind="ExternalOutput").ap()
    wl0names = {
        "w_dq": [D, c.QL], "g_q": [c.QL], "w_uq": [c.QL, c.H * c.QKD],
        "w_dkv": [D, c.KVL], "w_kr": [D, c.ROPE], "g_kv": [c.KVL],
        "w_uk": [c.KVL, c.H * c.NOPE], "w_uv": [c.KVL, c.H * c.VD], "w_o": [c.H * c.VD, D],
    }
    wl0 = {n: nc.dram_tensor(n, s, F32, kind="ExternalInput").ap() for n, s in wl0names.items()}
    w_qkv = nc.dram_tensor("w_qkv", [3 * D, D], F32, kind="ExternalInput").ap()
    w_out = nc.dram_tensor("w_out", [D, D], F32, kind="ExternalInput").ap()
    const_aps = {}
    for name, arr in consts.items():
        dt = {np.dtype(np.float32): F32, np.dtype(BF): BF16, np.dtype(np.int32): I32}[arr.dtype]
        const_aps[name] = nc.dram_tensor(name, list(arr.shape), dt, kind="ExternalInput").ap()

    xln = nc.dram_tensor("s_xln", [T, D], BF16).ap()
    y_fm = nc.dram_tensor("s_yfm", [D, T], BF16).ap()
    y3p = nc.dram_tensor("s_y3p", [D, T], F32).ap()
    x1 = nc.dram_tensor("s_x1", [T, D], F32).ap()
    x2 = nc.dram_tensor("s_x2", [T, D], F32).ap()
    x3 = nc.dram_tensor("s_x3", [T, D], F32).ap()
    scr0 = {
        "l0_q": nc.dram_tensor("s_l0q", [c.QL, T], BF16).ap(),
        "l0_k": nc.dram_tensor("s_l0k", [c.H, T // 128, 128, 128], BF16).ap(),
        "l0_v": nc.dram_tensor("s_l0v", [T // 128, c.H, 128, 128], BF16).ap(),
    }
    wl0_bf = {n: nc.dram_tensor("s_bf_" + n, s, BF16).ap() for n, s in wl0names.items()
              if n not in ("g_q", "g_kv")}
    scr1 = {
        "l1_wqkv_bf": nc.dram_tensor("s_wqkv_bf", [3 * D, D], BF16).ap(),
        "l1_wout_bf": nc.dram_tensor("s_wout_bf", [D, D], BF16).ap(),
        "l1_q": nc.dram_tensor("s_l1q", [D, T], BF16).ap(),
        "l1_v": nc.dram_tensor("s_l1v", [T // 128, c.H, 128, 128], BF16).ap(),
    }

    if use_collectives:
        w1s = nc.dram_tensor("w1s", [5632, COLS_W], F32, kind="ExternalInput").ap()
        w2s = nc.dram_tensor("w2s", [5632, COLS_W], F32, kind="ExternalInput").ap()
        w3s = nc.dram_tensor("w3s", [ROWS_W3, 5632], F32, kind="ExternalInput").ap()
        lut = nc.dram_tensor("lut", [1, 16], F32, kind="ExternalInput").ap()
        gis = nc.dram_tensor("gis", [5504, COLS_W], I32, kind="ExternalInput").ap()
        uis = nc.dram_tensor("uis", [5504, COLS_W], I32, kind="ExternalInput").ap()
        dis = nc.dram_tensor("dis", [ROWS_W3, 5504], I32, kind="ExternalInput").ap()
        gpart = nc.dram_tensor("s_gpart", [1, 3], F32).ap()
        gsum = nc.dram_tensor("s_gsum", [1, 3], F32, addr_space="Shared").ap()
        slab_l = nc.dram_tensor("s_slab_l", [SLAB_ROWS, COLS_W], BF16).ap()
        wq3_l = nc.dram_tensor("s_wq3l", [ROWS_W3, 5632], BF16).ap()
        dw_l = nc.dram_tensor("s_dwl", [ROWS_W3, 5504], BF16).ap()
        slab_g = nc.dram_tensor("s_slab_g", [8 * SLAB_ROWS, COLS_W], BF16, addr_space="Shared").ap()
        wq3_g = nc.dram_tensor("s_wq3g", [D, 5632], BF16, addr_space="Shared").ap()
        dw_g = nc.dram_tensor("s_dwg", [D, 5504], BF16, addr_space="Shared").ap()
    else:
        slab_g = nc.dram_tensor("slab_g", [8 * SLAB_ROWS, COLS_W], BF16, kind="ExternalInput").ap()
        wq3_g = nc.dram_tensor("wq3_g", [D, 5632], BF16, kind="ExternalInput").ap()
        dw_g = nc.dram_tensor("dw_g", [D, 5504], BF16, kind="ExternalInput").ap()

    def slab_wf(key):
        off = SLAB_OFF[key]
        def wf(o0, d0, no, nd, off=off):
            r, cc = d0 // COLS_W, d0 % COLS_W
            base = r * SLAB_ROWS + off
            return slab_g[base + o0:base + o0 + no, cc:cc + nd]
        return wf

    es = ExitStack()
    with es:
        ctx = Ctx(nc, es, c, const_aps)
        ctx.sandwich_bufs()
        p = ctx.p
        for nm2, ap2 in wl0_bf.items():
            rows = wl0names[nm2][0]
            r0 = 0
            while r0 < rows:
                rr = min(512, rows - r0)
                p.dma("po", ap2[r0:r0 + rr, :], wl0[nm2][r0:r0 + rr, :], sem="l0cast")
                r0 += rr
        l0cast_tok = p.tok("l0cast")
        ffn_ready = []
        if use_collectives:
            ctx.push_scope()
            st_a = emit_prep_a(ctx, {"w1s": w1s, "w2s": w2s, "w3s": w3s}, gpart)
            cc0 = p.op("po", lambda e: e.collective_compute(
                "AllReduce", ALU.add, replica_groups=[cores], ins=[gpart], outs=[gsum]),
                waits=[st_a], inc="cc0")
            osem = emit_prep_b(
                ctx, {"w1s": w1s, "w2s": w2s, "w3s": w3s, "gis": gis, "uis": uis, "dis": dis},
                None, lut,
                {"wq1": slab_l[SLAB_OFF["wq1"]:SLAB_OFF["wq1"] + 5632, :],
                 "wq2": slab_l[SLAB_OFF["wq2"]:SLAB_OFF["wq2"] + 5632, :],
                 "gw": slab_l[SLAB_OFF["gw"]:SLAB_OFF["gw"] + 5504, :],
                 "uw": slab_l[SLAB_OFF["uw"]:SLAB_OFF["uw"] + 5504, :],
                 "wq3": wq3_l, "dw": dw_l},
                gamma_pre=gsum, gamma_wait=[cc0])
            prep_done = p.tok(osem)
            cc1 = p.op("po", lambda e: e.collective_compute(
                "AllGather", ALU.bypass, replica_groups=[cores], ins=[slab_l], outs=[slab_g]),
                waits=[prep_done], inc="cc1")
            cc2 = p.op("po", lambda e: e.collective_compute(
                "AllGather", ALU.bypass, replica_groups=[cores], ins=[wq3_l], outs=[wq3_g]),
                waits=[prep_done], inc="cc2")
            cc3 = p.op("po", lambda e: e.collective_compute(
                "AllGather", ALU.bypass, replica_groups=[cores], ins=[dw_l], outs=[dw_g]),
                waits=[prep_done], inc="cc3")
            ffn_ready = [cc1, cc2, cc3]
        state = {}
        s0 = emit_sandwich(ctx, x_in, None, None, xln, True, state)
        ctx.push_scope()
        yt = emit_l0_attn(ctx, xln, s0, wl0, y_fm, scr0, wbf=wl0_bf, cast_ready=[l0cast_tok])
        ctx.pop_scope()
        if use_collectives:
            ctx.pop_scope()
        B = [(s, v) for (s, v) in p.all_toks() if not s.startswith("cc")]
        p.region(B)
        s1 = emit_sandwich(ctx, x_in, y_fm, x1, xln, False, state, y_ready=yt)
        yg = [p.tok("sw_ys0"), p.tok("sw_ys1")]
        p.region(B)
        ctx.push_scope()
        yt = emit_gated_ffn(ctx, xln, s1, slab_wf("wq1"), slab_wf("wq2"),
                            lambda d0, h0, nd, nh: wq3_g[d0:d0 + nd, h0:h0 + nh],
                            AF.Sigmoid, 5632, y_fm, y3p, {}, w_ready=ffn_ready, y_guard=yg)
        ctx.pop_scope()
        B = [(s, v) for (s, v) in p.all_toks() if not s.startswith("cc")]
        p.region(B)
        s2 = emit_sandwich(ctx, x1, y_fm, x2, xln, False, state, y_ready=yt)
        yg = [p.tok("sw_ys0"), p.tok("sw_ys1")]
        p.region(B)
        ctx.push_scope()
        yt = emit_l1_attn(ctx, xln, s2, w_qkv, w_out, y_fm, scr1, y_guard=yg)
        ctx.pop_scope()
        B = [(s, v) for (s, v) in p.all_toks() if not s.startswith("cc")]
        p.region(B)
        s3 = emit_sandwich(ctx, x2, y_fm, x3, xln, False, state, y_ready=yt)
        yg = [p.tok("sw_ys0"), p.tok("sw_ys1")]
        p.region(B)
        ctx.push_scope()
        yt = emit_gated_ffn(ctx, xln, s3, slab_wf("gw"), slab_wf("uw"),
                            lambda d0, h0, nd, nh: dw_g[d0:d0 + nd, h0:h0 + nh],
                            AF.Silu, 5504, y_fm, y3p, {}, w_ready=ffn_ready, y_guard=yg)
        ctx.pop_scope()
        B = [(s, v) for (s, v) in p.all_toks() if not s.startswith("cc")]
        p.region(B)
        emit_sandwich(ctx, x3, y_fm, x_out, xln, False, state, y_ready=yt, write_xln=False)
        p.op("sp", lambda e: e.nop(), waits=p.all_toks())
        p.emit(nc, es)
    return nc


# version 17
# speedup vs baseline: 1.0031x; 1.0031x over previous
import numpy as np
import ml_dtypes
from contextlib import ExitStack
import concourse.bass as bass
import concourse.mybir as mybir
from concourse.bass_utils import run_bass_kernel_spmd

F32 = mybir.dt.float32
BF16 = mybir.dt.bfloat16
I32 = mybir.dt.int32
AF = mybir.ActivationFunctionType
ALU = mybir.AluOpType
AX = mybir.AxisListType
BF = ml_dtypes.bfloat16


class CFG:
    T = 2048
    D = 2048
    H = 16
    HD = 128
    QL = 768; KVL = 512; NOPE = 128; ROPE = 64; VD = 128; QKD = 192
    F0 = 5632; F1 = 5504; LUT = 16
    EPS_LN = 1e-5; EPS_RMS = 1e-6; EPS_BIT = 1e-8; EPS_SM = 1e-8
    THETA = 10000.0


ENG_ATTR = {"pe": "tensor", "act": "scalar", "dve": "vector", "po": "gpsimd", "sp": "sync"}
ENGINES = ("pe", "act", "dve", "po", "sp")


class Prog:

    def __init__(self):
        self.ops = {e: [] for e in ENGINES}
        self.counts = {}
        self.sem_names = []
        self.pending_barrier = {}
        self.region_inject = {}

    def barrier(self, exclude=()):
        toks = [(s, v) for (s, v) in self.all_toks() if s not in exclude]
        self.pending_barrier = {e: list(toks) for e in ENGINES}

    def region(self, toks):
        self.region_inject = {e: list(toks) for e in ENGINES}

    def _reg(self, sem):
        if sem not in self.counts:
            self.counts[sem] = 0
            self.sem_names.append(sem)

    def tok(self, sem):
        return (sem, self.counts.get(sem, 0))

    def op(self, eng, fn, waits=(), inc=None, n=1):
        if eng in self.pending_barrier:
            waits = list(waits) + self.pending_barrier.pop(eng)
        if eng in self.region_inject:
            waits = list(waits) + self.region_inject.pop(eng)
        waits = tuple((s, v) for (s, v) in waits if s is not None and v > 0)
        for s, _ in waits:
            self._reg(s)
        tok = None
        if inc:
            self._reg(inc)
            self.counts[inc] += n
            tok = (inc, self.counts[inc])
        self.ops[eng].append((fn, waits, inc, n))
        return tok

    def dma(self, q, out, in_, waits=(), sem=None, transpose=False, slow_ok=False):
        assert q in ("sp", "act", "po")
        sem = sem or ("q_" + q)
        return self.op(
            q,
            (lambda e, o=out, i=in_, t=transpose, s=slow_ok:
             e.dma_start(out=o, in_=i, transpose=t, allow_slow_non_contiguous=s)),
            waits, sem, 16,
        )

    def all_toks(self):
        return [(s, self.counts[s]) for s in self.sem_names if self.counts[s] > 0]

    def emit(self, nc, es):
        sems = {name: es.enter_context(nc.semaphore(name)) for name in self.sem_names}
        block = es.enter_context(nc.Block())
        for eng in ENGINES:
            ops = self.ops[eng]
            if not ops:
                continue

            def make(ops=ops):
                def _(e):
                    waited = {}
                    for fn, waits, inc, n in ops:
                        for s, v in waits:
                            if waited.get(s, 0) < v:
                                e.wait_ge(sems[s], v)
                                waited[s] = v
                        inst = fn(e)
                        if inc:
                            inst.then_inc(sems[inc], n)
                return _

            getattr(block, ENG_ATTR[eng])(make())
        return sems


def host_constants(c: CFG):
    T = c.T
    consts = {}
    consts["c_id_bf"] = np.eye(128, dtype=BF)
    consts["c_ones_col_bf"] = np.ones((128, 1), dtype=BF)
    consts["c_ones_row_bf"] = np.ones((1, 128), dtype=BF)
    consts["c_ones_row_f32"] = np.ones((1, 128), dtype=np.float32)
    kk = np.arange(128)[:, None]
    jj = np.arange(896)[None, :]
    consts["c_trilw"] = (kk <= jj - 384).astype(BF)
    d = c.ROPE
    inv_freq = 1.0 / c.THETA ** (np.arange(0, d, 2, dtype=np.float32) / d)
    t = np.arange(T, dtype=np.float32)
    fr = np.outer(inv_freq, t).astype(np.float32)
    cos, sin = np.cos(fr).astype(np.float32), np.sin(fr).astype(np.float32)
    scale = np.float32(c.QKD ** (-0.5))
    A = np.concatenate([cos, cos], 0)
    Bm = np.concatenate([-sin, sin], 0)
    consts["c_rope_a_k"] = A.astype(BF)
    consts["c_rope_b_k"] = Bm.astype(BF)
    return consts


class Ctx:
    def __init__(self, nc, es, cfg, const_aps):
        self.nc = nc
        self.es = es
        self.c = cfg
        self.p = Prog()
        self.cap = const_aps
        self.psum = [es.enter_context(nc.psum_tensor(f"pb{i}", [128, 512], F32)) for i in range(8)]
        self.bank_free = [[] for _ in range(8)]
        self._uid = 0
        self.const_sb = {}
        self.scopes = []
        self.pool = {}
        self._swb = None
        self.sw_chain = []

    def sandwich_bufs(self):
        if self._swb is None:
            assert not self.scopes, "allocate sandwich bufs before any scope"
            D = self.c.D
            self._swb = dict(
                xt=[self.sb(f"swb_x{b}", [128, D], F32) for b in range(2)],
                ln=[self.sb(f"swb_l", [128, D], F32)] * 2,
                xl=[self.sb(f"swb_xl", [128, D], BF16)] * 2,
                yt=[self.sb(f"swb_y", [128, D], BF16)] * 2,

                st=self.sb("swb_st", [128, 12], F32),
            )
        return self._swb

    def uid(self):
        self._uid += 1
        return self._uid

    def sb(self, name, shape, dt):
        es = self.scopes[-1] if self.scopes else self.es
        side = "left" if self.scopes else "right"
        return es.enter_context(self.nc.sbuf_tensor(name, list(shape), dt, side=side))

    def sb_outer(self, name, shape, dt):
        return self.es.enter_context(self.nc.sbuf_tensor(name, list(shape), dt, side="right"))

    def push_scope(self):
        self.scopes.append(ExitStack())

    def pop_scope(self):
        self.scopes.pop().close()

    def strip_set(self, key, n=None):
        if n is None:
            n = self.c.D // 128
        u = self.uid()
        return [self.sb(f"pool_{key}{u}_{i}", [128, self.c.T], BF16) for i in range(n)]

    def bcast_f32(self, row_ap, bank, waits):
        p = self.p
        if not hasattr(self, "_bc") or self._bc is None:
            self._bc = (self.sb_outer("bc_hi", [1, 512], BF16),
                        self.sb_outer("bc_lo", [1, 512], BF16))
            self._bc_chain = None
        hi, lo = self._bc
        ones_row, or_tok = self.load_const("c_ones_row_bf")
        w = list(waits) + ([self._bc_chain] if self._bc_chain else [])
        t1 = p.op("dve", lambda e: e.tensor_copy(hi[0:1, :], row_ap), waits=w, inc="dve")
        t2 = p.op("dve", lambda e: e.tensor_tensor(lo[0:1, :], row_ap, hi[0:1, :], ALU.subtract),
                  waits=[t1], inc="dve")
        m1 = p.op("pe", lambda e: e.matmul(self.psum[bank][:], ones_row[:], hi[0:1, :],
                                           start=True, stop=False),
                  waits=[t1, or_tok] + (self.bank_free[bank] or []), inc="pe")
        m2 = p.op("pe", lambda e: e.matmul(self.psum[bank][:], ones_row[:], lo[0:1, :],
                                           start=False, stop=True),
                  waits=[t2], inc="pe")
        self._bc_chain = m2
        return m2

    def const_col(self, val):
        key = float(val)
        if key not in self.const_sb:
            t = self.sb_outer(f"ccol_{self._uid}_{len(self.const_sb)}", [128, 1], F32)
            self._uid += 1
            tok = self.p.op("dve", lambda e, t=t, v=key: e.memset(t[:], v), inc="dve")
            self.const_sb[key] = (t, tok)
        return self.const_sb[key]

    def load_const(self, name, scoped=False):
        if not scoped and name in self.const_sb:
            return self.const_sb[name]
        ap = self.cap[name]
        shape = list(ap.tensor.shape)
        t = (self.sb if scoped else self.sb_outer)(f"cs{self.uid()}_" + name, shape, ap.dtype)
        tok = self.p.dma("act", t[:], ap, sem=f"qc{self._uid}_" + name)
        if not scoped:
            self.const_sb[name] = (t, tok)
        return (t, tok)


def emit_sandwich(ctx, x_in, y_fm, x_out, xln_out, first, state, y_ready=(), write_xln=True):
    c, p = ctx.c, ctx.p
    T, D = c.T, c.D
    NT = T // 128
    u = ctx.uid()
    swb = ctx.sandwich_bufs()
    xt, ln, xl, yt, st = swb["xt"], swb["ln"], swb["xl"], swb["yt"], swb["st"]
    prev_chain = list(ctx.sw_chain)
    _seen = set()
    _real_op = p.op

    def _op(eng, fn, waits=(), inc=None, n=1):
        if eng not in _seen:
            _seen.add(eng)
            waits = list(waits) + prev_chain
        return _real_op(eng, fn, waits, inc, n)
    p = type("POpWrap", (), {"op": staticmethod(_op), "dma": ctx.p.dma, "tok": ctx.p.tok,
                             "all_toks": ctx.p.all_toks})()
    if first:
        state["negmu"] = [ctx.sb_outer(f"sw{u}_nm{i}", [128, 1], F32) for i in range(NT)]

    xs = [f"sw_xs{b}" for b in range(2)]
    ys = [f"sw_ys{b}" for b in range(2)]
    oss = [f"sw_os{b}" for b in range(2)]
    lss = [f"sw_ls{b}" for b in range(2)]
    last_use_x = [None, None]
    last_use_y = [None, None]
    last_store_o = [None, None]
    last_store_l = [None, None]
    sq_tok = None
    st_tok = None
    last_lt = None
    for i in range(NT):
        b = i % 2
        row = slice(i * 128, (i + 1) * 128)
        xtok = p.dma("sp", xt[b][:], x_in[row, :],
                     waits=(prev_chain if i < 2 else []) + ([last_use_x[b]] if last_use_x[b] else []),
                     sem=xs[b])
        if y_fm is not None:
            ytok = p.dma("sp", yt[b][:], y_fm[:, row],
                         waits=list(y_ready) + [t for t in last_use_y if t],
                         sem=ys[b], transpose=True)
            w_st = [st_tok] if st_tok else []
            t1 = p.op("dve", lambda e, b=b: e.tensor_reduce(st[:, 0:1], yt[b][:], AX.X, ALU.add, negate=True),
                      waits=[ytok] + w_st, inc="dve")
            t2 = p.op("act", lambda e, b=b: e.activation(xl[b][:], yt[b][:], AF.Square,
                                                         accum_out=st[:, 2:3]),
                      waits=[ytok, t1] + ([sq_tok] if sq_tok else []) +
                            [t for t in last_store_l if t], inc="act")
            t4 = p.op("dve", lambda e: e.tensor_scalar(st[:, 0:1], st[:, 0:1], 1.0 / D, None, ALU.mult),
                      waits=[t1], inc="dve")
            t5 = p.op("dve", lambda e: e.tensor_tensor(st[:, 4:5], st[:, 0:1], st[:, 0:1], ALU.mult),
                      waits=[t4], inc="dve")
            t6 = p.op("dve", lambda e: e.tensor_scalar(st[:, 3:4], st[:, 2:3], 1.0 / D, None, ALU.mult),
                      waits=[t2], inc="dve")
            t7 = p.op("dve", lambda e: e.tensor_tensor(st[:, 3:4], st[:, 3:4], st[:, 4:5], ALU.subtract),
                      waits=[t5, t6], inc="dve")
            epsc, epsc_tok = ctx.const_col(c.EPS_LN)
            t8 = p.op("act", lambda e, epsc=epsc: e.activation(st[:, 1:2], st[:, 3:4], AF.Ln, bias=epsc[:]),
                      waits=[t7, epsc_tok], inc="act")
            t9 = p.op("act", lambda e: e.activation(st[:, 1:2], st[:, 1:2], AF.Exp, scale=-0.5),
                      waits=[t8], inc="act")
            prev_store = last_store_o[0] if (last_store_o[0] and (not last_store_o[1] or last_store_o[0][1] > last_store_o[1][1])) else last_store_o[1]
            t10 = p.op("dve", lambda e, b=b: e.tensor_scalar(ln[b][:], yt[b][:], st[:, 0:1], st[:, 1:2],
                                                             ALU.add, ALU.mult),
                       waits=[t9] + ([prev_store] if prev_store else []), inc="dve")
            last_use_y[b] = t10
            t11 = p.op("dve", lambda e, b=b: e.tensor_tensor(ln[b][:], ln[b][:], xt[b][:], ALU.add),
                       waits=[t10, xtok], inc="dve")
            last_use_x[b] = t11
            sot = p.dma("sp", x_out[row, :], ln[b][:], waits=[t11], sem=oss[b])
            last_store_o[b] = sot
            xn, xn_tok = ln[b], t11
        else:
            xn, xn_tok = xt[b], xtok

        if not write_xln:
            continue
        if first:
            nm = state["negmu"][i]
            tm = p.op("dve", lambda e, xn=xn, nm=nm: e.tensor_reduce(nm[:], xn[:], AX.X, ALU.add, negate=True),
                      waits=[xn_tok], inc="dve")
            tm = p.op("dve", lambda e, nm=nm: e.tensor_scalar(nm[:], nm[:], 1.0 / D, None, ALU.mult),
                      waits=[tm], inc="dve")
        else:
            nm = state["negmu"][i]
            tm = state["negmu_tok"]
        s2 = p.op("act", lambda e, xn=xn, b=b: e.activation(xl[b][:], xn[:], AF.Square, accum_out=st[:, 7:8]),
                  waits=([xn_tok, t2] if y_fm is not None else [xn_tok]) +
                        [t for t in last_store_l if t] + ([st_tok] if st_tok else []),
                  inc="act")
        sq_tok = s2
        s3 = p.op("dve", lambda e, nm=nm: e.tensor_tensor(st[:, 8:9], nm[:], nm[:], ALU.mult),
                  waits=[tm], inc="dve")
        s4 = p.op("dve", lambda e: e.tensor_scalar(st[:, 9:10], st[:, 7:8], 1.0 / D, None, ALU.mult),
                  waits=[s2], inc="dve")
        s5 = p.op("dve", lambda e: e.tensor_tensor(st[:, 9:10], st[:, 9:10], st[:, 8:9], ALU.subtract),
                  waits=[s3, s4], inc="dve")
        epsc2, epsc2_tok = ctx.const_col(c.EPS_LN)
        s6 = p.op("act", lambda e, epsc2=epsc2: e.activation(st[:, 10:11], st[:, 9:10], AF.Ln, bias=epsc2[:]),
                  waits=[s5, epsc2_tok], inc="act")
        s7 = p.op("act", lambda e: e.activation(st[:, 10:11], st[:, 10:11], AF.Exp, scale=-0.5),
                  waits=[s6], inc="act")
        s8 = p.op("dve", lambda e, xn=xn, nm=nm, b=b: e.tensor_scalar(xl[b][:], xn[:], nm[:], st[:, 10:11],
                                                                      ALU.add, ALU.mult),
                  waits=[s7] + [t for t in last_store_l if t], inc="dve")
        st_tok = s8
        if y_fm is None:
            last_use_x[b] = s8
        lt = p.dma("sp", xln_out[row, :], xl[b][:], waits=[s8], sem=lss[b])
        last_store_l[b] = lt
        last_lt = lt
    if first:
        state["negmu_tok"] = p.tok("dve")
    ctx.sw_chain = [p.tok("dve"), p.tok("act"), p.tok("po")] +         [p.tok(s) for s in oss + lss + ys]
    if not write_xln:
        return [p.tok(oss[0]), p.tok(oss[1])]
    return [p.tok(lss[0]), p.tok(lss[1])]


def emit_load_xln_fm(ctx, xln_dram, ready_toks, dst_strips, reuse_waits=()):
    c, p = ctx.c, ctx.p
    T, D = c.T, c.D
    NS = D // 128
    u = ctx.uid()
    sem = f"xlnfm{u}"
    tok = None
    for i in range(NS):
        tok = p.dma("sp", dst_strips[i][:], xln_dram[:, i * 128:(i + 1) * 128],
                    waits=list(ready_toks) + list(reuse_waits), sem=sem, transpose=True)
    return (sem, tok[1])


def emit_l1_attn(ctx, xln_dram, xln_ready, w_qkv, w_out, y_fm, scr, y_guard=()):
    c, p = ctx.c, ctx.p
    T, D, H = c.T, c.D, c.H
    NS = D // 128
    NQ = T // 512
    NKT = T // 128
    u = ctx.uid()
    scale = float(c.HD ** (-0.5))
    SAMP_STEP = 8
    NSAMP = T // SAMP_STEP
    bf = ctx.bank_free

    idn, idn_tok = ctx.load_const("c_id_bf")
    ones_col, ocol_tok = ctx.load_const("c_ones_col_bf")
    ones_row, orow_tok = ctx.load_const("c_ones_row_bf")

    wq_bf, wo_bf = scr["l1_wqkv_bf"], scr["l1_wout_bf"]
    q_scr, v_scr = scr["l1_q"], scr["l1_v"]

    csem = f"l1c{u}"
    for j in range(12):
        p.dma("po", wq_bf[j * 512:(j + 1) * 512, :], w_qkv[j * 512:(j + 1) * 512, :], sem=csem)
    for j in range(4):
        ctok = p.dma("po", wo_bf[j * 512:(j + 1) * 512, :], w_out[j * 512:(j + 1) * 512, :], sem=csem)
    cast_all = (csem, ctok[1])

    xfm = ctx.strip_set("A")
    xfm_tok = emit_load_xln_fm(ctx, xln_dram, xln_ready, xfm,
                               reuse_waits=scr.pop("poolA_free", []))
    k_fm = ctx.strip_set("B")
    kfm_free = scr.pop("poolB_free", [])

    OC = 256
    wall = [ctx.sb(f"l1wa{u}_{b}", [128, NS * OC], BF16) for b in range(2)]
    wsem = [f"l1ws{u}_{b}" for b in range(2)]
    wall_use = [None, None]
    evb = [ctx.sb(f"l1ev{u}_{b}", [128, 512], BF16) for b in range(4)]
    ev_free = [None] * 4
    qs_sem = [f"l1qs{u}_{b}" for b in range(4)]
    vs_sem = [f"l1vs{u}_{b}" for b in range(4)]

    n_oc = 3 * D // OC
    for oc in range(n_oc):
        b = oc % 2
        o0 = oc * OC
        wtok = None
        for i in range(NS):
            wtok = p.dma("sp", wall[b][:, i * OC:(i + 1) * OC],
                         wq_bf[o0:o0 + OC, i * 128:(i + 1) * 128],
                         waits=[cast_all] + ([wall_use[b]] if wall_use[b] else []),
                         sem=wsem[b], transpose=True)
        wready = (wsem[b], wtok[1])
        if oc < 16:
            part_q = oc < 8
            for ho in range(2):
                h = (oc % 8) * 2 + ho
                for tc in range(NQ):
                    bi = (ho * NQ + tc) % 4
                    bank = ctx.psum[bi]
                    mm = None
                    for i in range(NS):
                        mm = p.op("pe", lambda e, bank=bank, b=b, i=i, ho=ho, tc=tc:
                                  e.matmul(bank[:], wall[b][:, i * OC + ho * 128:i * OC + (ho + 1) * 128],
                                           xfm[i][:, tc * 512:(tc + 1) * 512],
                                           start=(i == 0), stop=(i == NS - 1)),
                                  waits=[wready, xfm_tok] + (bf[bi] if i == 0 else []),
                                  inc="pe")
                    wall_use[b] = mm
                    if part_q:
                        es_ = bi
                        et = p.op("act", lambda e, bank=bank, es_=es_:
                                  e.activation(evb[es_][:], bank[:], AF.Copy, scale=scale),
                                  waits=[mm] + ([ev_free[es_]] if ev_free[es_] else []), inc="act")
                        bf[bi] = [et]
                        sx = p.dma("sp", q_scr[h * 128:(h + 1) * 128, tc * 512:(tc + 1) * 512],
                                   evb[es_][:], waits=[et], sem=qs_sem[es_])
                        ev_free[es_] = sx
                    else:
                        et = p.op("act", lambda e, bank=bank, h=h, tc=tc:
                                  e.activation(k_fm[h][:, tc * 512:(tc + 1) * 512], bank[:], AF.Copy),
                                  waits=[mm] + list(kfm_free), inc="act")
                        kfm_free = []
                        bf[bi] = [et]
        else:
            for tk in range(NKT):
                bi = tk % 4
                bank = ctx.psum[bi]
                mm = None
                for i in range(NS):
                    mm = p.op("pe", lambda e, bank=bank, b=b, i=i, tk=tk:
                              e.matmul(bank[:, 0:OC], xfm[i][:, tk * 128:(tk + 1) * 128],
                                       wall[b][:, i * OC:(i + 1) * OC],
                                       start=(i == 0), stop=(i == NS - 1)),
                              waits=[wready, xfm_tok] + (bf[bi] if i == 0 else []),
                              inc="pe")
                wall_use[b] = mm
                es_ = tk % 4
                et = p.op("act", lambda e, bank=bank, es_=es_:
                          e.activation(evb[es_][:, 0:OC], bank[:, 0:OC], AF.Copy),
                          waits=[mm] + ([ev_free[es_]] if ev_free[es_] else []), inc="act")
                bf[bi] = [et]
                sx = None
                h2 = (oc - 16) * 2
                for ho in range(2):
                    sx = p.dma("sp", v_scr[tk, h2 + ho], evb[es_][:, ho * 128:(ho + 1) * 128],
                               waits=[et], sem=vs_sem[es_])
                ev_free[es_] = sx
    qkv_done_pe = p.tok("pe")
    scat_done = [p.tok(s) for s in qs_sem] + [p.tok(s) for s in vs_sem]

    o_all = xfm
    pt = [ctx.sb(f"l1pt{u}_{i}", [128, 512], BF16) for i in range(3)]
    pt_use = [None] * 3
    qb_buf = [ctx.sb(f"l1qb{u}_{b}", [128, 512], BF16) for b in range(2)]
    qb_sem = [f"l1qbs{u}_{b}" for b in range(2)]
    qb_use = [None, None]
    vb_buf = [ctx.sb(f"l1vb{u}_{b}", [128, 128], BF16) for b in range(3)]
    vb_sem = [f"l1vbs{u}_{b}" for b in range(3)]
    vb_use = [None, None, None]
    nmcol = ctx.sb(f"l1nm{u}", [128, 4], F32)
    nmcol_bf = ctx.sb(f"l1nmb{u}", [128, 4], BF16)
    aug = ctx.sb(f"l1aug{u}", [1, 512], BF16)
    den = ctx.sb(f"l1den{u}", [1, 512], F32)
    rbc = ctx.sb(f"l1rbc{u}", [128, 512], F32)
    stb = [0, 1, 6]
    nm_use = None
    aug_use = None
    for h in range(H):
        for qb in range(NQ):
            bslot = (h * NQ + qb) % 2
            qtok = p.dma("sp", qb_buf[bslot][:], q_scr[h * 128:(h + 1) * 128, qb * 512:(qb + 1) * 512],
                         waits=scat_done + ([qb_use[bslot]] if qb_use[bslot] else []),
                         sem=qb_sem[bslot])
            red_toks = []
            for j in range(4):
                bi = stb[j % 2]
                bank = ctx.psum[bi]
                mm = p.op("pe", lambda e, bank=bank, bslot=bslot, j=j, h=h:
                          e.matmul(bank[:, 0:NSAMP], qb_buf[bslot][:, j * 128:(j + 1) * 128],
                                   k_fm[h][:, 0:T:SAMP_STEP], start=True, stop=True),
                          waits=[qtok] + bf[bi], inc="pe")
                rd = p.op("dve", lambda e, bank=bank, j=j:
                          e.tensor_reduce(nmcol[:, j:j + 1], bank[:, 0:NSAMP], AX.X, ALU.max, negate=True),
                          waits=[mm] + ([nm_use] if (j == 0 and nm_use) else []), inc="dve")
                bf[bi] = [rd]
                red_toks.append(rd)
            cv = p.op("dve", lambda e: e.tensor_copy(nmcol_bf[:], nmcol[:]),
                      waits=red_toks, inc="dve")
            as_tok = None
            for j in range(4):
                as_tok = p.op("pe", lambda e, j=j: e.matmul(ctx.psum[2][0:1, j * 128:(j + 1) * 128],
                                                            nmcol_bf[:, j:j + 1], idn[:],
                                                            start=True, stop=True),
                              waits=[cv, idn_tok] + (bf[2] if j == 0 else []),
                              inc="pe")
            nm_use = as_tok
            ag = p.op("act", lambda e: e.activation(aug[:], ctx.psum[2][0:1, :], AF.Copy),
                      waits=[as_tok] + ([aug_use] if aug_use else []), inc="act")
            bf[2] = [ag]
            den_mm = None
            o_mm = None
            for kt in range(NKT):
                bi = stb[kt % 3]
                bank = ctx.psum[bi]
                mm1 = p.op("pe", lambda e, bank=bank, h=h, kt=kt, bslot=bslot:
                           e.matmul(bank[:], k_fm[h][:, kt * 128:(kt + 1) * 128], qb_buf[bslot][:],
                                    start=True, stop=False),
                           waits=[qtok] + bf[bi], inc="pe")
                mm2 = p.op("pe", lambda e, bank=bank:
                           e.matmul(bank[:], ones_row[:], aug[:], start=False, stop=True),
                           waits=[ag, orow_tok], inc="pe")
                ps_ = kt % 3
                ex = p.op("act", lambda e, bank=bank, ps_=ps_:
                          e.activation(pt[ps_][:], bank[:], AF.Exp),
                          waits=[mm2] + ([pt_use[ps_]] if pt_use[ps_] else []), inc="act")
                bf[bi] = [ex]
                den_mm = p.op("pe", lambda e, kt=kt, ps_=ps_:
                              e.matmul(ctx.psum[3][0:1, :], ones_col[:], pt[ps_][:],
                                       start=(kt == 0), stop=(kt == NKT - 1)),
                              waits=[ex, ocol_tok] + (bf[3] if kt == 0 else []),
                              inc="pe")
                vslot = kt % 3
                vtok = p.dma("sp", vb_buf[vslot][:], v_scr[kt, h],
                             waits=scat_done + ([vb_use[vslot]] if vb_use[vslot] else []),
                             sem=vb_sem[vslot])
                o_mm = p.op("pe", lambda e, kt=kt, vslot=vslot, ps_=ps_:
                            e.matmul(ctx.psum[4][:], vb_buf[vslot][:], pt[ps_][:],
                                     start=(kt == 0), stop=(kt == NKT - 1)),
                            waits=[ex, vtok] + (bf[4] if kt == 0 else []),
                            inc="pe")
                vb_use[vslot] = o_mm
                pt_use[ps_] = o_mm
            d1 = p.op("act", lambda e: e.activation(den[:], ctx.psum[3][0:1, :], AF.Copy),
                      waits=[den_mm], inc="act")
            bf[3] = [d1]
            d2 = p.op("dve", lambda e: e.tensor_scalar(den[:], den[:], float(c.EPS_SM), None, ALU.add),
                      waits=[d1], inc="dve")
            d3 = p.op("dve", lambda e: e.reciprocal(den[:], den[:]), waits=[d2], inc="dve")
            bcm = ctx.bcast_f32(den[0:1, :], 5, [d3])
            rb = p.op("act", lambda e: e.activation(rbc[:], ctx.psum[5][:], AF.Copy),
                      waits=[bcm], inc="act")
            bf[5] = [rb]
            oe = p.op("dve", lambda e, h=h, qb=qb:
                      e.tensor_tensor(o_all[h][:, qb * 512:(qb + 1) * 512], ctx.psum[4][:], rbc[:], ALU.mult),
                      waits=[o_mm, rb, qkv_done_pe], inc="dve")
            bf[4] = [oe]
            aug_use = den_mm
    attn_done_dve = p.tok("dve")

    yev = [ctx.sb(f"l1ye{u}_{b}", [128, 512], BF16) for b in range(2)]
    yev_free = [None, None]
    ys_sem = [f"l1ys{u}_{b}" for b in range(2)]
    for dc in range(8):
        b = dc % 2
        wtok = None
        for hh in range(H):
            wtok = p.dma("sp", wall[b][:, hh * 256:(hh + 1) * 256],
                         wo_bf[dc * 256:(dc + 1) * 256, hh * 128:(hh + 1) * 128],
                         waits=[cast_all] + ([wall_use[b]] if wall_use[b] else []),
                         sem=wsem[b], transpose=True)
        wready = (wsem[b], wtok[1])
        for ds in range(2):
            for qb in range(NQ):
                bi = (ds * NQ + qb) % 4
                bank = ctx.psum[bi]
                mm = None
                for hh in range(H):
                    mm = p.op("pe", lambda e, bank=bank, b=b, hh=hh, ds=ds, qb=qb:
                              e.matmul(bank[:], wall[b][:, hh * 256 + ds * 128:hh * 256 + (ds + 1) * 128],
                                       o_all[hh][:, qb * 512:(qb + 1) * 512],
                                       start=(hh == 0), stop=(hh == H - 1)),
                              waits=[wready, attn_done_dve] + (bf[bi] if hh == 0 else []),
                              inc="pe")
                wall_use[b] = mm
                es_ = bi % 2
                et = p.op("act", lambda e, bank=bank, es_=es_:
                          e.activation(yev[es_][:], bank[:], AF.Copy),
                          waits=[mm] + ([yev_free[es_]] if yev_free[es_] else []), inc="act")
                bf[bi] = [et]
                sx = p.dma("sp", y_fm[(dc * 2 + ds) * 128:(dc * 2 + ds + 1) * 128, qb * 512:(qb + 1) * 512],
                           yev[es_][:], waits=[et] + list(y_guard), sem=ys_sem[es_])
                yev_free[es_] = sx
    scr["poolA_free"] = [p.tok("pe")]
    scr["poolB_free"] = [p.tok("pe")]
    return [p.tok(s) for s in ys_sem]


def build_l1_attn_launch(cfg, consts):
    nc = bass.Bass("TRN2", target_bir_lowering=False, debug=False)
    c = cfg
    T, D = c.T, c.D
    x_in = nc.dram_tensor("x_in", [T, D], F32, kind="ExternalInput").ap()
    w_qkv = nc.dram_tensor("w_qkv", [3 * D, D], F32, kind="ExternalInput").ap()
    w_out = nc.dram_tensor("w_out", [D, D], F32, kind="ExternalInput").ap()
    x_out = nc.dram_tensor("x_out", [T, D], F32, kind="ExternalOutput").ap()
    const_aps = {}
    for name, arr in consts.items():
        dt = {np.dtype(np.float32): F32, np.dtype(BF): BF16, np.dtype(np.int32): I32}[arr.dtype]
        const_aps[name] = nc.dram_tensor(name, list(arr.shape), dt, kind="ExternalInput").ap()

    xln = nc.dram_tensor("s_xln", [T, D], BF16).ap()
    y_fm = nc.dram_tensor("s_yfm", [D, T], BF16).ap()
    scr = {
        "l1_wqkv_bf": nc.dram_tensor("s_wqkv_bf", [3 * D, D], BF16).ap(),
        "l1_wout_bf": nc.dram_tensor("s_wout_bf", [D, D], BF16).ap(),
        "l1_q": nc.dram_tensor("s_q", [D, T], BF16).ap(),
        "l1_v": nc.dram_tensor("s_v", [T // 128, c.H, 128, 128], BF16).ap(),
    }
    es = ExitStack()
    with es:
        ctx = Ctx(nc, es, c, const_aps)
        state = {}
        ln_toks = emit_sandwich(ctx, x_in, None, None, xln, True, state)
        y_toks = emit_l1_attn(ctx, xln, ln_toks, w_qkv, w_out, y_fm, scr)
        emit_sandwich(ctx, x_in, y_fm, x_out, xln, False, state, y_ready=y_toks, write_xln=False)
        final = ctx.p.all_toks()
        ctx.p.op("sp", lambda e: e.nop(), waits=final)
        ctx.p.emit(nc, es)
    return nc



def emit_l0_attn(ctx, xln_dram, xln_ready, w, y_fm, scr, wbf=None, cast_ready=(), phase_limit=99, y_guard=()):
    c, p = ctx.c, ctx.p
    T, D, H = c.T, c.D, c.H
    NS = D // 128
    NQ = T // 512
    NKT = T // 128
    NTC = T // 512
    QL, KVL, NOPE, ROPE, VD = c.QL, c.KVL, c.NOPE, c.ROPE, c.VD
    u = ctx.uid()
    scale = float(c.QKD ** (-0.5))
    bf = ctx.bank_free

    ones_col, ocol_tok = ctx.load_const("c_ones_col_bf")
    ra, ra_tok = ctx.load_const("c_rope_a_k", scoped=True)
    rb, rb_tok = ctx.load_const("c_rope_b_k", scoped=True)
    trilw, trilw_tok = ctx.load_const("c_trilw", scoped=True)

    q_scr, k_scr, v_scr = scr["l0_q"], scr["l0_k"], scr["l0_v"]

    xfm = ctx.strip_set("A")
    xfm_tok = emit_load_xln_fm(ctx, xln_dram, xln_ready, xfm,
                               reuse_waits=scr.pop("poolA_free", []))

    wsrc_d = wbf if wbf is not None else w

    OC = 128
    wall = [ctx.sb(f"l0wa{u}_{b}", [128, NS * OC], BF16) for b in range(2)]
    wsem = [f"l0ws{u}_{b}" for b in range(2)]
    wall_use = [None, None]

    def wall_load(b, wsrc, o0, oc_w, nstrips, extra=()):
        tok = None
        for i in range(nstrips):
            tok = p.dma("sp", wall[b][:, i * OC:i * OC + oc_w],
                        wsrc[i * 128:(i + 1) * 128, o0:o0 + oc_w],
                        waits=list(extra) + list(cast_ready) + ([wall_use[b]] if wall_use[b] else []),
                        sem=wsem[b])
        return (wsem[b], tok[1])

    evb = [ctx.sb(f"l0ev{u}_{b}", [128, 512], BF16) for b in range(2)]
    ev_free = [None, None]
    sqq = ctx.sb(f"l0sqq{u}", [128, 512], BF16)
    sqq_use = None
    rbc = ctx.sb(f"l0rbc{u}", [128, 512], F32)
    rbckv = ctx.sb(f"l0rbckv{u}", [128, T], F32)
    kpe_r = ctx.sb(f"l0kpe{u}", [65, T], BF16)
    kpe_tmp = ctx.sb(f"l0kpt{u}", [64, 512], F32)
    kpe_sw = ctx.sb(f"l0kps{u}", [64, 512], F32)
    kab_pos = ctx.sb(f"l0kab{u}", [128, H], F32)
    kab_nope = ctx.sb(f"l0kabn{u}", [128, H], BF16)
    kab_pe = ctx.sb(f"l0kabp{u}", [64, 1], BF16)
    kab_tmp = ctx.sb(f"l0kabt{u}", [128, 1], F32)
    ctx.push_scope()
    gq = ctx.sb(f"l0gq{u}", [128, QL // 128], F32)
    gkv = ctx.sb(f"l0gkv{u}", [128, KVL // 128], F32)
    gq_tok = p.dma("sp", gq[:], w["g_q"].rearrange("(s p) -> p s", p=128), sem=f"l0gq{u}", slow_ok=True)
    gkv_tok = p.dma("sp", gkv[:], w["g_kv"].rearrange("(s p) -> p s", p=128), sem=f"l0gkv{u}", slow_ok=True)
    ssq_q = ctx.sb(f"l0ssqq{u}", [1, T], F32)
    ssq_kv = ctx.sb(f"l0ssqkv{u}", [1, T], F32)
    rstdq = ssq_q
    rstdkv = ssq_kv
    rstdkv_col = ctx.sb(f"l0rskvc{u}", [128, NKT], F32)

    kv_g = [ctx.sb(f"l0kv{u}_{i}", [128, T], BF16) for i in range(KVL // 128)]
    qs_sem = [f"l0qs{u}_{b}" for b in range(2)]

    def lat_proj(wsrc, LO, gcol, gtok, ssq_row, dst_kind):
        nonlocal sqq_use
        no_stats = (phase_limit == 101)
        no_evac = (phase_limit == 102)
        NOC = LO // OC
        for oc in range(NOC):
            b = oc % 2
            wtok = wall_load(b, wsrc, oc * OC, OC, NS)
            for tc in range(NTC):
                bi = (oc * NTC + tc) % 2
                bank = ctx.psum[bi]
                sbank = 4 + tc
                mm = None
                for i in range(NS):
                    mm = p.op("pe", lambda e, bank=bank, b=b, i=i, tc=tc:
                              e.matmul(bank[:], wall[b][:, i * OC:(i + 1) * OC],
                                       xfm[i][:, tc * 512:(tc + 1) * 512],
                                       start=(i == 0), stop=(i == NS - 1)),
                              waits=[wtok, xfm_tok] + (bf[bi] if i == 0 else []),
                              inc="pe")
                wall_use[b] = mm
                if no_stats:
                    sget = smm = mm
                else:
                    sget = p.op("act", lambda e, bank=bank: e.activation(sqq[:], bank[:], AF.Square),
                                waits=[mm] + ([sqq_use] if sqq_use else []), inc="act")
                    smm = p.op("pe", lambda e, sbank=sbank, oc=oc, NOC=NOC:
                               e.matmul(ctx.psum[sbank][0:1, 0:512], ones_col[:], sqq[:],
                                        start=(oc == 0), stop=(oc == NOC - 1)),
                               waits=[sget, ocol_tok] + (bf[sbank] if oc == 0 else []),
                               inc="pe")
                    sqq_use = smm
                if no_evac:
                    ge = p.op("act", lambda e, bank=bank: e.activation(sqq[:], bank[:], AF.Copy),
                              waits=[mm, gtok] + ([sqq_use] if (no_stats and sqq_use) else []), inc="act")
                    if no_stats:
                        sqq_use = ge
                elif dst_kind == "q":
                    es_ = bi
                    ge = p.op("dve", lambda e, bank=bank, oc=oc, es_=es_:
                              e.tensor_scalar(evb[es_][:], bank[:], gcol[:, oc:oc + 1], None, ALU.mult),
                              waits=[mm, gtok, sget] + ([ev_free[es_]] if ev_free[es_] else []), inc="dve")
                    sx = p.dma("sp", q_scr[oc * 128:(oc + 1) * 128, tc * 512:(tc + 1) * 512],
                               evb[es_][:], waits=[ge], sem=qs_sem[es_])
                    ev_free[es_] = sx
                else:
                    ge = p.op("dve", lambda e, bank=bank, oc=oc, tc=tc:
                              e.tensor_scalar(kv_g[oc][:, tc * 512:(tc + 1) * 512], bank[:],
                                              gcol[:, oc:oc + 1], None, ALU.mult),
                              waits=[mm, gtok, sget], inc="dve")
                bf[bi] = [sget, ge]
                if oc == NOC - 1 and not no_stats:
                    se = p.op("act", lambda e, tc=tc, sbank=sbank, ssq_row=ssq_row:
                              e.activation(ssq_row[:, tc * 512:(tc + 1) * 512],
                                           ctx.psum[sbank][0:1, 0:512], AF.Copy),
                              waits=[smm], inc="act")
                    bf[sbank] = [se]

    if phase_limit <= 0:
        ctx.pop_scope()
        return [p.tok("pe")]
    lat_proj(wsrc_d["w_dq"], QL, gq, gq_tok, ssq_q, "q")
    lat_proj(wsrc_d["w_dkv"], KVL, gkv, gkv_tok, ssq_kv, "kv")
    qlat_stored = [p.tok(s) for s in qs_sem]
    kv_ready = p.tok("dve")
    if phase_limit <= 1 or phase_limit in (101, 102):
        ctx.pop_scope()
        return [p.tok("pe")]

    epsc, epsc_tok = ctx.const_col(c.EPS_RMS)
    act_now = p.tok("act")
    r1 = p.op("dve", lambda e: e.tensor_scalar(rstdq[:], ssq_q[:], 1.0 / QL, None, ALU.mult),
              waits=[act_now], inc="dve")
    r2 = p.op("act", lambda e: e.activation(rstdq[:], rstdq[:], AF.Ln, bias=epsc[0:1, :]),
              waits=[r1, epsc_tok], inc="act")
    r3 = p.op("act", lambda e: e.activation(rstdq[:], rstdq[:], AF.Exp, scale=-0.5),
              waits=[r2], inc="act")
    rq_tok = p.op("dve", lambda e: e.tensor_scalar(rstdq[:], rstdq[:], scale, None, ALU.mult),
                  waits=[r3], inc="dve")
    r5 = p.op("dve", lambda e: e.tensor_scalar(rstdkv[:], ssq_kv[:], 1.0 / KVL, None, ALU.mult),
              waits=[act_now], inc="dve")
    r6 = p.op("act", lambda e: e.activation(rstdkv[:], rstdkv[:], AF.Ln, bias=epsc[0:1, :]),
              waits=[r5, epsc_tok], inc="act")
    rkv_tok = p.op("act", lambda e: e.activation(rstdkv[:], rstdkv[:], AF.Exp, scale=-0.5),
                   waits=[r6], inc="act")
    onescal = ctx.sb(f"l0one1{u}", [1, 1], F32)
    o1t = p.op("dve", lambda e: e.memset(onescal[:], 1.0), inc="dve")
    rcol_prev = None
    for tk in range(NKT):
        mmr = p.op("pe", lambda e, tk=tk: e.matmul(ctx.psum[7][:, tk:tk + 1],
                                                   rstdkv[:, tk * 128:(tk + 1) * 128], onescal[:],
                                                   start=True, stop=True),
                   waits=[rkv_tok, o1t] + (bf[7] if tk == 0 else []) +
                         ([rcol_prev] if rcol_prev else []),
                   inc="pe")
        rcol_prev = p.op("act", lambda e, tk=tk: e.activation(rstdkv_col[:, tk:tk + 1],
                                                              ctx.psum[7][:, tk:tk + 1], AF.Copy),
                         waits=[mmr], inc="act")
    bf[7] = [rcol_prev]
    rcol_all = rcol_prev
    if phase_limit <= 2:
        ctx.pop_scope()
        return [p.tok("pe")]

    on_tok = p.op("dve", lambda e: e.memset(kpe_r[64:65, :], 1.0), inc="dve")
    wtok = wall_load(0, wsrc_d["w_kr"], 0, 64, NS)
    rope_use = None
    for tc in range(NTC):
        bank = ctx.psum[2]
        mm = None
        for i in range(NS):
            mm = p.op("pe", lambda e, bank=bank, i=i, tc=tc:
                      e.matmul(bank[0:64, :], wall[0][:, i * OC:i * OC + 64],
                               xfm[i][:, tc * 512:(tc + 1) * 512],
                               start=(i == 0), stop=(i == NS - 1)),
                      waits=[wtok, xfm_tok] + (bf[2] if i == 0 else []),
                      inc="pe")
        wall_use[0] = mm
        e1 = p.op("act", lambda e, bank=bank: e.activation(kpe_tmp[:], bank[0:64, :], AF.Copy),
                  waits=[mm] + ([rope_use] if rope_use else []), inc="act")
        e2 = p.op("act", lambda e, bank=bank: e.activation(kpe_sw[0:32, :], bank[32:64, :], AF.Copy),
                  waits=[mm], inc="act")
        e3 = p.op("act", lambda e, bank=bank: e.activation(kpe_sw[32:64, :], bank[0:32, :], AF.Copy),
                  waits=[mm], inc="act")
        bf[2] = [e1, e2, e3]
        t1 = p.op("dve", lambda e, tc=tc: e.tensor_tensor(kpe_tmp[:], kpe_tmp[:],
                                                          ra[:, tc * 512:(tc + 1) * 512], ALU.mult),
                  waits=[e1, ra_tok], inc="dve")
        t2 = p.op("dve", lambda e, tc=tc: e.tensor_tensor(kpe_sw[:], kpe_sw[:],
                                                          rb[:, tc * 512:(tc + 1) * 512], ALU.mult),
                  waits=[e3, rb_tok], inc="dve")
        t3 = p.op("dve", lambda e, tc=tc: e.tensor_tensor(kpe_r[0:64, tc * 512:(tc + 1) * 512],
                                                          kpe_tmp[:], kpe_sw[:], ALU.add),
                  waits=[t1, t2], inc="dve")
        rope_use = t3
    kpe_all = p.tok("dve")
    if phase_limit <= 3:
        ctx.pop_scope()
        return [p.tok("pe")]

    NKV = KVL // 128
    ktile = ctx.sb(f"l0ktile{u}", [128, 512], BF16)
    ktile_use = None
    rbc_use = None
    ks_sem = f"l0ks{u}"

    kab_prev = {}
    rbckv_toks = []
    for tc in range(NTC):
        bm = ctx.bcast_f32(rstdkv[0:1, tc * 512:(tc + 1) * 512], 3, [rkv_tok])
        be = p.op("act", lambda e, tc=tc: e.activation(rbckv[:, tc * 512:(tc + 1) * 512],
                                                       ctx.psum[3][:], AF.Copy),
                  waits=[bm], inc="act")
        bf[3] = [be]
        rbckv_toks.append(be)
    for h in range(H):
        b = h % 2
        wtok = wall_load(b, wsrc_d["w_uk"], h * 128, OC, NKV)
        for tc in range(NTC):
            bi = (h * NTC + tc) % 2
            bank = ctx.psum[bi]
            mm = None
            for i in range(NKV):
                mm = p.op("pe", lambda e, bank=bank, b=b, i=i, tc=tc:
                          e.matmul(bank[:], wall[b][:, i * OC:(i + 1) * OC],
                                   kv_g[i][:, tc * 512:(tc + 1) * 512],
                                   start=(i == 0), stop=(i == NKV - 1)),
                          waits=[wtok, kv_ready] + (bf[bi] if i == 0 else []),
                          inc="pe")
            wall_use[b] = mm
            ke = p.op("dve", lambda e, bank=bank, tc=tc:
                      e.tensor_tensor(ktile[:], bank[:], rbckv[:, tc * 512:(tc + 1) * 512], ALU.mult),
                      waits=[mm, rbckv_toks[tc]] + (ktile_use or []), inc="dve")
            rbc_use = [ke]
            bf[bi] = [ke]
            sx = None
            for j in range(4):
                sx = p.dma("sp", k_scr[h, tc * 4 + j], ktile[:, j * 128:(j + 1) * 128],
                           waits=[ke], sem=ks_sem)
            ab = p.op("dve", lambda e: e.tensor_reduce(kab_tmp[:], ktile[:], AX.X, ALU.max,
                                                       apply_absolute_value=True),
                      waits=[ke], inc="dve")
            am = p.op("dve", lambda e, h=h, tc=tc:
                      (e.tensor_copy(kab_pos[:, h:h + 1], kab_tmp[:]) if tc == 0 else
                       e.tensor_tensor(kab_pos[:, h:h + 1], kab_pos[:, h:h + 1], kab_tmp[:], ALU.max)),
                      waits=[ab] + ([kab_prev.get(h)] if kab_prev.get(h) else []), inc="dve")
            kab_prev[h] = am
            ktile_use = [ab, sx]
        ng = p.op("dve", lambda e, h=h: e.tensor_scalar(kab_nope[:, h:h + 1], kab_pos[:, h:h + 1],
                                                        -1.0, None, ALU.mult),
                  waits=[kab_prev[h]], inc="dve")
        kab_prev[h] = ng
    k_spilled = p.tok(ks_sem)
    pb = p.op("dve", lambda e: e.tensor_reduce(kab_tmp[0:64, :], kpe_r[0:64, :], AX.X, ALU.max,
                                               apply_absolute_value=True, negate=True),
              waits=[kpe_all], inc="dve")
    pe_ab = p.op("dve", lambda e: e.tensor_copy(kab_pe[:], kab_tmp[0:64, :]),
                 waits=[pb], inc="dve")

    vs_sem = f"l0vs{u}"
    vtile_use = None
    for oc in range(H * VD // OC):
        b = oc % 2
        wtok = wall_load(b, wsrc_d["w_uv"], oc * OC, OC, NKV)
        for tk in range(NKT):
            bi = (oc * NKT + tk) % 2
            bank = ctx.psum[bi]
            mm = None
            for i in range(NKV):
                mm = p.op("pe", lambda e, bank=bank, b=b, i=i, tk=tk:
                          e.matmul(bank[:, 0:OC], kv_g[i][:, tk * 128:(tk + 1) * 128],
                                   wall[b][:, i * OC:(i + 1) * OC],
                                   start=(i == 0), stop=(i == NKV - 1)),
                          waits=[wtok, kv_ready] + (bf[bi] if i == 0 else []),
                          inc="pe")
            wall_use[b] = mm
            ve = p.op("dve", lambda e, bank=bank, tk=tk:
                      e.tensor_scalar(ktile[:, 0:OC], bank[:, 0:OC], rstdkv_col[:, tk:tk + 1],
                                      None, ALU.mult),
                      waits=[mm, rcol_all] + (vtile_use if vtile_use else (ktile_use or [])),
                      inc="dve")
            bf[bi] = [ve]
            sx = p.dma("sp", v_scr[tk, oc], ktile[:, 0:OC], waits=[ve], sem=vs_sem)
            vtile_use = [sx]
    v_spilled = p.tok(vs_sem)
    rbcq = rbckv
    kv_rbc_done = p.tok("dve")
    rbcq_toks = []
    for qb in range(NQ):
        bm = ctx.bcast_f32(rstdq[0:1, qb * 512:(qb + 1) * 512], 3, [rq_tok])
        be = p.op("act", lambda e, qb=qb: e.activation(rbcq[:, qb * 512:(qb + 1) * 512],
                                                       ctx.psum[3][:], AF.Copy),
                  waits=[bm, kv_rbc_done] + bf[3], inc="act")
        bf[3] = [be]
        rbcq_toks.append(be)
    ctx.pop_scope()
    p.region([(s2_, v2_) for (s2_, v2_) in p.all_toks() if not s2_.startswith("cc")])
    if phase_limit <= 4:
        ctx.pop_scope()
        return [p.tok("pe")]

    o_all = xfm
    qkv_done_pe = p.tok("pe")
    pt = [ctx.sb(f"l0pt{u}_{i}", [128, 512], BF16) for i in range(2)]
    pt_use = [None] * 2
    qn = ctx.sb(f"l0qn{u}", [128, 512], BF16)
    qaug = ctx.sb(f"l0qaug{u}", [65, 512], BF16)
    qpe_tmp, qpe_sw = kpe_tmp, kpe_sw
    absq = ctx.sb(f"l0absq{u}", [128, 512], BF16)
    absp = ctx.sb(f"l0absp{u}", [64, 512], BF16)
    den = ctx.sb(f"l0den{u}", [1, 512], F32)
    NQL = QL // 128
    qlb = [ctx.sb(f"l0qlb{u}_{b}", [128, NQL * 512], BF16) for b in range(2)]
    qlb_sem = [f"l0qlbs{u}_{b}" for b in range(2)]
    qlb_use = [None, None]
    _whb1 = ctx.sb(f"l0whb{u}", [128, NQL * 192], BF16)
    whb = [_whb1, _whb1]
    whb_sem = [f"l0whbs{u}", f"l0whbs{u}"]
    whb_use = [None, None]
    kb = [ctx.sb(f"l0kb{u}_{b}", [128, 128], BF16) for b in range(3)]
    kb_sem = [f"l0kbs{u}_{b}" for b in range(3)]
    kb_use = [None, None, None]
    vb = [ctx.sb(f"l0vb{u}_{b}", [128, 128], BF16) for b in range(3)]
    vb_sem = [f"l0vbs{u}_{b}" for b in range(3)]
    vb_use = [None, None, None]
    stb = [0, 1, 6]
    aug_use = None
    qn_use = None
    rope_use2 = rope_use
    absq_use = None
    orbc_use = rbc_use
    for h in range(H):
        bw = 0
        wtok = None
        for i in range(NQL):
            wtok = p.dma("sp", whb[bw][:, i * 192:(i + 1) * 192],
                         wsrc_d["w_uq"][i * 128:(i + 1) * 128, h * 192:(h + 1) * 192],
                         waits=list(cast_ready) + ([whb_use[bw]] if whb_use[bw] else []),
                         sem=whb_sem[bw])
        wready = (whb_sem[bw], wtok[1])
        for qb in range(NQ):
            bq = (h * NQ + qb) % 2
            qltok = None
            for i in range(NQL):
                qltok = p.dma("sp", qlb[bq][:, i * 512:(i + 1) * 512],
                              q_scr[i * 128:(i + 1) * 128, qb * 512:(qb + 1) * 512],
                              waits=qlat_stored + ([qlb_use[bq]] if qlb_use[bq] else []),
                              sem=qlb_sem[bq])
            qlready = (qlb_sem[bq], qltok[1])
            mmn = None
            for i in range(NQL):
                mmn = p.op("pe", lambda e, bq=bq, bw=bw, i=i:
                           e.matmul(ctx.psum[5][:], whb[bw][:, i * 192:i * 192 + 128],
                                    qlb[bq][:, i * 512:(i + 1) * 512],
                                    start=(i == 0), stop=(i == NQL - 1)),
                           waits=[wready, qlready] + (bf[5] if i == 0 else []), inc="pe")
            mmp = None
            for i in range(NQL):
                mmp = p.op("pe", lambda e, bq=bq, bw=bw, i=i:
                           e.matmul(ctx.psum[2][0:64, :], whb[bw][:, i * 192 + 128:(i + 1) * 192],
                                    qlb[bq][:, i * 512:(i + 1) * 512],
                                    start=(i == 0), stop=(i == NQL - 1)),
                           waits=[wready, qlready] + (bf[2] if i == 0 else []), inc="pe")
            whb_use[bw] = mmp
            qlb_use[bq] = mmp
            qe = p.op("dve", lambda e, qb=qb: e.tensor_tensor(qn[:], ctx.psum[5][:],
                                                              rbcq[:, qb * 512:(qb + 1) * 512], ALU.mult),
                      waits=[mmn, rbcq_toks[qb]] + ([qn_use] if qn_use else []), inc="dve")
            bf[5] = [qe]
            pe1 = p.op("act", lambda e: e.activation(qpe_tmp[:], ctx.psum[2][0:64, :], AF.Copy),
                       waits=[mmp] + ([rope_use2] if rope_use2 else []), inc="act")
            pe2 = p.op("act", lambda e: e.activation(qpe_sw[0:32, :], ctx.psum[2][32:64, :], AF.Copy),
                       waits=[mmp], inc="act")
            pe3 = p.op("act", lambda e: e.activation(qpe_sw[32:64, :], ctx.psum[2][0:32, :], AF.Copy),
                       waits=[mmp], inc="act")
            bf[2] = [pe1, pe2, pe3]
            u1 = p.op("dve", lambda e, qb=qb: e.tensor_tensor(qpe_tmp[:], qpe_tmp[:],
                                                              ra[:, qb * 512:(qb + 1) * 512], ALU.mult),
                      waits=[pe1, ra_tok], inc="dve")
            u2 = p.op("dve", lambda e, qb=qb: e.tensor_tensor(qpe_sw[:], qpe_sw[:],
                                                              rb[:, qb * 512:(qb + 1) * 512], ALU.mult),
                      waits=[pe3, rb_tok], inc="dve")
            u3 = p.op("dve", lambda e: e.tensor_tensor(qpe_tmp[:], qpe_tmp[:], qpe_sw[:], ALU.add),
                      waits=[u1, u2], inc="dve")
            u4 = p.op("dve", lambda e, qb=qb: e.tensor_tensor(qaug[0:64, :], qpe_tmp[:],
                                                              rbcq[0:64, qb * 512:(qb + 1) * 512],
                                                              ALU.mult),
                      waits=[u3, rbcq_toks[qb]] + ([aug_use] if aug_use else []), inc="dve")
            rope_use2 = u4
            a1 = p.op("act", lambda e: e.activation(absq[:], qn[:], AF.Abs),
                      waits=[qe] + ([absq_use] if absq_use else []), inc="act")
            a2 = p.op("act", lambda e: e.activation(absp[:], qaug[0:64, :], AF.Abs),
                      waits=[u4], inc="act")
            bm1 = p.op("pe", lambda e, h=h: e.matmul(ctx.psum[2][0:1, :], kab_nope[:, h:h + 1],
                                                     absq[:], start=True, stop=False),
                       waits=[a1, kab_prev[h], pe1, pe2, pe3], inc="pe")
            bm2 = p.op("pe", lambda e: e.matmul(ctx.psum[2][0:1, :], kab_pe[:], absp[:],
                                                start=False, stop=True),
                       waits=[a2, pe_ab], inc="pe")
            me = p.op("act", lambda e: e.activation(qaug[64:65, :], ctx.psum[2][0:1, :], AF.Copy),
                      waits=[bm2], inc="act")
            bf[2] = [me]
            aug_use = me
            absq_use = bm1
            nkt_q = 4 * qb + 4
            den_mm = None
            o_mm = None
            for kt in range(nkt_q):
                bi = stb[kt % 3]
                bank = ctx.psum[bi]
                kslot = kt % 3
                ktok = p.dma("sp", kb[kslot][:], k_scr[h, kt],
                             waits=[k_spilled] + ([kb_use[kslot]] if kb_use[kslot] else []),
                             sem=kb_sem[kslot])
                mm1 = p.op("pe", lambda e, bank=bank, kslot=kslot:
                           e.matmul(bank[:], kb[kslot][:], qn[:], start=True, stop=False),
                           waits=[ktok, qe] + bf[bi], inc="pe")
                mm2 = p.op("pe", lambda e, bank=bank, kt=kt:
                           e.matmul(bank[:], kpe_r[:, kt * 128:(kt + 1) * 128], qaug[:],
                                    start=False, stop=True),
                           waits=[me, u4, kpe_all, on_tok], inc="pe")
                kb_use[kslot] = mm1
                ps_ = kt % 2
                diag_off = kt * 128 - qb * 512
                if 0 <= diag_off < 512:
                    ex = p.op("act", lambda e, bank=bank: e.activation(sqq[:], bank[:], AF.Exp),
                              waits=[mm2, sqq_use], inc="act")
                    exm = p.op("dve", lambda e, ps_=ps_, off=diag_off:
                               e.tensor_tensor(pt[ps_][:], sqq[:],
                                               trilw[:, 384 - off:896 - off], ALU.mult),
                               waits=[ex, trilw_tok] + ([pt_use[ps_]] if pt_use[ps_] else []),
                               inc="dve")
                    sqq_use = exm
                    exf = exm
                else:
                    exf = p.op("act", lambda e, bank=bank, ps_=ps_:
                               e.activation(pt[ps_][:], bank[:], AF.Exp),
                               waits=[mm2] + ([pt_use[ps_]] if pt_use[ps_] else []), inc="act")
                bf[bi] = [exf]
                den_mm = p.op("pe", lambda e, kt=kt, ps_=ps_, last=(kt == nkt_q - 1):
                              e.matmul(ctx.psum[3][0:1, :], ones_col[:], pt[ps_][:],
                                       start=(kt == 0), stop=last),
                              waits=[exf, ocol_tok] + (bf[3] if kt == 0 else []), inc="pe")
                vslot = kt % 3
                vtok = p.dma("sp", vb[vslot][:], v_scr[kt, h],
                             waits=[v_spilled] + ([vb_use[vslot]] if vb_use[vslot] else []),
                             sem=vb_sem[vslot])
                o_mm = p.op("pe", lambda e, kt=kt, vslot=vslot, ps_=ps_, last=(kt == nkt_q - 1):
                            e.matmul(ctx.psum[4][:], vb[vslot][:], pt[ps_][:],
                                     start=(kt == 0), stop=last),
                            waits=[exf, vtok] + (bf[4] if kt == 0 else []), inc="pe")
                vb_use[vslot] = o_mm
                pt_use[ps_] = o_mm
            d1 = p.op("act", lambda e: e.activation(den[:], ctx.psum[3][0:1, :], AF.Copy),
                      waits=[den_mm], inc="act")
            bf[3] = [d1]
            d3 = p.op("dve", lambda e: e.reciprocal(den[:], den[:]), waits=[d1], inc="dve")
            bcm = ctx.bcast_f32(den[0:1, :], 3, [d3])
            rb2 = p.op("act", lambda e: e.activation(rbc[:], ctx.psum[3][:], AF.Copy),
                       waits=[bcm] + (orbc_use or []), inc="act")
            bf[3] = [rb2]
            oe = p.op("dve", lambda e, h=h, qb=qb:
                      e.tensor_tensor(o_all[h][:, qb * 512:(qb + 1) * 512], ctx.psum[4][:], rbc[:],
                                      ALU.mult),
                      waits=[o_mm, rb2, qkv_done_pe], inc="dve")
            bf[4] = [oe]
            orbc_use = [oe]
            qn_use = mm1
    attn_done_dve = p.tok("dve")
    if phase_limit <= 5:
        return [p.tok("pe")]

    WOC = 128
    yev = evb
    yev_free = list(ev_free)
    ys_sem = [f"l0ys{u}_{b}" for b in range(2)]
    for dc in range(D // WOC):
        b = dc % 2
        wtok = None
        for hh in range(H):
            wtok = p.dma("sp", wall[b][:, hh * WOC:(hh + 1) * WOC],
                         wsrc_d["w_o"][hh * 128:(hh + 1) * 128, dc * WOC:(dc + 1) * WOC],
                         waits=list(cast_ready) + ([wall_use[b]] if wall_use[b] else []), sem=wsem[b])
        wready = (wsem[b], wtok[1])
        for ds in range(WOC // 128):
            for qb in range(NQ):
                bi = (ds * NQ + qb) % 4
                bank = ctx.psum[bi]
                mm = None
                for hh in range(H):
                    mm = p.op("pe", lambda e, bank=bank, b=b, hh=hh, ds=ds, qb=qb:
                              e.matmul(bank[:], wall[b][:, hh * WOC + ds * 128:hh * WOC + (ds + 1) * 128],
                                       o_all[hh][:, qb * 512:(qb + 1) * 512],
                                       start=(hh == 0), stop=(hh == H - 1)),
                              waits=[wready, attn_done_dve] + (bf[bi] if hh == 0 else []),
                              inc="pe")
                wall_use[b] = mm
                es_ = bi % 2
                et = p.op("act", lambda e, bank=bank, es_=es_:
                          e.activation(yev[es_][:], bank[:], AF.Copy),
                          waits=[mm] + ([yev_free[es_]] if yev_free[es_] else []), inc="act")
                bf[bi] = [et]
                sx = p.dma("sp", y_fm[(dc * (WOC // 128) + ds) * 128:(dc * (WOC // 128) + ds + 1) * 128,
                                      qb * 512:(qb + 1) * 512],
                           yev[es_][:], waits=[et] + list(y_guard), sem=ys_sem[es_])
                yev_free[es_] = sx
    scr["poolA_free"] = [p.tok("pe")]
    return [p.tok(s) for s in ys_sem]


def build_l0_attn_launch(cfg, consts, phase_limit=99):
    nc = bass.Bass("TRN2", target_bir_lowering=False, debug=False)
    c = cfg
    T, D = c.T, c.D
    x_in = nc.dram_tensor("x_in", [T, D], F32, kind="ExternalInput").ap()
    wnames = {
        "w_dq": [D, c.QL], "g_q": [c.QL], "w_uq": [c.QL, c.H * c.QKD],
        "w_dkv": [D, c.KVL], "w_kr": [D, c.ROPE], "g_kv": [c.KVL],
        "w_uk": [c.KVL, c.H * c.NOPE], "w_uv": [c.KVL, c.H * c.VD], "w_o": [c.H * c.VD, D],
    }
    w = {n: nc.dram_tensor(n, s, F32, kind="ExternalInput").ap() for n, s in wnames.items()}
    x_out = nc.dram_tensor("x_out", [T, D], F32, kind="ExternalOutput").ap()
    const_aps = {}
    for name, arr in consts.items():
        dt = {np.dtype(np.float32): F32, np.dtype(BF): BF16, np.dtype(np.int32): I32}[arr.dtype]
        const_aps[name] = nc.dram_tensor(name, list(arr.shape), dt, kind="ExternalInput").ap()
    xln = nc.dram_tensor("s_xln", [T, D], BF16).ap()
    y_fm = nc.dram_tensor("s_yfm", [D, T], BF16).ap()
    scr = {
        "l0_q": nc.dram_tensor("s_q", [c.QL, T], BF16).ap(),
        "l0_k": nc.dram_tensor("s_k", [c.H, T // 128, 128, 128], BF16).ap(),
        "l0_v": nc.dram_tensor("s_v", [T // 128, c.H, 128, 128], BF16).ap(),
    }
    wl0_bf = {n: nc.dram_tensor("s_bf_" + n, s, BF16).ap() for n, s in wnames.items()
              if n not in ("g_q", "g_kv")}
    es = ExitStack()
    with es:
        ctx = Ctx(nc, es, c, const_aps)
        for nm2, ap2 in wl0_bf.items():
            rows = wnames[nm2][0]
            r0 = 0
            while r0 < rows:
                rr = min(512, rows - r0)
                ctx.p.dma("po", ap2[r0:r0 + rr, :], w[nm2][r0:r0 + rr, :], sem="l0cast")
                r0 += rr
        l0c = ctx.p.tok("l0cast")
        state = {}
        ln_toks = emit_sandwich(ctx, x_in, None, None, xln, True, state)
        y_toks = emit_l0_attn(ctx, xln, ln_toks, w, y_fm, scr, wbf=wl0_bf, cast_ready=[l0c],
                              phase_limit=phase_limit)
        if phase_limit > 6:
            emit_sandwich(ctx, x_in, y_fm, x_out, xln, False, state, y_ready=y_toks, write_xln=False)
        final = ctx.p.all_toks()
        ctx.p.op("sp", lambda e: e.nop(), waits=final)
        ctx.p.emit(nc, es)
    return nc

def emit_gated_ffn(ctx, xln_dram, xln_ready, w1f, w2f, w3f, act_fn, F, y_fm, y3p, scr,
                   w_ready=(), y_guard=()):
    c, p = ctx.c, ctx.p
    T, D = c.T, c.D
    NS = D // 128
    NTC = T // 512
    NF = F // 128
    u = ctx.uid()
    bf = ctx.bank_free

    xfm = ctx.strip_set("A")
    xfm_tok = emit_load_xln_fm(ctx, xln_dram, xln_ready, xfm)
    hid = ctx.strip_set("H", 11)

    OCW = 256
    wall = [[ctx.sb(f"gf{u}_wa{wi}{b}", [128, NS * OCW], BF16) for b in range(2)] for wi in range(2)]
    wsem = [["gf_ws%d%d" % (wi, b) for b in range(2)] for wi in range(2)]
    wall_use = [[None, None], [None, None]]
    wall3 = [ctx.sb(f"gf{u}_w3{b}", [128, 11 * 512], BF16) for b in range(2)]
    w3sem = ["gf_w3s%d" % b for b in range(2)]
    wall3_use = [None, None]
    evs = ctx.sb(f"gf{u}_ev", [128, 512], BF16)
    evs_use = None
    ybuf = [ctx.sb(f"gf{u}_yb{b}", [128, 512], F32) for b in range(2)]
    ybuf_use = [None, None]
    ypin = [ctx.sb(f"gf{u}_yp{b}", [128, 512], F32) for b in range(2)]
    ypin_sem = ["gf_yps" + str(b) for b in range(2)]
    ypin_use = [None, None]
    ybf = ctx.sb(f"gf{u}_ybf", [128, 512], BF16)
    ybf_use = None
    ys_sem = "gf_ys"
    yp_sem = "gf_yo"

    quarters = []
    base = 0
    for qi in range(4):
        nq = (NF + 3 - qi) // 4
        quarters.append((base, nq))
        base += nq
    assert base == NF

    hb_toks = [None] * 11
    for qi, (qbase, qn) in enumerate(quarters):
        oc0 = qbase * 128
        oc1 = (qbase + qn) * 128
        chunks = []
        o0 = oc0
        while o0 < oc1:
            ow = min(OCW, oc1 - o0)
            chunks.append((o0, ow))
            o0 += ow
        wtoks = {}

        def load_chunk(ci):
            o0, ow = chunks[ci]
            b = ci % 2
            for wi in range(2):
                wf = (w1f, w2f)[wi]
                tok = None
                for i in range(NS):
                    tok = p.dma("sp", wall[wi][b][:, i * OCW:i * OCW + ow],
                                wf(o0, i * 128, ow, 128),
                                waits=list(w_ready) +
                                      ([wall_use[wi][b]] if wall_use[wi][b] else []),
                                sem=wsem[wi][b], transpose=True)
                wtoks[(ci, wi)] = (wsem[wi][b], tok[1])

        load_chunk(0)
        for ci, (o0, ow) in enumerate(chunks):
            if ci + 1 < len(chunks):
                load_chunk(ci + 1)
            b = ci % 2
            now = ow // 128
            w1r, w2r = wtoks[(ci, 0)], wtoks[(ci, 1)]
            mm_last = [None, None]
            for so in range(now):
                j = (o0 - qbase * 128) // 128 + so
                for tc in range(NTC):
                    b1, b2 = (tc % 2) * 2, (tc % 2) * 2 + 1
                    mm1 = mm2 = None
                    for i in range(NS):
                        mm1 = p.op("pe", lambda e, b1=b1, i=i, tc=tc, so=so, b=b:
                                   e.matmul(ctx.psum[b1][:],
                                            wall[0][b][:, i * OCW + so * 128:i * OCW + (so + 1) * 128],
                                            xfm[i][:, tc * 512:(tc + 1) * 512],
                                            start=(i == 0), stop=(i == NS - 1)),
                                   waits=[w1r, xfm_tok] + (bf[b1] if i == 0 else []), inc="pe")
                    for i in range(NS):
                        mm2 = p.op("pe", lambda e, b2=b2, i=i, tc=tc, so=so, b=b:
                                   e.matmul(ctx.psum[b2][:],
                                            wall[1][b][:, i * OCW + so * 128:i * OCW + (so + 1) * 128],
                                            xfm[i][:, tc * 512:(tc + 1) * 512],
                                            start=(i == 0), stop=(i == NS - 1)),
                                   waits=[w2r, xfm_tok] + (bf[b2] if i == 0 else []), inc="pe")
                    mm_last = [mm1, mm2]
                    a1 = p.op("act", lambda e, b1=b1: e.activation(evs[:], ctx.psum[b1][:], act_fn),
                              waits=[mm1] + ([evs_use] if evs_use else []), inc="act")
                    bf[b1] = [a1]
                    hv = p.op("dve", lambda e, b2=b2, j=j, tc=tc:
                              e.tensor_tensor(hid[j][:, tc * 512:(tc + 1) * 512], evs[:],
                                              ctx.psum[b2][:], ALU.mult),
                              waits=[a1, mm2] + ([hb_toks[j]] if (tc == 0 and hb_toks[j]) else []),
                              inc="dve")
                    bf[b2] = [hv]
                    evs_use = hv
                    hb_toks[j] = hv
            wall_use[0][b], wall_use[1][b] = mm_last
        w3toks = {}

        def load_w3(dcg):
            b = dcg % 2
            tok = None
            for j in range(qn):
                tok = p.dma("sp", wall3[b][:, j * 512:(j + 1) * 512],
                            w3f(dcg * 512, (qbase + j) * 128, 512, 128),
                            waits=list(w_ready) + ([wall3_use[b]] if wall3_use[b] else []),
                            sem=w3sem[b], transpose=True)
            w3toks[dcg] = (w3sem[b], tok[1])

        load_w3(0)
        for dcg in range(4):
            if dcg + 1 < 4:
                load_w3(dcg + 1)
            b = dcg % 2
            wr = w3toks[dcg]
            mml = None
            for ds in range(4):
                dc = dcg * 4 + ds
                for tc in range(NTC):
                    bi = 4 + (ds * NTC + tc) % 2
                    mm = None
                    for j in range(qn):
                        mm = p.op("pe", lambda e, bi=bi, j=j, tc=tc, ds=ds, qn=qn, b=b:
                                  e.matmul(ctx.psum[bi][:],
                                           wall3[b][:, j * 512 + ds * 128:j * 512 + (ds + 1) * 128],
                                           hid[j][:, tc * 512:(tc + 1) * 512],
                                           start=(j == 0), stop=(j == qn - 1)),
                                  waits=[wr, hb_toks[j]] + (bf[bi] if j == 0 else []), inc="pe")
                    mml = mm
                    row = slice(dc * 128, (dc + 1) * 128)
                    colc = slice(tc * 512, (tc + 1) * 512)
                    yslot = tc % 2
                    if qi == 0:
                        ye = p.op("act", lambda e, bi=bi, yslot=yslot:
                                  e.activation(ybuf[yslot][:], ctx.psum[bi][:], AF.Copy),
                                  waits=[mm] + ([ybuf_use[yslot]] if ybuf_use[yslot] else []),
                                  inc="act")
                        bf[bi] = [ye]
                        sx = p.dma("sp", y3p[row, colc], ybuf[yslot][:], waits=[ye], sem=yp_sem)
                        ybuf_use[yslot] = sx
                    else:
                        ptok = p.dma("sp", ypin[yslot][:], y3p[row, colc],
                                     waits=[p.tok(yp_sem)] +
                                           ([ypin_use[yslot]] if ypin_use[yslot] else []),
                                     sem=ypin_sem[yslot])
                        if qi < 3:
                            ye = p.op("dve", lambda e, bi=bi, yslot=yslot:
                                      e.tensor_tensor(ybuf[yslot][:], ctx.psum[bi][:],
                                                      ypin[yslot][:], ALU.add),
                                      waits=[mm, ptok] + ([ybuf_use[yslot]] if ybuf_use[yslot] else []),
                                      inc="dve")
                            bf[bi] = [ye]
                            ypin_use[yslot] = ye
                            sx = p.dma("sp", y3p[row, colc], ybuf[yslot][:], waits=[ye], sem=yp_sem)
                            ybuf_use[yslot] = sx
                        else:
                            ye = p.op("dve", lambda e, bi=bi, yslot=yslot:
                                      e.tensor_tensor(ybf[:], ctx.psum[bi][:], ypin[yslot][:],
                                                      ALU.add),
                                      waits=[mm, ptok] + ([ybf_use] if ybf_use else []), inc="dve")
                            bf[bi] = [ye]
                            ypin_use[yslot] = ye
                            sx = p.dma("sp", y_fm[row, colc], ybf[:], waits=[ye] + list(y_guard),
                                       sem=ys_sem)
                            ybf_use = sx
            wall3_use[b] = mml
    return [p.tok(ys_sem)]


def build_ffn_launch(cfg, consts, act_fn, F):
    nc = bass.Bass("TRN2", target_bir_lowering=False, debug=False)
    c = cfg
    T, D = c.T, c.D
    x_in = nc.dram_tensor("x_in", [T, D], F32, kind="ExternalInput").ap()
    w1T = nc.dram_tensor("w1_bf", [F, D], BF16, kind="ExternalInput").ap()
    w2T = nc.dram_tensor("w2_bf", [F, D], BF16, kind="ExternalInput").ap()
    w3T = nc.dram_tensor("w3_bf", [D, F], BF16, kind="ExternalInput").ap()
    x_out = nc.dram_tensor("x_out", [T, D], F32, kind="ExternalOutput").ap()
    const_aps = {}
    for name, arr in consts.items():
        dt = {np.dtype(np.float32): F32, np.dtype(BF): BF16, np.dtype(np.int32): I32}[arr.dtype]
        const_aps[name] = nc.dram_tensor(name, list(arr.shape), dt, kind="ExternalInput").ap()
    xln = nc.dram_tensor("s_xln", [T, D], BF16).ap()
    y_fm = nc.dram_tensor("s_yfm", [D, T], BF16).ap()
    y3p = nc.dram_tensor("s_y3p", [D, T], F32).ap()
    es = ExitStack()
    with es:
        ctx = Ctx(nc, es, c, const_aps)
        state = {}
        ln_toks = emit_sandwich(ctx, x_in, None, None, xln, True, state)
        w1f = lambda o0, d0, no, nd: w1T[o0:o0 + no, d0:d0 + nd]
        w2f = lambda o0, d0, no, nd: w2T[o0:o0 + no, d0:d0 + nd]
        w3f = lambda d0, h0, nd, nh: w3T[d0:d0 + nd, h0:h0 + nh]
        y_toks = emit_gated_ffn(ctx, xln, ln_toks, w1f, w2f, w3f, act_fn, F, y_fm, y3p, {})
        emit_sandwich(ctx, x_in, y_fm, x_out, xln, False, state, y_ready=y_toks, write_xln=False)
        final = ctx.p.all_toks()
        ctx.p.op("sp", lambda e: e.nop(), waits=final)
        ctx.p.emit(nc, es)
    return nc

COLS_W = 2048 // 8
ROWS_W3 = 2048 // 8
N_ELEM_W = 5632 * 2048


def _prep_tiles(cfg):
    return [
        ("w1s", "wq1", 5632 // 128, [(0, COLS_W)], 0),
        ("w2s", "wq2", 5632 // 128, [(0, COLS_W)], 1),
        ("w3s", "wq3", ROWS_W3 // 128, [(0, 2048), (2048, 2048), (4096, 1536)], 2),
    ]


def build_prep_a(cfg):
    nc = bass.Bass("TRN2", target_bir_lowering=False, debug=False)
    w1s = nc.dram_tensor("w1s", [5632, COLS_W], F32, kind="ExternalInput").ap()
    w2s = nc.dram_tensor("w2s", [5632, COLS_W], F32, kind="ExternalInput").ap()
    w3s = nc.dram_tensor("w3s", [ROWS_W3, 5632], F32, kind="ExternalInput").ap()
    gpart = nc.dram_tensor("gpart", [1, 3], F32, kind="ExternalOutput").ap()
    es = ExitStack()
    with es:
        ctx = Ctx(nc, es, cfg, {})
        emit_prep_a(ctx, {"w1s": w1s, "w2s": w2s, "w3s": w3s}, gpart)
        ctx.p.op("sp", lambda e: e.nop(), waits=ctx.p.all_toks())
        ctx.p.emit(nc, es)
    return nc


def emit_prep_a(ctx, w, gpart):
    p = ctx.p
    u = ctx.uid()
    buf = [ctx.sb(f"pa{u}_b{b}", [128, 512], F32) for b in range(2)]
    bsem = [f"pa{u}_s{b}" for b in range(2)]
    buse = [None, None]
    acc = ctx.sb(f"pa{u}_acc", [128, 3], F32)
    red = ctx.sb(f"pa{u}_red", [128, 1], F32)
    onesf = ctx.sb(f"pa{u}_ones", [128, 1], F32)
    gp = ctx.sb(f"pa{u}_gp", [1, 3], F32)
    ot = p.op("po", lambda e: e.memset(onesf[:], 1.0), inc="po")
    zt = p.op("po", lambda e: e.memset(acc[:], 0.0), inc="po")
    srcs = [(w["w1s"], 5632 // 128, [(0, COLS_W)], 0),
            (w["w2s"], 5632 // 128, [(0, COLS_W)], 1),
            (w["w3s"], ROWS_W3 // 128, [(i * 512, 512) for i in range(11)], 2)]
    red_use = None
    acc_use = zt
    for src, ntile, colspans, j in srcs:
        for it in range(ntile):
            for c0, cw in colspans:
                b = it % 2
                tok = p.dma("act", buf[b][:, 0:cw], src[it * 128:(it + 1) * 128, c0:c0 + cw],
                            waits=([buse[b]] if buse[b] else []), sem=bsem[b])
                r = p.op("dve", lambda e, b=b, cw=cw:
                         e.tensor_reduce(red[:], buf[b][:, 0:cw], AX.X, ALU.add,
                                         apply_absolute_value=True),
                         waits=[tok] + ([red_use] if red_use else []), inc="dve")
                buse[b] = r
                a = p.op("dve", lambda e, j=j: e.tensor_tensor(acc[:, j:j + 1], acc[:, j:j + 1],
                                                               red[:], ALU.add),
                         waits=[r, acc_use], inc="dve")
                red_use = a
                acc_use = a
    prev = ctx.bank_free[0] or []
    mm = None
    for j in range(3):
        mm = p.op("pe", lambda e, j=j: e.matmul(ctx.psum[0][0:1, j:j + 1], acc[:, j:j + 1],
                                                onesf[:], start=True, stop=True),
                  waits=[acc_use, ot] + prev + ([mm] if mm else []), inc="pe")
    ev = p.op("act", lambda e: e.activation(gp[:], ctx.psum[0][0:1, 0:3], AF.Copy),
              waits=[mm], inc="act")
    ctx.bank_free[0] = [ev]
    st = p.dma("sp", gpart, gp[:], waits=[ev], sem=f"pa{u}_st")
    return st


def build_prep_b(cfg):
    nc = bass.Bass("TRN2", target_bir_lowering=False, debug=False)
    w1s = nc.dram_tensor("w1s", [5632, COLS_W], F32, kind="ExternalInput").ap()
    w2s = nc.dram_tensor("w2s", [5632, COLS_W], F32, kind="ExternalInput").ap()
    w3s = nc.dram_tensor("w3s", [ROWS_W3, 5632], F32, kind="ExternalInput").ap()
    gparts = nc.dram_tensor("gparts", [8, 3], F32, kind="ExternalInput").ap()
    lut = nc.dram_tensor("lut", [1, 16], F32, kind="ExternalInput").ap()
    gis = nc.dram_tensor("gis", [5504, COLS_W], I32, kind="ExternalInput").ap()
    uis = nc.dram_tensor("uis", [5504, COLS_W], I32, kind="ExternalInput").ap()
    dis = nc.dram_tensor("dis", [ROWS_W3, 5504], I32, kind="ExternalInput").ap()
    wq1 = nc.dram_tensor("wq1", [5632, COLS_W], BF16, kind="ExternalOutput").ap()
    wq2 = nc.dram_tensor("wq2", [5632, COLS_W], BF16, kind="ExternalOutput").ap()
    wq3 = nc.dram_tensor("wq3", [ROWS_W3, 5632], BF16, kind="ExternalOutput").ap()
    gw = nc.dram_tensor("gw", [5504, COLS_W], BF16, kind="ExternalOutput").ap()
    uw = nc.dram_tensor("uw", [5504, COLS_W], BF16, kind="ExternalOutput").ap()
    dw = nc.dram_tensor("dw", [ROWS_W3, 5504], BF16, kind="ExternalOutput").ap()
    es = ExitStack()
    with es:
        ctx = Ctx(nc, es, cfg, {})
        emit_prep_b(ctx, {"w1s": w1s, "w2s": w2s, "w3s": w3s, "gis": gis, "uis": uis,
                          "dis": dis}, gparts, lut,
                    {"wq1": wq1, "wq2": wq2, "wq3": wq3, "gw": gw, "uw": uw, "dw": dw})
        ctx.p.op("sp", lambda e: e.nop(), waits=ctx.p.all_toks())
        ctx.p.emit(nc, es)
    return nc


def emit_prep_b(ctx, w, gparts, lut, outs, gamma_pre=None, gamma_wait=()):
    c, p = ctx.c, ctx.p
    u = ctx.uid()
    bf = ctx.bank_free
    onesr = ctx.sb(f"pb{u}_or", [1, 128], F32)
    ones8 = ctx.sb(f"pb{u}_o8", [8, 1], F32)
    ot = p.op("po", lambda e: e.memset(onesr[:], 1.0), inc="po")
    o8 = p.op("po", lambda e: e.memset(ones8[:], 1.0), inc="po")
    if gamma_pre is None:
        gpt = ctx.sb(f"pb{u}_gp", [8, 3], F32)
        gptok = p.dma("act", gpt[:], gparts, sem=f"pb{u}_gp")
    lutr = ctx.sb(f"pb{u}_lut", [1, 16], F32)
    luttok = p.dma("act", lutr[:], lut, sem=f"pb{u}_lut")
    lutc = ctx.sb(f"pb{u}_lutc", [128, 16], F32)
    mm = p.op("pe", lambda e: e.matmul(ctx.psum[0][:, 0:16], onesr[:], lutr[:],
                                       start=True, stop=True),
              waits=[ot, luttok] + (bf[0] or []), inc="pe")
    lc = p.op("act", lambda e: e.activation(lutc[:], ctx.psum[0][:, 0:16], AF.Copy),
              waits=[mm], inc="act")
    bf[0] = [lc]
    grow = ctx.sb(f"pb{u}_grow", [1, 6], F32)
    if gamma_pre is None:
        gprev = None
        for j in range(3):
            gm = p.op("pe", lambda e, j=j: e.matmul(ctx.psum[1][0:1, j:j + 1], gpt[:, j:j + 1],
                                                    ones8[:], start=True, stop=True),
                      waits=[gptok, o8] + (bf[1] if j == 0 else []) + ([gprev] if gprev else []),
                      inc="pe")
            gprev = gm
        ge = p.op("act", lambda e: e.activation(grow[0:1, 0:3], ctx.psum[1][0:1, 0:3], AF.Copy),
                  waits=[gprev], inc="act")
        bf[1] = [ge]
    else:
        ge = p.dma("act", grow[0:1, 0:3], gamma_pre, waits=list(gamma_wait), sem=f"pb{u}_gpre")
    g2 = p.op("po", lambda e: e.tensor_scalar(grow[0:1, 0:3], grow[0:1, 0:3],
                                              0.5 / N_ELEM_W, 0.5 * c.EPS_BIT,
                                              ALU.mult, ALU.add),
              waits=[ge], inc="po")
    g3 = p.op("po", lambda e: e.tensor_scalar(grow[0:1, 3:6], grow[0:1, 0:3], -1.0, None, ALU.mult),
              waits=[g2], inc="po")
    thr = ctx.sb(f"pb{u}_thr", [128, 6], F32)
    bm = p.op("pe", lambda e: e.matmul(ctx.psum[2][:, 0:6], onesr[:], grow[:],
                                       start=True, stop=True),
              waits=[g3, ot] + (bf[2] or []), inc="pe")
    bt = p.op("act", lambda e: e.activation(thr[:], ctx.psum[2][:, 0:6], AF.Copy),
              waits=[bm], inc="act")
    bf[2] = [bt]
    buf = [ctx.sb(f"pb{u}_b{b}", [128, 512], F32) for b in range(2)]
    bsem = [f"pb{u}_s{b}" for b in range(2)]
    buse = [None, None]
    qa = ctx.sb(f"pb{u}_qa", [128, 512], BF16)
    qb2 = ctx.sb(f"pb{u}_qb", [128, 512], BF16)
    qo = ctx.sb(f"pb{u}_qo", [128, 512], BF16)
    osem = f"pb{u}_os"
    qo_use = None
    qa_use = None
    qsrcs = [(w["w1s"], outs["wq1"], 5632 // 128, [(0, COLS_W)], 0),
             (w["w2s"], outs["wq2"], 5632 // 128, [(0, COLS_W)], 1),
             (w["w3s"], outs["wq3"], ROWS_W3 // 128, [(i * 512, 512) for i in range(11)], 2)]
    for src, dst, ntile, colspans, j in qsrcs:
        for it in range(ntile):
            for c0, cw in colspans:
                b = it % 2
                tok = p.dma("act", buf[b][:, 0:cw], src[it * 128:(it + 1) * 128, c0:c0 + cw],
                            waits=([buse[b]] if buse[b] else []), sem=bsem[b])
                t1 = p.op("po", lambda e, b=b, cw=cw, j=j:
                          e.tensor_scalar(qa[:, 0:cw], buf[b][:, 0:cw], thr[:, j:j + 1],
                                          None, ALU.is_ge),
                          waits=[tok, bt] + ([qa_use] if qa_use else []), inc="po")
                t2 = p.op("po", lambda e, b=b, cw=cw, j=j:
                          e.tensor_scalar(qb2[:, 0:cw], buf[b][:, 0:cw], thr[:, 3 + j:4 + j],
                                          None, ALU.is_le),
                          waits=[tok, bt, t1], inc="po")
                buse[b] = t2
                t3 = p.op("po", lambda e, cw=cw: e.tensor_tensor(qo[:, 0:cw], qa[:, 0:cw],
                                                                 qb2[:, 0:cw], ALU.subtract),
                          waits=[t2] + ([qo_use] if qo_use else []), inc="po")
                qa_use = t3
                st = p.dma("sp", dst[it * 128:(it + 1) * 128, c0:c0 + cw], qo[:, 0:cw],
                           waits=[t3], sem=osem)
                qo_use = st
    ib = [ctx.sb(f"pb{u}_i{b}", [128, 704], I32) for b in range(2)]
    isem = [f"pb{u}_is{b}" for b in range(2)]
    iuse = [None, None]
    ibf = ctx.sb(f"pb{u}_ibf", [128, 704], BF16)
    macc = [ctx.sb(f"pb{u}_m{b}", [128, 704], BF16) for b in range(2)]
    mtile = ctx.sb(f"pb{u}_mt", [128, 704], BF16)
    ibf_use = None
    m_use = None
    gsrcs = [(w["gis"], outs["gw"], 5504 // 128, [(0, COLS_W)]),
             (w["uis"], outs["uw"], 5504 // 128, [(0, COLS_W)]),
             (w["dis"], outs["dw"], ROWS_W3 // 128, [(i * 688, 688) for i in range(8)])]
    for src, dst, ntile, colspans in gsrcs:
        for it in range(ntile):
            for c0, cw in colspans:
                b = it % 2
                tok = p.dma("act", ib[b][:, 0:cw], src[it * 128:(it + 1) * 128, c0:c0 + cw],
                            waits=([iuse[b]] if iuse[b] else []), sem=isem[b])
                cv = p.op("po", lambda e, b=b, cw=cw: e.tensor_copy(ibf[:, 0:cw], ib[b][:, 0:cw]),
                          waits=[tok] + ([ibf_use] if ibf_use else []), inc="po")
                iuse[b] = cv
                prev = None
                for j in range(16):
                    am = j % 2
                    if j == 0:
                        prev = p.op("po", lambda e, cw=cw, j=j, am=am:
                                    e.tensor_scalar(macc[am][:, 0:cw], ibf[:, 0:cw],
                                                    float(j), lutc[:, j:j + 1],
                                                    ALU.is_equal, ALU.mult),
                                    waits=[cv, lc] + ([m_use] if m_use else []), inc="po")
                    else:
                        mt = p.op("po", lambda e, cw=cw, j=j:
                                  e.tensor_scalar(mtile[:, 0:cw], ibf[:, 0:cw],
                                                  float(j), lutc[:, j:j + 1],
                                                  ALU.is_equal, ALU.mult),
                                  waits=[cv, lc, prev], inc="po")
                        prev = p.op("po", lambda e, cw=cw, am=am:
                                    e.tensor_tensor(macc[am][:, 0:cw], macc[1 - am][:, 0:cw],
                                                    mtile[:, 0:cw], ALU.add),
                                    waits=[mt], inc="po")
                ibf_use = prev
                st = p.dma("sp", dst[it * 128:(it + 1) * 128, c0:c0 + cw],
                           macc[1][:, 0:cw], waits=[prev], sem=osem)
                m_use = st
    return osem

def _pad_rows(a, rows):
    if a.shape[0] == rows:
        return np.ascontiguousarray(a)
    out = np.zeros((rows,) + a.shape[1:], a.dtype)
    out[:a.shape[0]] = a
    return out


def kernel(L0_attn_x, L0_attn_w_dq, L0_attn_g_q, L0_attn_w_uq, L0_attn_w_dkv,
           L0_attn_w_kr, L0_attn_g_kv, L0_attn_w_uk, L0_attn_w_uv, L0_attn_w_o,
           L0_ffn_w1, L0_ffn_w2, L0_ffn_w3, L1_attn_w_qkv, L1_attn_w_out,
           L1_ffn_lut, L1_ffn_gate_idx, L1_ffn_up_idx, L1_ffn_down_idx):
    c = CFG
    n = 8
    cores = list(range(n))
    consts = host_constants(c)
    x = np.ascontiguousarray(np.asarray(L0_attn_x, dtype=np.float32))

    c = CFG
    n = 8
    cores = list(range(n))
    consts = host_constants(c)
    x = np.ascontiguousarray(np.asarray(L0_attn_x, dtype=np.float32))
    cw, r3 = 2048 // n, 2048 // n
    wl0 = {"w_dq": np.asarray(L0_attn_w_dq), "g_q": np.asarray(L0_attn_g_q),
           "w_uq": np.asarray(L0_attn_w_uq), "w_dkv": np.asarray(L0_attn_w_dkv),
           "w_kr": np.asarray(L0_attn_w_kr), "g_kv": np.asarray(L0_attn_g_kv),
           "w_uk": np.asarray(L0_attn_w_uk), "w_uv": np.asarray(L0_attn_w_uv),
           "w_o": np.asarray(L0_attn_w_o)}
    lutr = np.ascontiguousarray(np.asarray(L1_ffn_lut, dtype=np.float32).reshape(1, 16))
    nc = build_fused(c, consts, use_collectives=True)
    in_maps = []
    for i in range(n):
        im = {"x_in": x[i], **wl0, "w_qkv": np.asarray(L1_attn_w_qkv),
              "w_out": np.asarray(L1_attn_w_out), **consts,
              "w1s": np.ascontiguousarray(np.asarray(L0_ffn_w1)[:, i * cw:(i + 1) * cw]),
              "w2s": np.ascontiguousarray(np.asarray(L0_ffn_w2)[:, i * cw:(i + 1) * cw]),
              "w3s": np.ascontiguousarray(np.asarray(L0_ffn_w3)[i * r3:(i + 1) * r3]),
              "lut": lutr,
              "gis": np.ascontiguousarray(np.asarray(L1_ffn_gate_idx)[:, i * cw:(i + 1) * cw]),
              "uis": np.ascontiguousarray(np.asarray(L1_ffn_up_idx)[:, i * cw:(i + 1) * cw]),
              "dis": np.ascontiguousarray(np.asarray(L1_ffn_down_idx)[i * r3:(i + 1) * r3])}
        in_maps.append(im)
    res = run_bass_kernel_spmd(nc, in_maps, core_ids=cores)
    return np.stack([np.asarray(res.results[i]["x_out"]) for i in range(n)], 0)


SLAB_ROWS = 5632 + 5632 + 5504 + 5504
SLAB_OFF = {"wq1": 0, "wq2": 5632, "gw": 11264, "uw": 16768}


def build_fused(cfg, consts, use_collectives=True):
    nc = bass.Bass("TRN2", target_bir_lowering=False, debug=False)
    c = cfg
    T, D = c.T, c.D
    cores = list(range(8))
    x_in = nc.dram_tensor("x_in", [T, D], F32, kind="ExternalInput").ap()
    x_out = nc.dram_tensor("x_out", [T, D], F32, kind="ExternalOutput").ap()
    wl0names = {
        "w_dq": [D, c.QL], "g_q": [c.QL], "w_uq": [c.QL, c.H * c.QKD],
        "w_dkv": [D, c.KVL], "w_kr": [D, c.ROPE], "g_kv": [c.KVL],
        "w_uk": [c.KVL, c.H * c.NOPE], "w_uv": [c.KVL, c.H * c.VD], "w_o": [c.H * c.VD, D],
    }
    wl0 = {n: nc.dram_tensor(n, s, F32, kind="ExternalInput").ap() for n, s in wl0names.items()}
    w_qkv = nc.dram_tensor("w_qkv", [3 * D, D], F32, kind="ExternalInput").ap()
    w_out = nc.dram_tensor("w_out", [D, D], F32, kind="ExternalInput").ap()
    const_aps = {}
    for name, arr in consts.items():
        dt = {np.dtype(np.float32): F32, np.dtype(BF): BF16, np.dtype(np.int32): I32}[arr.dtype]
        const_aps[name] = nc.dram_tensor(name, list(arr.shape), dt, kind="ExternalInput").ap()

    xln = nc.dram_tensor("s_xln", [T, D], BF16).ap()
    y_fm = nc.dram_tensor("s_yfm", [D, T], BF16).ap()
    y3p = nc.dram_tensor("s_y3p", [D, T], F32).ap()
    x1 = nc.dram_tensor("s_x1", [T, D], F32).ap()
    x2 = nc.dram_tensor("s_x2", [T, D], F32).ap()
    x3 = nc.dram_tensor("s_x3", [T, D], F32).ap()
    scr0 = {
        "l0_q": nc.dram_tensor("s_l0q", [c.QL, T], BF16).ap(),
        "l0_k": nc.dram_tensor("s_l0k", [c.H, T // 128, 128, 128], BF16).ap(),
        "l0_v": nc.dram_tensor("s_l0v", [T // 128, c.H, 128, 128], BF16).ap(),
    }
    wl0_bf = {n: nc.dram_tensor("s_bf_" + n, s, BF16).ap() for n, s in wl0names.items()
              if n not in ("g_q", "g_kv")}
    scr1 = {
        "l1_wqkv_bf": nc.dram_tensor("s_wqkv_bf", [3 * D, D], BF16).ap(),
        "l1_wout_bf": nc.dram_tensor("s_wout_bf", [D, D], BF16).ap(),
        "l1_q": nc.dram_tensor("s_l1q", [D, T], BF16).ap(),
        "l1_v": nc.dram_tensor("s_l1v", [T // 128, c.H, 128, 128], BF16).ap(),
    }

    if use_collectives:
        w1s = nc.dram_tensor("w1s", [5632, COLS_W], F32, kind="ExternalInput").ap()
        w2s = nc.dram_tensor("w2s", [5632, COLS_W], F32, kind="ExternalInput").ap()
        w3s = nc.dram_tensor("w3s", [ROWS_W3, 5632], F32, kind="ExternalInput").ap()
        lut = nc.dram_tensor("lut", [1, 16], F32, kind="ExternalInput").ap()
        gis = nc.dram_tensor("gis", [5504, COLS_W], I32, kind="ExternalInput").ap()
        uis = nc.dram_tensor("uis", [5504, COLS_W], I32, kind="ExternalInput").ap()
        dis = nc.dram_tensor("dis", [ROWS_W3, 5504], I32, kind="ExternalInput").ap()
        gpart = nc.dram_tensor("s_gpart", [1, 3], F32).ap()
        gsum = nc.dram_tensor("s_gsum", [1, 3], F32, addr_space="Shared").ap()
        slab_l = nc.dram_tensor("s_slab_l", [SLAB_ROWS, COLS_W], BF16).ap()
        wq3_l = nc.dram_tensor("s_wq3l", [ROWS_W3, 5632], BF16).ap()
        dw_l = nc.dram_tensor("s_dwl", [ROWS_W3, 5504], BF16).ap()
        slab_g = nc.dram_tensor("s_slab_g", [8 * SLAB_ROWS, COLS_W], BF16, addr_space="Shared").ap()
        wq3_g = nc.dram_tensor("s_wq3g", [D, 5632], BF16, addr_space="Shared").ap()
        dw_g = nc.dram_tensor("s_dwg", [D, 5504], BF16, addr_space="Shared").ap()
    else:
        slab_g = nc.dram_tensor("slab_g", [8 * SLAB_ROWS, COLS_W], BF16, kind="ExternalInput").ap()
        wq3_g = nc.dram_tensor("wq3_g", [D, 5632], BF16, kind="ExternalInput").ap()
        dw_g = nc.dram_tensor("dw_g", [D, 5504], BF16, kind="ExternalInput").ap()

    def slab_wf(key):
        off = SLAB_OFF[key]
        def wf(o0, d0, no, nd, off=off):
            r, cc = d0 // COLS_W, d0 % COLS_W
            base = r * SLAB_ROWS + off
            return slab_g[base + o0:base + o0 + no, cc:cc + nd]
        return wf

    es = ExitStack()
    with es:
        ctx = Ctx(nc, es, c, const_aps)
        ctx.sandwich_bufs()
        p = ctx.p
        for nm2, ap2 in wl0_bf.items():
            rows = wl0names[nm2][0]
            r0 = 0
            while r0 < rows:
                rr = min(512, rows - r0)
                p.dma("po", ap2[r0:r0 + rr, :], wl0[nm2][r0:r0 + rr, :], sem="l0cast")
                r0 += rr
        l0cast_tok = p.tok("l0cast")
        ffn_ready = []
        if use_collectives:
            ctx.push_scope()
            st_a = emit_prep_a(ctx, {"w1s": w1s, "w2s": w2s, "w3s": w3s}, gpart)
            cc0 = p.op("po", lambda e: e.collective_compute(
                "AllReduce", ALU.add, replica_groups=[cores], ins=[gpart], outs=[gsum]),
                waits=[st_a], inc="cc0")
            osem = emit_prep_b(
                ctx, {"w1s": w1s, "w2s": w2s, "w3s": w3s, "gis": gis, "uis": uis, "dis": dis},
                None, lut,
                {"wq1": slab_l[SLAB_OFF["wq1"]:SLAB_OFF["wq1"] + 5632, :],
                 "wq2": slab_l[SLAB_OFF["wq2"]:SLAB_OFF["wq2"] + 5632, :],
                 "gw": slab_l[SLAB_OFF["gw"]:SLAB_OFF["gw"] + 5504, :],
                 "uw": slab_l[SLAB_OFF["uw"]:SLAB_OFF["uw"] + 5504, :],
                 "wq3": wq3_l, "dw": dw_l},
                gamma_pre=gsum, gamma_wait=[cc0])
            prep_done = p.tok(osem)
            cc1 = p.op("po", lambda e: e.collective_compute(
                "AllGather", ALU.bypass, replica_groups=[cores], ins=[slab_l], outs=[slab_g]),
                waits=[prep_done], inc="cc1")
            cc2 = p.op("po", lambda e: e.collective_compute(
                "AllGather", ALU.bypass, replica_groups=[cores], ins=[wq3_l], outs=[wq3_g]),
                waits=[prep_done], inc="cc2")
            cc3 = p.op("po", lambda e: e.collective_compute(
                "AllGather", ALU.bypass, replica_groups=[cores], ins=[dw_l], outs=[dw_g]),
                waits=[prep_done], inc="cc3")
            ffn_ready = [cc1, cc2, cc3]
        state = {}
        s0 = emit_sandwich(ctx, x_in, None, None, xln, True, state)
        ctx.push_scope()
        yt = emit_l0_attn(ctx, xln, s0, wl0, y_fm, scr0, wbf=wl0_bf, cast_ready=[l0cast_tok])
        ctx.pop_scope()
        if use_collectives:
            ctx.pop_scope()
        B = [(s, v) for (s, v) in p.all_toks() if not s.startswith("cc")]
        p.region(B)
        s1 = emit_sandwich(ctx, x_in, y_fm, x1, xln, False, state, y_ready=yt)
        yg = [p.tok("sw_ys0"), p.tok("sw_ys1")]
        p.region(B)
        ctx.push_scope()
        yt = emit_gated_ffn(ctx, xln, s1, slab_wf("wq1"), slab_wf("wq2"),
                            lambda d0, h0, nd, nh: wq3_g[d0:d0 + nd, h0:h0 + nh],
                            AF.Sigmoid, 5632, y_fm, y3p, {}, w_ready=ffn_ready, y_guard=yg)
        ctx.pop_scope()
        B = [(s, v) for (s, v) in p.all_toks() if not s.startswith("cc")]
        p.region(B)
        s2 = emit_sandwich(ctx, x1, y_fm, x2, xln, False, state, y_ready=yt)
        yg = [p.tok("sw_ys0"), p.tok("sw_ys1")]
        p.region(B)
        ctx.push_scope()
        yt = emit_l1_attn(ctx, xln, s2, w_qkv, w_out, y_fm, scr1, y_guard=yg)
        ctx.pop_scope()
        B = [(s, v) for (s, v) in p.all_toks() if not s.startswith("cc")]
        p.region(B)
        s3 = emit_sandwich(ctx, x2, y_fm, x3, xln, False, state, y_ready=yt)
        yg = [p.tok("sw_ys0"), p.tok("sw_ys1")]
        p.region(B)
        ctx.push_scope()
        yt = emit_gated_ffn(ctx, xln, s3, slab_wf("gw"), slab_wf("uw"),
                            lambda d0, h0, nd, nh: dw_g[d0:d0 + nd, h0:h0 + nh],
                            AF.Silu, 5504, y_fm, y3p, {}, w_ready=ffn_ready, y_guard=yg)
        ctx.pop_scope()
        B = [(s, v) for (s, v) in p.all_toks() if not s.startswith("cc")]
        p.region(B)
        emit_sandwich(ctx, x3, y_fm, x_out, xln, False, state, y_ready=yt, write_xln=False)
        p.op("sp", lambda e: e.nop(), waits=p.all_toks())
        p.emit(nc, es)
    return nc


# version 19
# speedup vs baseline: 1.4928x; 1.4882x over previous
import numpy as np
import ml_dtypes
from contextlib import ExitStack
import concourse.bass as bass
import concourse.mybir as mybir
from concourse.bass_utils import run_bass_kernel_spmd

F32 = mybir.dt.float32
BF16 = mybir.dt.bfloat16
I32 = mybir.dt.int32
AF = mybir.ActivationFunctionType
ALU = mybir.AluOpType
AX = mybir.AxisListType
BF = ml_dtypes.bfloat16


class CFG:
    T = 2048
    D = 2048
    H = 16
    HD = 128
    QL = 768; KVL = 512; NOPE = 128; ROPE = 64; VD = 128; QKD = 192
    F0 = 5632; F1 = 5504; LUT = 16
    EPS_LN = 1e-5; EPS_RMS = 1e-6; EPS_BIT = 1e-8; EPS_SM = 1e-8
    THETA = 10000.0


ENG_ATTR = {"pe": "tensor", "act": "scalar", "dve": "vector", "po": "gpsimd", "sp": "sync"}
ENGINES = ("pe", "act", "dve", "po", "sp")


class Prog:

    def __init__(self):
        self.ops = {e: [] for e in ENGINES}
        self.counts = {}
        self.sem_names = []
        self.pending_barrier = {}
        self.region_inject = {}

    def barrier(self, exclude=()):
        toks = [(s, v) for (s, v) in self.all_toks() if s not in exclude]
        self.pending_barrier = {e: list(toks) for e in ENGINES}

    def region(self, toks):
        self.region_inject = {e: list(toks) for e in ENGINES}

    def _reg(self, sem):
        if sem not in self.counts:
            self.counts[sem] = 0
            self.sem_names.append(sem)

    def tok(self, sem):
        return (sem, self.counts.get(sem, 0))

    def op(self, eng, fn, waits=(), inc=None, n=1):
        if eng in self.pending_barrier:
            waits = list(waits) + self.pending_barrier.pop(eng)
        if eng in self.region_inject:
            waits = list(waits) + self.region_inject.pop(eng)
        waits = tuple((s, v) for (s, v) in waits if s is not None and v > 0)
        for s, _ in waits:
            self._reg(s)
        tok = None
        if inc:
            self._reg(inc)
            self.counts[inc] += n
            tok = (inc, self.counts[inc])
        self.ops[eng].append((fn, waits, inc, n))
        return tok

    def dma(self, q, out, in_, waits=(), sem=None, transpose=False, slow_ok=False):
        assert q in ("sp", "act", "po")
        sem = sem or ("q_" + q)
        return self.op(
            q,
            (lambda e, o=out, i=in_, t=transpose, s=slow_ok:
             e.dma_start(out=o, in_=i, transpose=t, allow_slow_non_contiguous=s)),
            waits, sem, 16,
        )

    def all_toks(self):
        return [(s, self.counts[s]) for s in self.sem_names if self.counts[s] > 0]

    def emit(self, nc, es):
        sems = {name: es.enter_context(nc.semaphore(name)) for name in self.sem_names}
        block = es.enter_context(nc.Block())
        for eng in ENGINES:
            ops = self.ops[eng]
            if not ops:
                continue

            def make(ops=ops):
                def _(e):
                    waited = {}
                    for fn, waits, inc, n in ops:
                        for s, v in waits:
                            if waited.get(s, 0) < v:
                                e.wait_ge(sems[s], v)
                                waited[s] = v
                        inst = fn(e)
                        if inc:
                            inst.then_inc(sems[inc], n)
                return _

            getattr(block, ENG_ATTR[eng])(make())
        return sems


def host_constants(c: CFG):
    T = c.T
    consts = {}
    consts["c_id_bf"] = np.eye(128, dtype=BF)
    consts["c_ones_col_bf"] = np.ones((128, 1), dtype=BF)
    consts["c_ones_row_bf"] = np.ones((1, 128), dtype=BF)
    consts["c_ones_row_f32"] = np.ones((1, 128), dtype=np.float32)
    kk = np.arange(128)[:, None]
    jj = np.arange(896)[None, :]
    consts["c_trilw"] = (kk <= jj - 384).astype(BF)
    d = c.ROPE
    inv_freq = 1.0 / c.THETA ** (np.arange(0, d, 2, dtype=np.float32) / d)
    t = np.arange(T, dtype=np.float32)
    fr = np.outer(inv_freq, t).astype(np.float32)
    cos, sin = np.cos(fr).astype(np.float32), np.sin(fr).astype(np.float32)
    scale = np.float32(c.QKD ** (-0.5))
    A = np.concatenate([cos, cos], 0)
    Bm = np.concatenate([-sin, sin], 0)
    consts["c_rope_a_k"] = A.astype(BF)
    consts["c_rope_b_k"] = Bm.astype(BF)
    return consts


class Ctx:
    def __init__(self, nc, es, cfg, const_aps):
        self.nc = nc
        self.es = es
        self.c = cfg
        self.p = Prog()
        self.cap = const_aps
        self.psum = [es.enter_context(nc.psum_tensor(f"pb{i}", [128, 512], F32)) for i in range(8)]
        self.bank_free = [[] for _ in range(8)]
        self._uid = 0
        self.const_sb = {}
        self.scopes = []
        self.pool = {}
        self._swb = None
        self.sw_chain = []

    def sandwich_bufs(self):
        if self._swb is None:
            assert not self.scopes, "allocate sandwich bufs before any scope"
            D = self.c.D
            self._swb = dict(
                xt=[self.sb(f"swb_x{b}", [128, D], F32) for b in range(2)],
                ln=[self.sb(f"swb_l", [128, D], F32)] * 2,
                xl=[self.sb(f"swb_xl", [128, D], BF16)] * 2,
                yt=[self.sb(f"swb_y", [128, D], BF16)] * 2,

                st=self.sb("swb_st", [128, 12], F32),
            )
        return self._swb

    def uid(self):
        self._uid += 1
        return self._uid

    def sb(self, name, shape, dt):
        es = self.scopes[-1] if self.scopes else self.es
        side = "left" if self.scopes else "right"
        return es.enter_context(self.nc.sbuf_tensor(name, list(shape), dt, side=side))

    def sb_outer(self, name, shape, dt):
        return self.es.enter_context(self.nc.sbuf_tensor(name, list(shape), dt, side="right"))

    def push_scope(self):
        self.scopes.append(ExitStack())

    def pop_scope(self):
        self.scopes.pop().close()

    def strip_set(self, key, n=None):
        if n is None:
            n = self.c.D // 128
        u = self.uid()
        return [self.sb(f"pool_{key}{u}_{i}", [128, self.c.T], BF16) for i in range(n)]

    def bcast_f32(self, row_ap, bank, waits):
        p = self.p
        if not hasattr(self, "_bc") or self._bc is None:
            self._bc = (self.sb_outer("bc_hi", [1, 512], BF16),
                        self.sb_outer("bc_lo", [1, 512], BF16))
            self._bc_chain = None
        hi, lo = self._bc
        ones_row, or_tok = self.load_const("c_ones_row_bf")
        w = list(waits) + ([self._bc_chain] if self._bc_chain else [])
        t1 = p.op("dve", lambda e: e.tensor_copy(hi[0:1, :], row_ap), waits=w, inc="dve")
        t2 = p.op("dve", lambda e: e.tensor_tensor(lo[0:1, :], row_ap, hi[0:1, :], ALU.subtract),
                  waits=[t1], inc="dve")
        m1 = p.op("pe", lambda e: e.matmul(self.psum[bank][:], ones_row[:], hi[0:1, :],
                                           start=True, stop=False),
                  waits=[t1, or_tok] + (self.bank_free[bank] or []), inc="pe")
        m2 = p.op("pe", lambda e: e.matmul(self.psum[bank][:], ones_row[:], lo[0:1, :],
                                           start=False, stop=True),
                  waits=[t2], inc="pe")
        self._bc_chain = m2
        return m2

    def const_col(self, val):
        key = float(val)
        if key not in self.const_sb:
            t = self.sb_outer(f"ccol_{self._uid}_{len(self.const_sb)}", [128, 1], F32)
            self._uid += 1
            tok = self.p.op("dve", lambda e, t=t, v=key: e.memset(t[:], v), inc="dve")
            self.const_sb[key] = (t, tok)
        return self.const_sb[key]

    def load_const(self, name, scoped=False):
        if not scoped and name in self.const_sb:
            return self.const_sb[name]
        ap = self.cap[name]
        shape = list(ap.tensor.shape)
        t = (self.sb if scoped else self.sb_outer)(f"cs{self.uid()}_" + name, shape, ap.dtype)
        tok = self.p.dma("act", t[:], ap, sem=f"qc{self._uid}_" + name)
        if not scoped:
            self.const_sb[name] = (t, tok)
        return (t, tok)


def emit_sandwich(ctx, x_in, y_fm, x_out, xln_out, first, state, y_ready=(), write_xln=True):
    c, p = ctx.c, ctx.p
    T, D = c.T, c.D
    NT = T // 128
    u = ctx.uid()
    swb = ctx.sandwich_bufs()
    xt, ln, xl, yt, st = swb["xt"], swb["ln"], swb["xl"], swb["yt"], swb["st"]
    prev_chain = list(ctx.sw_chain)
    _seen = set()
    _real_op = p.op

    def _op(eng, fn, waits=(), inc=None, n=1):
        if eng not in _seen:
            _seen.add(eng)
            waits = list(waits) + prev_chain
        return _real_op(eng, fn, waits, inc, n)
    p = type("POpWrap", (), {"op": staticmethod(_op), "dma": ctx.p.dma, "tok": ctx.p.tok,
                             "all_toks": ctx.p.all_toks})()
    if first:
        state["negmu"] = [ctx.sb_outer(f"sw{u}_nm{i}", [128, 1], F32) for i in range(NT)]

    xs = [f"sw_xs{b}" for b in range(2)]
    ys = [f"sw_ys{b}" for b in range(2)]
    oss = [f"sw_os{b}" for b in range(2)]
    lss = [f"sw_ls{b}" for b in range(2)]
    last_use_x = [None, None]
    last_use_y = [None, None]
    last_store_o = [None, None]
    last_store_l = [None, None]
    sq_tok = None
    st_tok = None
    last_lt = None
    for i in range(NT):
        b = i % 2
        row = slice(i * 128, (i + 1) * 128)
        xtok = p.dma("sp", xt[b][:], x_in[row, :],
                     waits=(prev_chain if i < 2 else []) + ([last_use_x[b]] if last_use_x[b] else []),
                     sem=xs[b])
        if y_fm is not None:
            ytok = p.dma("sp", yt[b][:], y_fm[:, row],
                         waits=list(y_ready) + [t for t in last_use_y if t],
                         sem=ys[b], transpose=True)
            w_st = [st_tok] if st_tok else []
            t1 = p.op("dve", lambda e, b=b: e.tensor_reduce(st[:, 0:1], yt[b][:], AX.X, ALU.add, negate=True),
                      waits=[ytok] + w_st, inc="dve")
            t2 = p.op("act", lambda e, b=b: e.activation(xl[b][:], yt[b][:], AF.Square,
                                                         accum_out=st[:, 2:3]),
                      waits=[ytok, t1] + ([sq_tok] if sq_tok else []) +
                            [t for t in last_store_l if t], inc="act")
            t4 = p.op("dve", lambda e: e.tensor_scalar(st[:, 0:1], st[:, 0:1], 1.0 / D, None, ALU.mult),
                      waits=[t1], inc="dve")
            t5 = p.op("dve", lambda e: e.tensor_tensor(st[:, 4:5], st[:, 0:1], st[:, 0:1], ALU.mult),
                      waits=[t4], inc="dve")
            t6 = p.op("dve", lambda e: e.tensor_scalar(st[:, 3:4], st[:, 2:3], 1.0 / D, None, ALU.mult),
                      waits=[t2], inc="dve")
            t7 = p.op("dve", lambda e: e.tensor_tensor(st[:, 3:4], st[:, 3:4], st[:, 4:5], ALU.subtract),
                      waits=[t5, t6], inc="dve")
            epsc, epsc_tok = ctx.const_col(c.EPS_LN)
            t8 = p.op("act", lambda e, epsc=epsc: e.activation(st[:, 1:2], st[:, 3:4], AF.Ln, bias=epsc[:]),
                      waits=[t7, epsc_tok], inc="act")
            t9 = p.op("act", lambda e: e.activation(st[:, 1:2], st[:, 1:2], AF.Exp, scale=-0.5),
                      waits=[t8], inc="act")
            prev_store = last_store_o[0] if (last_store_o[0] and (not last_store_o[1] or last_store_o[0][1] > last_store_o[1][1])) else last_store_o[1]
            t10 = p.op("dve", lambda e, b=b: e.tensor_scalar(ln[b][:], yt[b][:], st[:, 0:1], st[:, 1:2],
                                                             ALU.add, ALU.mult),
                       waits=[t9] + ([prev_store] if prev_store else []), inc="dve")
            last_use_y[b] = t10
            t11 = p.op("dve", lambda e, b=b: e.tensor_tensor(ln[b][:], ln[b][:], xt[b][:], ALU.add),
                       waits=[t10, xtok], inc="dve")
            last_use_x[b] = t11
            sot = p.dma("sp", x_out[row, :], ln[b][:], waits=[t11], sem=oss[b])
            last_store_o[b] = sot
            xn, xn_tok = ln[b], t11
        else:
            xn, xn_tok = xt[b], xtok

        if not write_xln:
            continue
        if first:
            nm = state["negmu"][i]
            tm = p.op("dve", lambda e, xn=xn, nm=nm: e.tensor_reduce(nm[:], xn[:], AX.X, ALU.add, negate=True),
                      waits=[xn_tok], inc="dve")
            tm = p.op("dve", lambda e, nm=nm: e.tensor_scalar(nm[:], nm[:], 1.0 / D, None, ALU.mult),
                      waits=[tm], inc="dve")
        else:
            nm = state["negmu"][i]
            tm = state["negmu_tok"]
        s2 = p.op("act", lambda e, xn=xn, b=b: e.activation(xl[b][:], xn[:], AF.Square, accum_out=st[:, 7:8]),
                  waits=([xn_tok, t2] if y_fm is not None else [xn_tok]) +
                        [t for t in last_store_l if t] + ([st_tok] if st_tok else []),
                  inc="act")
        sq_tok = s2
        s3 = p.op("dve", lambda e, nm=nm: e.tensor_tensor(st[:, 8:9], nm[:], nm[:], ALU.mult),
                  waits=[tm], inc="dve")
        s4 = p.op("dve", lambda e: e.tensor_scalar(st[:, 9:10], st[:, 7:8], 1.0 / D, None, ALU.mult),
                  waits=[s2], inc="dve")
        s5 = p.op("dve", lambda e: e.tensor_tensor(st[:, 9:10], st[:, 9:10], st[:, 8:9], ALU.subtract),
                  waits=[s3, s4], inc="dve")
        epsc2, epsc2_tok = ctx.const_col(c.EPS_LN)
        s6 = p.op("act", lambda e, epsc2=epsc2: e.activation(st[:, 10:11], st[:, 9:10], AF.Ln, bias=epsc2[:]),
                  waits=[s5, epsc2_tok], inc="act")
        s7 = p.op("act", lambda e: e.activation(st[:, 10:11], st[:, 10:11], AF.Exp, scale=-0.5),
                  waits=[s6], inc="act")
        s8 = p.op("dve", lambda e, xn=xn, nm=nm, b=b: e.tensor_scalar(xl[b][:], xn[:], nm[:], st[:, 10:11],
                                                                      ALU.add, ALU.mult),
                  waits=[s7] + [t for t in last_store_l if t], inc="dve")
        st_tok = s8
        if y_fm is None:
            last_use_x[b] = s8
        lt = p.dma("sp", xln_out[row, :], xl[b][:], waits=[s8], sem=lss[b])
        last_store_l[b] = lt
        last_lt = lt
    if first:
        state["negmu_tok"] = p.tok("dve")
    ctx.sw_chain = [p.tok("dve"), p.tok("act"), p.tok("po")] +         [p.tok(s) for s in oss + lss + ys]
    if not write_xln:
        return [p.tok(oss[0]), p.tok(oss[1])]
    return [p.tok(lss[0]), p.tok(lss[1])]


def emit_load_xln_fm(ctx, xln_dram, ready_toks, dst_strips, reuse_waits=()):
    c, p = ctx.c, ctx.p
    T, D = c.T, c.D
    NS = D // 128
    u = ctx.uid()
    sem = f"xlnfm{u}"
    tok = None
    for i in range(NS):
        tok = p.dma("sp", dst_strips[i][:], xln_dram[:, i * 128:(i + 1) * 128],
                    waits=list(ready_toks) + list(reuse_waits), sem=sem, transpose=True)
    return (sem, tok[1])


def emit_l1_attn(ctx, xln_dram, xln_ready, w_qkv, w_out, y_fm, scr, y_guard=()):
    c, p = ctx.c, ctx.p
    T, D, H = c.T, c.D, c.H
    NS = D // 128
    NQ = T // 512
    NKT = T // 128
    u = ctx.uid()
    scale = float(c.HD ** (-0.5))
    SAMP_STEP = 8
    NSAMP = T // SAMP_STEP
    bf = ctx.bank_free

    idn, idn_tok = ctx.load_const("c_id_bf")
    ones_col, ocol_tok = ctx.load_const("c_ones_col_bf")
    ones_row, orow_tok = ctx.load_const("c_ones_row_bf")

    wq_bf, wo_bf = scr["l1_wqkv_bf"], scr["l1_wout_bf"]
    q_scr, v_scr = scr["l1_q"], scr["l1_v"]

    csem = f"l1c{u}"
    for j in range(12):
        p.dma("po", wq_bf[j * 512:(j + 1) * 512, :], w_qkv[j * 512:(j + 1) * 512, :], sem=csem)
    for j in range(4):
        ctok = p.dma("po", wo_bf[j * 512:(j + 1) * 512, :], w_out[j * 512:(j + 1) * 512, :], sem=csem)
    cast_all = (csem, ctok[1])

    xfm = ctx.strip_set("A")
    xfm_tok = emit_load_xln_fm(ctx, xln_dram, xln_ready, xfm,
                               reuse_waits=scr.pop("poolA_free", []))
    k_fm = ctx.strip_set("B")
    kfm_free = scr.pop("poolB_free", [])

    OC = 256
    wall = [ctx.sb(f"l1wa{u}_{b}", [128, NS * OC], BF16) for b in range(2)]
    wsem = [f"l1ws{u}_{b}" for b in range(2)]
    wall_use = [None, None]
    evb = [ctx.sb(f"l1ev{u}_{b}", [128, 512], BF16) for b in range(4)]
    ev_free = [None] * 4
    qs_sem = [f"l1qs{u}_{b}" for b in range(4)]
    vs_sem = [f"l1vs{u}_{b}" for b in range(4)]

    n_oc = 3 * D // OC
    for oc in range(n_oc):
        b = oc % 2
        o0 = oc * OC
        wtok = None
        for i in range(NS):
            wtok = p.dma("sp", wall[b][:, i * OC:(i + 1) * OC],
                         wq_bf[o0:o0 + OC, i * 128:(i + 1) * 128],
                         waits=[cast_all] + ([wall_use[b]] if wall_use[b] else []),
                         sem=wsem[b], transpose=True)
        wready = (wsem[b], wtok[1])
        if oc < 16:
            part_q = oc < 8
            for ho in range(2):
                h = (oc % 8) * 2 + ho
                for tc in range(NQ):
                    bi = (ho * NQ + tc) % 4
                    bank = ctx.psum[bi]
                    mm = None
                    for i in range(NS):
                        mm = p.op("pe", lambda e, bank=bank, b=b, i=i, ho=ho, tc=tc:
                                  e.matmul(bank[:], wall[b][:, i * OC + ho * 128:i * OC + (ho + 1) * 128],
                                           xfm[i][:, tc * 512:(tc + 1) * 512],
                                           start=(i == 0), stop=(i == NS - 1)),
                                  waits=[wready, xfm_tok] + (bf[bi] if i == 0 else []),
                                  inc="pe")
                    wall_use[b] = mm
                    if part_q:
                        es_ = bi
                        et = p.op("act", lambda e, bank=bank, es_=es_:
                                  e.activation(evb[es_][:], bank[:], AF.Copy, scale=scale),
                                  waits=[mm] + ([ev_free[es_]] if ev_free[es_] else []), inc="act")
                        bf[bi] = [et]
                        sx = p.dma("sp", q_scr[h * 128:(h + 1) * 128, tc * 512:(tc + 1) * 512],
                                   evb[es_][:], waits=[et], sem=qs_sem[es_])
                        ev_free[es_] = sx
                    else:
                        et = p.op("act", lambda e, bank=bank, h=h, tc=tc:
                                  e.activation(k_fm[h][:, tc * 512:(tc + 1) * 512], bank[:], AF.Copy),
                                  waits=[mm] + list(kfm_free), inc="act")
                        kfm_free = []
                        bf[bi] = [et]
        else:
            for tk in range(NKT):
                bi = tk % 4
                bank = ctx.psum[bi]
                mm = None
                for i in range(NS):
                    mm = p.op("pe", lambda e, bank=bank, b=b, i=i, tk=tk:
                              e.matmul(bank[:, 0:OC], xfm[i][:, tk * 128:(tk + 1) * 128],
                                       wall[b][:, i * OC:(i + 1) * OC],
                                       start=(i == 0), stop=(i == NS - 1)),
                              waits=[wready, xfm_tok] + (bf[bi] if i == 0 else []),
                              inc="pe")
                wall_use[b] = mm
                es_ = tk % 4
                et = p.op("act", lambda e, bank=bank, es_=es_:
                          e.activation(evb[es_][:, 0:OC], bank[:, 0:OC], AF.Copy),
                          waits=[mm] + ([ev_free[es_]] if ev_free[es_] else []), inc="act")
                bf[bi] = [et]
                sx = None
                h2 = (oc - 16) * 2
                for ho in range(2):
                    sx = p.dma("sp", v_scr[tk, h2 + ho], evb[es_][:, ho * 128:(ho + 1) * 128],
                               waits=[et], sem=vs_sem[es_])
                ev_free[es_] = sx
    qkv_done_pe = p.tok("pe")
    scat_done = [p.tok(s) for s in qs_sem] + [p.tok(s) for s in vs_sem]

    o_all = xfm
    pt = [ctx.sb(f"l1pt{u}_{i}", [128, 512], BF16) for i in range(3)]
    pt_use = [None] * 3
    qb_buf = [ctx.sb(f"l1qb{u}_{b}", [128, 512], BF16) for b in range(2)]
    qb_sem = [f"l1qbs{u}_{b}" for b in range(2)]
    qb_use = [None, None]
    vb_buf = [ctx.sb(f"l1vb{u}_{b}", [128, 128], BF16) for b in range(3)]
    vb_sem = [f"l1vbs{u}_{b}" for b in range(3)]
    vb_use = [None, None, None]
    nmcol = ctx.sb(f"l1nm{u}", [128, 4], F32)
    nmcol_bf = ctx.sb(f"l1nmb{u}", [128, 4], BF16)
    aug = ctx.sb(f"l1aug{u}", [1, 512], BF16)
    den = ctx.sb(f"l1den{u}", [1, 512], F32)
    rbc = ctx.sb(f"l1rbc{u}", [128, 512], F32)
    stb = [0, 1, 6]
    nm_use = None
    aug_use = None
    for h in range(H):
        for qb in range(NQ):
            bslot = (h * NQ + qb) % 2
            qtok = p.dma("sp", qb_buf[bslot][:], q_scr[h * 128:(h + 1) * 128, qb * 512:(qb + 1) * 512],
                         waits=scat_done + ([qb_use[bslot]] if qb_use[bslot] else []),
                         sem=qb_sem[bslot])
            red_toks = []
            for j in range(4):
                bi = stb[j % 2]
                bank = ctx.psum[bi]
                mm = p.op("pe", lambda e, bank=bank, bslot=bslot, j=j, h=h:
                          e.matmul(bank[:, 0:NSAMP], qb_buf[bslot][:, j * 128:(j + 1) * 128],
                                   k_fm[h][:, 0:T:SAMP_STEP], start=True, stop=True),
                          waits=[qtok] + bf[bi], inc="pe")
                rd = p.op("dve", lambda e, bank=bank, j=j:
                          e.tensor_reduce(nmcol[:, j:j + 1], bank[:, 0:NSAMP], AX.X, ALU.max, negate=True),
                          waits=[mm] + ([nm_use] if (j == 0 and nm_use) else []), inc="dve")
                bf[bi] = [rd]
                red_toks.append(rd)
            cv = p.op("dve", lambda e: e.tensor_copy(nmcol_bf[:], nmcol[:]),
                      waits=red_toks, inc="dve")
            as_tok = None
            for j in range(4):
                as_tok = p.op("pe", lambda e, j=j: e.matmul(ctx.psum[2][0:1, j * 128:(j + 1) * 128],
                                                            nmcol_bf[:, j:j + 1], idn[:],
                                                            start=True, stop=True),
                              waits=[cv, idn_tok] + (bf[2] if j == 0 else []),
                              inc="pe")
            nm_use = as_tok
            ag = p.op("act", lambda e: e.activation(aug[:], ctx.psum[2][0:1, :], AF.Copy),
                      waits=[as_tok] + ([aug_use] if aug_use else []), inc="act")
            bf[2] = [ag]
            den_mm = None
            o_mm = None
            for kt in range(NKT):
                bi = stb[kt % 3]
                bank = ctx.psum[bi]
                mm1 = p.op("pe", lambda e, bank=bank, h=h, kt=kt, bslot=bslot:
                           e.matmul(bank[:], k_fm[h][:, kt * 128:(kt + 1) * 128], qb_buf[bslot][:],
                                    start=True, stop=False),
                           waits=[qtok] + bf[bi], inc="pe")
                mm2 = p.op("pe", lambda e, bank=bank:
                           e.matmul(bank[:], ones_row[:], aug[:], start=False, stop=True),
                           waits=[ag, orow_tok], inc="pe")
                ps_ = kt % 3
                ex = p.op("act", lambda e, bank=bank, ps_=ps_:
                          e.activation(pt[ps_][:], bank[:], AF.Exp),
                          waits=[mm2] + ([pt_use[ps_]] if pt_use[ps_] else []), inc="act")
                bf[bi] = [ex]
                den_mm = p.op("pe", lambda e, kt=kt, ps_=ps_:
                              e.matmul(ctx.psum[3][0:1, :], ones_col[:], pt[ps_][:],
                                       start=(kt == 0), stop=(kt == NKT - 1)),
                              waits=[ex, ocol_tok] + (bf[3] if kt == 0 else []),
                              inc="pe")
                vslot = kt % 3
                vtok = p.dma("sp", vb_buf[vslot][:], v_scr[kt, h],
                             waits=scat_done + ([vb_use[vslot]] if vb_use[vslot] else []),
                             sem=vb_sem[vslot])
                o_mm = p.op("pe", lambda e, kt=kt, vslot=vslot, ps_=ps_:
                            e.matmul(ctx.psum[4][:], vb_buf[vslot][:], pt[ps_][:],
                                     start=(kt == 0), stop=(kt == NKT - 1)),
                            waits=[ex, vtok] + (bf[4] if kt == 0 else []),
                            inc="pe")
                vb_use[vslot] = o_mm
                pt_use[ps_] = o_mm
            d1 = p.op("act", lambda e: e.activation(den[:], ctx.psum[3][0:1, :], AF.Copy),
                      waits=[den_mm], inc="act")
            bf[3] = [d1]
            d2 = p.op("dve", lambda e: e.tensor_scalar(den[:], den[:], float(c.EPS_SM), None, ALU.add),
                      waits=[d1], inc="dve")
            d3 = p.op("dve", lambda e: e.reciprocal(den[:], den[:]), waits=[d2], inc="dve")
            bcm = ctx.bcast_f32(den[0:1, :], 5, [d3])
            rb = p.op("act", lambda e: e.activation(rbc[:], ctx.psum[5][:], AF.Copy),
                      waits=[bcm], inc="act")
            bf[5] = [rb]
            oe = p.op("dve", lambda e, h=h, qb=qb:
                      e.tensor_tensor(o_all[h][:, qb * 512:(qb + 1) * 512], ctx.psum[4][:], rbc[:], ALU.mult),
                      waits=[o_mm, rb, qkv_done_pe], inc="dve")
            bf[4] = [oe]
            aug_use = den_mm
    attn_done_dve = p.tok("dve")

    yev = [ctx.sb(f"l1ye{u}_{b}", [128, 512], BF16) for b in range(2)]
    yev_free = [None, None]
    ys_sem = [f"l1ys{u}_{b}" for b in range(2)]
    for dc in range(8):
        b = dc % 2
        wtok = None
        for hh in range(H):
            wtok = p.dma("sp", wall[b][:, hh * 256:(hh + 1) * 256],
                         wo_bf[dc * 256:(dc + 1) * 256, hh * 128:(hh + 1) * 128],
                         waits=[cast_all] + ([wall_use[b]] if wall_use[b] else []),
                         sem=wsem[b], transpose=True)
        wready = (wsem[b], wtok[1])
        for ds in range(2):
            for qb in range(NQ):
                bi = (ds * NQ + qb) % 4
                bank = ctx.psum[bi]
                mm = None
                for hh in range(H):
                    mm = p.op("pe", lambda e, bank=bank, b=b, hh=hh, ds=ds, qb=qb:
                              e.matmul(bank[:], wall[b][:, hh * 256 + ds * 128:hh * 256 + (ds + 1) * 128],
                                       o_all[hh][:, qb * 512:(qb + 1) * 512],
                                       start=(hh == 0), stop=(hh == H - 1)),
                              waits=[wready, attn_done_dve] + (bf[bi] if hh == 0 else []),
                              inc="pe")
                wall_use[b] = mm
                es_ = bi % 2
                et = p.op("act", lambda e, bank=bank, es_=es_:
                          e.activation(yev[es_][:], bank[:], AF.Copy),
                          waits=[mm] + ([yev_free[es_]] if yev_free[es_] else []), inc="act")
                bf[bi] = [et]
                sx = p.dma("sp", y_fm[(dc * 2 + ds) * 128:(dc * 2 + ds + 1) * 128, qb * 512:(qb + 1) * 512],
                           yev[es_][:], waits=[et] + list(y_guard), sem=ys_sem[es_])
                yev_free[es_] = sx
    scr["poolA_free"] = [p.tok("pe")]
    scr["poolB_free"] = [p.tok("pe")]
    return [p.tok(s) for s in ys_sem]


def build_l1_attn_launch(cfg, consts):
    nc = bass.Bass("TRN2", target_bir_lowering=False, debug=False)
    c = cfg
    T, D = c.T, c.D
    x_in = nc.dram_tensor("x_in", [T, D], F32, kind="ExternalInput").ap()
    w_qkv = nc.dram_tensor("w_qkv", [3 * D, D], F32, kind="ExternalInput").ap()
    w_out = nc.dram_tensor("w_out", [D, D], F32, kind="ExternalInput").ap()
    x_out = nc.dram_tensor("x_out", [T, D], F32, kind="ExternalOutput").ap()
    const_aps = {}
    for name, arr in consts.items():
        dt = {np.dtype(np.float32): F32, np.dtype(BF): BF16, np.dtype(np.int32): I32}[arr.dtype]
        const_aps[name] = nc.dram_tensor(name, list(arr.shape), dt, kind="ExternalInput").ap()

    xln = nc.dram_tensor("s_xln", [T, D], BF16).ap()
    y_fm = nc.dram_tensor("s_yfm", [D, T], BF16).ap()
    scr = {
        "l1_wqkv_bf": nc.dram_tensor("s_wqkv_bf", [3 * D, D], BF16).ap(),
        "l1_wout_bf": nc.dram_tensor("s_wout_bf", [D, D], BF16).ap(),
        "l1_q": nc.dram_tensor("s_q", [D, T], BF16).ap(),
        "l1_v": nc.dram_tensor("s_v", [T // 128, c.H, 128, 128], BF16).ap(),
    }
    es = ExitStack()
    with es:
        ctx = Ctx(nc, es, c, const_aps)
        state = {}
        ln_toks = emit_sandwich(ctx, x_in, None, None, xln, True, state)
        y_toks = emit_l1_attn(ctx, xln, ln_toks, w_qkv, w_out, y_fm, scr)
        emit_sandwich(ctx, x_in, y_fm, x_out, xln, False, state, y_ready=y_toks, write_xln=False)
        final = ctx.p.all_toks()
        ctx.p.op("sp", lambda e: e.nop(), waits=final)
        ctx.p.emit(nc, es)
    return nc



def emit_l0_attn(ctx, xln_dram, xln_ready, w, y_fm, scr, wbf=None, cast_ready=(), phase_limit=99, y_guard=()):
    c, p = ctx.c, ctx.p
    T, D, H = c.T, c.D, c.H
    NS = D // 128
    NQ = T // 512
    NKT = T // 128
    NTC = T // 512
    QL, KVL, NOPE, ROPE, VD = c.QL, c.KVL, c.NOPE, c.ROPE, c.VD
    u = ctx.uid()
    scale = float(c.QKD ** (-0.5))
    bf = ctx.bank_free

    ones_col, ocol_tok = ctx.load_const("c_ones_col_bf")
    ra, ra_tok = ctx.load_const("c_rope_a_k", scoped=True)
    rb, rb_tok = ctx.load_const("c_rope_b_k", scoped=True)
    trilw, trilw_tok = ctx.load_const("c_trilw", scoped=True)

    q_scr, k_scr, v_scr = scr["l0_q"], scr["l0_k"], scr["l0_v"]

    xfm = ctx.strip_set("A")
    xfm_tok = emit_load_xln_fm(ctx, xln_dram, xln_ready, xfm,
                               reuse_waits=scr.pop("poolA_free", []))

    wsrc_d = wbf if wbf is not None else w

    OC = 128
    wall = [ctx.sb(f"l0wa{u}_{b}", [128, NS * OC], BF16) for b in range(2)]
    wsem = [f"l0ws{u}_{b}" for b in range(2)]
    wall_use = [None, None]

    def wall_load(b, wsrc, o0, oc_w, nstrips, extra=()):
        tok = None
        for i in range(nstrips):
            tok = p.dma("sp", wall[b][:, i * OC:i * OC + oc_w],
                        wsrc[i * 128:(i + 1) * 128, o0:o0 + oc_w],
                        waits=list(extra) + list(cast_ready) + ([wall_use[b]] if wall_use[b] else []),
                        sem=wsem[b])
        return (wsem[b], tok[1])

    evb = [ctx.sb(f"l0ev{u}_{b}", [128, 512], BF16) for b in range(2)]
    ev_free = [None, None]
    sqq = ctx.sb(f"l0sqq{u}", [128, 512], BF16)
    sqq_use = None
    rbc = ctx.sb(f"l0rbc{u}", [128, 512], F32)
    rbckv = ctx.sb(f"l0rbckv{u}", [128, T], F32)
    kpe_r = ctx.sb(f"l0kpe{u}", [65, T], BF16)
    kpe_tmp = ctx.sb(f"l0kpt{u}", [64, 512], F32)
    kpe_sw = ctx.sb(f"l0kps{u}", [64, 512], F32)
    kab_pos = ctx.sb(f"l0kab{u}", [128, H], F32)
    kab_nope = ctx.sb(f"l0kabn{u}", [128, H], BF16)
    kab_pe = ctx.sb(f"l0kabp{u}", [64, 1], BF16)
    kab_tmp = ctx.sb(f"l0kabt{u}", [128, 1], F32)
    ctx.push_scope()
    gq = ctx.sb(f"l0gq{u}", [128, QL // 128], F32)
    gkv = ctx.sb(f"l0gkv{u}", [128, KVL // 128], F32)
    gq_tok = p.dma("sp", gq[:], w["g_q"].rearrange("(s p) -> p s", p=128), sem=f"l0gq{u}", slow_ok=True)
    gkv_tok = p.dma("sp", gkv[:], w["g_kv"].rearrange("(s p) -> p s", p=128), sem=f"l0gkv{u}", slow_ok=True)
    ssq_q = ctx.sb(f"l0ssqq{u}", [1, T], F32)
    ssq_kv = ctx.sb(f"l0ssqkv{u}", [1, T], F32)
    rstdq = ssq_q
    rstdkv = ssq_kv
    rstdkv_col = ctx.sb(f"l0rskvc{u}", [128, NKT], F32)

    kv_g = [ctx.sb(f"l0kv{u}_{i}", [128, T], BF16) for i in range(KVL // 128)]
    qs_sem = [f"l0qs{u}_{b}" for b in range(2)]

    def lat_proj(wsrc, LO, gcol, gtok, ssq_row, dst_kind):
        nonlocal sqq_use
        no_stats = (phase_limit == 101)
        no_evac = (phase_limit == 102)
        NOC = LO // OC
        for oc in range(NOC):
            b = oc % 2
            wtok = wall_load(b, wsrc, oc * OC, OC, NS)
            for tc in range(NTC):
                bi = (oc * NTC + tc) % 2
                bank = ctx.psum[bi]
                sbank = 4 + tc
                mm = None
                for i in range(NS):
                    mm = p.op("pe", lambda e, bank=bank, b=b, i=i, tc=tc:
                              e.matmul(bank[:], wall[b][:, i * OC:(i + 1) * OC],
                                       xfm[i][:, tc * 512:(tc + 1) * 512],
                                       start=(i == 0), stop=(i == NS - 1)),
                              waits=[wtok, xfm_tok] + (bf[bi] if i == 0 else []),
                              inc="pe")
                wall_use[b] = mm
                if no_stats:
                    sget = smm = mm
                else:
                    sget = p.op("act", lambda e, bank=bank: e.activation(sqq[:], bank[:], AF.Square),
                                waits=[mm] + ([sqq_use] if sqq_use else []), inc="act")
                    smm = p.op("pe", lambda e, sbank=sbank, oc=oc, NOC=NOC:
                               e.matmul(ctx.psum[sbank][0:1, 0:512], ones_col[:], sqq[:],
                                        start=(oc == 0), stop=(oc == NOC - 1)),
                               waits=[sget, ocol_tok] + (bf[sbank] if oc == 0 else []),
                               inc="pe")
                    sqq_use = smm
                if no_evac:
                    ge = p.op("act", lambda e, bank=bank: e.activation(sqq[:], bank[:], AF.Copy),
                              waits=[mm, gtok] + ([sqq_use] if (no_stats and sqq_use) else []), inc="act")
                    if no_stats:
                        sqq_use = ge
                elif dst_kind == "q":
                    es_ = bi
                    ge = p.op("dve", lambda e, bank=bank, oc=oc, es_=es_:
                              e.tensor_scalar(evb[es_][:], bank[:], gcol[:, oc:oc + 1], None, ALU.mult),
                              waits=[mm, gtok, sget] + ([ev_free[es_]] if ev_free[es_] else []), inc="dve")
                    sx = p.dma("sp", q_scr[oc * 128:(oc + 1) * 128, tc * 512:(tc + 1) * 512],
                               evb[es_][:], waits=[ge], sem=qs_sem[es_])
                    ev_free[es_] = sx
                else:
                    ge = p.op("dve", lambda e, bank=bank, oc=oc, tc=tc:
                              e.tensor_scalar(kv_g[oc][:, tc * 512:(tc + 1) * 512], bank[:],
                                              gcol[:, oc:oc + 1], None, ALU.mult),
                              waits=[mm, gtok, sget], inc="dve")
                bf[bi] = [sget, ge]
                if oc == NOC - 1 and not no_stats:
                    se = p.op("act", lambda e, tc=tc, sbank=sbank, ssq_row=ssq_row:
                              e.activation(ssq_row[:, tc * 512:(tc + 1) * 512],
                                           ctx.psum[sbank][0:1, 0:512], AF.Copy),
                              waits=[smm], inc="act")
                    bf[sbank] = [se]

    if phase_limit <= 0:
        ctx.pop_scope()
        return [p.tok("pe")]
    lat_proj(wsrc_d["w_dq"], QL, gq, gq_tok, ssq_q, "q")
    lat_proj(wsrc_d["w_dkv"], KVL, gkv, gkv_tok, ssq_kv, "kv")
    qlat_stored = [p.tok(s) for s in qs_sem]
    kv_ready = p.tok("dve")
    if phase_limit <= 1 or phase_limit in (101, 102):
        ctx.pop_scope()
        return [p.tok("pe")]

    epsc, epsc_tok = ctx.const_col(c.EPS_RMS)
    act_now = p.tok("act")
    r1 = p.op("dve", lambda e: e.tensor_scalar(rstdq[:], ssq_q[:], 1.0 / QL, None, ALU.mult),
              waits=[act_now], inc="dve")
    r2 = p.op("act", lambda e: e.activation(rstdq[:], rstdq[:], AF.Ln, bias=epsc[0:1, :]),
              waits=[r1, epsc_tok], inc="act")
    r3 = p.op("act", lambda e: e.activation(rstdq[:], rstdq[:], AF.Exp, scale=-0.5),
              waits=[r2], inc="act")
    rq_tok = p.op("dve", lambda e: e.tensor_scalar(rstdq[:], rstdq[:], scale, None, ALU.mult),
                  waits=[r3], inc="dve")
    r5 = p.op("dve", lambda e: e.tensor_scalar(rstdkv[:], ssq_kv[:], 1.0 / KVL, None, ALU.mult),
              waits=[act_now], inc="dve")
    r6 = p.op("act", lambda e: e.activation(rstdkv[:], rstdkv[:], AF.Ln, bias=epsc[0:1, :]),
              waits=[r5, epsc_tok], inc="act")
    rkv_tok = p.op("act", lambda e: e.activation(rstdkv[:], rstdkv[:], AF.Exp, scale=-0.5),
                   waits=[r6], inc="act")
    onescal = ctx.sb(f"l0one1{u}", [1, 1], F32)
    o1t = p.op("dve", lambda e: e.memset(onescal[:], 1.0), inc="dve")
    rcol_prev = None
    for tk in range(NKT):
        mmr = p.op("pe", lambda e, tk=tk: e.matmul(ctx.psum[7][:, tk:tk + 1],
                                                   rstdkv[:, tk * 128:(tk + 1) * 128], onescal[:],
                                                   start=True, stop=True),
                   waits=[rkv_tok, o1t] + (bf[7] if tk == 0 else []) +
                         ([rcol_prev] if rcol_prev else []),
                   inc="pe")
        rcol_prev = p.op("act", lambda e, tk=tk: e.activation(rstdkv_col[:, tk:tk + 1],
                                                              ctx.psum[7][:, tk:tk + 1], AF.Copy),
                         waits=[mmr], inc="act")
    bf[7] = [rcol_prev]
    rcol_all = rcol_prev
    if phase_limit <= 2:
        ctx.pop_scope()
        return [p.tok("pe")]

    on_tok = p.op("dve", lambda e: e.memset(kpe_r[64:65, :], 1.0), inc="dve")
    wtok = wall_load(0, wsrc_d["w_kr"], 0, 64, NS)
    rope_use = None
    for tc in range(NTC):
        bank = ctx.psum[2]
        mm = None
        for i in range(NS):
            mm = p.op("pe", lambda e, bank=bank, i=i, tc=tc:
                      e.matmul(bank[0:64, :], wall[0][:, i * OC:i * OC + 64],
                               xfm[i][:, tc * 512:(tc + 1) * 512],
                               start=(i == 0), stop=(i == NS - 1)),
                      waits=[wtok, xfm_tok] + (bf[2] if i == 0 else []),
                      inc="pe")
        wall_use[0] = mm
        e1 = p.op("act", lambda e, bank=bank: e.activation(kpe_tmp[:], bank[0:64, :], AF.Copy),
                  waits=[mm] + ([rope_use] if rope_use else []), inc="act")
        e2 = p.op("act", lambda e, bank=bank: e.activation(kpe_sw[0:32, :], bank[32:64, :], AF.Copy),
                  waits=[mm], inc="act")
        e3 = p.op("act", lambda e, bank=bank: e.activation(kpe_sw[32:64, :], bank[0:32, :], AF.Copy),
                  waits=[mm], inc="act")
        bf[2] = [e1, e2, e3]
        t1 = p.op("dve", lambda e, tc=tc: e.tensor_tensor(kpe_tmp[:], kpe_tmp[:],
                                                          ra[:, tc * 512:(tc + 1) * 512], ALU.mult),
                  waits=[e1, ra_tok], inc="dve")
        t2 = p.op("dve", lambda e, tc=tc: e.tensor_tensor(kpe_sw[:], kpe_sw[:],
                                                          rb[:, tc * 512:(tc + 1) * 512], ALU.mult),
                  waits=[e3, rb_tok], inc="dve")
        t3 = p.op("dve", lambda e, tc=tc: e.tensor_tensor(kpe_r[0:64, tc * 512:(tc + 1) * 512],
                                                          kpe_tmp[:], kpe_sw[:], ALU.add),
                  waits=[t1, t2], inc="dve")
        rope_use = t3
    kpe_all = p.tok("dve")
    if phase_limit <= 3:
        ctx.pop_scope()
        return [p.tok("pe")]

    NKV = KVL // 128
    ktile = ctx.sb(f"l0ktile{u}", [128, 512], BF16)
    ktile_use = None
    rbc_use = None
    ks_sem = f"l0ks{u}"

    kab_prev = {}
    rbckv_toks = []
    for tc in range(NTC):
        bm = ctx.bcast_f32(rstdkv[0:1, tc * 512:(tc + 1) * 512], 3, [rkv_tok])
        be = p.op("act", lambda e, tc=tc: e.activation(rbckv[:, tc * 512:(tc + 1) * 512],
                                                       ctx.psum[3][:], AF.Copy),
                  waits=[bm], inc="act")
        bf[3] = [be]
        rbckv_toks.append(be)
    for h in range(H):
        b = h % 2
        wtok = wall_load(b, wsrc_d["w_uk"], h * 128, OC, NKV)
        for tc in range(NTC):
            bi = (h * NTC + tc) % 2
            bank = ctx.psum[bi]
            mm = None
            for i in range(NKV):
                mm = p.op("pe", lambda e, bank=bank, b=b, i=i, tc=tc:
                          e.matmul(bank[:], wall[b][:, i * OC:(i + 1) * OC],
                                   kv_g[i][:, tc * 512:(tc + 1) * 512],
                                   start=(i == 0), stop=(i == NKV - 1)),
                          waits=[wtok, kv_ready] + (bf[bi] if i == 0 else []),
                          inc="pe")
            wall_use[b] = mm
            ke = p.op("dve", lambda e, bank=bank, tc=tc:
                      e.tensor_tensor(ktile[:], bank[:], rbckv[:, tc * 512:(tc + 1) * 512], ALU.mult),
                      waits=[mm, rbckv_toks[tc]] + (ktile_use or []), inc="dve")
            rbc_use = [ke]
            bf[bi] = [ke]
            sx = None
            for j in range(4):
                sx = p.dma("sp", k_scr[h, tc * 4 + j], ktile[:, j * 128:(j + 1) * 128],
                           waits=[ke], sem=ks_sem)
            ab = p.op("dve", lambda e: e.tensor_reduce(kab_tmp[:], ktile[:], AX.X, ALU.max,
                                                       apply_absolute_value=True),
                      waits=[ke], inc="dve")
            am = p.op("dve", lambda e, h=h, tc=tc:
                      (e.tensor_copy(kab_pos[:, h:h + 1], kab_tmp[:]) if tc == 0 else
                       e.tensor_tensor(kab_pos[:, h:h + 1], kab_pos[:, h:h + 1], kab_tmp[:], ALU.max)),
                      waits=[ab] + ([kab_prev.get(h)] if kab_prev.get(h) else []), inc="dve")
            kab_prev[h] = am
            ktile_use = [ab, sx]
        ng = p.op("dve", lambda e, h=h: e.tensor_scalar(kab_nope[:, h:h + 1], kab_pos[:, h:h + 1],
                                                        -1.0, None, ALU.mult),
                  waits=[kab_prev[h]], inc="dve")
        kab_prev[h] = ng
    k_spilled = p.tok(ks_sem)
    pb = p.op("dve", lambda e: e.tensor_reduce(kab_tmp[0:64, :], kpe_r[0:64, :], AX.X, ALU.max,
                                               apply_absolute_value=True, negate=True),
              waits=[kpe_all], inc="dve")
    pe_ab = p.op("dve", lambda e: e.tensor_copy(kab_pe[:], kab_tmp[0:64, :]),
                 waits=[pb], inc="dve")

    vs_sem = f"l0vs{u}"
    vtile_use = None
    for oc in range(H * VD // OC):
        b = oc % 2
        wtok = wall_load(b, wsrc_d["w_uv"], oc * OC, OC, NKV)
        for tk in range(NKT):
            bi = (oc * NKT + tk) % 2
            bank = ctx.psum[bi]
            mm = None
            for i in range(NKV):
                mm = p.op("pe", lambda e, bank=bank, b=b, i=i, tk=tk:
                          e.matmul(bank[:, 0:OC], kv_g[i][:, tk * 128:(tk + 1) * 128],
                                   wall[b][:, i * OC:(i + 1) * OC],
                                   start=(i == 0), stop=(i == NKV - 1)),
                          waits=[wtok, kv_ready] + (bf[bi] if i == 0 else []),
                          inc="pe")
            wall_use[b] = mm
            ve = p.op("dve", lambda e, bank=bank, tk=tk:
                      e.tensor_scalar(ktile[:, 0:OC], bank[:, 0:OC], rstdkv_col[:, tk:tk + 1],
                                      None, ALU.mult),
                      waits=[mm, rcol_all] + (vtile_use if vtile_use else (ktile_use or [])),
                      inc="dve")
            bf[bi] = [ve]
            sx = p.dma("sp", v_scr[tk, oc], ktile[:, 0:OC], waits=[ve], sem=vs_sem)
            vtile_use = [sx]
    v_spilled = p.tok(vs_sem)
    rbcq = rbckv
    kv_rbc_done = p.tok("dve")
    rbcq_toks = []
    for qb in range(NQ):
        bm = ctx.bcast_f32(rstdq[0:1, qb * 512:(qb + 1) * 512], 3, [rq_tok])
        be = p.op("act", lambda e, qb=qb: e.activation(rbcq[:, qb * 512:(qb + 1) * 512],
                                                       ctx.psum[3][:], AF.Copy),
                  waits=[bm, kv_rbc_done] + bf[3], inc="act")
        bf[3] = [be]
        rbcq_toks.append(be)
    ctx.pop_scope()
    p.region([(s2_, v2_) for (s2_, v2_) in p.all_toks() if not s2_.startswith("cc")])
    if phase_limit <= 4:
        ctx.pop_scope()
        return [p.tok("pe")]

    o_all = xfm
    qkv_done_pe = p.tok("pe")
    pt = [ctx.sb(f"l0pt{u}_{i}", [128, 512], BF16) for i in range(2)]
    pt_use = [None] * 2
    qn = ctx.sb(f"l0qn{u}", [128, 512], BF16)
    qaug = ctx.sb(f"l0qaug{u}", [65, 512], BF16)
    qpe_tmp, qpe_sw = kpe_tmp, kpe_sw
    absq = ctx.sb(f"l0absq{u}", [128, 512], BF16)
    absp = ctx.sb(f"l0absp{u}", [64, 512], BF16)
    den = ctx.sb(f"l0den{u}", [1, 512], F32)
    NQL = QL // 128
    qlb = [ctx.sb(f"l0qlb{u}_{b}", [128, NQL * 512], BF16) for b in range(2)]
    qlb_sem = [f"l0qlbs{u}_{b}" for b in range(2)]
    qlb_use = [None, None]
    _whb1 = ctx.sb(f"l0whb{u}", [128, NQL * 192], BF16)
    whb = [_whb1, _whb1]
    whb_sem = [f"l0whbs{u}", f"l0whbs{u}"]
    whb_use = [None, None]
    kb = [ctx.sb(f"l0kb{u}_{b}", [128, 128], BF16) for b in range(3)]
    kb_sem = [f"l0kbs{u}_{b}" for b in range(3)]
    kb_use = [None, None, None]
    vb = [ctx.sb(f"l0vb{u}_{b}", [128, 128], BF16) for b in range(3)]
    vb_sem = [f"l0vbs{u}_{b}" for b in range(3)]
    vb_use = [None, None, None]
    stb = [0, 1, 6]
    aug_use = None
    qn_use = None
    rope_use2 = rope_use
    absq_use = None
    orbc_use = rbc_use
    for h in range(H):
        bw = 0
        wtok = None
        for i in range(NQL):
            wtok = p.dma("sp", whb[bw][:, i * 192:(i + 1) * 192],
                         wsrc_d["w_uq"][i * 128:(i + 1) * 128, h * 192:(h + 1) * 192],
                         waits=list(cast_ready) + ([whb_use[bw]] if whb_use[bw] else []),
                         sem=whb_sem[bw])
        wready = (whb_sem[bw], wtok[1])
        for qb in range(NQ):
            bq = (h * NQ + qb) % 2
            qltok = None
            for i in range(NQL):
                qltok = p.dma("sp", qlb[bq][:, i * 512:(i + 1) * 512],
                              q_scr[i * 128:(i + 1) * 128, qb * 512:(qb + 1) * 512],
                              waits=qlat_stored + ([qlb_use[bq]] if qlb_use[bq] else []),
                              sem=qlb_sem[bq])
            qlready = (qlb_sem[bq], qltok[1])
            mmn = None
            for i in range(NQL):
                mmn = p.op("pe", lambda e, bq=bq, bw=bw, i=i:
                           e.matmul(ctx.psum[5][:], whb[bw][:, i * 192:i * 192 + 128],
                                    qlb[bq][:, i * 512:(i + 1) * 512],
                                    start=(i == 0), stop=(i == NQL - 1)),
                           waits=[wready, qlready] + (bf[5] if i == 0 else []), inc="pe")
            mmp = None
            for i in range(NQL):
                mmp = p.op("pe", lambda e, bq=bq, bw=bw, i=i:
                           e.matmul(ctx.psum[2][0:64, :], whb[bw][:, i * 192 + 128:(i + 1) * 192],
                                    qlb[bq][:, i * 512:(i + 1) * 512],
                                    start=(i == 0), stop=(i == NQL - 1)),
                           waits=[wready, qlready] + (bf[2] if i == 0 else []), inc="pe")
            whb_use[bw] = mmp
            qlb_use[bq] = mmp
            qe = p.op("dve", lambda e, qb=qb: e.tensor_tensor(qn[:], ctx.psum[5][:],
                                                              rbcq[:, qb * 512:(qb + 1) * 512], ALU.mult),
                      waits=[mmn, rbcq_toks[qb]] + ([qn_use] if qn_use else []), inc="dve")
            bf[5] = [qe]
            pe1 = p.op("act", lambda e: e.activation(qpe_tmp[:], ctx.psum[2][0:64, :], AF.Copy),
                       waits=[mmp] + ([rope_use2] if rope_use2 else []), inc="act")
            pe2 = p.op("act", lambda e: e.activation(qpe_sw[0:32, :], ctx.psum[2][32:64, :], AF.Copy),
                       waits=[mmp], inc="act")
            pe3 = p.op("act", lambda e: e.activation(qpe_sw[32:64, :], ctx.psum[2][0:32, :], AF.Copy),
                       waits=[mmp], inc="act")
            bf[2] = [pe1, pe2, pe3]
            u1 = p.op("dve", lambda e, qb=qb: e.tensor_tensor(qpe_tmp[:], qpe_tmp[:],
                                                              ra[:, qb * 512:(qb + 1) * 512], ALU.mult),
                      waits=[pe1, ra_tok], inc="dve")
            u2 = p.op("dve", lambda e, qb=qb: e.tensor_tensor(qpe_sw[:], qpe_sw[:],
                                                              rb[:, qb * 512:(qb + 1) * 512], ALU.mult),
                      waits=[pe3, rb_tok], inc="dve")
            u3 = p.op("dve", lambda e: e.tensor_tensor(qpe_tmp[:], qpe_tmp[:], qpe_sw[:], ALU.add),
                      waits=[u1, u2], inc="dve")
            u4 = p.op("dve", lambda e, qb=qb: e.tensor_tensor(qaug[0:64, :], qpe_tmp[:],
                                                              rbcq[0:64, qb * 512:(qb + 1) * 512],
                                                              ALU.mult),
                      waits=[u3, rbcq_toks[qb]] + ([aug_use] if aug_use else []), inc="dve")
            rope_use2 = u4
            a1 = p.op("act", lambda e: e.activation(absq[:], qn[:], AF.Abs),
                      waits=[qe] + ([absq_use] if absq_use else []), inc="act")
            a2 = p.op("act", lambda e: e.activation(absp[:], qaug[0:64, :], AF.Abs),
                      waits=[u4], inc="act")
            bm1 = p.op("pe", lambda e, h=h: e.matmul(ctx.psum[2][0:1, :], kab_nope[:, h:h + 1],
                                                     absq[:], start=True, stop=False),
                       waits=[a1, kab_prev[h], pe1, pe2, pe3], inc="pe")
            bm2 = p.op("pe", lambda e: e.matmul(ctx.psum[2][0:1, :], kab_pe[:], absp[:],
                                                start=False, stop=True),
                       waits=[a2, pe_ab], inc="pe")
            me = p.op("act", lambda e: e.activation(qaug[64:65, :], ctx.psum[2][0:1, :], AF.Copy),
                      waits=[bm2], inc="act")
            bf[2] = [me]
            aug_use = me
            absq_use = bm1
            nkt_q = 4 * qb + 4
            den_mm = None
            o_mm = None
            for kt in range(nkt_q):
                bi = stb[kt % 3]
                bank = ctx.psum[bi]
                kslot = kt % 3
                ktok = p.dma("sp", kb[kslot][:], k_scr[h, kt],
                             waits=[k_spilled] + ([kb_use[kslot]] if kb_use[kslot] else []),
                             sem=kb_sem[kslot])
                mm1 = p.op("pe", lambda e, bank=bank, kslot=kslot:
                           e.matmul(bank[:], kb[kslot][:], qn[:], start=True, stop=False),
                           waits=[ktok, qe] + bf[bi], inc="pe")
                mm2 = p.op("pe", lambda e, bank=bank, kt=kt:
                           e.matmul(bank[:], kpe_r[:, kt * 128:(kt + 1) * 128], qaug[:],
                                    start=False, stop=True),
                           waits=[me, u4, kpe_all, on_tok], inc="pe")
                kb_use[kslot] = mm1
                ps_ = kt % 2
                diag_off = kt * 128 - qb * 512
                if 0 <= diag_off < 512:
                    ex = p.op("act", lambda e, bank=bank: e.activation(sqq[:], bank[:], AF.Exp),
                              waits=[mm2, sqq_use], inc="act")
                    exm = p.op("dve", lambda e, ps_=ps_, off=diag_off:
                               e.tensor_tensor(pt[ps_][:], sqq[:],
                                               trilw[:, 384 - off:896 - off], ALU.mult),
                               waits=[ex, trilw_tok] + ([pt_use[ps_]] if pt_use[ps_] else []),
                               inc="dve")
                    sqq_use = exm
                    exf = exm
                else:
                    exf = p.op("act", lambda e, bank=bank, ps_=ps_:
                               e.activation(pt[ps_][:], bank[:], AF.Exp),
                               waits=[mm2] + ([pt_use[ps_]] if pt_use[ps_] else []), inc="act")
                bf[bi] = [exf]
                den_mm = p.op("pe", lambda e, kt=kt, ps_=ps_, last=(kt == nkt_q - 1):
                              e.matmul(ctx.psum[3][0:1, :], ones_col[:], pt[ps_][:],
                                       start=(kt == 0), stop=last),
                              waits=[exf, ocol_tok] + (bf[3] if kt == 0 else []), inc="pe")
                vslot = kt % 3
                vtok = p.dma("sp", vb[vslot][:], v_scr[kt, h],
                             waits=[v_spilled] + ([vb_use[vslot]] if vb_use[vslot] else []),
                             sem=vb_sem[vslot])
                o_mm = p.op("pe", lambda e, kt=kt, vslot=vslot, ps_=ps_, last=(kt == nkt_q - 1):
                            e.matmul(ctx.psum[4][:], vb[vslot][:], pt[ps_][:],
                                     start=(kt == 0), stop=last),
                            waits=[exf, vtok] + (bf[4] if kt == 0 else []), inc="pe")
                vb_use[vslot] = o_mm
                pt_use[ps_] = o_mm
            d1 = p.op("act", lambda e: e.activation(den[:], ctx.psum[3][0:1, :], AF.Copy),
                      waits=[den_mm], inc="act")
            bf[3] = [d1]
            d3 = p.op("dve", lambda e: e.reciprocal(den[:], den[:]), waits=[d1], inc="dve")
            bcm = ctx.bcast_f32(den[0:1, :], 3, [d3])
            rb2 = p.op("act", lambda e: e.activation(rbc[:], ctx.psum[3][:], AF.Copy),
                       waits=[bcm] + (orbc_use or []), inc="act")
            bf[3] = [rb2]
            oe = p.op("dve", lambda e, h=h, qb=qb:
                      e.tensor_tensor(o_all[h][:, qb * 512:(qb + 1) * 512], ctx.psum[4][:], rbc[:],
                                      ALU.mult),
                      waits=[o_mm, rb2, qkv_done_pe], inc="dve")
            bf[4] = [oe]
            orbc_use = [oe]
            qn_use = mm1
    attn_done_dve = p.tok("dve")
    if phase_limit <= 5:
        return [p.tok("pe")]

    WOC = 128
    yev = evb
    yev_free = list(ev_free)
    ys_sem = [f"l0ys{u}_{b}" for b in range(2)]
    for dc in range(D // WOC):
        b = dc % 2
        wtok = None
        for hh in range(H):
            wtok = p.dma("sp", wall[b][:, hh * WOC:(hh + 1) * WOC],
                         wsrc_d["w_o"][hh * 128:(hh + 1) * 128, dc * WOC:(dc + 1) * WOC],
                         waits=list(cast_ready) + ([wall_use[b]] if wall_use[b] else []), sem=wsem[b])
        wready = (wsem[b], wtok[1])
        for ds in range(WOC // 128):
            for qb in range(NQ):
                bi = (ds * NQ + qb) % 4
                bank = ctx.psum[bi]
                mm = None
                for hh in range(H):
                    mm = p.op("pe", lambda e, bank=bank, b=b, hh=hh, ds=ds, qb=qb:
                              e.matmul(bank[:], wall[b][:, hh * WOC + ds * 128:hh * WOC + (ds + 1) * 128],
                                       o_all[hh][:, qb * 512:(qb + 1) * 512],
                                       start=(hh == 0), stop=(hh == H - 1)),
                              waits=[wready, attn_done_dve] + (bf[bi] if hh == 0 else []),
                              inc="pe")
                wall_use[b] = mm
                es_ = bi % 2
                et = p.op("act", lambda e, bank=bank, es_=es_:
                          e.activation(yev[es_][:], bank[:], AF.Copy),
                          waits=[mm] + ([yev_free[es_]] if yev_free[es_] else []), inc="act")
                bf[bi] = [et]
                sx = p.dma("sp", y_fm[(dc * (WOC // 128) + ds) * 128:(dc * (WOC // 128) + ds + 1) * 128,
                                      qb * 512:(qb + 1) * 512],
                           yev[es_][:], waits=[et] + list(y_guard), sem=ys_sem[es_])
                yev_free[es_] = sx
    scr["poolA_free"] = [p.tok("pe")]
    return [p.tok(s) for s in ys_sem]


def build_l0_attn_launch(cfg, consts, phase_limit=99):
    nc = bass.Bass("TRN2", target_bir_lowering=False, debug=False)
    c = cfg
    T, D = c.T, c.D
    x_in = nc.dram_tensor("x_in", [T, D], F32, kind="ExternalInput").ap()
    wnames = {
        "w_dq": [D, c.QL], "g_q": [c.QL], "w_uq": [c.QL, c.H * c.QKD],
        "w_dkv": [D, c.KVL], "w_kr": [D, c.ROPE], "g_kv": [c.KVL],
        "w_uk": [c.KVL, c.H * c.NOPE], "w_uv": [c.KVL, c.H * c.VD], "w_o": [c.H * c.VD, D],
    }
    w = {n: nc.dram_tensor(n, s, F32, kind="ExternalInput").ap() for n, s in wnames.items()}
    x_out = nc.dram_tensor("x_out", [T, D], F32, kind="ExternalOutput").ap()
    const_aps = {}
    for name, arr in consts.items():
        dt = {np.dtype(np.float32): F32, np.dtype(BF): BF16, np.dtype(np.int32): I32}[arr.dtype]
        const_aps[name] = nc.dram_tensor(name, list(arr.shape), dt, kind="ExternalInput").ap()
    xln = nc.dram_tensor("s_xln", [T, D], BF16).ap()
    y_fm = nc.dram_tensor("s_yfm", [D, T], BF16).ap()
    scr = {
        "l0_q": nc.dram_tensor("s_q", [c.QL, T], BF16).ap(),
        "l0_k": nc.dram_tensor("s_k", [c.H, T // 128, 128, 128], BF16).ap(),
        "l0_v": nc.dram_tensor("s_v", [T // 128, c.H, 128, 128], BF16).ap(),
    }
    wl0_bf = {n: nc.dram_tensor("s_bf_" + n, s, BF16).ap() for n, s in wnames.items()
              if n not in ("g_q", "g_kv")}
    es = ExitStack()
    with es:
        ctx = Ctx(nc, es, c, const_aps)
        for nm2, ap2 in wl0_bf.items():
            rows = wnames[nm2][0]
            r0 = 0
            while r0 < rows:
                rr = min(512, rows - r0)
                ctx.p.dma("po", ap2[r0:r0 + rr, :], w[nm2][r0:r0 + rr, :], sem="l0cast")
                r0 += rr
        l0c = ctx.p.tok("l0cast")
        state = {}
        ln_toks = emit_sandwich(ctx, x_in, None, None, xln, True, state)
        y_toks = emit_l0_attn(ctx, xln, ln_toks, w, y_fm, scr, wbf=wl0_bf, cast_ready=[l0c],
                              phase_limit=phase_limit)
        if phase_limit > 6:
            emit_sandwich(ctx, x_in, y_fm, x_out, xln, False, state, y_ready=y_toks, write_xln=False)
        final = ctx.p.all_toks()
        ctx.p.op("sp", lambda e: e.nop(), waits=final)
        ctx.p.emit(nc, es)
    return nc

def emit_gated_ffn(ctx, xln_dram, xln_ready, w1f, w2f, w3f, act_fn, F, y_fm, y3p, scr,
                   w_ready=(), y_guard=()):
    c, p = ctx.c, ctx.p
    T, D = c.T, c.D
    NS = D // 128
    NTC = T // 512
    NF = F // 128
    u = ctx.uid()
    bf = ctx.bank_free

    xfm = ctx.strip_set("A")
    xfm_tok = emit_load_xln_fm(ctx, xln_dram, xln_ready, xfm)
    hid = ctx.strip_set("H", 11)

    OCW = 256
    wall = [[ctx.sb(f"gf{u}_wa{wi}{b}", [128, NS * OCW], BF16) for b in range(2)] for wi in range(2)]
    wsem = [["gf_ws%d%d" % (wi, b) for b in range(2)] for wi in range(2)]
    wall_use = [[None, None], [None, None]]
    wall3 = [ctx.sb(f"gf{u}_w3{b}", [128, 11 * 512], BF16) for b in range(2)]
    w3sem = ["gf_w3s%d" % b for b in range(2)]
    wall3_use = [None, None]
    evs = ctx.sb(f"gf{u}_ev", [128, 512], BF16)
    evs_use = None
    ybuf = [ctx.sb(f"gf{u}_yb{b}", [128, 512], F32) for b in range(2)]
    ybuf_use = [None, None]
    ypin = [ctx.sb(f"gf{u}_yp{b}", [128, 512], F32) for b in range(2)]
    ypin_sem = ["gf_yps" + str(b) for b in range(2)]
    ypin_use = [None, None]
    ybf = ctx.sb(f"gf{u}_ybf", [128, 512], BF16)
    ybf_use = None
    ys_sem = "gf_ys"
    yp_sem = "gf_yo"

    quarters = []
    base = 0
    for qi in range(4):
        nq = (NF + 3 - qi) // 4
        quarters.append((base, nq))
        base += nq
    assert base == NF

    hb_toks = [None] * 11
    for qi, (qbase, qn) in enumerate(quarters):
        oc0 = qbase * 128
        oc1 = (qbase + qn) * 128
        chunks = []
        o0 = oc0
        while o0 < oc1:
            ow = min(OCW, oc1 - o0)
            chunks.append((o0, ow))
            o0 += ow
        wtoks = {}

        def load_chunk(ci):
            o0, ow = chunks[ci]
            b = ci % 2
            for wi in range(2):
                wf = (w1f, w2f)[wi]
                tok = None
                for i in range(NS):
                    tok = p.dma("sp", wall[wi][b][:, i * OCW:i * OCW + ow],
                                wf(o0, i * 128, ow, 128),
                                waits=list(w_ready) +
                                      ([wall_use[wi][b]] if wall_use[wi][b] else []),
                                sem=wsem[wi][b], transpose=True)
                wtoks[(ci, wi)] = (wsem[wi][b], tok[1])

        load_chunk(0)
        for ci, (o0, ow) in enumerate(chunks):
            if ci + 1 < len(chunks):
                load_chunk(ci + 1)
            b = ci % 2
            now = ow // 128
            w1r, w2r = wtoks[(ci, 0)], wtoks[(ci, 1)]
            mm_last = [None, None]
            for so in range(now):
                j = (o0 - qbase * 128) // 128 + so
                for tc in range(NTC):
                    b1, b2 = (tc % 2) * 2, (tc % 2) * 2 + 1
                    mm1 = mm2 = None
                    for i in range(NS):
                        mm1 = p.op("pe", lambda e, b1=b1, i=i, tc=tc, so=so, b=b:
                                   e.matmul(ctx.psum[b1][:],
                                            wall[0][b][:, i * OCW + so * 128:i * OCW + (so + 1) * 128],
                                            xfm[i][:, tc * 512:(tc + 1) * 512],
                                            start=(i == 0), stop=(i == NS - 1)),
                                   waits=[w1r, xfm_tok] + (bf[b1] if i == 0 else []), inc="pe")
                    for i in range(NS):
                        mm2 = p.op("pe", lambda e, b2=b2, i=i, tc=tc, so=so, b=b:
                                   e.matmul(ctx.psum[b2][:],
                                            wall[1][b][:, i * OCW + so * 128:i * OCW + (so + 1) * 128],
                                            xfm[i][:, tc * 512:(tc + 1) * 512],
                                            start=(i == 0), stop=(i == NS - 1)),
                                   waits=[w2r, xfm_tok] + (bf[b2] if i == 0 else []), inc="pe")
                    mm_last = [mm1, mm2]
                    a1 = p.op("act", lambda e, b1=b1: e.activation(evs[:], ctx.psum[b1][:], act_fn),
                              waits=[mm1] + ([evs_use] if evs_use else []), inc="act")
                    bf[b1] = [a1]
                    hv = p.op("dve", lambda e, b2=b2, j=j, tc=tc:
                              e.tensor_tensor(hid[j][:, tc * 512:(tc + 1) * 512], evs[:],
                                              ctx.psum[b2][:], ALU.mult),
                              waits=[a1, mm2] + ([hb_toks[j]] if (tc == 0 and hb_toks[j]) else []),
                              inc="dve")
                    bf[b2] = [hv]
                    evs_use = hv
                    hb_toks[j] = hv
            wall_use[0][b], wall_use[1][b] = mm_last
        w3toks = {}

        def load_w3(dcg):
            b = dcg % 2
            tok = None
            for j in range(qn):
                tok = p.dma("sp", wall3[b][:, j * 512:(j + 1) * 512],
                            w3f(dcg * 512, (qbase + j) * 128, 512, 128),
                            waits=list(w_ready) + ([wall3_use[b]] if wall3_use[b] else []),
                            sem=w3sem[b], transpose=True)
            w3toks[dcg] = (w3sem[b], tok[1])

        load_w3(0)
        for dcg in range(4):
            if dcg + 1 < 4:
                load_w3(dcg + 1)
            b = dcg % 2
            wr = w3toks[dcg]
            mml = None
            for ds in range(4):
                dc = dcg * 4 + ds
                for tc in range(NTC):
                    bi = 4 + (ds * NTC + tc) % 2
                    mm = None
                    for j in range(qn):
                        mm = p.op("pe", lambda e, bi=bi, j=j, tc=tc, ds=ds, qn=qn, b=b:
                                  e.matmul(ctx.psum[bi][:],
                                           wall3[b][:, j * 512 + ds * 128:j * 512 + (ds + 1) * 128],
                                           hid[j][:, tc * 512:(tc + 1) * 512],
                                           start=(j == 0), stop=(j == qn - 1)),
                                  waits=[wr, hb_toks[j]] + (bf[bi] if j == 0 else []), inc="pe")
                    mml = mm
                    row = slice(dc * 128, (dc + 1) * 128)
                    colc = slice(tc * 512, (tc + 1) * 512)
                    yslot = tc % 2
                    if qi == 0:
                        ye = p.op("act", lambda e, bi=bi, yslot=yslot:
                                  e.activation(ybuf[yslot][:], ctx.psum[bi][:], AF.Copy),
                                  waits=[mm] + ([ybuf_use[yslot]] if ybuf_use[yslot] else []),
                                  inc="act")
                        bf[bi] = [ye]
                        sx = p.dma("sp", y3p[row, colc], ybuf[yslot][:], waits=[ye], sem=yp_sem)
                        ybuf_use[yslot] = sx
                    else:
                        ptok = p.dma("sp", ypin[yslot][:], y3p[row, colc],
                                     waits=[p.tok(yp_sem)] +
                                           ([ypin_use[yslot]] if ypin_use[yslot] else []),
                                     sem=ypin_sem[yslot])
                        if qi < 3:
                            ye = p.op("dve", lambda e, bi=bi, yslot=yslot:
                                      e.tensor_tensor(ybuf[yslot][:], ctx.psum[bi][:],
                                                      ypin[yslot][:], ALU.add),
                                      waits=[mm, ptok] + ([ybuf_use[yslot]] if ybuf_use[yslot] else []),
                                      inc="dve")
                            bf[bi] = [ye]
                            ypin_use[yslot] = ye
                            sx = p.dma("sp", y3p[row, colc], ybuf[yslot][:], waits=[ye], sem=yp_sem)
                            ybuf_use[yslot] = sx
                        else:
                            ye = p.op("dve", lambda e, bi=bi, yslot=yslot:
                                      e.tensor_tensor(ybf[:], ctx.psum[bi][:], ypin[yslot][:],
                                                      ALU.add),
                                      waits=[mm, ptok] + ([ybf_use] if ybf_use else []), inc="dve")
                            bf[bi] = [ye]
                            ypin_use[yslot] = ye
                            sx = p.dma("sp", y_fm[row, colc], ybf[:], waits=[ye] + list(y_guard),
                                       sem=ys_sem)
                            ybf_use = sx
            wall3_use[b] = mml
    return [p.tok(ys_sem)]


def build_ffn_launch(cfg, consts, act_fn, F):
    nc = bass.Bass("TRN2", target_bir_lowering=False, debug=False)
    c = cfg
    T, D = c.T, c.D
    x_in = nc.dram_tensor("x_in", [T, D], F32, kind="ExternalInput").ap()
    w1T = nc.dram_tensor("w1_bf", [F, D], BF16, kind="ExternalInput").ap()
    w2T = nc.dram_tensor("w2_bf", [F, D], BF16, kind="ExternalInput").ap()
    w3T = nc.dram_tensor("w3_bf", [D, F], BF16, kind="ExternalInput").ap()
    x_out = nc.dram_tensor("x_out", [T, D], F32, kind="ExternalOutput").ap()
    const_aps = {}
    for name, arr in consts.items():
        dt = {np.dtype(np.float32): F32, np.dtype(BF): BF16, np.dtype(np.int32): I32}[arr.dtype]
        const_aps[name] = nc.dram_tensor(name, list(arr.shape), dt, kind="ExternalInput").ap()
    xln = nc.dram_tensor("s_xln", [T, D], BF16).ap()
    y_fm = nc.dram_tensor("s_yfm", [D, T], BF16).ap()
    y3p = nc.dram_tensor("s_y3p", [D, T], F32).ap()
    es = ExitStack()
    with es:
        ctx = Ctx(nc, es, c, const_aps)
        state = {}
        ln_toks = emit_sandwich(ctx, x_in, None, None, xln, True, state)
        w1f = lambda o0, d0, no, nd: w1T[o0:o0 + no, d0:d0 + nd]
        w2f = lambda o0, d0, no, nd: w2T[o0:o0 + no, d0:d0 + nd]
        w3f = lambda d0, h0, nd, nh: w3T[d0:d0 + nd, h0:h0 + nh]
        y_toks = emit_gated_ffn(ctx, xln, ln_toks, w1f, w2f, w3f, act_fn, F, y_fm, y3p, {})
        emit_sandwich(ctx, x_in, y_fm, x_out, xln, False, state, y_ready=y_toks, write_xln=False)
        final = ctx.p.all_toks()
        ctx.p.op("sp", lambda e: e.nop(), waits=final)
        ctx.p.emit(nc, es)
    return nc

COLS_W = 2048 // 8
ROWS_W3 = 2048 // 8
N_ELEM_W = 5632 * 2048


def _prep_tiles(cfg):
    return [
        ("w1s", "wq1", 5632 // 128, [(0, COLS_W)], 0),
        ("w2s", "wq2", 5632 // 128, [(0, COLS_W)], 1),
        ("w3s", "wq3", ROWS_W3 // 128, [(0, 2048), (2048, 2048), (4096, 1536)], 2),
    ]


def build_prep_a(cfg):
    nc = bass.Bass("TRN2", target_bir_lowering=False, debug=False)
    w1s = nc.dram_tensor("w1s", [5632, COLS_W], F32, kind="ExternalInput").ap()
    w2s = nc.dram_tensor("w2s", [5632, COLS_W], F32, kind="ExternalInput").ap()
    w3s = nc.dram_tensor("w3s", [ROWS_W3, 5632], F32, kind="ExternalInput").ap()
    gpart = nc.dram_tensor("gpart", [1, 3], F32, kind="ExternalOutput").ap()
    es = ExitStack()
    with es:
        ctx = Ctx(nc, es, cfg, {})
        emit_prep_a(ctx, {"w1s": w1s, "w2s": w2s, "w3s": w3s}, gpart)
        ctx.p.op("sp", lambda e: e.nop(), waits=ctx.p.all_toks())
        ctx.p.emit(nc, es)
    return nc


def emit_prep_a(ctx, w, gpart):
    p = ctx.p
    u = ctx.uid()
    ctx.push_scope()
    buf = [ctx.sb(f"pa{u}_b{b}", [128, 1024], F32) for b in range(2)]
    bsem = [f"pa{u}_s{b}" for b in range(2)]
    buse = [None, None]
    acc = ctx.sb(f"pa{u}_acc", [128, 3], F32)
    red = ctx.sb(f"pa{u}_red", [128, 1], F32)
    onesf = ctx.sb(f"pa{u}_ones", [128, 1], F32)
    gp = ctx.sb(f"pa{u}_gp", [1, 3], F32)
    ot = p.op("dve", lambda e: e.memset(onesf[:], 1.0), inc="dve")
    zt = p.op("dve", lambda e: e.memset(acc[:], 0.0), inc="dve")

    def groups(nrows, ncols, maxw=1024):
        ntile = nrows // 128
        if ncols >= maxw:
            for it in range(ntile):
                c0 = 0
                while c0 < ncols:
                    cw = min(maxw, ncols - c0)
                    yield it * 128, 1, c0, cw
                    c0 += cw
        else:
            rg = maxw // ncols
            it = 0
            while it < ntile:
                nt = min(rg, ntile - it)
                yield it * 128, nt, 0, ncols
                it += nt

    def gview(src2, r0, nt, c0, cw):
        if nt == 1:
            return src2[r0:r0 + 128, c0:c0 + cw]
        return src2[r0:r0 + nt * 128, c0:c0 + cw].rearrange("(t p) c -> p t c", p=128)

    def bview(t3, wdt, nt):
        if nt == 1:
            return t3[:, 0:wdt]
        return t3[:, 0:wdt].rearrange("p (t c) -> p t c", t=nt)

    red_use = None
    acc_use = zt
    for src2, j2, (nr, ncx) in ((w["w1s"], 0, (5632, COLS_W)), (w["w2s"], 1, (5632, COLS_W)),
                                (w["w3s"], 2, (ROWS_W3, 5632))):
        for r0, nt, c0, cw in groups(nr, ncx):
            b = (r0 // 128 + c0) % 2
            wdt = nt * cw
            tok = p.dma("act", bview(buf[b], wdt, nt), gview(src2, r0, nt, c0, cw),
                        waits=([buse[b]] if buse[b] else []), sem=bsem[b])
            r = p.op("dve", lambda e, b=b, wdt=wdt:
                     e.tensor_reduce(red[:], buf[b][:, 0:wdt], AX.X, ALU.add,
                                     apply_absolute_value=True),
                     waits=[tok] + ([red_use] if red_use else []), inc="dve")
            buse[b] = r
            a = p.op("dve", lambda e, j2=j2: e.tensor_tensor(acc[:, j2:j2 + 1], acc[:, j2:j2 + 1],
                                                             red[:], ALU.add),
                     waits=[r, acc_use], inc="dve")
            red_use = a
            acc_use = a
    prev = ctx.bank_free[0] or []
    mm = None
    for j2 in range(3):
        mm = p.op("pe", lambda e, j2=j2: e.matmul(ctx.psum[0][0:1, j2:j2 + 1], acc[:, j2:j2 + 1],
                                                  onesf[:], start=True, stop=True),
                  waits=[acc_use, ot] + prev + ([mm] if mm else []), inc="pe")
    ev = p.op("act", lambda e: e.activation(gp[:], ctx.psum[0][0:1, 0:3], AF.Copy),
              waits=[mm], inc="act")
    ctx.bank_free[0] = [ev]
    st = p.dma("po", gpart, gp[:], waits=[ev], sem=f"pa{u}_st")
    ctx.pop_scope()
    return st


def emit_prep_b(ctx, w, gparts, lut, outs, gamma_pre=None, gamma_wait=()):
    c, p = ctx.c, ctx.p
    u = ctx.uid()
    bf = ctx.bank_free
    ctx.push_scope()
    onesr = ctx.sb(f"pb{u}_or", [1, 128], F32)
    ones8 = ctx.sb(f"pb{u}_o8", [8, 1], F32)
    ot = p.op("dve", lambda e: e.memset(onesr[:], 1.0), inc="dve")
    o8 = p.op("dve", lambda e: e.memset(ones8[:], 1.0), inc="dve")
    if gamma_pre is None:
        gpt = ctx.sb(f"pb{u}_gp", [8, 3], F32)
        gptok = p.dma("act", gpt[:], gparts, sem=f"pb{u}_gp")
    lutr = ctx.sb(f"pb{u}_lut", [1, 16], F32)
    luttok = p.dma("act", lutr[:], lut, sem=f"pb{u}_lut")
    lutc = ctx.sb(f"pb{u}_lutc", [128, 16], F32)
    mm = p.op("pe", lambda e: e.matmul(ctx.psum[0][:, 0:16], onesr[:], lutr[:],
                                       start=True, stop=True),
              waits=[ot, luttok] + (bf[0] or []), inc="pe")
    lc = p.op("act", lambda e: e.activation(lutc[:], ctx.psum[0][:, 0:16], AF.Copy),
              waits=[mm], inc="act")
    bf[0] = [lc]
    grow = ctx.sb(f"pb{u}_grow", [1, 6], F32)
    if gamma_pre is None:
        gprev = None
        for j2 in range(3):
            gm = p.op("pe", lambda e, j2=j2: e.matmul(ctx.psum[1][0:1, j2:j2 + 1], gpt[:, j2:j2 + 1],
                                                      ones8[:], start=True, stop=True),
                      waits=[gptok, o8] + (bf[1] if j2 == 0 else []) + ([gprev] if gprev else []),
                      inc="pe")
            gprev = gm
        ge = p.op("act", lambda e: e.activation(grow[0:1, 0:3], ctx.psum[1][0:1, 0:3], AF.Copy),
                  waits=[gprev], inc="act")
        bf[1] = [ge]
    else:
        ge = p.dma("act", grow[0:1, 0:3], gamma_pre, waits=list(gamma_wait), sem=f"pb{u}_gpre")
    g2 = p.op("dve", lambda e: e.tensor_scalar(grow[0:1, 0:3], grow[0:1, 0:3],
                                               0.5 / N_ELEM_W, 0.5 * c.EPS_BIT,
                                               ALU.mult, ALU.add),
              waits=[ge], inc="dve")
    g3 = p.op("dve", lambda e: e.tensor_scalar(grow[0:1, 3:6], grow[0:1, 0:3], -1.0, None, ALU.mult),
              waits=[g2], inc="dve")
    thr = ctx.sb(f"pb{u}_thr", [128, 6], F32)
    bm = p.op("pe", lambda e: e.matmul(ctx.psum[2][:, 0:6], onesr[:], grow[:],
                                       start=True, stop=True),
              waits=[g3, ot] + (bf[2] or []), inc="pe")
    bt = p.op("act", lambda e: e.activation(thr[:], ctx.psum[2][:, 0:6], AF.Copy),
              waits=[bm], inc="act")
    bf[2] = [bt]

    def groups(nrows, ncols, maxw=1024):
        ntile = nrows // 128
        if ncols >= maxw:
            for it in range(ntile):
                c0 = 0
                while c0 < ncols:
                    cw = min(maxw, ncols - c0)
                    yield it * 128, 1, c0, cw
                    c0 += cw
        else:
            rg = maxw // ncols
            it = 0
            while it < ntile:
                nt = min(rg, ntile - it)
                yield it * 128, nt, 0, ncols
                it += nt

    def gview(src2, r0, nt, c0, cw):
        if nt == 1:
            return src2[r0:r0 + 128, c0:c0 + cw]
        return src2[r0:r0 + nt * 128, c0:c0 + cw].rearrange("(t p) c -> p t c", p=128)

    def bview(t3, wdt, nt):
        if nt == 1:
            return t3[:, 0:wdt]
        return t3[:, 0:wdt].rearrange("p (t c) -> p t c", t=nt)

    osem = f"pb{u}_os"
    ctx.push_scope()
    buf = [ctx.sb(f"pb{u}_b{b}", [128, 1024], F32) for b in range(2)]
    bsem = [f"pb{u}_s{b}" for b in range(2)]
    buse = [None, None]
    qa = ctx.sb(f"pb{u}_qa", [128, 1024], BF16)
    qb2 = ctx.sb(f"pb{u}_qb", [128, 1024], BF16)
    qo = ctx.sb(f"pb{u}_qo", [128, 1024], BF16)
    qo_use = None
    qa_use = None
    for src2, dst, j2, (nr, ncx) in ((w["w1s"], outs["wq1"], 0, (5632, COLS_W)),
                                     (w["w2s"], outs["wq2"], 1, (5632, COLS_W)),
                                     (w["w3s"], outs["wq3"], 2, (ROWS_W3, 5632))):
        for r0, nt, c0, cw in groups(nr, ncx):
            b = (r0 // 128 + c0) % 2
            wdt = nt * cw
            tok = p.dma("act", bview(buf[b], wdt, nt), gview(src2, r0, nt, c0, cw),
                        waits=([buse[b]] if buse[b] else []), sem=bsem[b])
            t1 = p.op("dve", lambda e, b=b, wdt=wdt, j2=j2:
                      e.tensor_scalar(qa[:, 0:wdt], buf[b][:, 0:wdt], thr[:, j2:j2 + 1],
                                      None, ALU.is_ge),
                      waits=[tok, bt] + ([qa_use] if qa_use else []), inc="dve")
            t2 = p.op("dve", lambda e, b=b, wdt=wdt, j2=j2:
                      e.tensor_scalar(qb2[:, 0:wdt], buf[b][:, 0:wdt], thr[:, 3 + j2:4 + j2],
                                      None, ALU.is_le),
                      waits=[tok, bt, t1], inc="dve")
            buse[b] = t2
            t3 = p.op("dve", lambda e, wdt=wdt: e.tensor_tensor(qo[:, 0:wdt], qa[:, 0:wdt],
                                                                qb2[:, 0:wdt], ALU.subtract),
                      waits=[t2] + ([qo_use] if qo_use else []), inc="dve")
            qa_use = t3
            st = p.dma("po", gview(dst, r0, nt, c0, cw), bview(qo, wdt, nt), waits=[t3], sem=osem)
            qo_use = st
    ctx.pop_scope()
    p.region([(s9, v9) for (s9, v9) in p.all_toks() if not s9.startswith("cc")])
    ctx.push_scope()
    ib = [ctx.sb(f"pb{u}_i{b}", [128, 1024], I32) for b in range(2)]
    isem = [f"pb{u}_is{b}" for b in range(2)]
    iuse = [None, None]
    ibf = ctx.sb(f"pb{u}_ibf", [128, 1024], BF16)
    macc = [ctx.sb(f"pb{u}_m{b}", [128, 1024], BF16) for b in range(2)]
    mtile = ctx.sb(f"pb{u}_mt", [128, 1024], BF16)
    ibf_use = None
    m_use = None
    for src2, dst, (nr, ncx) in ((w["gis"], outs["gw"], (5504, COLS_W)),
                                 (w["uis"], outs["uw"], (5504, COLS_W)),
                                 (w["dis"], outs["dw"], (ROWS_W3, 5504))):
        for r0, nt, c0, cw in groups(nr, ncx):
            b = (r0 // 128 + c0) % 2
            wdt = nt * cw
            tok = p.dma("act", bview(ib[b], wdt, nt), gview(src2, r0, nt, c0, cw),
                        waits=([iuse[b]] if iuse[b] else []), sem=isem[b])
            cv = p.op("dve", lambda e, b=b, wdt=wdt: e.tensor_copy(ibf[:, 0:wdt], ib[b][:, 0:wdt]),
                      waits=[tok] + ([ibf_use] if ibf_use else []), inc="dve")
            iuse[b] = cv
            prev = None
            for j2 in range(16):
                am = j2 % 2
                if j2 == 0:
                    prev = p.op("dve", lambda e, wdt=wdt, j2=j2, am=am:
                                e.tensor_scalar(macc[am][:, 0:wdt], ibf[:, 0:wdt],
                                                float(j2), lutc[:, j2:j2 + 1],
                                                ALU.is_equal, ALU.mult),
                                waits=[cv, lc] + ([m_use] if m_use else []), inc="dve")
                else:
                    mt = p.op("dve", lambda e, wdt=wdt, j2=j2:
                              e.tensor_scalar(mtile[:, 0:wdt], ibf[:, 0:wdt],
                                              float(j2), lutc[:, j2:j2 + 1],
                                              ALU.is_equal, ALU.mult),
                              waits=[cv, lc, prev], inc="dve")
                    prev = p.op("dve", lambda e, wdt=wdt, am=am:
                                e.tensor_tensor(macc[am][:, 0:wdt], macc[1 - am][:, 0:wdt],
                                                mtile[:, 0:wdt], ALU.add),
                                waits=[mt], inc="dve")
            ibf_use = prev
            st = p.dma("po", gview(dst, r0, nt, c0, cw), bview(macc[1], wdt, nt), waits=[prev], sem=osem)
            m_use = st
    ctx.pop_scope()
    ctx.pop_scope()
    return osem


def build_prep_b(cfg):
    nc = bass.Bass("TRN2", target_bir_lowering=False, debug=False)
    w1s = nc.dram_tensor("w1s", [5632, COLS_W], F32, kind="ExternalInput").ap()
    w2s = nc.dram_tensor("w2s", [5632, COLS_W], F32, kind="ExternalInput").ap()
    w3s = nc.dram_tensor("w3s", [ROWS_W3, 5632], F32, kind="ExternalInput").ap()
    gparts = nc.dram_tensor("gparts", [8, 3], F32, kind="ExternalInput").ap()
    lut = nc.dram_tensor("lut", [1, 16], F32, kind="ExternalInput").ap()
    gis = nc.dram_tensor("gis", [5504, COLS_W], I32, kind="ExternalInput").ap()
    uis = nc.dram_tensor("uis", [5504, COLS_W], I32, kind="ExternalInput").ap()
    dis = nc.dram_tensor("dis", [ROWS_W3, 5504], I32, kind="ExternalInput").ap()
    wq1 = nc.dram_tensor("wq1", [5632, COLS_W], BF16, kind="ExternalOutput").ap()
    wq2 = nc.dram_tensor("wq2", [5632, COLS_W], BF16, kind="ExternalOutput").ap()
    wq3 = nc.dram_tensor("wq3", [ROWS_W3, 5632], BF16, kind="ExternalOutput").ap()
    gw = nc.dram_tensor("gw", [5504, COLS_W], BF16, kind="ExternalOutput").ap()
    uw = nc.dram_tensor("uw", [5504, COLS_W], BF16, kind="ExternalOutput").ap()
    dw = nc.dram_tensor("dw", [ROWS_W3, 5504], BF16, kind="ExternalOutput").ap()
    es = ExitStack()
    with es:
        ctx = Ctx(nc, es, cfg, {})
        emit_prep_b(ctx, {"w1s": w1s, "w2s": w2s, "w3s": w3s, "gis": gis, "uis": uis,
                          "dis": dis}, gparts, lut,
                    {"wq1": wq1, "wq2": wq2, "wq3": wq3, "gw": gw, "uw": uw, "dw": dw})
        ctx.p.op("sp", lambda e: e.nop(), waits=ctx.p.all_toks())
        ctx.p.emit(nc, es)
    return nc


def _pad_rows(a, rows):
    if a.shape[0] == rows:
        return np.ascontiguousarray(a)
    out = np.zeros((rows,) + a.shape[1:], a.dtype)
    out[:a.shape[0]] = a
    return out


def kernel(L0_attn_x, L0_attn_w_dq, L0_attn_g_q, L0_attn_w_uq, L0_attn_w_dkv,
           L0_attn_w_kr, L0_attn_g_kv, L0_attn_w_uk, L0_attn_w_uv, L0_attn_w_o,
           L0_ffn_w1, L0_ffn_w2, L0_ffn_w3, L1_attn_w_qkv, L1_attn_w_out,
           L1_ffn_lut, L1_ffn_gate_idx, L1_ffn_up_idx, L1_ffn_down_idx):
    c = CFG
    n = 8
    cores = list(range(n))
    consts = host_constants(c)
    x = np.ascontiguousarray(np.asarray(L0_attn_x, dtype=np.float32))

    c = CFG
    n = 8
    cores = list(range(n))
    consts = host_constants(c)
    x = np.ascontiguousarray(np.asarray(L0_attn_x, dtype=np.float32))
    cw, r3 = 2048 // n, 2048 // n
    wl0 = {"w_dq": np.asarray(L0_attn_w_dq), "g_q": np.asarray(L0_attn_g_q),
           "w_uq": np.asarray(L0_attn_w_uq), "w_dkv": np.asarray(L0_attn_w_dkv),
           "w_kr": np.asarray(L0_attn_w_kr), "g_kv": np.asarray(L0_attn_g_kv),
           "w_uk": np.asarray(L0_attn_w_uk), "w_uv": np.asarray(L0_attn_w_uv),
           "w_o": np.asarray(L0_attn_w_o)}
    lutr = np.ascontiguousarray(np.asarray(L1_ffn_lut, dtype=np.float32).reshape(1, 16))
    nc = build_fused(c, consts, use_collectives=True)
    in_maps = []
    for i in range(n):
        im = {"x_in": x[i], **wl0, "w_qkv": np.asarray(L1_attn_w_qkv),
              "w_out": np.asarray(L1_attn_w_out), **consts,
              "w1s": np.ascontiguousarray(np.asarray(L0_ffn_w1)[:, i * cw:(i + 1) * cw]),
              "w2s": np.ascontiguousarray(np.asarray(L0_ffn_w2)[:, i * cw:(i + 1) * cw]),
              "w3s": np.ascontiguousarray(np.asarray(L0_ffn_w3)[i * r3:(i + 1) * r3]),
              "lut": lutr,
              "gis": np.ascontiguousarray(np.asarray(L1_ffn_gate_idx)[:, i * cw:(i + 1) * cw]),
              "uis": np.ascontiguousarray(np.asarray(L1_ffn_up_idx)[:, i * cw:(i + 1) * cw]),
              "dis": np.ascontiguousarray(np.asarray(L1_ffn_down_idx)[i * r3:(i + 1) * r3])}
        in_maps.append(im)
    res = run_bass_kernel_spmd(nc, in_maps, core_ids=cores)
    return np.stack([np.asarray(res.results[i]["x_out"]) for i in range(n)], 0)


SLAB_ROWS = 5632 + 5632 + 5504 + 5504
SLAB_OFF = {"wq1": 0, "wq2": 5632, "gw": 11264, "uw": 16768}


def build_fused(cfg, consts, use_collectives=True):
    nc = bass.Bass("TRN2", target_bir_lowering=False, debug=False)
    c = cfg
    T, D = c.T, c.D
    cores = list(range(8))
    x_in = nc.dram_tensor("x_in", [T, D], F32, kind="ExternalInput").ap()
    x_out = nc.dram_tensor("x_out", [T, D], F32, kind="ExternalOutput").ap()
    wl0names = {
        "w_dq": [D, c.QL], "g_q": [c.QL], "w_uq": [c.QL, c.H * c.QKD],
        "w_dkv": [D, c.KVL], "w_kr": [D, c.ROPE], "g_kv": [c.KVL],
        "w_uk": [c.KVL, c.H * c.NOPE], "w_uv": [c.KVL, c.H * c.VD], "w_o": [c.H * c.VD, D],
    }
    wl0 = {n: nc.dram_tensor(n, s, F32, kind="ExternalInput").ap() for n, s in wl0names.items()}
    w_qkv = nc.dram_tensor("w_qkv", [3 * D, D], F32, kind="ExternalInput").ap()
    w_out = nc.dram_tensor("w_out", [D, D], F32, kind="ExternalInput").ap()
    const_aps = {}
    for name, arr in consts.items():
        dt = {np.dtype(np.float32): F32, np.dtype(BF): BF16, np.dtype(np.int32): I32}[arr.dtype]
        const_aps[name] = nc.dram_tensor(name, list(arr.shape), dt, kind="ExternalInput").ap()

    xln = nc.dram_tensor("s_xln", [T, D], BF16).ap()
    y_fm = nc.dram_tensor("s_yfm", [D, T], BF16).ap()
    y3p = nc.dram_tensor("s_y3p", [D, T], F32).ap()
    x1 = nc.dram_tensor("s_x1", [T, D], F32).ap()
    x2 = nc.dram_tensor("s_x2", [T, D], F32).ap()
    x3 = nc.dram_tensor("s_x3", [T, D], F32).ap()
    scr0 = {
        "l0_q": nc.dram_tensor("s_l0q", [c.QL, T], BF16).ap(),
        "l0_k": nc.dram_tensor("s_l0k", [c.H, T // 128, 128, 128], BF16).ap(),
        "l0_v": nc.dram_tensor("s_l0v", [T // 128, c.H, 128, 128], BF16).ap(),
    }
    wl0_bf = {n: nc.dram_tensor("s_bf_" + n, s, BF16).ap() for n, s in wl0names.items()
              if n not in ("g_q", "g_kv")}
    scr1 = {
        "l1_wqkv_bf": nc.dram_tensor("s_wqkv_bf", [3 * D, D], BF16).ap(),
        "l1_wout_bf": nc.dram_tensor("s_wout_bf", [D, D], BF16).ap(),
        "l1_q": nc.dram_tensor("s_l1q", [D, T], BF16).ap(),
        "l1_v": nc.dram_tensor("s_l1v", [T // 128, c.H, 128, 128], BF16).ap(),
    }

    if use_collectives:
        w1s = nc.dram_tensor("w1s", [5632, COLS_W], F32, kind="ExternalInput").ap()
        w2s = nc.dram_tensor("w2s", [5632, COLS_W], F32, kind="ExternalInput").ap()
        w3s = nc.dram_tensor("w3s", [ROWS_W3, 5632], F32, kind="ExternalInput").ap()
        lut = nc.dram_tensor("lut", [1, 16], F32, kind="ExternalInput").ap()
        gis = nc.dram_tensor("gis", [5504, COLS_W], I32, kind="ExternalInput").ap()
        uis = nc.dram_tensor("uis", [5504, COLS_W], I32, kind="ExternalInput").ap()
        dis = nc.dram_tensor("dis", [ROWS_W3, 5504], I32, kind="ExternalInput").ap()
        gpart = nc.dram_tensor("s_gpart", [1, 3], F32).ap()
        gsum = nc.dram_tensor("s_gsum", [1, 3], F32, addr_space="Shared").ap()
        slab_l = nc.dram_tensor("s_slab_l", [SLAB_ROWS, COLS_W], BF16).ap()
        wq3_l = nc.dram_tensor("s_wq3l", [ROWS_W3, 5632], BF16).ap()
        dw_l = nc.dram_tensor("s_dwl", [ROWS_W3, 5504], BF16).ap()
        slab_g = nc.dram_tensor("s_slab_g", [8 * SLAB_ROWS, COLS_W], BF16, addr_space="Shared").ap()
        wq3_g = nc.dram_tensor("s_wq3g", [D, 5632], BF16, addr_space="Shared").ap()
        dw_g = nc.dram_tensor("s_dwg", [D, 5504], BF16, addr_space="Shared").ap()
    else:
        slab_g = nc.dram_tensor("slab_g", [8 * SLAB_ROWS, COLS_W], BF16, kind="ExternalInput").ap()
        wq3_g = nc.dram_tensor("wq3_g", [D, 5632], BF16, kind="ExternalInput").ap()
        dw_g = nc.dram_tensor("dw_g", [D, 5504], BF16, kind="ExternalInput").ap()

    def slab_wf(key):
        off = SLAB_OFF[key]
        def wf(o0, d0, no, nd, off=off):
            r, cc = d0 // COLS_W, d0 % COLS_W
            base = r * SLAB_ROWS + off
            return slab_g[base + o0:base + o0 + no, cc:cc + nd]
        return wf

    es = ExitStack()
    with es:
        ctx = Ctx(nc, es, c, const_aps)
        ctx.sandwich_bufs()
        p = ctx.p
        for nm2, ap2 in wl0_bf.items():
            rows = wl0names[nm2][0]
            r0 = 0
            while r0 < rows:
                rr = min(512, rows - r0)
                p.dma("po", ap2[r0:r0 + rr, :], wl0[nm2][r0:r0 + rr, :], sem="l0cast")
                r0 += rr
        l0cast_tok = p.tok("l0cast")
        ffn_ready = []
        if use_collectives:
            ctx.push_scope()
            st_a = emit_prep_a(ctx, {"w1s": w1s, "w2s": w2s, "w3s": w3s}, gpart)
            cc0 = p.op("po", lambda e: e.collective_compute(
                "AllReduce", ALU.add, replica_groups=[cores], ins=[gpart], outs=[gsum]),
                waits=[st_a], inc="cc0")
            osem = emit_prep_b(
                ctx, {"w1s": w1s, "w2s": w2s, "w3s": w3s, "gis": gis, "uis": uis, "dis": dis},
                None, lut,
                {"wq1": slab_l[SLAB_OFF["wq1"]:SLAB_OFF["wq1"] + 5632, :],
                 "wq2": slab_l[SLAB_OFF["wq2"]:SLAB_OFF["wq2"] + 5632, :],
                 "gw": slab_l[SLAB_OFF["gw"]:SLAB_OFF["gw"] + 5504, :],
                 "uw": slab_l[SLAB_OFF["uw"]:SLAB_OFF["uw"] + 5504, :],
                 "wq3": wq3_l, "dw": dw_l},
                gamma_pre=gsum, gamma_wait=[cc0])
            prep_done = p.tok(osem)
            cc1 = p.op("po", lambda e: e.collective_compute(
                "AllGather", ALU.bypass, replica_groups=[cores], ins=[slab_l], outs=[slab_g]),
                waits=[prep_done], inc="cc1")
            cc2 = p.op("po", lambda e: e.collective_compute(
                "AllGather", ALU.bypass, replica_groups=[cores], ins=[wq3_l], outs=[wq3_g]),
                waits=[prep_done], inc="cc2")
            cc3 = p.op("po", lambda e: e.collective_compute(
                "AllGather", ALU.bypass, replica_groups=[cores], ins=[dw_l], outs=[dw_g]),
                waits=[prep_done], inc="cc3")
            ffn_ready = [cc1, cc2, cc3]
        state = {}
        s0 = emit_sandwich(ctx, x_in, None, None, xln, True, state)
        ctx.push_scope()
        yt = emit_l0_attn(ctx, xln, s0, wl0, y_fm, scr0, wbf=wl0_bf, cast_ready=[l0cast_tok])
        ctx.pop_scope()
        if use_collectives:
            ctx.pop_scope()
        B = [(s, v) for (s, v) in p.all_toks() if not s.startswith("cc")]
        p.region(B)
        s1 = emit_sandwich(ctx, x_in, y_fm, x1, xln, False, state, y_ready=yt)
        yg = [p.tok("sw_ys0"), p.tok("sw_ys1")]
        p.region(B)
        ctx.push_scope()
        yt = emit_gated_ffn(ctx, xln, s1, slab_wf("wq1"), slab_wf("wq2"),
                            lambda d0, h0, nd, nh: wq3_g[d0:d0 + nd, h0:h0 + nh],
                            AF.Sigmoid, 5632, y_fm, y3p, {}, w_ready=ffn_ready, y_guard=yg)
        ctx.pop_scope()
        B = [(s, v) for (s, v) in p.all_toks() if not s.startswith("cc")]
        p.region(B)
        s2 = emit_sandwich(ctx, x1, y_fm, x2, xln, False, state, y_ready=yt)
        yg = [p.tok("sw_ys0"), p.tok("sw_ys1")]
        p.region(B)
        ctx.push_scope()
        yt = emit_l1_attn(ctx, xln, s2, w_qkv, w_out, y_fm, scr1, y_guard=yg)
        ctx.pop_scope()
        B = [(s, v) for (s, v) in p.all_toks() if not s.startswith("cc")]
        p.region(B)
        s3 = emit_sandwich(ctx, x2, y_fm, x3, xln, False, state, y_ready=yt)
        yg = [p.tok("sw_ys0"), p.tok("sw_ys1")]
        p.region(B)
        ctx.push_scope()
        yt = emit_gated_ffn(ctx, xln, s3, slab_wf("gw"), slab_wf("uw"),
                            lambda d0, h0, nd, nh: dw_g[d0:d0 + nd, h0:h0 + nh],
                            AF.Silu, 5504, y_fm, y3p, {}, w_ready=ffn_ready, y_guard=yg)
        ctx.pop_scope()
        B = [(s, v) for (s, v) in p.all_toks() if not s.startswith("cc")]
        p.region(B)
        emit_sandwich(ctx, x3, y_fm, x_out, xln, False, state, y_ready=yt, write_xln=False)
        p.op("sp", lambda e: e.nop(), waits=p.all_toks())
        p.emit(nc, es)
    return nc


# version 21 (final)
# speedup vs baseline: 1.5146x; 1.0146x over previous
import numpy as np
import ml_dtypes
from contextlib import ExitStack
import concourse.bass as bass
import concourse.mybir as mybir
from concourse.bass_utils import run_bass_kernel_spmd

F32 = mybir.dt.float32
BF16 = mybir.dt.bfloat16
I32 = mybir.dt.int32
AF = mybir.ActivationFunctionType
ALU = mybir.AluOpType
AX = mybir.AxisListType
BF = ml_dtypes.bfloat16


class CFG:
    T = 2048
    D = 2048
    H = 16
    HD = 128
    QL = 768; KVL = 512; NOPE = 128; ROPE = 64; VD = 128; QKD = 192
    F0 = 5632; F1 = 5504; LUT = 16
    EPS_LN = 1e-5; EPS_RMS = 1e-6; EPS_BIT = 1e-8; EPS_SM = 1e-8
    THETA = 10000.0


ENG_ATTR = {"pe": "tensor", "act": "scalar", "dve": "vector", "po": "gpsimd", "sp": "sync"}
ENGINES = ("pe", "act", "dve", "po", "sp")


class Prog:

    def __init__(self):
        self.ops = {e: [] for e in ENGINES}
        self.counts = {}
        self.sem_names = []
        self.pending_barrier = {}
        self.region_inject = {}

    def barrier(self, exclude=()):
        toks = [(s, v) for (s, v) in self.all_toks() if s not in exclude]
        self.pending_barrier = {e: list(toks) for e in ENGINES}

    def region(self, toks):
        self.region_inject = {e: list(toks) for e in ENGINES}

    def _reg(self, sem):
        if sem not in self.counts:
            self.counts[sem] = 0
            self.sem_names.append(sem)

    def tok(self, sem):
        return (sem, self.counts.get(sem, 0))

    def op(self, eng, fn, waits=(), inc=None, n=1):
        if eng in self.pending_barrier:
            waits = list(waits) + self.pending_barrier.pop(eng)
        if eng in self.region_inject:
            waits = list(waits) + self.region_inject.pop(eng)
        waits = tuple((s, v) for (s, v) in waits if s is not None and v > 0)
        for s, _ in waits:
            self._reg(s)
        tok = None
        if inc:
            self._reg(inc)
            self.counts[inc] += n
            tok = (inc, self.counts[inc])
        self.ops[eng].append((fn, waits, inc, n))
        return tok

    def dma(self, q, out, in_, waits=(), sem=None, transpose=False, slow_ok=False):
        assert q in ("sp", "act", "po")
        sem = sem or ("q_" + q)
        return self.op(
            q,
            (lambda e, o=out, i=in_, t=transpose, s=slow_ok:
             e.dma_start(out=o, in_=i, transpose=t, allow_slow_non_contiguous=s)),
            waits, sem, 16,
        )

    def all_toks(self):
        return [(s, self.counts[s]) for s in self.sem_names if self.counts[s] > 0]

    def emit(self, nc, es):
        sems = {name: es.enter_context(nc.semaphore(name)) for name in self.sem_names}
        block = es.enter_context(nc.Block())
        for eng in ENGINES:
            ops = self.ops[eng]
            if not ops:
                continue

            def make(ops=ops):
                def _(e):
                    waited = {}
                    for fn, waits, inc, n in ops:
                        for s, v in waits:
                            if waited.get(s, 0) < v:
                                e.wait_ge(sems[s], v)
                                waited[s] = v
                        inst = fn(e)
                        if inc:
                            inst.then_inc(sems[inc], n)
                return _

            getattr(block, ENG_ATTR[eng])(make())
        return sems


def host_constants(c: CFG):
    T = c.T
    consts = {}
    consts["c_id_bf"] = np.eye(128, dtype=BF)
    consts["c_ones_col_bf"] = np.ones((128, 1), dtype=BF)
    consts["c_ones_row_bf"] = np.ones((1, 128), dtype=BF)
    consts["c_ones_row_f32"] = np.ones((1, 128), dtype=np.float32)
    kk = np.arange(128)[:, None]
    jj = np.arange(896)[None, :]
    consts["c_trilw"] = (kk <= jj - 384).astype(BF)
    d = c.ROPE
    inv_freq = 1.0 / c.THETA ** (np.arange(0, d, 2, dtype=np.float32) / d)
    t = np.arange(T, dtype=np.float32)
    fr = np.outer(inv_freq, t).astype(np.float32)
    cos, sin = np.cos(fr).astype(np.float32), np.sin(fr).astype(np.float32)
    scale = np.float32(c.QKD ** (-0.5))
    A = np.concatenate([cos, cos], 0)
    Bm = np.concatenate([-sin, sin], 0)
    consts["c_rope_a_k"] = A.astype(BF)
    consts["c_rope_b_k"] = Bm.astype(BF)
    return consts


class Ctx:
    def __init__(self, nc, es, cfg, const_aps):
        self.nc = nc
        self.es = es
        self.c = cfg
        self.p = Prog()
        self.cap = const_aps
        self.psum = [es.enter_context(nc.psum_tensor(f"pb{i}", [128, 512], F32)) for i in range(8)]
        self.bank_free = [[] for _ in range(8)]
        self._uid = 0
        self.const_sb = {}
        self.scopes = []
        self.pool = {}
        self._swb = None
        self.sw_chain = []

    def sandwich_bufs(self):
        if self._swb is None:
            assert not self.scopes, "allocate sandwich bufs before any scope"
            D = self.c.D
            self._swb = dict(
                xt=[self.sb(f"swb_x{b}", [128, D], F32) for b in range(2)],
                ln=[self.sb(f"swb_l", [128, D], F32)] * 2,
                xl=[self.sb(f"swb_xl", [128, D], BF16)] * 2,
                yt=[self.sb(f"swb_y", [128, D], BF16)] * 2,

                st=self.sb("swb_st", [128, 12], F32),
            )
        return self._swb

    def uid(self):
        self._uid += 1
        return self._uid

    def sb(self, name, shape, dt):
        es = self.scopes[-1] if self.scopes else self.es
        side = "left" if self.scopes else "right"
        return es.enter_context(self.nc.sbuf_tensor(name, list(shape), dt, side=side))

    def sb_outer(self, name, shape, dt):
        return self.es.enter_context(self.nc.sbuf_tensor(name, list(shape), dt, side="right"))

    def push_scope(self):
        self.scopes.append(ExitStack())

    def pop_scope(self):
        self.scopes.pop().close()

    def strip_set(self, key, n=None):
        if n is None:
            n = self.c.D // 128
        u = self.uid()
        return [self.sb(f"pool_{key}{u}_{i}", [128, self.c.T], BF16) for i in range(n)]

    def bcast_f32(self, row_ap, bank, waits):
        p = self.p
        if not hasattr(self, "_bc") or self._bc is None:
            self._bc = (self.sb_outer("bc_hi", [1, 512], BF16),
                        self.sb_outer("bc_lo", [1, 512], BF16))
            self._bc_chain = None
        hi, lo = self._bc
        ones_row, or_tok = self.load_const("c_ones_row_bf")
        w = list(waits) + ([self._bc_chain] if self._bc_chain else [])
        t1 = p.op("dve", lambda e: e.tensor_copy(hi[0:1, :], row_ap), waits=w, inc="dve")
        t2 = p.op("dve", lambda e: e.tensor_tensor(lo[0:1, :], row_ap, hi[0:1, :], ALU.subtract),
                  waits=[t1], inc="dve")
        m1 = p.op("pe", lambda e: e.matmul(self.psum[bank][:], ones_row[:], hi[0:1, :],
                                           start=True, stop=False),
                  waits=[t1, or_tok] + (self.bank_free[bank] or []), inc="pe")
        m2 = p.op("pe", lambda e: e.matmul(self.psum[bank][:], ones_row[:], lo[0:1, :],
                                           start=False, stop=True),
                  waits=[t2], inc="pe")
        self._bc_chain = m2
        return m2

    def const_col(self, val):
        key = float(val)
        if key not in self.const_sb:
            t = self.sb_outer(f"ccol_{self._uid}_{len(self.const_sb)}", [128, 1], F32)
            self._uid += 1
            tok = self.p.op("dve", lambda e, t=t, v=key: e.memset(t[:], v), inc="dve")
            self.const_sb[key] = (t, tok)
        return self.const_sb[key]

    def load_const(self, name, scoped=False):
        if not scoped and name in self.const_sb:
            return self.const_sb[name]
        ap = self.cap[name]
        shape = list(ap.tensor.shape)
        t = (self.sb if scoped else self.sb_outer)(f"cs{self.uid()}_" + name, shape, ap.dtype)
        tok = self.p.dma("act", t[:], ap, sem=f"qc{self._uid}_" + name)
        if not scoped:
            self.const_sb[name] = (t, tok)
        return (t, tok)


def emit_sandwich(ctx, x_in, y_fm, x_out, xln_out, first, state, y_ready=(), write_xln=True):
    c, p = ctx.c, ctx.p
    T, D = c.T, c.D
    NT = T // 128
    u = ctx.uid()
    swb = ctx.sandwich_bufs()
    xt, ln, xl, yt, st = swb["xt"], swb["ln"], swb["xl"], swb["yt"], swb["st"]
    prev_chain = list(ctx.sw_chain)
    _seen = set()
    _real_op = p.op

    def _op(eng, fn, waits=(), inc=None, n=1):
        if eng not in _seen:
            _seen.add(eng)
            waits = list(waits) + prev_chain
        return _real_op(eng, fn, waits, inc, n)
    p = type("POpWrap", (), {"op": staticmethod(_op), "dma": ctx.p.dma, "tok": ctx.p.tok,
                             "all_toks": ctx.p.all_toks})()
    if first:
        state["negmu"] = [ctx.sb_outer(f"sw{u}_nm{i}", [128, 1], F32) for i in range(NT)]

    xs = [f"sw_xs{b}" for b in range(2)]
    ys = [f"sw_ys{b}" for b in range(2)]
    oss = [f"sw_os{b}" for b in range(2)]
    lss = [f"sw_ls{b}" for b in range(2)]
    last_use_x = [None, None]
    last_use_y = [None, None]
    last_store_o = [None, None]
    last_store_l = [None, None]
    sq_tok = None
    st_tok = None
    last_lt = None
    for i in range(NT):
        b = i % 2
        row = slice(i * 128, (i + 1) * 128)
        xtok = p.dma("sp", xt[b][:], x_in[row, :],
                     waits=(prev_chain if i < 2 else []) + ([last_use_x[b]] if last_use_x[b] else []),
                     sem=xs[b])
        if y_fm is not None:
            ytok = p.dma("sp", yt[b][:], y_fm[:, row],
                         waits=list(y_ready) + [t for t in last_use_y if t],
                         sem=ys[b], transpose=True)
            w_st = [st_tok] if st_tok else []
            t1 = p.op("dve", lambda e, b=b: e.tensor_reduce(st[:, 0:1], yt[b][:], AX.X, ALU.add, negate=True),
                      waits=[ytok] + w_st, inc="dve")
            t2 = p.op("act", lambda e, b=b: e.activation(xl[b][:], yt[b][:], AF.Square,
                                                         accum_out=st[:, 2:3]),
                      waits=[ytok, t1] + ([sq_tok] if sq_tok else []) +
                            [t for t in last_store_l if t], inc="act")
            t4 = p.op("dve", lambda e: e.tensor_scalar(st[:, 0:1], st[:, 0:1], 1.0 / D, None, ALU.mult),
                      waits=[t1], inc="dve")
            t5 = p.op("dve", lambda e: e.tensor_tensor(st[:, 4:5], st[:, 0:1], st[:, 0:1], ALU.mult),
                      waits=[t4], inc="dve")
            t6 = p.op("dve", lambda e: e.tensor_scalar(st[:, 3:4], st[:, 2:3], 1.0 / D, None, ALU.mult),
                      waits=[t2], inc="dve")
            t7 = p.op("dve", lambda e: e.tensor_tensor(st[:, 3:4], st[:, 3:4], st[:, 4:5], ALU.subtract),
                      waits=[t5, t6], inc="dve")
            epsc, epsc_tok = ctx.const_col(c.EPS_LN)
            t8 = p.op("act", lambda e, epsc=epsc: e.activation(st[:, 1:2], st[:, 3:4], AF.Ln, bias=epsc[:]),
                      waits=[t7, epsc_tok], inc="act")
            t9 = p.op("act", lambda e: e.activation(st[:, 1:2], st[:, 1:2], AF.Exp, scale=-0.5),
                      waits=[t8], inc="act")
            prev_store = last_store_o[0] if (last_store_o[0] and (not last_store_o[1] or last_store_o[0][1] > last_store_o[1][1])) else last_store_o[1]
            t10 = p.op("dve", lambda e, b=b: e.tensor_scalar(ln[b][:], yt[b][:], st[:, 0:1], st[:, 1:2],
                                                             ALU.add, ALU.mult),
                       waits=[t9] + ([prev_store] if prev_store else []), inc="dve")
            last_use_y[b] = t10
            t11 = p.op("dve", lambda e, b=b: e.tensor_tensor(ln[b][:], ln[b][:], xt[b][:], ALU.add),
                       waits=[t10, xtok], inc="dve")
            last_use_x[b] = t11
            sot = p.dma("sp", x_out[row, :], ln[b][:], waits=[t11], sem=oss[b])
            last_store_o[b] = sot
            xn, xn_tok = ln[b], t11
        else:
            xn, xn_tok = xt[b], xtok

        if not write_xln:
            continue
        if first:
            nm = state["negmu"][i]
            tm = p.op("dve", lambda e, xn=xn, nm=nm: e.tensor_reduce(nm[:], xn[:], AX.X, ALU.add, negate=True),
                      waits=[xn_tok], inc="dve")
            tm = p.op("dve", lambda e, nm=nm: e.tensor_scalar(nm[:], nm[:], 1.0 / D, None, ALU.mult),
                      waits=[tm], inc="dve")
        else:
            nm = state["negmu"][i]
            tm = state["negmu_tok"]
        s2 = p.op("act", lambda e, xn=xn, b=b: e.activation(xl[b][:], xn[:], AF.Square, accum_out=st[:, 7:8]),
                  waits=([xn_tok, t2] if y_fm is not None else [xn_tok]) +
                        [t for t in last_store_l if t] + ([st_tok] if st_tok else []),
                  inc="act")
        sq_tok = s2
        s3 = p.op("dve", lambda e, nm=nm: e.tensor_tensor(st[:, 8:9], nm[:], nm[:], ALU.mult),
                  waits=[tm], inc="dve")
        s4 = p.op("dve", lambda e: e.tensor_scalar(st[:, 9:10], st[:, 7:8], 1.0 / D, None, ALU.mult),
                  waits=[s2], inc="dve")
        s5 = p.op("dve", lambda e: e.tensor_tensor(st[:, 9:10], st[:, 9:10], st[:, 8:9], ALU.subtract),
                  waits=[s3, s4], inc="dve")
        epsc2, epsc2_tok = ctx.const_col(c.EPS_LN)
        s6 = p.op("act", lambda e, epsc2=epsc2: e.activation(st[:, 10:11], st[:, 9:10], AF.Ln, bias=epsc2[:]),
                  waits=[s5, epsc2_tok], inc="act")
        s7 = p.op("act", lambda e: e.activation(st[:, 10:11], st[:, 10:11], AF.Exp, scale=-0.5),
                  waits=[s6], inc="act")
        s8 = p.op("dve", lambda e, xn=xn, nm=nm, b=b: e.tensor_scalar(xl[b][:], xn[:], nm[:], st[:, 10:11],
                                                                      ALU.add, ALU.mult),
                  waits=[s7] + [t for t in last_store_l if t], inc="dve")
        st_tok = s8
        if y_fm is None:
            last_use_x[b] = s8
        lt = p.dma("sp", xln_out[row, :], xl[b][:], waits=[s8], sem=lss[b])
        last_store_l[b] = lt
        last_lt = lt
    if first:
        state["negmu_tok"] = p.tok("dve")
    ctx.sw_chain = [p.tok("dve"), p.tok("act"), p.tok("po")] +         [p.tok(s) for s in oss + lss + ys]
    if not write_xln:
        return [p.tok(oss[0]), p.tok(oss[1])]
    return [p.tok(lss[0]), p.tok(lss[1])]


def emit_load_xln_fm(ctx, xln_dram, ready_toks, dst_strips, reuse_waits=()):
    c, p = ctx.c, ctx.p
    T, D = c.T, c.D
    NS = D // 128
    u = ctx.uid()
    sem = f"xlnfm{u}"
    tok = None
    for i in range(NS):
        tok = p.dma("sp", dst_strips[i][:], xln_dram[:, i * 128:(i + 1) * 128],
                    waits=list(ready_toks) + list(reuse_waits), sem=sem, transpose=True)
    return (sem, tok[1])


def emit_l1_attn(ctx, xln_dram, xln_ready, w_qkv, w_out, y_fm, scr, y_guard=()):
    c, p = ctx.c, ctx.p
    T, D, H = c.T, c.D, c.H
    NS = D // 128
    NQ = T // 512
    NKT = T // 128
    u = ctx.uid()
    scale = float(c.HD ** (-0.5))
    SAMP_STEP = 8
    NSAMP = T // SAMP_STEP
    bf = ctx.bank_free

    idn, idn_tok = ctx.load_const("c_id_bf")
    ones_col, ocol_tok = ctx.load_const("c_ones_col_bf")
    ones_row, orow_tok = ctx.load_const("c_ones_row_bf")

    wq_bf, wo_bf = scr["l1_wqkv_bf"], scr["l1_wout_bf"]
    q_scr, v_scr = scr["l1_q"], scr["l1_v"]

    csem = f"l1c{u}"
    for j in range(12):
        p.dma("po", wq_bf[j * 512:(j + 1) * 512, :], w_qkv[j * 512:(j + 1) * 512, :], sem=csem)
    for j in range(4):
        ctok = p.dma("po", wo_bf[j * 512:(j + 1) * 512, :], w_out[j * 512:(j + 1) * 512, :], sem=csem)
    cast_all = (csem, ctok[1])

    xfm = ctx.strip_set("A")
    xfm_tok = emit_load_xln_fm(ctx, xln_dram, xln_ready, xfm,
                               reuse_waits=scr.pop("poolA_free", []))
    k_fm = ctx.strip_set("B")
    kfm_free = scr.pop("poolB_free", [])

    OC = 256
    wall = [ctx.sb(f"l1wa{u}_{b}", [128, NS * OC], BF16) for b in range(2)]
    wsem = [f"l1ws{u}_{b}" for b in range(2)]
    wall_use = [None, None]
    evb = [ctx.sb(f"l1ev{u}_{b}", [128, 512], BF16) for b in range(4)]
    ev_free = [None] * 4
    qs_sem = [f"l1qs{u}_{b}" for b in range(4)]
    vs_sem = [f"l1vs{u}_{b}" for b in range(4)]

    n_oc = 3 * D // OC
    for oc in range(n_oc):
        b = oc % 2
        o0 = oc * OC
        wtok = None
        for i in range(NS):
            wtok = p.dma("sp", wall[b][:, i * OC:(i + 1) * OC],
                         wq_bf[o0:o0 + OC, i * 128:(i + 1) * 128],
                         waits=[cast_all] + ([wall_use[b]] if wall_use[b] else []),
                         sem=wsem[b], transpose=True)
        wready = (wsem[b], wtok[1])
        if oc < 16:
            part_q = oc < 8
            for ho in range(2):
                h = (oc % 8) * 2 + ho
                for tc in range(NQ):
                    bi = (ho * NQ + tc) % 4
                    bank = ctx.psum[bi]
                    mm = None
                    for i in range(NS):
                        mm = p.op("pe", lambda e, bank=bank, b=b, i=i, ho=ho, tc=tc:
                                  e.matmul(bank[:], wall[b][:, i * OC + ho * 128:i * OC + (ho + 1) * 128],
                                           xfm[i][:, tc * 512:(tc + 1) * 512],
                                           start=(i == 0), stop=(i == NS - 1)),
                                  waits=[wready, xfm_tok] + (bf[bi] if i == 0 else []),
                                  inc="pe")
                    wall_use[b] = mm
                    if part_q:
                        es_ = bi
                        et = p.op("act", lambda e, bank=bank, es_=es_:
                                  e.activation(evb[es_][:], bank[:], AF.Copy, scale=scale),
                                  waits=[mm] + ([ev_free[es_]] if ev_free[es_] else []), inc="act")
                        bf[bi] = [et]
                        sx = p.dma("sp", q_scr[h * 128:(h + 1) * 128, tc * 512:(tc + 1) * 512],
                                   evb[es_][:], waits=[et], sem=qs_sem[es_])
                        ev_free[es_] = sx
                    else:
                        et = p.op("act", lambda e, bank=bank, h=h, tc=tc:
                                  e.activation(k_fm[h][:, tc * 512:(tc + 1) * 512], bank[:], AF.Copy),
                                  waits=[mm] + list(kfm_free), inc="act")
                        kfm_free = []
                        bf[bi] = [et]
        else:
            for tk in range(NKT):
                bi = tk % 4
                bank = ctx.psum[bi]
                mm = None
                for i in range(NS):
                    mm = p.op("pe", lambda e, bank=bank, b=b, i=i, tk=tk:
                              e.matmul(bank[:, 0:OC], xfm[i][:, tk * 128:(tk + 1) * 128],
                                       wall[b][:, i * OC:(i + 1) * OC],
                                       start=(i == 0), stop=(i == NS - 1)),
                              waits=[wready, xfm_tok] + (bf[bi] if i == 0 else []),
                              inc="pe")
                wall_use[b] = mm
                es_ = tk % 4
                et = p.op("act", lambda e, bank=bank, es_=es_:
                          e.activation(evb[es_][:, 0:OC], bank[:, 0:OC], AF.Copy),
                          waits=[mm] + ([ev_free[es_]] if ev_free[es_] else []), inc="act")
                bf[bi] = [et]
                sx = None
                h2 = (oc - 16) * 2
                for ho in range(2):
                    sx = p.dma("sp", v_scr[tk, h2 + ho], evb[es_][:, ho * 128:(ho + 1) * 128],
                               waits=[et], sem=vs_sem[es_])
                ev_free[es_] = sx
    qkv_done_pe = p.tok("pe")
    scat_done = [p.tok(s) for s in qs_sem] + [p.tok(s) for s in vs_sem]

    o_all = xfm
    pt = [ctx.sb(f"l1pt{u}_{i}", [128, 512], BF16) for i in range(3)]
    pt_use = [None] * 3
    qb_buf = [ctx.sb(f"l1qb{u}_{b}", [128, 512], BF16) for b in range(2)]
    qb_sem = [f"l1qbs{u}_{b}" for b in range(2)]
    qb_use = [None, None]
    vb_buf = [ctx.sb(f"l1vb{u}_{b}", [128, 128], BF16) for b in range(3)]
    vb_sem = [f"l1vbs{u}_{b}" for b in range(3)]
    vb_use = [None, None, None]
    nmcol = ctx.sb(f"l1nm{u}", [128, 4], F32)
    nmcol_bf = ctx.sb(f"l1nmb{u}", [128, 4], BF16)
    aug = ctx.sb(f"l1aug{u}", [1, 512], BF16)
    den = ctx.sb(f"l1den{u}", [1, 512], F32)
    rbc = ctx.sb(f"l1rbc{u}", [128, 512], F32)
    stb = [0, 1, 6]
    nm_use = None
    aug_use = None
    for h in range(H):
        for qb in range(NQ):
            bslot = (h * NQ + qb) % 2
            qtok = p.dma("sp", qb_buf[bslot][:], q_scr[h * 128:(h + 1) * 128, qb * 512:(qb + 1) * 512],
                         waits=scat_done + ([qb_use[bslot]] if qb_use[bslot] else []),
                         sem=qb_sem[bslot])
            red_toks = []
            for j in range(4):
                bi = stb[j % 2]
                bank = ctx.psum[bi]
                mm = p.op("pe", lambda e, bank=bank, bslot=bslot, j=j, h=h:
                          e.matmul(bank[:, 0:NSAMP], qb_buf[bslot][:, j * 128:(j + 1) * 128],
                                   k_fm[h][:, 0:T:SAMP_STEP], start=True, stop=True),
                          waits=[qtok] + bf[bi], inc="pe")
                rd = p.op("dve", lambda e, bank=bank, j=j:
                          e.tensor_reduce(nmcol[:, j:j + 1], bank[:, 0:NSAMP], AX.X, ALU.max, negate=True),
                          waits=[mm] + ([nm_use] if (j == 0 and nm_use) else []), inc="dve")
                bf[bi] = [rd]
                red_toks.append(rd)
            cv = p.op("dve", lambda e: e.tensor_copy(nmcol_bf[:], nmcol[:]),
                      waits=red_toks, inc="dve")
            as_tok = None
            for j in range(4):
                as_tok = p.op("pe", lambda e, j=j: e.matmul(ctx.psum[2][0:1, j * 128:(j + 1) * 128],
                                                            nmcol_bf[:, j:j + 1], idn[:],
                                                            start=True, stop=True),
                              waits=[cv, idn_tok] + (bf[2] if j == 0 else []),
                              inc="pe")
            nm_use = as_tok
            ag = p.op("act", lambda e: e.activation(aug[:], ctx.psum[2][0:1, :], AF.Copy),
                      waits=[as_tok] + ([aug_use] if aug_use else []), inc="act")
            bf[2] = [ag]
            den_mm = None
            o_mm = None
            for kt in range(NKT):
                bi = stb[kt % 3]
                bank = ctx.psum[bi]
                mm1 = p.op("pe", lambda e, bank=bank, h=h, kt=kt, bslot=bslot:
                           e.matmul(bank[:], k_fm[h][:, kt * 128:(kt + 1) * 128], qb_buf[bslot][:],
                                    start=True, stop=False),
                           waits=[qtok] + bf[bi], inc="pe")
                mm2 = p.op("pe", lambda e, bank=bank:
                           e.matmul(bank[:], ones_row[:], aug[:], start=False, stop=True),
                           waits=[ag, orow_tok], inc="pe")
                ps_ = kt % 3
                ex = p.op("act", lambda e, bank=bank, ps_=ps_:
                          e.activation(pt[ps_][:], bank[:], AF.Exp),
                          waits=[mm2] + ([pt_use[ps_]] if pt_use[ps_] else []), inc="act")
                bf[bi] = [ex]
                den_mm = p.op("pe", lambda e, kt=kt, ps_=ps_:
                              e.matmul(ctx.psum[3][0:1, :], ones_col[:], pt[ps_][:],
                                       start=(kt == 0), stop=(kt == NKT - 1)),
                              waits=[ex, ocol_tok] + (bf[3] if kt == 0 else []),
                              inc="pe")
                vslot = kt % 3
                vtok = p.dma("sp", vb_buf[vslot][:], v_scr[kt, h],
                             waits=scat_done + ([vb_use[vslot]] if vb_use[vslot] else []),
                             sem=vb_sem[vslot])
                o_mm = p.op("pe", lambda e, kt=kt, vslot=vslot, ps_=ps_:
                            e.matmul(ctx.psum[4][:], vb_buf[vslot][:], pt[ps_][:],
                                     start=(kt == 0), stop=(kt == NKT - 1)),
                            waits=[ex, vtok] + (bf[4] if kt == 0 else []),
                            inc="pe")
                vb_use[vslot] = o_mm
                pt_use[ps_] = o_mm
            d1 = p.op("act", lambda e: e.activation(den[:], ctx.psum[3][0:1, :], AF.Copy),
                      waits=[den_mm], inc="act")
            bf[3] = [d1]
            d2 = p.op("dve", lambda e: e.tensor_scalar(den[:], den[:], float(c.EPS_SM), None, ALU.add),
                      waits=[d1], inc="dve")
            d3 = p.op("dve", lambda e: e.reciprocal(den[:], den[:]), waits=[d2], inc="dve")
            bcm = ctx.bcast_f32(den[0:1, :], 5, [d3])
            rb = p.op("act", lambda e: e.activation(rbc[:], ctx.psum[5][:], AF.Copy),
                      waits=[bcm], inc="act")
            bf[5] = [rb]
            oe = p.op("dve", lambda e, h=h, qb=qb:
                      e.tensor_tensor(o_all[h][:, qb * 512:(qb + 1) * 512], ctx.psum[4][:], rbc[:], ALU.mult),
                      waits=[o_mm, rb, qkv_done_pe], inc="dve")
            bf[4] = [oe]
            aug_use = den_mm
    attn_done_dve = p.tok("dve")

    yev = [ctx.sb(f"l1ye{u}_{b}", [128, 512], BF16) for b in range(2)]
    yev_free = [None, None]
    ys_sem = [f"l1ys{u}_{b}" for b in range(2)]
    for dc in range(8):
        b = dc % 2
        wtok = None
        for hh in range(H):
            wtok = p.dma("sp", wall[b][:, hh * 256:(hh + 1) * 256],
                         wo_bf[dc * 256:(dc + 1) * 256, hh * 128:(hh + 1) * 128],
                         waits=[cast_all] + ([wall_use[b]] if wall_use[b] else []),
                         sem=wsem[b], transpose=True)
        wready = (wsem[b], wtok[1])
        for ds in range(2):
            for qb in range(NQ):
                bi = (ds * NQ + qb) % 4
                bank = ctx.psum[bi]
                mm = None
                for hh in range(H):
                    mm = p.op("pe", lambda e, bank=bank, b=b, hh=hh, ds=ds, qb=qb:
                              e.matmul(bank[:], wall[b][:, hh * 256 + ds * 128:hh * 256 + (ds + 1) * 128],
                                       o_all[hh][:, qb * 512:(qb + 1) * 512],
                                       start=(hh == 0), stop=(hh == H - 1)),
                              waits=[wready, attn_done_dve] + (bf[bi] if hh == 0 else []),
                              inc="pe")
                wall_use[b] = mm
                es_ = bi % 2
                et = p.op("act", lambda e, bank=bank, es_=es_:
                          e.activation(yev[es_][:], bank[:], AF.Copy),
                          waits=[mm] + ([yev_free[es_]] if yev_free[es_] else []), inc="act")
                bf[bi] = [et]
                sx = p.dma("sp", y_fm[(dc * 2 + ds) * 128:(dc * 2 + ds + 1) * 128, qb * 512:(qb + 1) * 512],
                           yev[es_][:], waits=[et] + list(y_guard), sem=ys_sem[es_])
                yev_free[es_] = sx
    scr["poolA_free"] = [p.tok("pe")]
    scr["poolB_free"] = [p.tok("pe")]
    return [p.tok(s) for s in ys_sem]


def build_l1_attn_launch(cfg, consts):
    nc = bass.Bass("TRN2", target_bir_lowering=False, debug=False)
    c = cfg
    T, D = c.T, c.D
    x_in = nc.dram_tensor("x_in", [T, D], F32, kind="ExternalInput").ap()
    w_qkv = nc.dram_tensor("w_qkv", [3 * D, D], F32, kind="ExternalInput").ap()
    w_out = nc.dram_tensor("w_out", [D, D], F32, kind="ExternalInput").ap()
    x_out = nc.dram_tensor("x_out", [T, D], F32, kind="ExternalOutput").ap()
    const_aps = {}
    for name, arr in consts.items():
        dt = {np.dtype(np.float32): F32, np.dtype(BF): BF16, np.dtype(np.int32): I32}[arr.dtype]
        const_aps[name] = nc.dram_tensor(name, list(arr.shape), dt, kind="ExternalInput").ap()

    xln = nc.dram_tensor("s_xln", [T, D], BF16).ap()
    y_fm = nc.dram_tensor("s_yfm", [D, T], BF16).ap()
    scr = {
        "l1_wqkv_bf": nc.dram_tensor("s_wqkv_bf", [3 * D, D], BF16).ap(),
        "l1_wout_bf": nc.dram_tensor("s_wout_bf", [D, D], BF16).ap(),
        "l1_q": nc.dram_tensor("s_q", [D, T], BF16).ap(),
        "l1_v": nc.dram_tensor("s_v", [T // 128, c.H, 128, 128], BF16).ap(),
    }
    es = ExitStack()
    with es:
        ctx = Ctx(nc, es, c, const_aps)
        state = {}
        ln_toks = emit_sandwich(ctx, x_in, None, None, xln, True, state)
        y_toks = emit_l1_attn(ctx, xln, ln_toks, w_qkv, w_out, y_fm, scr)
        emit_sandwich(ctx, x_in, y_fm, x_out, xln, False, state, y_ready=y_toks, write_xln=False)
        final = ctx.p.all_toks()
        ctx.p.op("sp", lambda e: e.nop(), waits=final)
        ctx.p.emit(nc, es)
    return nc



def emit_l0_attn(ctx, xln_dram, xln_ready, w, y_fm, scr, wbf=None, cast_ready=(), phase_limit=99, y_guard=()):
    c, p = ctx.c, ctx.p
    T, D, H = c.T, c.D, c.H
    NS = D // 128
    NQ = T // 512
    NKT = T // 128
    NTC = T // 512
    QL, KVL, NOPE, ROPE, VD = c.QL, c.KVL, c.NOPE, c.ROPE, c.VD
    u = ctx.uid()
    scale = float(c.QKD ** (-0.5))
    bf = ctx.bank_free

    ones_col, ocol_tok = ctx.load_const("c_ones_col_bf")
    ra, ra_tok = ctx.load_const("c_rope_a_k", scoped=True)
    rb, rb_tok = ctx.load_const("c_rope_b_k", scoped=True)
    trilw, trilw_tok = ctx.load_const("c_trilw", scoped=True)

    q_scr, k_scr, v_scr = scr["l0_q"], scr["l0_k"], scr["l0_v"]

    xfm = ctx.strip_set("A")
    xfm_tok = emit_load_xln_fm(ctx, xln_dram, xln_ready, xfm,
                               reuse_waits=scr.pop("poolA_free", []))

    wsrc_d = wbf if wbf is not None else w

    OC = 128
    wall = [ctx.sb(f"l0wa{u}_{b}", [128, NS * OC], BF16) for b in range(2)]
    wsem = [f"l0ws{u}_{b}" for b in range(2)]
    wall_use = [None, None]

    def wall_load(b, wsrc, o0, oc_w, nstrips, extra=()):
        tok = None
        for i in range(nstrips):
            tok = p.dma("sp", wall[b][:, i * OC:i * OC + oc_w],
                        wsrc[i * 128:(i + 1) * 128, o0:o0 + oc_w],
                        waits=list(extra) + list(cast_ready) + ([wall_use[b]] if wall_use[b] else []),
                        sem=wsem[b])
        return (wsem[b], tok[1])

    evb = [ctx.sb(f"l0ev{u}_{b}", [128, 512], BF16) for b in range(2)]
    ev_free = [None, None]
    sqq = ctx.sb(f"l0sqq{u}", [128, 512], BF16)
    sqq_use = None
    rbc = ctx.sb(f"l0rbc{u}", [128, 512], F32)
    rbckv = ctx.sb(f"l0rbckv{u}", [128, T], F32)
    kpe_r = ctx.sb(f"l0kpe{u}", [65, T], BF16)
    kpe_tmp = ctx.sb(f"l0kpt{u}", [64, 512], F32)
    kpe_sw = ctx.sb(f"l0kps{u}", [64, 512], F32)
    kab_pos = ctx.sb(f"l0kab{u}", [128, H], F32)
    kab_nope = ctx.sb(f"l0kabn{u}", [128, H], BF16)
    kab_pe = ctx.sb(f"l0kabp{u}", [64, 1], BF16)
    kab_tmp = ctx.sb(f"l0kabt{u}", [128, 1], F32)
    ctx.push_scope()
    gq = ctx.sb(f"l0gq{u}", [128, QL // 128], F32)
    gkv = ctx.sb(f"l0gkv{u}", [128, KVL // 128], F32)
    gq_tok = p.dma("sp", gq[:], w["g_q"].rearrange("(s p) -> p s", p=128), sem=f"l0gq{u}", slow_ok=True)
    gkv_tok = p.dma("sp", gkv[:], w["g_kv"].rearrange("(s p) -> p s", p=128), sem=f"l0gkv{u}", slow_ok=True)
    ssq_q = ctx.sb(f"l0ssqq{u}", [1, T], F32)
    ssq_kv = ctx.sb(f"l0ssqkv{u}", [1, T], F32)
    rstdq = ssq_q
    rstdkv = ssq_kv
    rstdkv_col = ctx.sb(f"l0rskvc{u}", [128, NKT], F32)

    kv_g = [ctx.sb(f"l0kv{u}_{i}", [128, T], BF16) for i in range(KVL // 128)]
    qs_sem = [f"l0qs{u}_{b}" for b in range(2)]

    def lat_proj(wsrc, LO, gcol, gtok, ssq_row, dst_kind):
        nonlocal sqq_use
        no_stats = (phase_limit == 101)
        no_evac = (phase_limit == 102)
        NOC = LO // OC
        for oc in range(NOC):
            b = oc % 2
            wtok = wall_load(b, wsrc, oc * OC, OC, NS)
            for tc in range(NTC):
                bi = (oc * NTC + tc) % 2
                bank = ctx.psum[bi]
                sbank = 4 + tc
                mm = None
                for i in range(NS):
                    mm = p.op("pe", lambda e, bank=bank, b=b, i=i, tc=tc:
                              e.matmul(bank[:], wall[b][:, i * OC:(i + 1) * OC],
                                       xfm[i][:, tc * 512:(tc + 1) * 512],
                                       start=(i == 0), stop=(i == NS - 1)),
                              waits=[wtok, xfm_tok] + (bf[bi] if i == 0 else []),
                              inc="pe")
                wall_use[b] = mm
                if no_stats:
                    sget = smm = mm
                else:
                    sget = p.op("act", lambda e, bank=bank: e.activation(sqq[:], bank[:], AF.Square),
                                waits=[mm] + ([sqq_use] if sqq_use else []), inc="act")
                    smm = p.op("pe", lambda e, sbank=sbank, oc=oc, NOC=NOC:
                               e.matmul(ctx.psum[sbank][0:1, 0:512], ones_col[:], sqq[:],
                                        start=(oc == 0), stop=(oc == NOC - 1)),
                               waits=[sget, ocol_tok] + (bf[sbank] if oc == 0 else []),
                               inc="pe")
                    sqq_use = smm
                if no_evac:
                    ge = p.op("act", lambda e, bank=bank: e.activation(sqq[:], bank[:], AF.Copy),
                              waits=[mm, gtok] + ([sqq_use] if (no_stats and sqq_use) else []), inc="act")
                    if no_stats:
                        sqq_use = ge
                elif dst_kind == "q":
                    es_ = bi
                    ge = p.op("dve", lambda e, bank=bank, oc=oc, es_=es_:
                              e.tensor_scalar(evb[es_][:], bank[:], gcol[:, oc:oc + 1], None, ALU.mult),
                              waits=[mm, gtok, sget] + ([ev_free[es_]] if ev_free[es_] else []), inc="dve")
                    sx = p.dma("sp", q_scr[oc * 128:(oc + 1) * 128, tc * 512:(tc + 1) * 512],
                               evb[es_][:], waits=[ge], sem=qs_sem[es_])
                    ev_free[es_] = sx
                else:
                    ge = p.op("dve", lambda e, bank=bank, oc=oc, tc=tc:
                              e.tensor_scalar(kv_g[oc][:, tc * 512:(tc + 1) * 512], bank[:],
                                              gcol[:, oc:oc + 1], None, ALU.mult),
                              waits=[mm, gtok, sget], inc="dve")
                bf[bi] = [sget, ge]
                if oc == NOC - 1 and not no_stats:
                    se = p.op("act", lambda e, tc=tc, sbank=sbank, ssq_row=ssq_row:
                              e.activation(ssq_row[:, tc * 512:(tc + 1) * 512],
                                           ctx.psum[sbank][0:1, 0:512], AF.Copy),
                              waits=[smm], inc="act")
                    bf[sbank] = [se]

    if phase_limit <= 0:
        ctx.pop_scope()
        return [p.tok("pe")]
    lat_proj(wsrc_d["w_dq"], QL, gq, gq_tok, ssq_q, "q")
    lat_proj(wsrc_d["w_dkv"], KVL, gkv, gkv_tok, ssq_kv, "kv")
    qlat_stored = [p.tok(s) for s in qs_sem]
    kv_ready = p.tok("dve")
    if phase_limit <= 1 or phase_limit in (101, 102):
        ctx.pop_scope()
        return [p.tok("pe")]

    epsc, epsc_tok = ctx.const_col(c.EPS_RMS)
    act_now = p.tok("act")
    r1 = p.op("dve", lambda e: e.tensor_scalar(rstdq[:], ssq_q[:], 1.0 / QL, None, ALU.mult),
              waits=[act_now], inc="dve")
    r2 = p.op("act", lambda e: e.activation(rstdq[:], rstdq[:], AF.Ln, bias=epsc[0:1, :]),
              waits=[r1, epsc_tok], inc="act")
    r3 = p.op("act", lambda e: e.activation(rstdq[:], rstdq[:], AF.Exp, scale=-0.5),
              waits=[r2], inc="act")
    rq_tok = p.op("dve", lambda e: e.tensor_scalar(rstdq[:], rstdq[:], scale, None, ALU.mult),
                  waits=[r3], inc="dve")
    r5 = p.op("dve", lambda e: e.tensor_scalar(rstdkv[:], ssq_kv[:], 1.0 / KVL, None, ALU.mult),
              waits=[act_now], inc="dve")
    r6 = p.op("act", lambda e: e.activation(rstdkv[:], rstdkv[:], AF.Ln, bias=epsc[0:1, :]),
              waits=[r5, epsc_tok], inc="act")
    rkv_tok = p.op("act", lambda e: e.activation(rstdkv[:], rstdkv[:], AF.Exp, scale=-0.5),
                   waits=[r6], inc="act")
    onescal = ctx.sb(f"l0one1{u}", [1, 1], F32)
    o1t = p.op("dve", lambda e: e.memset(onescal[:], 1.0), inc="dve")
    rcol_prev = None
    for tk in range(NKT):
        mmr = p.op("pe", lambda e, tk=tk: e.matmul(ctx.psum[7][:, tk:tk + 1],
                                                   rstdkv[:, tk * 128:(tk + 1) * 128], onescal[:],
                                                   start=True, stop=True),
                   waits=[rkv_tok, o1t] + (bf[7] if tk == 0 else []) +
                         ([rcol_prev] if rcol_prev else []),
                   inc="pe")
        rcol_prev = p.op("act", lambda e, tk=tk: e.activation(rstdkv_col[:, tk:tk + 1],
                                                              ctx.psum[7][:, tk:tk + 1], AF.Copy),
                         waits=[mmr], inc="act")
    bf[7] = [rcol_prev]
    rcol_all = rcol_prev
    if phase_limit <= 2:
        ctx.pop_scope()
        return [p.tok("pe")]

    on_tok = p.op("dve", lambda e: e.memset(kpe_r[64:65, :], 1.0), inc="dve")
    wtok = wall_load(0, wsrc_d["w_kr"], 0, 64, NS)
    rope_use = None
    for tc in range(NTC):
        bank = ctx.psum[2]
        mm = None
        for i in range(NS):
            mm = p.op("pe", lambda e, bank=bank, i=i, tc=tc:
                      e.matmul(bank[0:64, :], wall[0][:, i * OC:i * OC + 64],
                               xfm[i][:, tc * 512:(tc + 1) * 512],
                               start=(i == 0), stop=(i == NS - 1)),
                      waits=[wtok, xfm_tok] + (bf[2] if i == 0 else []),
                      inc="pe")
        wall_use[0] = mm
        e1 = p.op("act", lambda e, bank=bank: e.activation(kpe_tmp[:], bank[0:64, :], AF.Copy),
                  waits=[mm] + ([rope_use] if rope_use else []), inc="act")
        e2 = p.op("act", lambda e, bank=bank: e.activation(kpe_sw[0:32, :], bank[32:64, :], AF.Copy),
                  waits=[mm], inc="act")
        e3 = p.op("act", lambda e, bank=bank: e.activation(kpe_sw[32:64, :], bank[0:32, :], AF.Copy),
                  waits=[mm], inc="act")
        bf[2] = [e1, e2, e3]
        t1 = p.op("dve", lambda e, tc=tc: e.tensor_tensor(kpe_tmp[:], kpe_tmp[:],
                                                          ra[:, tc * 512:(tc + 1) * 512], ALU.mult),
                  waits=[e1, ra_tok], inc="dve")
        t2 = p.op("dve", lambda e, tc=tc: e.tensor_tensor(kpe_sw[:], kpe_sw[:],
                                                          rb[:, tc * 512:(tc + 1) * 512], ALU.mult),
                  waits=[e3, rb_tok], inc="dve")
        t3 = p.op("dve", lambda e, tc=tc: e.tensor_tensor(kpe_r[0:64, tc * 512:(tc + 1) * 512],
                                                          kpe_tmp[:], kpe_sw[:], ALU.add),
                  waits=[t1, t2], inc="dve")
        rope_use = t3
    kpe_all = p.tok("dve")
    if phase_limit <= 3:
        ctx.pop_scope()
        return [p.tok("pe")]

    NKV = KVL // 128
    ktile = ctx.sb(f"l0ktile{u}", [128, 512], BF16)
    ktile_use = None
    rbc_use = None
    ks_sem = f"l0ks{u}"

    kab_prev = {}
    rbckv_toks = []
    for tc in range(NTC):
        bm = ctx.bcast_f32(rstdkv[0:1, tc * 512:(tc + 1) * 512], 3, [rkv_tok])
        be = p.op("act", lambda e, tc=tc: e.activation(rbckv[:, tc * 512:(tc + 1) * 512],
                                                       ctx.psum[3][:], AF.Copy),
                  waits=[bm], inc="act")
        bf[3] = [be]
        rbckv_toks.append(be)
    for h in range(H):
        b = h % 2
        wtok = wall_load(b, wsrc_d["w_uk"], h * 128, OC, NKV)
        for tc in range(NTC):
            bi = (h * NTC + tc) % 2
            bank = ctx.psum[bi]
            mm = None
            for i in range(NKV):
                mm = p.op("pe", lambda e, bank=bank, b=b, i=i, tc=tc:
                          e.matmul(bank[:], wall[b][:, i * OC:(i + 1) * OC],
                                   kv_g[i][:, tc * 512:(tc + 1) * 512],
                                   start=(i == 0), stop=(i == NKV - 1)),
                          waits=[wtok, kv_ready] + (bf[bi] if i == 0 else []),
                          inc="pe")
            wall_use[b] = mm
            ke = p.op("dve", lambda e, bank=bank, tc=tc:
                      e.tensor_tensor(ktile[:], bank[:], rbckv[:, tc * 512:(tc + 1) * 512], ALU.mult),
                      waits=[mm, rbckv_toks[tc]] + (ktile_use or []), inc="dve")
            rbc_use = [ke]
            bf[bi] = [ke]
            sx = None
            for j in range(4):
                sx = p.dma("sp", k_scr[h, tc * 4 + j], ktile[:, j * 128:(j + 1) * 128],
                           waits=[ke], sem=ks_sem)
            ab = p.op("dve", lambda e: e.tensor_reduce(kab_tmp[:], ktile[:], AX.X, ALU.max,
                                                       apply_absolute_value=True),
                      waits=[ke], inc="dve")
            am = p.op("dve", lambda e, h=h, tc=tc:
                      (e.tensor_copy(kab_pos[:, h:h + 1], kab_tmp[:]) if tc == 0 else
                       e.tensor_tensor(kab_pos[:, h:h + 1], kab_pos[:, h:h + 1], kab_tmp[:], ALU.max)),
                      waits=[ab] + ([kab_prev.get(h)] if kab_prev.get(h) else []), inc="dve")
            kab_prev[h] = am
            ktile_use = [ab, sx]
        ng = p.op("dve", lambda e, h=h: e.tensor_scalar(kab_nope[:, h:h + 1], kab_pos[:, h:h + 1],
                                                        -1.0, None, ALU.mult),
                  waits=[kab_prev[h]], inc="dve")
        kab_prev[h] = ng
    k_spilled = p.tok(ks_sem)
    pb = p.op("dve", lambda e: e.tensor_reduce(kab_tmp[0:64, :], kpe_r[0:64, :], AX.X, ALU.max,
                                               apply_absolute_value=True, negate=True),
              waits=[kpe_all], inc="dve")
    pe_ab = p.op("dve", lambda e: e.tensor_copy(kab_pe[:], kab_tmp[0:64, :]),
                 waits=[pb], inc="dve")

    vs_sem = f"l0vs{u}"
    vtile_use = None
    for oc in range(H * VD // OC):
        b = oc % 2
        wtok = wall_load(b, wsrc_d["w_uv"], oc * OC, OC, NKV)
        for tk in range(NKT):
            bi = (oc * NKT + tk) % 2
            bank = ctx.psum[bi]
            mm = None
            for i in range(NKV):
                mm = p.op("pe", lambda e, bank=bank, b=b, i=i, tk=tk:
                          e.matmul(bank[:, 0:OC], kv_g[i][:, tk * 128:(tk + 1) * 128],
                                   wall[b][:, i * OC:(i + 1) * OC],
                                   start=(i == 0), stop=(i == NKV - 1)),
                          waits=[wtok, kv_ready] + (bf[bi] if i == 0 else []),
                          inc="pe")
            wall_use[b] = mm
            ve = p.op("dve", lambda e, bank=bank, tk=tk:
                      e.tensor_scalar(ktile[:, 0:OC], bank[:, 0:OC], rstdkv_col[:, tk:tk + 1],
                                      None, ALU.mult),
                      waits=[mm, rcol_all] + (vtile_use if vtile_use else (ktile_use or [])),
                      inc="dve")
            bf[bi] = [ve]
            sx = p.dma("sp", v_scr[tk, oc], ktile[:, 0:OC], waits=[ve], sem=vs_sem)
            vtile_use = [sx]
    v_spilled = p.tok(vs_sem)
    rbcq = rbckv
    kv_rbc_done = p.tok("dve")
    rbcq_toks = []
    for qb in range(NQ):
        bm = ctx.bcast_f32(rstdq[0:1, qb * 512:(qb + 1) * 512], 3, [rq_tok])
        be = p.op("act", lambda e, qb=qb: e.activation(rbcq[:, qb * 512:(qb + 1) * 512],
                                                       ctx.psum[3][:], AF.Copy),
                  waits=[bm, kv_rbc_done] + bf[3], inc="act")
        bf[3] = [be]
        rbcq_toks.append(be)
    ctx.pop_scope()
    p.region([(s2_, v2_) for (s2_, v2_) in p.all_toks() if not s2_.startswith("cc")])
    if phase_limit <= 4:
        ctx.pop_scope()
        return [p.tok("pe")]

    o_all = xfm
    qkv_done_pe = p.tok("pe")
    pt = [ctx.sb(f"l0pt{u}_{i}", [128, 512], BF16) for i in range(2)]
    pt_use = [None] * 2
    qn = [ctx.sb(f"l0qn{u}_{i}", [128, 512], BF16) for i in range(2)]
    qaug = [ctx.sb(f"l0qaug{u}_{i}", [65, 512], BF16) for i in range(2)]
    qpe_tmp, qpe_sw = kpe_tmp, kpe_sw
    absq = ctx.sb(f"l0absq{u}", [128, 512], BF16)
    absp = ctx.sb(f"l0absp{u}", [64, 512], BF16)
    den = ctx.sb(f"l0den{u}", [1, 512], F32)
    NQL = QL // 128
    q_all = ctx.sb(f"l0qall{u}", [128, NQL * T], BF16)
    qall_sem = f"l0qalls{u}"
    qltok = None
    for i in range(NQL):
        qltok = p.dma("act", q_all[:, i * T:(i + 1) * T], q_scr[i * 128:(i + 1) * 128, :],
                      waits=qlat_stored, sem=qall_sem)
    qall_ready = (qall_sem, qltok[1])
    whb = [ctx.sb(f"l0whb{u}_{b}", [128, NQL * 192], BF16) for b in range(2)]
    whb_sem = [f"l0whbs{u}_{b}" for b in range(2)]
    whb_use = [None, None]
    kh = [ctx.sb(f"l0kh{u}_{b}", [128, T], BF16) for b in range(2)]
    kh_sem = [f"l0khs{u}_{b}" for b in range(2)]
    kh_use = [None, None]
    vh = [ctx.sb(f"l0vh{u}_{b}", [128, T], BF16) for b in range(2)]
    vh_sem = [f"l0vhs{u}_{b}" for b in range(2)]
    vh_use = [None, None]
    stb = [0, 1, 6]
    aug_use = [None, None]
    qn_use = [None, None]
    rope_use2 = rope_use
    absq_use = None
    absp_use = None
    orbc_use = rbc_use
    iters = [(h, qb) for h in range(H) for qb in range(NQ)]
    wready_h = {}
    khready_h = {}
    vhready_h = {}

    def pre(idx):
        nonlocal rope_use2, absq_use, absp_use
        h, qb = iters[idx]
        sl = idx % 2
        bh = h % 2
        if qb == 0:
            wtok = None
            for i in range(NQL):
                wtok = p.dma("sp", whb[bh][:, i * 192:(i + 1) * 192],
                             wsrc_d["w_uq"][i * 128:(i + 1) * 128, h * 192:(h + 1) * 192],
                             waits=list(cast_ready) + ([whb_use[bh]] if whb_use[bh] else []),
                             sem=whb_sem[bh])
            wready_h[h] = (whb_sem[bh], wtok[1])
            ktok = None
            for kt in range(NKT):
                ktok = p.dma("sp", kh[bh][:, kt * 128:(kt + 1) * 128], k_scr[h, kt],
                             waits=[k_spilled] + ([kh_use[bh]] if kh_use[bh] else []),
                             sem=kh_sem[bh])
            khready_h[h] = (kh_sem[bh], ktok[1])
            vtok = None
            for kt in range(NKT):
                vtok = p.dma("act", vh[bh][:, kt * 128:(kt + 1) * 128], v_scr[kt, h],
                             waits=[v_spilled] + ([vh_use[bh]] if vh_use[bh] else []),
                             sem=vh_sem[bh])
            vhready_h[h] = (vh_sem[bh], vtok[1])
        wready = wready_h[h]
        mmn = None
        for i in range(NQL):
            mmn = p.op("pe", lambda e, bh=bh, i=i, qb=qb:
                       e.matmul(ctx.psum[5][:], whb[bh][:, i * 192:i * 192 + 128],
                                q_all[:, i * T + qb * 512:i * T + (qb + 1) * 512],
                                start=(i == 0), stop=(i == NQL - 1)),
                       waits=[wready, qall_ready] + (bf[5] if i == 0 else []), inc="pe")
        mmp = None
        for i in range(NQL):
            mmp = p.op("pe", lambda e, bh=bh, i=i, qb=qb:
                       e.matmul(ctx.psum[2][0:64, :], whb[bh][:, i * 192 + 128:(i + 1) * 192],
                                q_all[:, i * T + qb * 512:i * T + (qb + 1) * 512],
                                start=(i == 0), stop=(i == NQL - 1)),
                       waits=[wready, qall_ready] + (bf[2] if i == 0 else []), inc="pe")
        whb_use[bh] = mmp
        qe = p.op("dve", lambda e, qb=qb, sl=sl:
                  e.tensor_tensor(qn[sl][:], ctx.psum[5][:],
                                  rbcq[:, qb * 512:(qb + 1) * 512], ALU.mult),
                  waits=[mmn, rbcq_toks[qb]] + ([qn_use[sl]] if qn_use[sl] else []), inc="dve")
        bf[5] = [qe]
        pe1 = p.op("act", lambda e: e.activation(qpe_tmp[:], ctx.psum[2][0:64, :], AF.Copy),
                   waits=[mmp] + ([rope_use2] if rope_use2 else []), inc="act")
        pe2 = p.op("act", lambda e: e.activation(qpe_sw[0:32, :], ctx.psum[2][32:64, :], AF.Copy),
                   waits=[mmp], inc="act")
        pe3 = p.op("act", lambda e: e.activation(qpe_sw[32:64, :], ctx.psum[2][0:32, :], AF.Copy),
                   waits=[mmp], inc="act")
        bf[2] = [pe1, pe2, pe3]
        u1 = p.op("dve", lambda e, qb=qb: e.tensor_tensor(qpe_tmp[:], qpe_tmp[:],
                                                          ra[:, qb * 512:(qb + 1) * 512], ALU.mult),
                  waits=[pe1, ra_tok], inc="dve")
        u2 = p.op("dve", lambda e, qb=qb: e.tensor_tensor(qpe_sw[:], qpe_sw[:],
                                                          rb[:, qb * 512:(qb + 1) * 512], ALU.mult),
                  waits=[pe3, rb_tok], inc="dve")
        u3 = p.op("dve", lambda e: e.tensor_tensor(qpe_tmp[:], qpe_tmp[:], qpe_sw[:], ALU.add),
                  waits=[u1, u2], inc="dve")
        u4 = p.op("dve", lambda e, qb=qb, sl=sl:
                  e.tensor_tensor(qaug[sl][0:64, :], qpe_tmp[:],
                                  rbcq[0:64, qb * 512:(qb + 1) * 512], ALU.mult),
                  waits=[u3, rbcq_toks[qb]] + ([aug_use[sl]] if aug_use[sl] else []), inc="dve")
        rope_use2 = u4
        a1 = p.op("act", lambda e, sl=sl: e.activation(absq[:], qn[sl][:], AF.Abs),
                  waits=[qe] + ([absq_use] if absq_use else []), inc="act")
        a2 = p.op("act", lambda e, sl=sl: e.activation(absp[:], qaug[sl][0:64, :], AF.Abs),
                  waits=[u4] + ([absp_use] if absp_use else []), inc="act")
        return dict(qb=qb, h=h, sl=sl, bh=bh, qe=qe, u4=u4, a1=a1, a2=a2,
                    pe123=[pe1, pe2, pe3])

    def bound(st):
        nonlocal absq_use, absp_use
        h, sl = st["h"], st["sl"]
        bm1 = p.op("pe", lambda e, h=h: e.matmul(ctx.psum[2][0:1, :], kab_nope[:, h:h + 1],
                                                 absq[:], start=True, stop=False),
                   waits=[st["a1"], kab_prev[h]] + st["pe123"], inc="pe")
        bm2 = p.op("pe", lambda e: e.matmul(ctx.psum[2][0:1, :], kab_pe[:], absp[:],
                                            start=False, stop=True),
                   waits=[st["a2"], pe_ab], inc="pe")
        me = p.op("act", lambda e, sl=sl: e.activation(qaug[sl][64:65, :], ctx.psum[2][0:1, :], AF.Copy),
                  waits=[bm2], inc="act")
        bf[2] = [me]
        absq_use = bm1
        absp_use = bm2
        st["me"] = me

    def body_st(idx, st):
        nonlocal sqq_use
        qb, h, sl, bh = st["qb"], st["h"], st["sl"], st["bh"]
        nkt_q = 4 * qb + 4
        den_mm = None
        o_mm = None
        mm1 = None
        mm2 = None
        for kt in range(nkt_q):
            bi = stb[kt % 3]
            bank = ctx.psum[bi]
            mm1 = p.op("pe", lambda e, bank=bank, kt=kt, bh=bh, sl=sl:
                       e.matmul(bank[:], kh[bh][:, kt * 128:(kt + 1) * 128], qn[sl][:],
                                start=True, stop=False),
                       waits=[khready_h[h], st["qe"]] + bf[bi], inc="pe")
            mm2 = p.op("pe", lambda e, bank=bank, kt=kt, sl=sl:
                       e.matmul(bank[:], kpe_r[:, kt * 128:(kt + 1) * 128], qaug[sl][:],
                                start=False, stop=True),
                       waits=[st["me"], st["u4"], kpe_all, on_tok], inc="pe")
            ps_ = kt % 2
            diag_off = kt * 128 - qb * 512
            if 0 <= diag_off < 512:
                ex = p.op("act", lambda e, bank=bank: e.activation(sqq[:], bank[:], AF.Exp),
                          waits=[mm2, sqq_use], inc="act")
                exm = p.op("dve", lambda e, ps_=ps_, off=diag_off:
                           e.tensor_tensor(pt[ps_][:], sqq[:],
                                           trilw[:, 384 - off:896 - off], ALU.mult),
                           waits=[ex, trilw_tok] + ([pt_use[ps_]] if pt_use[ps_] else []),
                           inc="dve")
                sqq_use = exm
                exf = exm
            else:
                exf = p.op("act", lambda e, bank=bank, ps_=ps_:
                           e.activation(pt[ps_][:], bank[:], AF.Exp),
                           waits=[mm2] + ([pt_use[ps_]] if pt_use[ps_] else []), inc="act")
            bf[bi] = [exf]
            den_mm = p.op("pe", lambda e, kt=kt, ps_=ps_, last=(kt == nkt_q - 1):
                          e.matmul(ctx.psum[3][0:1, :], ones_col[:], pt[ps_][:],
                                   start=(kt == 0), stop=last),
                          waits=[exf, ocol_tok] + (bf[3] if kt == 0 else []), inc="pe")
            o_mm = p.op("pe", lambda e, kt=kt, bh=bh, ps_=ps_, last=(kt == nkt_q - 1):
                        e.matmul(ctx.psum[4][:], vh[bh][:, kt * 128:(kt + 1) * 128], pt[ps_][:],
                                 start=(kt == 0), stop=last),
                        waits=[exf, vhready_h[h]] + (bf[4] if kt == 0 else []), inc="pe")
            pt_use[ps_] = o_mm
        qn_use[sl] = mm1
        aug_use[sl] = mm2
        kh_use[bh] = mm1
        vh_use[bh] = o_mm
        st["den_mm"] = den_mm
        st["o_mm"] = o_mm

    def body_tail(st):
        nonlocal orbc_use
        qb, h = st["qb"], st["h"]
        d1 = p.op("act", lambda e: e.activation(den[:], ctx.psum[3][0:1, :], AF.Copy),
                  waits=[st["den_mm"]], inc="act")
        bf[3] = [d1]
        d3 = p.op("dve", lambda e: e.reciprocal(den[:], den[:]), waits=[d1], inc="dve")
        bcm = ctx.bcast_f32(den[0:1, :], 3, [d3])
        rb2 = p.op("act", lambda e: e.activation(rbc[:], ctx.psum[3][:], AF.Copy),
                   waits=[bcm] + (orbc_use or []), inc="act")
        bf[3] = [rb2]
        oe = p.op("dve", lambda e, h=h, qb=qb:
                  e.tensor_tensor(o_all[h][:, qb * 512:(qb + 1) * 512], ctx.psum[4][:], rbc[:],
                                  ALU.mult),
                  waits=[st["o_mm"], rb2, qkv_done_pe], inc="dve")
        bf[4] = [oe]
        orbc_use = [oe]

    sts = {}
    NIT = len(iters)
    for idx in range(NIT):
        if idx == 0:
            sts[0] = pre(0)
            bound(sts[0])
        if idx + 1 < NIT:
            sts[idx + 1] = pre(idx + 1)
        if idx > 0:
            body_tail(sts.pop(idx - 1))
        body_st(idx, sts[idx])
        if idx + 1 < NIT:
            bound(sts[idx + 1])
    body_tail(sts.pop(NIT - 1))
    attn_done_dve = p.tok("dve")
    if phase_limit <= 5:
        return [p.tok("pe")]

    WOC = 128
    yev = evb
    yev_free = list(ev_free)
    ys_sem = [f"l0ys{u}_{b}" for b in range(2)]
    for dc in range(D // WOC):
        b = dc % 2
        wtok = None
        for hh in range(H):
            wtok = p.dma("sp", wall[b][:, hh * WOC:(hh + 1) * WOC],
                         wsrc_d["w_o"][hh * 128:(hh + 1) * 128, dc * WOC:(dc + 1) * WOC],
                         waits=list(cast_ready) + ([wall_use[b]] if wall_use[b] else []), sem=wsem[b])
        wready = (wsem[b], wtok[1])
        for ds in range(WOC // 128):
            for qb in range(NQ):
                bi = (ds * NQ + qb) % 4
                bank = ctx.psum[bi]
                mm = None
                for hh in range(H):
                    mm = p.op("pe", lambda e, bank=bank, b=b, hh=hh, ds=ds, qb=qb:
                              e.matmul(bank[:], wall[b][:, hh * WOC + ds * 128:hh * WOC + (ds + 1) * 128],
                                       o_all[hh][:, qb * 512:(qb + 1) * 512],
                                       start=(hh == 0), stop=(hh == H - 1)),
                              waits=[wready, attn_done_dve] + (bf[bi] if hh == 0 else []),
                              inc="pe")
                wall_use[b] = mm
                es_ = bi % 2
                et = p.op("act", lambda e, bank=bank, es_=es_:
                          e.activation(yev[es_][:], bank[:], AF.Copy),
                          waits=[mm] + ([yev_free[es_]] if yev_free[es_] else []), inc="act")
                bf[bi] = [et]
                sx = p.dma("sp", y_fm[(dc * (WOC // 128) + ds) * 128:(dc * (WOC // 128) + ds + 1) * 128,
                                      qb * 512:(qb + 1) * 512],
                           yev[es_][:], waits=[et] + list(y_guard), sem=ys_sem[es_])
                yev_free[es_] = sx
    scr["poolA_free"] = [p.tok("pe")]
    return [p.tok(s) for s in ys_sem]


def build_l0_attn_launch(cfg, consts, phase_limit=99):
    nc = bass.Bass("TRN2", target_bir_lowering=False, debug=False)
    c = cfg
    T, D = c.T, c.D
    x_in = nc.dram_tensor("x_in", [T, D], F32, kind="ExternalInput").ap()
    wnames = {
        "w_dq": [D, c.QL], "g_q": [c.QL], "w_uq": [c.QL, c.H * c.QKD],
        "w_dkv": [D, c.KVL], "w_kr": [D, c.ROPE], "g_kv": [c.KVL],
        "w_uk": [c.KVL, c.H * c.NOPE], "w_uv": [c.KVL, c.H * c.VD], "w_o": [c.H * c.VD, D],
    }
    w = {n: nc.dram_tensor(n, s, F32, kind="ExternalInput").ap() for n, s in wnames.items()}
    x_out = nc.dram_tensor("x_out", [T, D], F32, kind="ExternalOutput").ap()
    const_aps = {}
    for name, arr in consts.items():
        dt = {np.dtype(np.float32): F32, np.dtype(BF): BF16, np.dtype(np.int32): I32}[arr.dtype]
        const_aps[name] = nc.dram_tensor(name, list(arr.shape), dt, kind="ExternalInput").ap()
    xln = nc.dram_tensor("s_xln", [T, D], BF16).ap()
    y_fm = nc.dram_tensor("s_yfm", [D, T], BF16).ap()
    scr = {
        "l0_q": nc.dram_tensor("s_q", [c.QL, T], BF16).ap(),
        "l0_k": nc.dram_tensor("s_k", [c.H, T // 128, 128, 128], BF16).ap(),
        "l0_v": nc.dram_tensor("s_v", [T // 128, c.H, 128, 128], BF16).ap(),
    }
    wl0_bf = {n: nc.dram_tensor("s_bf_" + n, s, BF16).ap() for n, s in wnames.items()
              if n not in ("g_q", "g_kv")}
    es = ExitStack()
    with es:
        ctx = Ctx(nc, es, c, const_aps)
        for nm2, ap2 in wl0_bf.items():
            rows = wnames[nm2][0]
            r0 = 0
            while r0 < rows:
                rr = min(512, rows - r0)
                ctx.p.dma("po", ap2[r0:r0 + rr, :], w[nm2][r0:r0 + rr, :], sem="l0cast")
                r0 += rr
        l0c = ctx.p.tok("l0cast")
        state = {}
        ln_toks = emit_sandwich(ctx, x_in, None, None, xln, True, state)
        y_toks = emit_l0_attn(ctx, xln, ln_toks, w, y_fm, scr, wbf=wl0_bf, cast_ready=[l0c],
                              phase_limit=phase_limit)
        if phase_limit > 6:
            emit_sandwich(ctx, x_in, y_fm, x_out, xln, False, state, y_ready=y_toks, write_xln=False)
        final = ctx.p.all_toks()
        ctx.p.op("sp", lambda e: e.nop(), waits=final)
        ctx.p.emit(nc, es)
    return nc

def emit_gated_ffn(ctx, xln_dram, xln_ready, w1f, w2f, w3f, act_fn, F, y_fm, y3p, scr,
                   w_ready=(), y_guard=()):
    c, p = ctx.c, ctx.p
    T, D = c.T, c.D
    NS = D // 128
    NTC = T // 512
    NF = F // 128
    u = ctx.uid()
    bf = ctx.bank_free

    xfm = ctx.strip_set("A")
    xfm_tok = emit_load_xln_fm(ctx, xln_dram, xln_ready, xfm)
    hid = ctx.strip_set("H", 11)

    OCW = 256
    wall = [[ctx.sb(f"gf{u}_wa{wi}{b}", [128, NS * OCW], BF16) for b in range(2)] for wi in range(2)]
    wsem = [["gf_ws%d%d" % (wi, b) for b in range(2)] for wi in range(2)]
    wall_use = [[None, None], [None, None]]
    wall3 = [ctx.sb(f"gf{u}_w3{b}", [128, 11 * 512], BF16) for b in range(2)]
    w3sem = ["gf_w3s%d" % b for b in range(2)]
    wall3_use = [None, None]
    evs = ctx.sb(f"gf{u}_ev", [128, 512], BF16)
    evs_use = None
    ybuf = [ctx.sb(f"gf{u}_yb{b}", [128, 512], F32) for b in range(2)]
    ybuf_use = [None, None]
    ypin = [ctx.sb(f"gf{u}_yp{b}", [128, 512], F32) for b in range(2)]
    ypin_sem = ["gf_yps" + str(b) for b in range(2)]
    ypin_use = [None, None]
    ybf = ctx.sb(f"gf{u}_ybf", [128, 512], BF16)
    ybf_use = None
    ys_sem = "gf_ys"
    yp_sem = "gf_yo"

    quarters = []
    base = 0
    for qi in range(4):
        nq = (NF + 3 - qi) // 4
        quarters.append((base, nq))
        base += nq
    assert base == NF

    hb_toks = [None] * 11
    for qi, (qbase, qn) in enumerate(quarters):
        oc0 = qbase * 128
        oc1 = (qbase + qn) * 128
        chunks = []
        o0 = oc0
        while o0 < oc1:
            ow = min(OCW, oc1 - o0)
            chunks.append((o0, ow))
            o0 += ow
        wtoks = {}

        def load_chunk(ci):
            o0, ow = chunks[ci]
            b = ci % 2
            for wi in range(2):
                wf = (w1f, w2f)[wi]
                tok = None
                for i in range(NS):
                    tok = p.dma("sp", wall[wi][b][:, i * OCW:i * OCW + ow],
                                wf(o0, i * 128, ow, 128),
                                waits=list(w_ready) +
                                      ([wall_use[wi][b]] if wall_use[wi][b] else []),
                                sem=wsem[wi][b], transpose=True)
                wtoks[(ci, wi)] = (wsem[wi][b], tok[1])

        load_chunk(0)
        for ci, (o0, ow) in enumerate(chunks):
            if ci + 1 < len(chunks):
                load_chunk(ci + 1)
            b = ci % 2
            now = ow // 128
            w1r, w2r = wtoks[(ci, 0)], wtoks[(ci, 1)]
            mm_last = [None, None]
            for so in range(now):
                j = (o0 - qbase * 128) // 128 + so
                for tc in range(NTC):
                    b1, b2 = (tc % 2) * 2, (tc % 2) * 2 + 1
                    mm1 = mm2 = None
                    for i in range(NS):
                        mm1 = p.op("pe", lambda e, b1=b1, i=i, tc=tc, so=so, b=b:
                                   e.matmul(ctx.psum[b1][:],
                                            wall[0][b][:, i * OCW + so * 128:i * OCW + (so + 1) * 128],
                                            xfm[i][:, tc * 512:(tc + 1) * 512],
                                            start=(i == 0), stop=(i == NS - 1)),
                                   waits=[w1r, xfm_tok] + (bf[b1] if i == 0 else []), inc="pe")
                    for i in range(NS):
                        mm2 = p.op("pe", lambda e, b2=b2, i=i, tc=tc, so=so, b=b:
                                   e.matmul(ctx.psum[b2][:],
                                            wall[1][b][:, i * OCW + so * 128:i * OCW + (so + 1) * 128],
                                            xfm[i][:, tc * 512:(tc + 1) * 512],
                                            start=(i == 0), stop=(i == NS - 1)),
                                   waits=[w2r, xfm_tok] + (bf[b2] if i == 0 else []), inc="pe")
                    mm_last = [mm1, mm2]
                    a1 = p.op("act", lambda e, b1=b1: e.activation(evs[:], ctx.psum[b1][:], act_fn),
                              waits=[mm1] + ([evs_use] if evs_use else []), inc="act")
                    bf[b1] = [a1]
                    hv = p.op("dve", lambda e, b2=b2, j=j, tc=tc:
                              e.tensor_tensor(hid[j][:, tc * 512:(tc + 1) * 512], evs[:],
                                              ctx.psum[b2][:], ALU.mult),
                              waits=[a1, mm2] + ([hb_toks[j]] if (tc == 0 and hb_toks[j]) else []),
                              inc="dve")
                    bf[b2] = [hv]
                    evs_use = hv
                    hb_toks[j] = hv
            wall_use[0][b], wall_use[1][b] = mm_last
        w3toks = {}

        def load_w3(dcg):
            b = dcg % 2
            tok = None
            for j in range(qn):
                tok = p.dma("sp", wall3[b][:, j * 512:(j + 1) * 512],
                            w3f(dcg * 512, (qbase + j) * 128, 512, 128),
                            waits=list(w_ready) + ([wall3_use[b]] if wall3_use[b] else []),
                            sem=w3sem[b], transpose=True)
            w3toks[dcg] = (w3sem[b], tok[1])

        load_w3(0)
        for dcg in range(4):
            if dcg + 1 < 4:
                load_w3(dcg + 1)
            b = dcg % 2
            wr = w3toks[dcg]
            mml = None
            for ds in range(4):
                dc = dcg * 4 + ds
                for tc in range(NTC):
                    bi = 4 + (ds * NTC + tc) % 2
                    mm = None
                    for j in range(qn):
                        mm = p.op("pe", lambda e, bi=bi, j=j, tc=tc, ds=ds, qn=qn, b=b:
                                  e.matmul(ctx.psum[bi][:],
                                           wall3[b][:, j * 512 + ds * 128:j * 512 + (ds + 1) * 128],
                                           hid[j][:, tc * 512:(tc + 1) * 512],
                                           start=(j == 0), stop=(j == qn - 1)),
                                  waits=[wr, hb_toks[j]] + (bf[bi] if j == 0 else []), inc="pe")
                    mml = mm
                    row = slice(dc * 128, (dc + 1) * 128)
                    colc = slice(tc * 512, (tc + 1) * 512)
                    yslot = tc % 2
                    if qi == 0:
                        ye = p.op("act", lambda e, bi=bi, yslot=yslot:
                                  e.activation(ybuf[yslot][:], ctx.psum[bi][:], AF.Copy),
                                  waits=[mm] + ([ybuf_use[yslot]] if ybuf_use[yslot] else []),
                                  inc="act")
                        bf[bi] = [ye]
                        sx = p.dma("sp", y3p[row, colc], ybuf[yslot][:], waits=[ye], sem=yp_sem)
                        ybuf_use[yslot] = sx
                    else:
                        ptok = p.dma("sp", ypin[yslot][:], y3p[row, colc],
                                     waits=[p.tok(yp_sem)] +
                                           ([ypin_use[yslot]] if ypin_use[yslot] else []),
                                     sem=ypin_sem[yslot])
                        if qi < 3:
                            ye = p.op("dve", lambda e, bi=bi, yslot=yslot:
                                      e.tensor_tensor(ybuf[yslot][:], ctx.psum[bi][:],
                                                      ypin[yslot][:], ALU.add),
                                      waits=[mm, ptok] + ([ybuf_use[yslot]] if ybuf_use[yslot] else []),
                                      inc="dve")
                            bf[bi] = [ye]
                            ypin_use[yslot] = ye
                            sx = p.dma("sp", y3p[row, colc], ybuf[yslot][:], waits=[ye], sem=yp_sem)
                            ybuf_use[yslot] = sx
                        else:
                            ye = p.op("dve", lambda e, bi=bi, yslot=yslot:
                                      e.tensor_tensor(ybf[:], ctx.psum[bi][:], ypin[yslot][:],
                                                      ALU.add),
                                      waits=[mm, ptok] + ([ybf_use] if ybf_use else []), inc="dve")
                            bf[bi] = [ye]
                            ypin_use[yslot] = ye
                            sx = p.dma("sp", y_fm[row, colc], ybf[:], waits=[ye] + list(y_guard),
                                       sem=ys_sem)
                            ybf_use = sx
            wall3_use[b] = mml
    return [p.tok(ys_sem)]


def build_ffn_launch(cfg, consts, act_fn, F):
    nc = bass.Bass("TRN2", target_bir_lowering=False, debug=False)
    c = cfg
    T, D = c.T, c.D
    x_in = nc.dram_tensor("x_in", [T, D], F32, kind="ExternalInput").ap()
    w1T = nc.dram_tensor("w1_bf", [F, D], BF16, kind="ExternalInput").ap()
    w2T = nc.dram_tensor("w2_bf", [F, D], BF16, kind="ExternalInput").ap()
    w3T = nc.dram_tensor("w3_bf", [D, F], BF16, kind="ExternalInput").ap()
    x_out = nc.dram_tensor("x_out", [T, D], F32, kind="ExternalOutput").ap()
    const_aps = {}
    for name, arr in consts.items():
        dt = {np.dtype(np.float32): F32, np.dtype(BF): BF16, np.dtype(np.int32): I32}[arr.dtype]
        const_aps[name] = nc.dram_tensor(name, list(arr.shape), dt, kind="ExternalInput").ap()
    xln = nc.dram_tensor("s_xln", [T, D], BF16).ap()
    y_fm = nc.dram_tensor("s_yfm", [D, T], BF16).ap()
    y3p = nc.dram_tensor("s_y3p", [D, T], F32).ap()
    es = ExitStack()
    with es:
        ctx = Ctx(nc, es, c, const_aps)
        state = {}
        ln_toks = emit_sandwich(ctx, x_in, None, None, xln, True, state)
        w1f = lambda o0, d0, no, nd: w1T[o0:o0 + no, d0:d0 + nd]
        w2f = lambda o0, d0, no, nd: w2T[o0:o0 + no, d0:d0 + nd]
        w3f = lambda d0, h0, nd, nh: w3T[d0:d0 + nd, h0:h0 + nh]
        y_toks = emit_gated_ffn(ctx, xln, ln_toks, w1f, w2f, w3f, act_fn, F, y_fm, y3p, {})
        emit_sandwich(ctx, x_in, y_fm, x_out, xln, False, state, y_ready=y_toks, write_xln=False)
        final = ctx.p.all_toks()
        ctx.p.op("sp", lambda e: e.nop(), waits=final)
        ctx.p.emit(nc, es)
    return nc

COLS_W = 2048 // 8
ROWS_W3 = 2048 // 8
N_ELEM_W = 5632 * 2048


def _prep_tiles(cfg):
    return [
        ("w1s", "wq1", 5632 // 128, [(0, COLS_W)], 0),
        ("w2s", "wq2", 5632 // 128, [(0, COLS_W)], 1),
        ("w3s", "wq3", ROWS_W3 // 128, [(0, 2048), (2048, 2048), (4096, 1536)], 2),
    ]


def build_prep_a(cfg):
    nc = bass.Bass("TRN2", target_bir_lowering=False, debug=False)
    w1s = nc.dram_tensor("w1s", [5632, COLS_W], F32, kind="ExternalInput").ap()
    w2s = nc.dram_tensor("w2s", [5632, COLS_W], F32, kind="ExternalInput").ap()
    w3s = nc.dram_tensor("w3s", [ROWS_W3, 5632], F32, kind="ExternalInput").ap()
    gpart = nc.dram_tensor("gpart", [1, 3], F32, kind="ExternalOutput").ap()
    es = ExitStack()
    with es:
        ctx = Ctx(nc, es, cfg, {})
        emit_prep_a(ctx, {"w1s": w1s, "w2s": w2s, "w3s": w3s}, gpart)
        ctx.p.op("sp", lambda e: e.nop(), waits=ctx.p.all_toks())
        ctx.p.emit(nc, es)
    return nc


def emit_prep_a(ctx, w, gpart):
    p = ctx.p
    u = ctx.uid()
    ctx.push_scope()
    buf = [ctx.sb(f"pa{u}_b{b}", [128, 1024], F32) for b in range(2)]
    bsem = [f"pa{u}_s{b}" for b in range(2)]
    buse = [None, None]
    acc = ctx.sb(f"pa{u}_acc", [128, 3], F32)
    red = ctx.sb(f"pa{u}_red", [128, 1], F32)
    onesf = ctx.sb(f"pa{u}_ones", [128, 1], F32)
    gp = ctx.sb(f"pa{u}_gp", [1, 3], F32)
    ot = p.op("dve", lambda e: e.memset(onesf[:], 1.0), inc="dve")
    zt = p.op("dve", lambda e: e.memset(acc[:], 0.0), inc="dve")

    def groups(nrows, ncols, maxw=1024):
        ntile = nrows // 128
        if ncols >= maxw:
            for it in range(ntile):
                c0 = 0
                while c0 < ncols:
                    cw = min(maxw, ncols - c0)
                    yield it * 128, 1, c0, cw
                    c0 += cw
        else:
            rg = maxw // ncols
            it = 0
            while it < ntile:
                nt = min(rg, ntile - it)
                yield it * 128, nt, 0, ncols
                it += nt

    def gview(src2, r0, nt, c0, cw):
        if nt == 1:
            return src2[r0:r0 + 128, c0:c0 + cw]
        return src2[r0:r0 + nt * 128, c0:c0 + cw].rearrange("(t p) c -> p t c", p=128)

    def bview(t3, wdt, nt):
        if nt == 1:
            return t3[:, 0:wdt]
        return t3[:, 0:wdt].rearrange("p (t c) -> p t c", t=nt)

    red_use = None
    acc_use = zt
    for src2, j2, (nr, ncx) in ((w["w1s"], 0, (5632, COLS_W)), (w["w2s"], 1, (5632, COLS_W)),
                                (w["w3s"], 2, (ROWS_W3, 5632))):
        for r0, nt, c0, cw in groups(nr, ncx):
            b = (r0 // 128 + c0) % 2
            wdt = nt * cw
            tok = p.dma("act", bview(buf[b], wdt, nt), gview(src2, r0, nt, c0, cw),
                        waits=([buse[b]] if buse[b] else []), sem=bsem[b])
            r = p.op("dve", lambda e, b=b, wdt=wdt:
                     e.tensor_reduce(red[:], buf[b][:, 0:wdt], AX.X, ALU.add,
                                     apply_absolute_value=True),
                     waits=[tok] + ([red_use] if red_use else []), inc="dve")
            buse[b] = r
            a = p.op("dve", lambda e, j2=j2: e.tensor_tensor(acc[:, j2:j2 + 1], acc[:, j2:j2 + 1],
                                                             red[:], ALU.add),
                     waits=[r, acc_use], inc="dve")
            red_use = a
            acc_use = a
    prev = ctx.bank_free[0] or []
    mm = None
    for j2 in range(3):
        mm = p.op("pe", lambda e, j2=j2: e.matmul(ctx.psum[0][0:1, j2:j2 + 1], acc[:, j2:j2 + 1],
                                                  onesf[:], start=True, stop=True),
                  waits=[acc_use, ot] + prev + ([mm] if mm else []), inc="pe")
    ev = p.op("act", lambda e: e.activation(gp[:], ctx.psum[0][0:1, 0:3], AF.Copy),
              waits=[mm], inc="act")
    ctx.bank_free[0] = [ev]
    st = p.dma("po", gpart, gp[:], waits=[ev], sem=f"pa{u}_st")
    ctx.pop_scope()
    return st


def emit_prep_b(ctx, w, gparts, lut, outs, gamma_pre=None, gamma_wait=()):
    c, p = ctx.c, ctx.p
    u = ctx.uid()
    bf = ctx.bank_free
    ctx.push_scope()
    onesr = ctx.sb(f"pb{u}_or", [1, 128], F32)
    ones8 = ctx.sb(f"pb{u}_o8", [8, 1], F32)
    ot = p.op("dve", lambda e: e.memset(onesr[:], 1.0), inc="dve")
    o8 = p.op("dve", lambda e: e.memset(ones8[:], 1.0), inc="dve")
    if gamma_pre is None:
        gpt = ctx.sb(f"pb{u}_gp", [8, 3], F32)
        gptok = p.dma("act", gpt[:], gparts, sem=f"pb{u}_gp")
    lutr = ctx.sb(f"pb{u}_lut", [1, 16], F32)
    luttok = p.dma("act", lutr[:], lut, sem=f"pb{u}_lut")
    lutc = ctx.sb(f"pb{u}_lutc", [128, 16], F32)
    mm = p.op("pe", lambda e: e.matmul(ctx.psum[0][:, 0:16], onesr[:], lutr[:],
                                       start=True, stop=True),
              waits=[ot, luttok] + (bf[0] or []), inc="pe")
    lc = p.op("act", lambda e: e.activation(lutc[:], ctx.psum[0][:, 0:16], AF.Copy),
              waits=[mm], inc="act")
    bf[0] = [lc]
    grow = ctx.sb(f"pb{u}_grow", [1, 6], F32)
    if gamma_pre is None:
        gprev = None
        for j2 in range(3):
            gm = p.op("pe", lambda e, j2=j2: e.matmul(ctx.psum[1][0:1, j2:j2 + 1], gpt[:, j2:j2 + 1],
                                                      ones8[:], start=True, stop=True),
                      waits=[gptok, o8] + (bf[1] if j2 == 0 else []) + ([gprev] if gprev else []),
                      inc="pe")
            gprev = gm
        ge = p.op("act", lambda e: e.activation(grow[0:1, 0:3], ctx.psum[1][0:1, 0:3], AF.Copy),
                  waits=[gprev], inc="act")
        bf[1] = [ge]
    else:
        ge = p.dma("act", grow[0:1, 0:3], gamma_pre, waits=list(gamma_wait), sem=f"pb{u}_gpre")
    g2 = p.op("dve", lambda e: e.tensor_scalar(grow[0:1, 0:3], grow[0:1, 0:3],
                                               0.5 / N_ELEM_W, 0.5 * c.EPS_BIT,
                                               ALU.mult, ALU.add),
              waits=[ge], inc="dve")
    g3 = p.op("dve", lambda e: e.tensor_scalar(grow[0:1, 3:6], grow[0:1, 0:3], -1.0, None, ALU.mult),
              waits=[g2], inc="dve")
    thr = ctx.sb(f"pb{u}_thr", [128, 6], F32)
    bm = p.op("pe", lambda e: e.matmul(ctx.psum[2][:, 0:6], onesr[:], grow[:],
                                       start=True, stop=True),
              waits=[g3, ot] + (bf[2] or []), inc="pe")
    bt = p.op("act", lambda e: e.activation(thr[:], ctx.psum[2][:, 0:6], AF.Copy),
              waits=[bm], inc="act")
    bf[2] = [bt]

    def groups(nrows, ncols, maxw=1024):
        ntile = nrows // 128
        if ncols >= maxw:
            for it in range(ntile):
                c0 = 0
                while c0 < ncols:
                    cw = min(maxw, ncols - c0)
                    yield it * 128, 1, c0, cw
                    c0 += cw
        else:
            rg = maxw // ncols
            it = 0
            while it < ntile:
                nt = min(rg, ntile - it)
                yield it * 128, nt, 0, ncols
                it += nt

    def gview(src2, r0, nt, c0, cw):
        if nt == 1:
            return src2[r0:r0 + 128, c0:c0 + cw]
        return src2[r0:r0 + nt * 128, c0:c0 + cw].rearrange("(t p) c -> p t c", p=128)

    def bview(t3, wdt, nt):
        if nt == 1:
            return t3[:, 0:wdt]
        return t3[:, 0:wdt].rearrange("p (t c) -> p t c", t=nt)

    osem = f"pb{u}_os"
    ctx.push_scope()
    buf = [ctx.sb(f"pb{u}_b{b}", [128, 1024], F32) for b in range(2)]
    bsem = [f"pb{u}_s{b}" for b in range(2)]
    buse = [None, None]
    qa = ctx.sb(f"pb{u}_qa", [128, 1024], BF16)
    qb2 = ctx.sb(f"pb{u}_qb", [128, 1024], BF16)
    qo = ctx.sb(f"pb{u}_qo", [128, 1024], BF16)
    qo_use = None
    qa_use = None
    for src2, dst, j2, (nr, ncx) in ((w["w1s"], outs["wq1"], 0, (5632, COLS_W)),
                                     (w["w2s"], outs["wq2"], 1, (5632, COLS_W)),
                                     (w["w3s"], outs["wq3"], 2, (ROWS_W3, 5632))):
        for r0, nt, c0, cw in groups(nr, ncx):
            b = (r0 // 128 + c0) % 2
            wdt = nt * cw
            tok = p.dma("act", bview(buf[b], wdt, nt), gview(src2, r0, nt, c0, cw),
                        waits=([buse[b]] if buse[b] else []), sem=bsem[b])
            t1 = p.op("dve", lambda e, b=b, wdt=wdt, j2=j2:
                      e.tensor_scalar(qa[:, 0:wdt], buf[b][:, 0:wdt], thr[:, j2:j2 + 1],
                                      None, ALU.is_ge),
                      waits=[tok, bt] + ([qa_use] if qa_use else []), inc="dve")
            t2 = p.op("dve", lambda e, b=b, wdt=wdt, j2=j2:
                      e.tensor_scalar(qb2[:, 0:wdt], buf[b][:, 0:wdt], thr[:, 3 + j2:4 + j2],
                                      None, ALU.is_le),
                      waits=[tok, bt, t1], inc="dve")
            buse[b] = t2
            t3 = p.op("dve", lambda e, wdt=wdt: e.tensor_tensor(qo[:, 0:wdt], qa[:, 0:wdt],
                                                                qb2[:, 0:wdt], ALU.subtract),
                      waits=[t2] + ([qo_use] if qo_use else []), inc="dve")
            qa_use = t3
            st = p.dma("po", gview(dst, r0, nt, c0, cw), bview(qo, wdt, nt), waits=[t3], sem=osem)
            qo_use = st
    ctx.pop_scope()
    p.region([(s9, v9) for (s9, v9) in p.all_toks() if not s9.startswith("cc")])
    ctx.push_scope()
    ib = [ctx.sb(f"pb{u}_i{b}", [128, 1024], I32) for b in range(2)]
    isem = [f"pb{u}_is{b}" for b in range(2)]
    iuse = [None, None]
    ibf = ctx.sb(f"pb{u}_ibf", [128, 1024], BF16)
    macc = [ctx.sb(f"pb{u}_m{b}", [128, 1024], BF16) for b in range(2)]
    mtile = ctx.sb(f"pb{u}_mt", [128, 1024], BF16)
    ibf_use = None
    m_use = None
    for src2, dst, (nr, ncx) in ((w["gis"], outs["gw"], (5504, COLS_W)),
                                 (w["uis"], outs["uw"], (5504, COLS_W)),
                                 (w["dis"], outs["dw"], (ROWS_W3, 5504))):
        for r0, nt, c0, cw in groups(nr, ncx):
            b = (r0 // 128 + c0) % 2
            wdt = nt * cw
            tok = p.dma("act", bview(ib[b], wdt, nt), gview(src2, r0, nt, c0, cw),
                        waits=([iuse[b]] if iuse[b] else []), sem=isem[b])
            cv = p.op("dve", lambda e, b=b, wdt=wdt: e.tensor_copy(ibf[:, 0:wdt], ib[b][:, 0:wdt]),
                      waits=[tok] + ([ibf_use] if ibf_use else []), inc="dve")
            iuse[b] = cv
            prev = None
            for j2 in range(16):
                am = j2 % 2
                if j2 == 0:
                    prev = p.op("dve", lambda e, wdt=wdt, j2=j2, am=am:
                                e.tensor_scalar(macc[am][:, 0:wdt], ibf[:, 0:wdt],
                                                float(j2), lutc[:, j2:j2 + 1],
                                                ALU.is_equal, ALU.mult),
                                waits=[cv, lc] + ([m_use] if m_use else []), inc="dve")
                else:
                    mt = p.op("dve", lambda e, wdt=wdt, j2=j2:
                              e.tensor_scalar(mtile[:, 0:wdt], ibf[:, 0:wdt],
                                              float(j2), lutc[:, j2:j2 + 1],
                                              ALU.is_equal, ALU.mult),
                              waits=[cv, lc, prev], inc="dve")
                    prev = p.op("dve", lambda e, wdt=wdt, am=am:
                                e.tensor_tensor(macc[am][:, 0:wdt], macc[1 - am][:, 0:wdt],
                                                mtile[:, 0:wdt], ALU.add),
                                waits=[mt], inc="dve")
            ibf_use = prev
            st = p.dma("po", gview(dst, r0, nt, c0, cw), bview(macc[1], wdt, nt), waits=[prev], sem=osem)
            m_use = st
    ctx.pop_scope()
    ctx.pop_scope()
    return osem


def build_prep_b(cfg):
    nc = bass.Bass("TRN2", target_bir_lowering=False, debug=False)
    w1s = nc.dram_tensor("w1s", [5632, COLS_W], F32, kind="ExternalInput").ap()
    w2s = nc.dram_tensor("w2s", [5632, COLS_W], F32, kind="ExternalInput").ap()
    w3s = nc.dram_tensor("w3s", [ROWS_W3, 5632], F32, kind="ExternalInput").ap()
    gparts = nc.dram_tensor("gparts", [8, 3], F32, kind="ExternalInput").ap()
    lut = nc.dram_tensor("lut", [1, 16], F32, kind="ExternalInput").ap()
    gis = nc.dram_tensor("gis", [5504, COLS_W], I32, kind="ExternalInput").ap()
    uis = nc.dram_tensor("uis", [5504, COLS_W], I32, kind="ExternalInput").ap()
    dis = nc.dram_tensor("dis", [ROWS_W3, 5504], I32, kind="ExternalInput").ap()
    wq1 = nc.dram_tensor("wq1", [5632, COLS_W], BF16, kind="ExternalOutput").ap()
    wq2 = nc.dram_tensor("wq2", [5632, COLS_W], BF16, kind="ExternalOutput").ap()
    wq3 = nc.dram_tensor("wq3", [ROWS_W3, 5632], BF16, kind="ExternalOutput").ap()
    gw = nc.dram_tensor("gw", [5504, COLS_W], BF16, kind="ExternalOutput").ap()
    uw = nc.dram_tensor("uw", [5504, COLS_W], BF16, kind="ExternalOutput").ap()
    dw = nc.dram_tensor("dw", [ROWS_W3, 5504], BF16, kind="ExternalOutput").ap()
    es = ExitStack()
    with es:
        ctx = Ctx(nc, es, cfg, {})
        emit_prep_b(ctx, {"w1s": w1s, "w2s": w2s, "w3s": w3s, "gis": gis, "uis": uis,
                          "dis": dis}, gparts, lut,
                    {"wq1": wq1, "wq2": wq2, "wq3": wq3, "gw": gw, "uw": uw, "dw": dw})
        ctx.p.op("sp", lambda e: e.nop(), waits=ctx.p.all_toks())
        ctx.p.emit(nc, es)
    return nc


def _pad_rows(a, rows):
    if a.shape[0] == rows:
        return np.ascontiguousarray(a)
    out = np.zeros((rows,) + a.shape[1:], a.dtype)
    out[:a.shape[0]] = a
    return out


def kernel(L0_attn_x, L0_attn_w_dq, L0_attn_g_q, L0_attn_w_uq, L0_attn_w_dkv,
           L0_attn_w_kr, L0_attn_g_kv, L0_attn_w_uk, L0_attn_w_uv, L0_attn_w_o,
           L0_ffn_w1, L0_ffn_w2, L0_ffn_w3, L1_attn_w_qkv, L1_attn_w_out,
           L1_ffn_lut, L1_ffn_gate_idx, L1_ffn_up_idx, L1_ffn_down_idx):
    c = CFG
    n = 8
    cores = list(range(n))
    consts = host_constants(c)
    x = np.ascontiguousarray(np.asarray(L0_attn_x, dtype=np.float32))

    c = CFG
    n = 8
    cores = list(range(n))
    consts = host_constants(c)
    x = np.ascontiguousarray(np.asarray(L0_attn_x, dtype=np.float32))
    cw, r3 = 2048 // n, 2048 // n
    wl0 = {"w_dq": np.asarray(L0_attn_w_dq), "g_q": np.asarray(L0_attn_g_q),
           "w_uq": np.asarray(L0_attn_w_uq), "w_dkv": np.asarray(L0_attn_w_dkv),
           "w_kr": np.asarray(L0_attn_w_kr), "g_kv": np.asarray(L0_attn_g_kv),
           "w_uk": np.asarray(L0_attn_w_uk), "w_uv": np.asarray(L0_attn_w_uv),
           "w_o": np.asarray(L0_attn_w_o)}
    lutr = np.ascontiguousarray(np.asarray(L1_ffn_lut, dtype=np.float32).reshape(1, 16))
    nc = build_fused(c, consts, use_collectives=True)
    in_maps = []
    for i in range(n):
        im = {"x_in": x[i], **wl0, "w_qkv": np.asarray(L1_attn_w_qkv),
              "w_out": np.asarray(L1_attn_w_out), **consts,
              "w1s": np.ascontiguousarray(np.asarray(L0_ffn_w1)[:, i * cw:(i + 1) * cw]),
              "w2s": np.ascontiguousarray(np.asarray(L0_ffn_w2)[:, i * cw:(i + 1) * cw]),
              "w3s": np.ascontiguousarray(np.asarray(L0_ffn_w3)[i * r3:(i + 1) * r3]),
              "lut": lutr,
              "gis": np.ascontiguousarray(np.asarray(L1_ffn_gate_idx)[:, i * cw:(i + 1) * cw]),
              "uis": np.ascontiguousarray(np.asarray(L1_ffn_up_idx)[:, i * cw:(i + 1) * cw]),
              "dis": np.ascontiguousarray(np.asarray(L1_ffn_down_idx)[i * r3:(i + 1) * r3])}
        in_maps.append(im)
    res = run_bass_kernel_spmd(nc, in_maps, core_ids=cores)
    return np.stack([np.asarray(res.results[i]["x_out"]) for i in range(n)], 0)


SLAB_ROWS = 5632 + 5632 + 5504 + 5504
SLAB_OFF = {"wq1": 0, "wq2": 5632, "gw": 11264, "uw": 16768}


def build_fused(cfg, consts, use_collectives=True):
    nc = bass.Bass("TRN2", target_bir_lowering=False, debug=False)
    c = cfg
    T, D = c.T, c.D
    cores = list(range(8))
    x_in = nc.dram_tensor("x_in", [T, D], F32, kind="ExternalInput").ap()
    x_out = nc.dram_tensor("x_out", [T, D], F32, kind="ExternalOutput").ap()
    wl0names = {
        "w_dq": [D, c.QL], "g_q": [c.QL], "w_uq": [c.QL, c.H * c.QKD],
        "w_dkv": [D, c.KVL], "w_kr": [D, c.ROPE], "g_kv": [c.KVL],
        "w_uk": [c.KVL, c.H * c.NOPE], "w_uv": [c.KVL, c.H * c.VD], "w_o": [c.H * c.VD, D],
    }
    wl0 = {n: nc.dram_tensor(n, s, F32, kind="ExternalInput").ap() for n, s in wl0names.items()}
    w_qkv = nc.dram_tensor("w_qkv", [3 * D, D], F32, kind="ExternalInput").ap()
    w_out = nc.dram_tensor("w_out", [D, D], F32, kind="ExternalInput").ap()
    const_aps = {}
    for name, arr in consts.items():
        dt = {np.dtype(np.float32): F32, np.dtype(BF): BF16, np.dtype(np.int32): I32}[arr.dtype]
        const_aps[name] = nc.dram_tensor(name, list(arr.shape), dt, kind="ExternalInput").ap()

    xln = nc.dram_tensor("s_xln", [T, D], BF16).ap()
    y_fm = nc.dram_tensor("s_yfm", [D, T], BF16).ap()
    y3p = nc.dram_tensor("s_y3p", [D, T], F32).ap()
    x1 = nc.dram_tensor("s_x1", [T, D], F32).ap()
    x2 = nc.dram_tensor("s_x2", [T, D], F32).ap()
    x3 = nc.dram_tensor("s_x3", [T, D], F32).ap()
    scr0 = {
        "l0_q": nc.dram_tensor("s_l0q", [c.QL, T], BF16).ap(),
        "l0_k": nc.dram_tensor("s_l0k", [c.H, T // 128, 128, 128], BF16).ap(),
        "l0_v": nc.dram_tensor("s_l0v", [T // 128, c.H, 128, 128], BF16).ap(),
    }
    wl0_bf = {n: nc.dram_tensor("s_bf_" + n, s, BF16).ap() for n, s in wl0names.items()
              if n not in ("g_q", "g_kv")}
    scr1 = {
        "l1_wqkv_bf": nc.dram_tensor("s_wqkv_bf", [3 * D, D], BF16).ap(),
        "l1_wout_bf": nc.dram_tensor("s_wout_bf", [D, D], BF16).ap(),
        "l1_q": nc.dram_tensor("s_l1q", [D, T], BF16).ap(),
        "l1_v": nc.dram_tensor("s_l1v", [T // 128, c.H, 128, 128], BF16).ap(),
    }

    if use_collectives:
        w1s = nc.dram_tensor("w1s", [5632, COLS_W], F32, kind="ExternalInput").ap()
        w2s = nc.dram_tensor("w2s", [5632, COLS_W], F32, kind="ExternalInput").ap()
        w3s = nc.dram_tensor("w3s", [ROWS_W3, 5632], F32, kind="ExternalInput").ap()
        lut = nc.dram_tensor("lut", [1, 16], F32, kind="ExternalInput").ap()
        gis = nc.dram_tensor("gis", [5504, COLS_W], I32, kind="ExternalInput").ap()
        uis = nc.dram_tensor("uis", [5504, COLS_W], I32, kind="ExternalInput").ap()
        dis = nc.dram_tensor("dis", [ROWS_W3, 5504], I32, kind="ExternalInput").ap()
        gpart = nc.dram_tensor("s_gpart", [1, 3], F32).ap()
        gsum = nc.dram_tensor("s_gsum", [1, 3], F32, addr_space="Shared").ap()
        slab_l = nc.dram_tensor("s_slab_l", [SLAB_ROWS, COLS_W], BF16).ap()
        wq3_l = nc.dram_tensor("s_wq3l", [ROWS_W3, 5632], BF16).ap()
        dw_l = nc.dram_tensor("s_dwl", [ROWS_W3, 5504], BF16).ap()
        slab_g = nc.dram_tensor("s_slab_g", [8 * SLAB_ROWS, COLS_W], BF16, addr_space="Shared").ap()
        wq3_g = nc.dram_tensor("s_wq3g", [D, 5632], BF16, addr_space="Shared").ap()
        dw_g = nc.dram_tensor("s_dwg", [D, 5504], BF16, addr_space="Shared").ap()
    else:
        slab_g = nc.dram_tensor("slab_g", [8 * SLAB_ROWS, COLS_W], BF16, kind="ExternalInput").ap()
        wq3_g = nc.dram_tensor("wq3_g", [D, 5632], BF16, kind="ExternalInput").ap()
        dw_g = nc.dram_tensor("dw_g", [D, 5504], BF16, kind="ExternalInput").ap()

    def slab_wf(key):
        off = SLAB_OFF[key]
        def wf(o0, d0, no, nd, off=off):
            r, cc = d0 // COLS_W, d0 % COLS_W
            base = r * SLAB_ROWS + off
            return slab_g[base + o0:base + o0 + no, cc:cc + nd]
        return wf

    es = ExitStack()
    with es:
        ctx = Ctx(nc, es, c, const_aps)
        ctx.sandwich_bufs()
        p = ctx.p
        for nm2, ap2 in wl0_bf.items():
            rows = wl0names[nm2][0]
            r0 = 0
            while r0 < rows:
                rr = min(512, rows - r0)
                p.dma("po", ap2[r0:r0 + rr, :], wl0[nm2][r0:r0 + rr, :], sem="l0cast")
                r0 += rr
        l0cast_tok = p.tok("l0cast")
        ffn_ready = []
        if use_collectives:
            ctx.push_scope()
            st_a = emit_prep_a(ctx, {"w1s": w1s, "w2s": w2s, "w3s": w3s}, gpart)
            cc0 = p.op("po", lambda e: e.collective_compute(
                "AllReduce", ALU.add, replica_groups=[cores], ins=[gpart], outs=[gsum]),
                waits=[st_a], inc="cc0")
            osem = emit_prep_b(
                ctx, {"w1s": w1s, "w2s": w2s, "w3s": w3s, "gis": gis, "uis": uis, "dis": dis},
                None, lut,
                {"wq1": slab_l[SLAB_OFF["wq1"]:SLAB_OFF["wq1"] + 5632, :],
                 "wq2": slab_l[SLAB_OFF["wq2"]:SLAB_OFF["wq2"] + 5632, :],
                 "gw": slab_l[SLAB_OFF["gw"]:SLAB_OFF["gw"] + 5504, :],
                 "uw": slab_l[SLAB_OFF["uw"]:SLAB_OFF["uw"] + 5504, :],
                 "wq3": wq3_l, "dw": dw_l},
                gamma_pre=gsum, gamma_wait=[cc0])
            prep_done = p.tok(osem)
            cc1 = p.op("po", lambda e: e.collective_compute(
                "AllGather", ALU.bypass, replica_groups=[cores], ins=[slab_l], outs=[slab_g]),
                waits=[prep_done], inc="cc1")
            cc2 = p.op("po", lambda e: e.collective_compute(
                "AllGather", ALU.bypass, replica_groups=[cores], ins=[wq3_l], outs=[wq3_g]),
                waits=[prep_done], inc="cc2")
            cc3 = p.op("po", lambda e: e.collective_compute(
                "AllGather", ALU.bypass, replica_groups=[cores], ins=[dw_l], outs=[dw_g]),
                waits=[prep_done], inc="cc3")
            ffn_ready = [cc1, cc2, cc3]
        state = {}
        s0 = emit_sandwich(ctx, x_in, None, None, xln, True, state)
        ctx.push_scope()
        yt = emit_l0_attn(ctx, xln, s0, wl0, y_fm, scr0, wbf=wl0_bf, cast_ready=[l0cast_tok])
        ctx.pop_scope()
        if use_collectives:
            ctx.pop_scope()
        B = [(s, v) for (s, v) in p.all_toks() if not s.startswith("cc")]
        p.region(B)
        s1 = emit_sandwich(ctx, x_in, y_fm, x1, xln, False, state, y_ready=yt)
        yg = [p.tok("sw_ys0"), p.tok("sw_ys1")]
        p.region(B)
        ctx.push_scope()
        yt = emit_gated_ffn(ctx, xln, s1, slab_wf("wq1"), slab_wf("wq2"),
                            lambda d0, h0, nd, nh: wq3_g[d0:d0 + nd, h0:h0 + nh],
                            AF.Sigmoid, 5632, y_fm, y3p, {}, w_ready=ffn_ready, y_guard=yg)
        ctx.pop_scope()
        B = [(s, v) for (s, v) in p.all_toks() if not s.startswith("cc")]
        p.region(B)
        s2 = emit_sandwich(ctx, x1, y_fm, x2, xln, False, state, y_ready=yt)
        yg = [p.tok("sw_ys0"), p.tok("sw_ys1")]
        p.region(B)
        ctx.push_scope()
        yt = emit_l1_attn(ctx, xln, s2, w_qkv, w_out, y_fm, scr1, y_guard=yg)
        ctx.pop_scope()
        B = [(s, v) for (s, v) in p.all_toks() if not s.startswith("cc")]
        p.region(B)
        s3 = emit_sandwich(ctx, x2, y_fm, x3, xln, False, state, y_ready=yt)
        yg = [p.tok("sw_ys0"), p.tok("sw_ys1")]
        p.region(B)
        ctx.push_scope()
        yt = emit_gated_ffn(ctx, xln, s3, slab_wf("gw"), slab_wf("uw"),
                            lambda d0, h0, nd, nh: dw_g[d0:d0 + nd, h0:h0 + nh],
                            AF.Silu, 5504, y_fm, y3p, {}, w_ready=ffn_ready, y_guard=yg)
        ctx.pop_scope()
        B = [(s, v) for (s, v) in p.all_toks() if not s.startswith("cc")]
        p.region(B)
        emit_sandwich(ctx, x3, y_fm, x_out, xln, False, state, y_ready=yt, write_xln=False)
        p.op("sp", lambda e: e.nop(), waits=p.all_toks())
        p.emit(nc, es)
    return nc
